# version 04
# baseline (speedup 1.0000x reference)
import numpy as np
import math
from contextlib import ExitStack
import ml_dtypes
import concourse.bass as bass
import concourse.mybir as mybir
from concourse.bass_utils import run_bass_kernel_spmd
from concourse.alu_op_type import AluOpType

F32 = mybir.dt.float32
BF16 = mybir.dt.bfloat16
AF = mybir.ActivationFunctionType
AX = mybir.AxisListType
BF = ml_dtypes.bfloat16

QUEUE_ENG = {'qg': 'g', 'qs': 'y', 'qa': 's'}
DMA_KEYS = ('qg', 'qs', 'qa')


class Buf:
    __slots__ = ('h', 'name', 'last_w', 'readers')
    def __init__(self, h, name):
        self.h = h
        self.name = name
        self.last_w = None
        self.readers = []
    def __getitem__(self, sl):
        return self.h[sl]
    @property
    def ap(self):
        return self.h[:]


class K:
    def __init__(self, name="prog"):
        self.nc = bass.Bass("TRN2", target_bir_lowering=False, debug=False)
        self.stack = ExitStack()
        self.eng = {'t': self.nc.tensor, 'v': self.nc.vector, 's': self.nc.scalar,
                    'g': self.nc.gpsimd, 'y': self.nc.sync}
        self.cnt = {k: 0 for k in ('t', 'v', 's', 'g', 'y')}
        self.dma_cnt = {k: 0 for k in ('qg', 'qs', 'qa')}
        self.water = {k: {} for k in ('t', 'v', 's', 'g', 'y')}
        self.sems = None
        self.block = None
        self.ops = 0
        self.bufs = []
        self.cc_cnt = 0
        self._ext_in = {}
        self._ext_out = {}

    def dram_in(self, name, shape, dtype=F32):
        ap = self.nc.dram_tensor(name, list(shape), dtype, kind="ExternalInput").ap()
        b = Buf(ap, name)
        self.bufs.append(b)
        self._ext_in[name] = b
        return b

    def dram_out(self, name, shape, dtype=F32):
        ap = self.nc.dram_tensor(name, list(shape), dtype, kind="ExternalOutput").ap()
        b = Buf(ap, name)
        self.bufs.append(b)
        self._ext_out[name] = b
        return b

    def dram_tmp(self, name, shape, dtype=F32):
        ap = self.nc.dram_tensor(name, list(shape), dtype, kind="Internal").ap()
        b = Buf(ap, name)
        self.bufs.append(b)
        return b

    def begin(self, n_phases=12, n_consts=128):
        self._consts_in = self.dram_in("consts_pool", [128, n_consts], F32)
        self.semsets = []
        for p in range(n_phases):
            s = {}
            for kk in ('t', 'v', 's', 'g', 'y', 'qg', 'qs', 'qa'):
                s[kk] = self.stack.enter_context(self.nc.semaphore(f"sm{p}_{kk}"))
            self.semsets.append(s)
        self.cc_sem = self.stack.enter_context(self.nc.semaphore("sm_cc"))
        self.phase = 0
        self.sems = dict(self.semsets[0])
        self.sems['cc'] = self.cc_sem
        self.block = self.stack.enter_context(self.nc.Block())
        ids = sorted({s.num for st in self.semsets for s in st.values()} | {self.cc_sem.num})
        if ids == list(range(ids[0], ids[-1] + 1)):
            self.eng['g'].sem_clear(range(ids[0], ids[-1] + 1))
        else:
            for st in self.semsets:
                for s in st.values():
                    self.eng['g'].sem_clear(s)
            self.eng['g'].sem_clear(self.cc_sem)
        self.nc.all_engine_barrier()
        self._consts_s = self.sbuf(self.stack, "consts_s", [128, n_consts], F32)
        self._const_map = {}
        self._n_consts = n_consts
        self.dma('qs', self._consts_s.ap, self._consts_in.ap, [self._consts_in], [self._consts_s])
        return self

    def constf(self, val):
        val = float(val)
        if val not in self._const_map:
            i = len(self._const_map)
            assert i < self._n_consts, "const pool full"
            self._const_map[val] = i
            self.nc.const_aps.register(F32, val, self._consts_s[:, i:i+1])
        i = self._const_map[val]
        return self._consts_s[:, i:i+1]

    def const_array(self):
        arr = np.zeros((128, self._n_consts), np.float32)
        for v, i in self._const_map.items():
            arr[:, i] = v
        return arr

    def sbuf(self, stack, name, shape, dtype=F32):
        h = stack.enter_context(self.nc.sbuf_tensor(name, list(shape), dtype))
        b = Buf(h, name)
        self.bufs.append(b)
        return b

    def psum(self, stack, name, shape, dtype=F32):
        h = stack.enter_context(self.nc.psum_tensor(name, list(shape), dtype))
        b = Buf(h, name)
        self.bufs.append(b)
        return b

    def end(self):
        self.barrier_all()
        self.stack.close()
        self.nc.finalize()
        return self.nc

    def _wait(self, consumer, dep_key, dep_val):
        if dep_val <= 0:
            return
        if dep_key in DMA_KEYS:
            dep_val = self.dma_cnt[dep_key]
        w = self.water[consumer]
        if w.get(dep_key, 0) >= dep_val:
            return
        mult = 16 if dep_key in DMA_KEYS else 1
        self.eng[consumer].wait_ge(self.sems[dep_key], dep_val * mult)
        w[dep_key] = dep_val

    def _deps(self, ekey, reads, writes):
        for b in reads:
            if b.last_w is not None:
                pk, pv = b.last_w
                if pk != ekey or ekey != 't':
                    self._wait(ekey, pk, pv)
        for b in writes:
            if b.last_w is not None:
                pk, pv = b.last_w
                if pk != ekey or ekey != 't':
                    self._wait(ekey, pk, pv)
            for (rk, rv) in b.readers:
                if rk != ekey or ekey != 't':
                    self._wait(ekey, rk, rv)

    def _post(self, key, reads, writes):
        if key in ('qg', 'qs', 'qa'):
            self.dma_cnt[key] += 1
            tag = (key, self.dma_cnt[key])
        else:
            self.cnt[key] += 1
            tag = (key, self.cnt[key])
        for b in reads:
            b.readers.append(tag)
        for b in writes:
            b.last_w = tag
            b.readers = []
        self.ops += 1
        return tag

    def op(self, ekey, fn, reads=(), writes=()):
        self._deps(ekey, list(reads), list(writes))
        inst = fn(self.eng[ekey])
        inst.then_inc(self.sems[ekey], 1)
        return self._post(ekey, list(reads), list(writes))

    def dma(self, qkey, out_ap, in_ap, reads=(), writes=(), **kw):
        ekey = QUEUE_ENG[qkey]
        reads, writes = list(reads), list(writes)
        for b in reads:
            if b.last_w is not None:
                pk, pv = b.last_w
                self._wait(ekey, pk, pv)
        for b in writes:
            if b.last_w is not None:
                pk, pv = b.last_w
                self._wait(ekey, pk, pv)
            for (rk, rv) in b.readers:
                if rk != ekey:
                    self._wait(ekey, rk, rv)
        self.eng[ekey].dma_start(out=out_ap, in_=in_ap, **kw).then_inc(self.sems[qkey], 16)
        return self._post(qkey, reads, writes)

    def collective(self, in_ap, out_ap, reads, out_buf):
        ekey = 'g'
        for b in list(reads):
            if b.last_w is not None:
                pk, pv = b.last_w
                self._wait(ekey, pk, pv)
        self.eng['g'].collective_compute(
            "AllGather", AluOpType.bypass,
            replica_groups=[list(range(8))],
            ins=[in_ap.opt()], outs=[out_ap.opt()],
        ).then_inc(self.cc_sem)
        self.cc_cnt += 1
        tag = ('cc', self.cc_cnt)
        for b in list(reads):
            b.readers.append(tag)
        out_buf.last_w = tag
        out_buf.readers = []
        self.ops += 1
        return tag

    def barrier_all(self):
        snap = dict(self.cnt)
        dsnap = dict(self.dma_cnt)
        for ekey in ('t', 'v', 's', 'g', 'y'):
            for pk, pv in snap.items():
                if pk != ekey:
                    self._wait(ekey, pk, pv)
            for pk, pv in dsnap.items():
                self._wait(ekey, pk, pv)

    def phase_boundary(self):
        self.barrier_all()
        self.phase += 1
        self.sems = dict(self.semsets[self.phase])
        self.sems['cc'] = self.cc_sem
        self.cnt = {k: 0 for k in ('t', 'v', 's', 'g', 'y')}
        self.dma_cnt = {k: 0 for k in ('qg', 'qs', 'qa')}
        for w in self.water.values():
            cckeep = w.get('cc')
            w.clear()
            if cckeep:
                w['cc'] = cckeep
        for b in self.bufs:
            if not (b.last_w and b.last_w[0] == 'cc'):
                b.last_w = None
            b.readers = [r for r in b.readers if r[0] == 'cc']

    def mm(self, out_ap, lT_ap, r_ap, rd, wr, start=True, stop=True):
        return self.op('t', lambda e: e.matmul(out_ap, lT_ap, r_ap, start=start, stop=stop),
                       reads=rd, writes=wr)

    def trans(self, out_ap, in_ap, ident_ap, rd, wr):
        return self.op('t', lambda e: e.transpose(out_ap, in_ap, ident_ap), reads=rd, writes=wr)

    def act(self, out_ap, in_ap, func, rd, wr, **kw):
        rd = list(rd)
        try:
            nparts = out_ap.shape[0]
        except Exception:
            nparts = 128
        for key in ('bias', 'scale'):
            v = kw.get(key)
            if isinstance(v, (int, float)):
                self.constf(v)
                i = self._const_map[float(v)]
                kw[key] = self._consts_s[0:nparts, i:i+1]
                if self._consts_s not in rd:
                    rd.append(self._consts_s)
        return self.op('s', lambda e: e.activation(out_ap, in_ap, func, **kw), reads=rd, writes=wr)

    def tt(self, out_ap, a_ap, b_ap, aop, rd, wr, eng='v'):
        return self.op(eng, lambda e: e.tensor_tensor(out_ap, a_ap, b_ap, aop), reads=rd, writes=wr)

    def ts(self, out_ap, a_ap, s1, s2, op0, op1, rd, wr, eng='v', **kw):
        if s2 is None:
            return self.op(eng, lambda e: e.tensor_scalar(out_ap, a_ap, s1, None, op0, **kw), reads=rd, writes=wr)
        return self.op(eng, lambda e: e.tensor_scalar(out_ap, a_ap, s1, s2, op0, op1, **kw), reads=rd, writes=wr)

    def stt(self, out_ap, in0_ap, scalar, in1_ap, op0, op1, rd, wr, eng='v'):
        return self.op(eng, lambda e: e.scalar_tensor_tensor(out_ap, in0_ap, scalar, in1_ap, op0, op1), reads=rd, writes=wr)

    def copy(self, out_ap, in_ap, rd, wr, eng='v'):
        return self.op(eng, lambda e: e.tensor_copy(out_ap, in_ap), reads=rd, writes=wr)

    def memset(self, ap, val, rd, wr, eng='v'):
        return self.op(eng, lambda e: e.memset(ap, val), reads=rd, writes=wr)


N_TOK, D = 8192, 2048
NCORE = 8
SEG = N_TOK // NCORE
NT = SEG // 128
ND = D // 128
H, HD, CH = 16, 128, 128
NCH = SEG // CH
NHID = 64
TC = 64
NCHK = SEG // TC
DS = 64
NE, FM, FS = 8, 1408, 5632
NOC_M = FM // 128
SERIALIZE_COLL = False
GU_SH = 2 * 4 * D // NCORE
P_SH = D // NCORE


def _slopes(n_heads):
    start = 2.0 ** (-2.0 ** (-(math.log2(n_heads) - 3)))
    return np.array([start ** (i + 1) for i in range(n_heads)], dtype=np.float32)

LAM = np.exp(-_slopes(H)).astype(np.float32)
POS = np.arange(CH, dtype=np.float32)
DECAY_Q = (LAM[:, None] ** (POS + 1.0)).astype(np.float32)
DECAY_K = (LAM[:, None] ** (CH - 1.0 - POS)).astype(np.float32)
_diff = POS[:, None] - POS[None, :]
DMASK = np.where(_diff >= 0, LAM[:, None, None] ** np.maximum(_diff, 0.0), 0.0).astype(np.float32)
LAM_C = (LAM ** CH).astype(np.float32)
LAM_SEG = (LAM ** SEG).astype(np.float32)


def host_tables():
    t = {}
    t['ident_bf'] = np.eye(128, dtype=BF)
    t['ident_f32'] = np.eye(128, dtype=np.float32)
    t['dq_b'] = np.broadcast_to(DECAY_Q[:, None, :], (H, 128, 128)).copy()
    t['dk_col'] = DECAY_K.T.copy()
    t['dmaskT'] = DMASK.transpose(0, 2, 1).copy()
    idx = np.arange(TC)
    t['tli_f32'] = (idx[:, None] <= idx[None, :]).astype(np.float32)
    t['strilneg_bf'] = np.where(idx[:, None] > idx[None, :], -1.0, 0.0).astype(BF)
    t['striuneg_bf'] = np.where(idx[:, None] < idx[None, :], -1.0, 0.0).astype(BF)
    t['striu_bf'] = (idx[:, None] < idx[None, :]).astype(BF)
    t['striuincl_bf'] = (idx[:, None] <= idx[None, :]).astype(BF)
    t['ones64c_f32'] = np.ones((TC, 1), np.float32)
    t['ones64r_f32'] = np.ones((1, TC), np.float32)
    t['ones128c_bf'] = np.ones((128, 1), BF)
    t['ones1r_f32'] = np.ones((1, 128), np.float32)
    gmask = np.zeros((128, 2), np.float32); gmask[0:64, 0] = 1; gmask[64:128, 1] = 1
    t['gmask'] = gmask.astype(BF)
    t['gmask2f'] = gmask.T.copy()
    t['gmask2b'] = gmask.T.astype(BF)
    return t


def emit_ln_tile(k, x_ap, x_buf, out_ap, out_buf, stats, eps=1e-5, width=D):
    nblk = (width + 511) // 512
    ntrip = 2 * nblk
    for i in range(nblk):
        w = min(512, width - i*512)
        k.op('v', lambda e, i=i, w=w: e.bn_stats(stats[:, 6*i:6*i+6], x_ap[:, i*512:i*512+w]),
             reads=[x_buf], writes=[stats])
    sv = stats.h[:, 0:6*nblk].rearrange("p (n t) -> p t n", t=3)
    mview, m2view = sv[:, 1, :], sv[:, 2, :]
    k.op('v', lambda e: e.reduce_sum(stats[:, 24:25], mview, AX.X), reads=[stats], writes=[stats])
    k.op('v', lambda e: e.reduce_sum(stats[:, 25:26], m2view, AX.X), reads=[stats], writes=[stats])
    k.op('v', lambda e: e.tensor_tensor(stats[:, 32:32+ntrip], mview, mview, AluOpType.mult),
         reads=[stats], writes=[stats])
    k.op('v', lambda e: e.reduce_sum(stats[:, 26:27], stats[:, 32:32+ntrip], AX.X),
         reads=[stats], writes=[stats])
    k.ts(stats[:, 27:28], stats[:, 24:25], 1.0/ntrip, None, AluOpType.mult, None, [stats], [stats])
    k.ts(stats[:, 28:29], stats[:, 25:26], 1.0/width, None, AluOpType.mult, None, [stats], [stats])
    k.stt(stats[:, 28:29], stats[:, 26:27], 1.0/ntrip, stats[:, 28:29], AluOpType.mult, AluOpType.add,
          [stats], [stats])
    k.tt(stats[:, 30:31], stats[:, 27:28], stats[:, 27:28], AluOpType.mult, [stats], [stats])
    k.tt(stats[:, 28:29], stats[:, 28:29], stats[:, 30:31], AluOpType.subtract, [stats], [stats])
    k.act(stats[:, 26:27], stats[:, 28:29], AF.Ln, [stats], [stats], bias=float(eps))
    k.act(stats[:, 31:32], stats[:, 26:27], AF.Exp, [stats], [stats], scale=-0.5)
    k.stt(stats[:, 30:31], stats[:, 27:28], -1.0, stats[:, 31:32], AluOpType.mult, AluOpType.mult,
          [stats], [stats])
    k.ts(out_ap, x_ap, stats[:, 31:32], stats[:, 30:31], AluOpType.mult, AluOpType.add,
         [x_buf, stats], [out_buf])


def build_fused(debug=False, debug2=False):
    k = K()
    x_in = k.dram_in("x_seg", [SEG, D], F32)
    wqkvT = k.dram_in("wqkvT", [D, 3*D], F32)
    bqkv_cols = k.dram_in("bqkv_cols", [128, 3*ND], F32)
    bv_b = k.dram_in("bv_b", [128, D], F32)
    woT = k.dram_in("woT", [D, D], F32)
    bo_b = k.dram_in("bo_b", [128, D], F32)
    sw_in = k.dram_in("sw", [128, NCORE*H], F32)
    wgu_in = k.dram_in("wguT_sh", [D, GU_SH], F32)
    wp_in = k.dram_in("wpT_sh", [4*D, P_SH], F32)
    wkvqaT_in = k.dram_in("wkvqaT", [D, 4*DS], F32)
    wab_in = k.dram_in("wa_b", [DS, 1], F32)
    lam_in = k.dram_in("lam", [DS, 1], F32)
    dnsel_in = k.dram_in("dnsel", [64, 16], F32)
    nwb_in = k.dram_in("nwb", [64, NCHK*64], F32)
    wo1T_in = k.dram_in("wo1T", [DS, D], F32)
    gw_in = k.dram_in("gate9T", [D, 16], F32)
    efc1_in = k.dram_in("efc1T", [NE, D, FM], F32)
    efc2_in = k.dram_in("efc2T", [NE, D, FM], F32)
    eprj_in = k.dram_in("eprjT", [NE, FM, D], F32)
    sfc1_in = k.dram_in("sfc1T", [D, FS], F32)
    sfc2_in = k.dram_in("sfc2T", [D, FS], F32)
    sprj_in = k.dram_in("sprjT", [FS, D], F32)
    ident = k.dram_in("ident_bf", [128, 128], BF16)
    identf_in = k.dram_in("ident_f32", [128, 128], F32)
    dq_b = k.dram_in("dq_b", [H, 128, 128], F32)
    dk_col = k.dram_in("dk_col", [128, H], F32)
    dmaskT = k.dram_in("dmaskT", [H, 128, 128], F32)
    tli_in = k.dram_in("tli_f32", [TC, TC], F32)
    sln_in = k.dram_in("strilneg_bf", [TC, TC], BF16)
    sun_in = k.dram_in("striuneg_bf", [TC, TC], BF16)
    su_in = k.dram_in("striu_bf", [TC, TC], BF16)
    sui_in = k.dram_in("striuincl_bf", [TC, TC], BF16)
    oc_in = k.dram_in("ones64c_f32", [64, 1], F32)
    or_in = k.dram_in("ones64r_f32", [1, 64], F32)
    o128c_in = k.dram_in("ones128c_bf", [128, 1], BF16)
    o1r_in = k.dram_in("ones1r_f32", [1, 128], F32)
    gm_in = k.dram_in("gmask", [128, 2], BF16)
    gm2f_in = k.dram_in("gmask2f", [2, 128], F32)
    gm2b_in = k.dram_in("gmask2b", [2, 128], BF16)
    out_out = k.dram_out("out_seg", [SEG, D], F32)
    x1_d = k.dram_tmp("x1_d", [SEG, D], F32)
    x2_d = k.dram_tmp("x2_d", [SEG, D], F32)
    x3_d = k.dram_tmp("x3_d", [SEG, D], F32)
    qgu_sh = k.dram_tmp("qgu_sh", [D, GU_SH], BF16)
    qgu_g = k.dram_tmp("qgu_g", [NCORE*D, GU_SH], BF16)
    qp_sh = k.dram_tmp("qp_sh", [4*D, P_SH], BF16)
    qp_g = k.dram_tmp("qp_g", [NCORE*4*D, P_SH], BF16)
    hd_d = k.dram_tmp("hd_d", [128, NHID*SEG], BF16)
    sseg_l = k.dram_tmp("sseg_l", [H*128, 128], F32)
    sseg_g = k.dram_tmp("sseg_g", [NCORE*H*128, 128], F32)
    dn_l = k.dram_tmp("dn_l", [128, 64], F32)
    dn_g = k.dram_tmp("dn_g", [NCORE*128, 64], F32)

    k.begin(n_phases=12)

    ident_s = k.sbuf(k.stack, "ident_s", [128, 128], BF16)
    identf = k.sbuf(k.stack, "identf", [128, 128], F32)
    stats = k.sbuf(k.stack, "stats", [128, 48], F32)
    k.dma('qs', ident_s.ap, ident.ap, [ident], [ident_s])
    k.dma('qs', identf.ap, identf_in.ap, [identf_in], [identf])

    with ExitStack() as ph:
        wt = [k.sbuf(ph, f"q_wt{i}", [128, 512], BF16) for i in range(2)]
        sg = k.sbuf(ph, "q_sg", [128, 512], BF16)
        wabs = k.sbuf(ph, "q_wabs", [128, 512], BF16)
        s_bf = k.sbuf(ph, "q_sbf", [2, 512], BF16)
        rs = k.sbuf(ph, "q_rs", [2, 512], F32)
        z = k.sbuf(ph, "q_z", [128, 512], F32)
        m = k.sbuf(ph, "q_m", [128, 512], F32)
        w1 = k.sbuf(ph, "q_w1", [128, 512], BF16)
        wq = [k.sbuf(ph, f"q_wq{i}", [128, 512], BF16) for i in range(2)]
        gm = k.sbuf(ph, "q_gm", [128, 2], BF16)
        gm2f = k.sbuf(ph, "q_gm2f", [2, 128], F32)
        gm2b = k.sbuf(ph, "q_gm2b", [2, 128], BF16)
        psG = k.psum(ph, "q_psG", [128, 512], F32)
        psR = k.psum(ph, "q_psR", [128, 512], F32)
        psS = k.psum(ph, "q_psS", [128, 512], F32)
        k.dma('qs', gm.ap, gm_in.ap, [gm_in], [gm])
        k.dma('qs', gm2f.ap, gm2f_in.ap, [gm2f_in], [gm2f])
        k.dma('qs', gm2b.ap, gm2b_in.ap, [gm2b_in], [gm2b])

        def quant_tile(src_ap, src_buf, dst_ap, dst_buf, F, idx):
            wtb, wqb = wt[idx % 2], wq[idx % 2]
            k.dma('qg', wtb[:, 0:F], src_ap, [src_buf], [wtb])
            k.act(sg[:, 0:F], wtb[:, 0:F], AF.Sign, [wtb], [sg])
            k.act(wabs[:, 0:F], wtb[:, 0:F], AF.Abs, [wtb], [wabs])
            k.mm(psG[0:2, 0:F], gm.ap, wabs[:, 0:F], [gm, wabs], [psG])
            k.ts(s_bf[:, 0:F], psG[0:2, 0:F], 1.0/64.0, 1e-8, AluOpType.mult, AluOpType.max,
                 [psG], [s_bf])
            k.op('v', lambda e, F=F: e.reciprocal(rs[:, 0:F], s_bf[:, 0:F]), reads=[s_bf], writes=[rs])
            k.mm(psR[:, 0:F], gm2f.ap, rs[:, 0:F], [gm2f, rs], [psR])
            k.mm(psS[:, 0:F], gm2b.ap, s_bf[:, 0:F], [gm2b, s_bf], [psS])
            k.tt(z[:, 0:F], wabs[:, 0:F], psR[:, 0:F], AluOpType.mult, [wabs, psR], [z])
            k.ts(m[:, 0:F], z[:, 0:F], 0.5, None, AluOpType.is_gt, None, [z], [m])
            k.tt(w1[:, 0:F], m[:, 0:F], psS[:, 0:F], AluOpType.mult, [m, psS], [w1])
            k.tt(wqb[:, 0:F], w1[:, 0:F], sg[:, 0:F], AluOpType.mult, [w1, sg], [wqb])
            k.dma('qs', dst_ap, wqb[:, 0:F], [wqb], [dst_buf])

        wgur = wgu_in.h.rearrange("(t p) c -> t p c", p=128)
        qgur = qgu_sh.h.rearrange("(t p) c -> t p c", p=128)
        idx = 0
        for t in range(D // 128):
            for f in range(GU_SH // 512):
                quant_tile(wgur[t, :, f*512:(f+1)*512], wgu_in, qgur[t, :, f*512:(f+1)*512],
                           qgu_sh, 512, idx); idx += 1
        wpr = wp_in.h.rearrange("(t p) c -> t p c", p=128)
        qpr = qp_sh.h.rearrange("(t p) c -> t p c", p=128)
        for t in range(4*D // 128):
            quant_tile(wpr[t, :, 0:P_SH], wp_in, qpr[t, :, 0:P_SH], qp_sh, P_SH, idx); idx += 1

        k.collective(qgu_sh.h, qgu_g.h, [qgu_sh], qgu_g)
        k.collective(qp_sh.h, qp_g.h, [qp_sh], qp_g)
        if SERIALIZE_COLL:
            k.eng['g'].wait_ge(k.cc_sem, 2)
        k.phase_boundary()

    with ExitStack() as sc0:
        qf = k.sbuf(sc0, "a_qf", [128, ND*SEG], BF16)
        Sst = k.sbuf(sc0, "a_Sst", [128, H*128], F32)
        attn_bf = k.sbuf(sc0, "a_attn", [128, ND*SEG], BF16)

        with ExitStack() as scKV:
            kf = k.sbuf(scKV, "a_kf", [128, ND*SEG], BF16)
            vt = k.sbuf(scKV, "a_vt", [128, NT*D], BF16)

            with ExitStack() as ph:
                xt = [k.sbuf(ph, f"a_xt{i}", [128, D], F32) for i in range(2)]
                stage = k.sbuf(ph, "a_stage", [128, D], BF16)
                ln1f = k.sbuf(ph, "a_ln1f", [128, ND*SEG], BF16)
                wb = [k.sbuf(ph, f"a_wb{i}", [128, ND*128], BF16) for i in range(2)]
                bqk_s = k.sbuf(ph, "a_bqk", [128, 3*ND], F32)
                bvb_s = k.sbuf(ph, "a_bvb", [128, D], BF16)
                psA = k.psum(ph, "a_psA", [128, 512], F32)
                psB = k.psum(ph, "a_psB", [128, 512], F32)
                psT = k.psum(ph, "a_psT", [128, 1024], BF16)

                k.dma('qs', bqk_s.ap, bqkv_cols.ap, [bqkv_cols], [bqk_s])
                k.dma('qg', bvb_s.ap, bv_b.ap, [bv_b], [bvb_s])
                if debug2:
                    stA = k.sbuf(ph, "a_stA", [128, 48*NT], F32)
                xr = x_in.h.rearrange("(n p) d -> n p d", p=128)
                for i in range(NT):
                    k.dma('qs', xt[i % 2].ap, xr[i], [x_in], [xt[i % 2]])
                    if debug2:
                        stv = Buf(stA.h[:, 48*i:48*(i+1)], f"stA{i}")
                        k.bufs.append(stv)
                        emit_ln_tile(k, xt[i % 2].ap, xt[i % 2], stage.ap, stage, stv)
                    else:
                        emit_ln_tile(k, xt[i % 2].ap, xt[i % 2], stage.ap, stage, stats)
                    for dt in range(ND):
                        k.trans(psT[:, 0:128], stage[:, dt*128:(dt+1)*128], ident_s.ap, [stage, ident_s], [psT])
                        k.copy(ln1f[:, dt*SEG + i*128: dt*SEG + (i+1)*128], psT[:, 0:128], [psT], [ln1f])
                wr = wqkvT.h.rearrange("(t p) c -> p t c", p=128)
                for oc in range(2*ND):
                    wbuf = wb[oc % 2]
                    k.dma('qg', wbuf.h[:, 0:ND*128].rearrange("p (t c) -> p t c", c=128),
                          wr[:, :, oc*128:(oc+1)*128], [wqkvT], [wbuf])
                    dst = qf if oc < ND else kf
                    blk = oc % ND
                    for n in range(2):
                        ps = psA if n == 0 else psB
                        for dt in range(ND):
                            k.mm(ps[:, 0:512], wbuf[:, dt*128:(dt+1)*128],
                                 ln1f[:, dt*SEG + n*512: dt*SEG + (n+1)*512],
                                 [wbuf, ln1f], [ps], start=(dt == 0), stop=(dt == ND-1))
                        k.act(dst[:, blk*SEG + n*512: blk*SEG + (n+1)*512], ps[:, 0:512], AF.Identity,
                              [ps, bqk_s], [dst], bias=bqk_s[:, oc:oc+1])
                for vc in range(ND):
                    wbuf = wb[vc % 2]
                    k.dma('qg', wbuf.h[:, 0:ND*128].rearrange("p (t c) -> p t c", c=128),
                          wr[:, :, 2*D + vc*128: 2*D + (vc+1)*128], [wqkvT], [wbuf])
                    for ti in range(NT):
                        ps = psA if ti % 2 == 0 else psB
                        for dt in range(ND):
                            k.mm(ps[:, 0:128], ln1f[:, dt*SEG + ti*128: dt*SEG + (ti+1)*128],
                                 wbuf[:, dt*128:(dt+1)*128],
                                 [wbuf, ln1f], [ps], start=(dt == 0), stop=(dt == ND-1))
                        k.tt(vt[:, ti*D + vc*128: ti*D + (vc+1)*128], ps[:, 0:128],
                             bvb_s[:, vc*128:(vc+1)*128], AluOpType.add, [ps, bvb_s], [vt])
                if debug2:
                    dqa = k.dram_out("dbg_qfA1", [128, ND*SEG], BF16)
                    dlf = k.dram_out("dbg_ln1f", [128, ND*SEG], BF16)
                    dst_ = k.dram_out("dbg_stats", [128, 48*NT], F32)
                    k.dma('qs', dqa.h, qf.ap, [qf], [dqa])
                    k.dma('qs', dlf.h, ln1f.ap, [ln1f], [dlf])
                    k.dma('qs', dst_.h, stA.ap, [stA], [dst_])
                k.phase_boundary()

            with ExitStack() as ph:
                kdt = k.sbuf(ph, "a_kdt", [128, NCH*128], BF16)
                Sbf = k.sbuf(ph, "a_Sbf", [128, 128], BF16)
                dqb_s = k.sbuf(ph, "a_dqb", [128, H*128], BF16)
                dkc_s = k.sbuf(ph, "a_dkc", [128, H], F32)
                dmk_s = k.sbuf(ph, "a_dmk", [128, H*128], BF16)
                qd_c = k.sbuf(ph, "a_qdc", [128, 128], BF16)
                sc_c = k.sbuf(ph, "a_scc", [128, 128], BF16)
                psSC = k.psum(ph, "a_psSC", [128, 512], F32)
                psO = k.psum(ph, "a_psO", [128, 512], F32)
                psSU = k.psum(ph, "a_psSU", [128, 512], F32)
                psT = k.psum(ph, "a_psT2", [128, 1024], BF16)

                k.dma('qg', dqb_s.h[:, :].rearrange("p (h c) -> p h c", c=128),
                      dq_b.h.rearrange("h p c -> p h c"), [dq_b], [dqb_s])
                k.dma('qs', dkc_s.ap, dk_col.ap, [dk_col], [dkc_s])
                k.dma('qg', dmk_s.h[:, :].rearrange("p (h c) -> p h c", c=128),
                      dmaskT.h.rearrange("h p c -> p h c"), [dmaskT], [dmk_s])
                for h in range(H):
                    for c in range(NCH):
                        k.trans(psT[:, 0:128], kf[:, h*SEG + c*128: h*SEG + (c+1)*128], ident_s.ap,
                                [kf, ident_s], [psT])
                        k.act(kdt[:, c*128:(c+1)*128], psT[:, 0:128], AF.Copy, [psT, dkc_s], [kdt],
                              scale=dkc_s[:, h:h+1])
                    for c in range(NCH):
                        k.mm(psSC[:, 0:128], kf[:, h*SEG + c*128: h*SEG + (c+1)*128],
                             qf[:, h*SEG + c*128: h*SEG + (c+1)*128], [kf, qf], [psSC])
                        k.tt(sc_c.ap, psSC[:, 0:128], dmk_s[:, h*128:(h+1)*128], AluOpType.mult,
                             [psSC, dmk_s], [sc_c])
                        k.tt(qd_c.ap, qf[:, h*SEG + c*128: h*SEG + (c+1)*128], dqb_s[:, h*128:(h+1)*128],
                             AluOpType.mult, [qf, dqb_s], [qd_c])
                        k.mm(psO[:, 0:128], vt[:, c*D + h*128: c*D + (h+1)*128], sc_c.ap,
                             [vt, sc_c], [psO], start=True, stop=(c == 0))
                        if c > 0:
                            k.mm(psO[:, 0:128], Sbf.ap, qd_c.ap, [Sbf, qd_c], [psO], start=False, stop=True)
                        k.copy(attn_bf[:, h*SEG + c*128:h*SEG + (c+1)*128], psO[:, 0:128], [psO], [attn_bf])
                        k.mm(psSU[:, 0:128], kdt[:, c*128:(c+1)*128], vt[:, c*D + h*128: c*D + (h+1)*128],
                             [kdt, vt], [psSU])
                        if c == 0:
                            k.copy(Sst[:, h*128:(h+1)*128], psSU[:, 0:128], [psSU], [Sst])
                        else:
                            k.stt(Sst[:, h*128:(h+1)*128], Sst[:, h*128:(h+1)*128], float(LAM_C[h]),
                                  psSU[:, 0:128], AluOpType.mult, AluOpType.add, [psSU, Sst], [Sst])
                        if c < NCH - 1:
                            k.copy(Sbf.ap, Sst[:, h*128:(h+1)*128], [Sst], [Sbf])
                k.dma('qs', sseg_l.h.rearrange("(h p) c -> p h c", p=128),
                      Sst.h[:, :].rearrange("p (h c) -> p h c", c=128), [Sst], [sseg_l])
                k.collective(sseg_l.h, sseg_g.h, [sseg_l], sseg_g)
                if debug2:
                    dq = k.dram_out("dbg_qf", [128, ND*SEG], BF16)
                    dk = k.dram_out("dbg_kf", [128, ND*SEG], BF16)
                    dv = k.dram_out("dbg_vt", [128, NT*D], BF16)
                    da = k.dram_out("dbg_attnA", [128, ND*SEG], BF16)
                    dS = k.dram_out("dbg_Sst", [128, H*128], F32)
                    k.dma('qs', dq.h, qf.ap, [qf], [dq])
                    k.dma('qs', dk.h, kf.ap, [kf], [dk])
                    k.dma('qs', dv.h, vt.ap, [vt], [dv])
                    k.dma('qs', da.h, attn_bf.ap, [attn_bf], [da])
                    k.dma('qs', dS.h, Sst.ap, [Sst], [dS])
                k.phase_boundary()

        with ExitStack() as ph:
            sseg_h = k.sbuf(ph, "b_sseg", [128, NCORE*128], F32)
            S_in = k.sbuf(ph, "b_Sin", [128, 128], F32)
            S_bf = k.sbuf(ph, "b_Sbf", [128, 128], BF16)
            qd_c = k.sbuf(ph, "b_qdc", [128, 128], BF16)
            sw_s = k.sbuf(ph, "b_sw", [128, NCORE*H], F32)
            dqb_s = k.sbuf(ph, "b_dqb", [128, H*128], BF16)
            wob = [k.sbuf(ph, f"b_wob{i}", [128, ND*512], BF16) for i in range(4)]
            bob_s = k.sbuf(ph, "b_bob", [128, D], F32)
            xt = [k.sbuf(ph, f"b_xt{i}", [128, D], F32) for i in range(2)]
            o_t = k.sbuf(ph, "b_ot", [128, D], F32)
            x1_t = k.sbuf(ph, "b_x1t", [128, D], F32)
            psC = k.psum(ph, "b_psC", [128, 512], F32)
            psO1 = k.psum(ph, "b_psO1", [128, 512], F32)
            psO2 = k.psum(ph, "b_psO2", [128, 512], F32)

            k.dma('qs', sw_s.ap, sw_in.ap, [sw_in], [sw_s])
            k.dma('qg', dqb_s.h[:, :].rearrange("p (h c) -> p h c", c=128),
                  dq_b.h.rearrange("h p c -> p h c"), [dq_b], [dqb_s])
            k.dma('qs', bob_s.ap, bo_b.ap, [bo_b], [bob_s])

            ssegr = sseg_g.h.rearrange("(r h p) c -> h p r c", r=NCORE, p=128)
            for h in range(H):
                k.dma('qs', sseg_h.h[:, :].rearrange("p (j c) -> p j c", c=128),
                      ssegr[h], [sseg_g], [sseg_h])
                k.memset(S_in.ap, 0.0, [], [S_in])
                for j in range(NCORE - 1):
                    k.stt(S_in.ap, sseg_h[:, j*128:(j+1)*128], sw_s[:, j*H+h:j*H+h+1], S_in.ap,
                          AluOpType.mult, AluOpType.add, [sseg_h, sw_s, S_in], [S_in])
                for c in range(NCH):
                    k.ts(S_bf.ap, S_in.ap, float(LAM_C[h] ** c), None, AluOpType.mult, None,
                         [S_in], [S_bf])
                    k.tt(qd_c.ap, qf[:, h*SEG + c*128:h*SEG + (c+1)*128], dqb_s[:, h*128:(h+1)*128],
                         AluOpType.mult, [qf, dqb_s], [qd_c])
                    k.mm(psC[:, 0:128], S_bf.ap, qd_c.ap, [S_bf, qd_c], [psC])
                    k.tt(attn_bf[:, h*SEG + c*128:h*SEG + (c+1)*128],
                         attn_bf[:, h*SEG + c*128:h*SEG + (c+1)*128],
                         psC[:, 0:128], AluOpType.add, [psC], [attn_bf])

            wr = woT.h.rearrange("(t p) c -> p t c", p=128)
            xr = x_in.h.rearrange("(n p) d -> n p d", p=128)
            x1r = x1_d.h.rearrange("(n p) d -> n p d", p=128)
            for nc_ in range(4):
                k.dma('qg', wob[nc_].h[:, :].rearrange("p (t c) -> p t c", c=512),
                      wr[:, :, nc_*512:(nc_+1)*512], [woT], [wob[nc_]])
            for tg in range(NT):
                k.dma('qs', xt[tg % 2].ap, xr[tg], [x_in], [xt[tg % 2]])
                for nc_ in range(4):
                    ps = psO1 if nc_ % 2 == 0 else psO2
                    wbuf = wob[nc_]
                    for et in range(ND):
                        k.mm(ps[:, 0:512], attn_bf[:, et*SEG + tg*128: et*SEG + (tg+1)*128],
                             wbuf[:, et*512:(et+1)*512], [attn_bf, wbuf], [ps],
                             start=(et == 0), stop=(et == ND-1))
                    k.tt(o_t[:, nc_*512:(nc_+1)*512], ps[:, 0:512], bob_s[:, nc_*512:(nc_+1)*512],
                         AluOpType.add, [ps, bob_s], [o_t])
                emit_ln_tile(k, o_t.ap, o_t, x1_t.ap, x1_t, stats)
                k.tt(x1_t.ap, x1_t.ap, xt[tg % 2].ap, AluOpType.add, [x1_t, xt[tg % 2]], [x1_t])
                k.dma('qs', x1r[tg], x1_t.ap, [x1_t], [x1_d])
            k.phase_boundary()

    with ExitStack() as dsc:
        o128c = k.sbuf(dsc, "d_o128c", [128, 1], BF16)
        o1r = k.sbuf(dsc, "d_o1r", [1, 128], F32)
        sqrow = k.sbuf(dsc, "d_sqrow", [1, SEG], F32)
        rstd = k.sbuf(dsc, "d_rstd", [1, SEG], F32)
        rstd_b = k.sbuf(dsc, "d_rstdb", [128, SEG], F32)
        k.dma('qs', o128c.ap, o128c_in.ap, [o128c_in], [o128c])
        k.dma('qs', o1r.ap, o1r_in.ap, [o1r_in], [o1r])
        xr1 = x1_d.h.rearrange("(n p) d -> n p d", p=128)

        with ExitStack() as ph:
            xt = [k.sbuf(ph, f"d_xt{i}", [128, D], F32) for i in range(2)]
            stage = k.sbuf(ph, "d_stage", [128, D], BF16)
            ln3f = k.sbuf(ph, "d_ln3f", [128, ND*SEG], BF16)
            hw = k.sbuf(ph, "d_hw", [128, SEG], BF16)
            hsq = k.sbuf(ph, "d_hsq", [128, SEG], BF16)
            wg = [k.sbuf(ph, f"d_wg{i}", [128, ND*128], BF16) for i in range(2)]
            wu = [k.sbuf(ph, f"d_wu{i}", [128, ND*128], BF16) for i in range(2)]
            gs = k.sbuf(ph, "d_gs", [128, SEG], BF16)
            us = k.sbuf(ph, "d_us", [128, SEG], BF16)
            psA = k.psum(ph, "d_psA", [128, 512], F32)
            psB = k.psum(ph, "d_psB", [128, 512], F32)
            psC = k.psum(ph, "d_psC", [128, 512], F32)
            psD = k.psum(ph, "d_psD", [128, 512], F32)
            psT = k.psum(ph, "d_psT", [128, 1024], BF16)
            psQ = k.psum(ph, "d_psQ", [128, 512], F32)

            k.memset(sqrow.ap, 0.0, [], [sqrow])
            for i in range(NT):
                k.dma('qs', xt[i % 2].ap, xr1[i], [x1_d], [xt[i % 2]])
                emit_ln_tile(k, xt[i % 2].ap, xt[i % 2], stage.ap, stage, stats)
                for dt in range(ND):
                    k.trans(psT[:, 0:128], stage[:, dt*128:(dt+1)*128], ident_s.ap, [stage, ident_s], [psT])
                    k.copy(ln3f[:, dt*SEG + i*128: dt*SEG + (i+1)*128], psT[:, 0:128], [psT], [ln3f])

            qgur = qgu_g.h.rearrange("(r t p) c -> r p t c", r=NCORE, p=128)
            NGC = GU_SH
            for hc in range(NHID):
                bg, bu = wg[hc % 2], wu[hc % 2]
                gcol = hc * 128
                r, lc = gcol // NGC, gcol % NGC
                k.dma('qg', bg.h[:, :].rearrange("p (t c) -> p t c", c=128),
                      qgur[r][:, :, lc:lc+128], [qgu_g], [bg])
                ucol = 4*D + hc * 128
                r2, lc2 = ucol // NGC, ucol % NGC
                k.dma('qg', bu.h[:, :].rearrange("p (t c) -> p t c", c=128),
                      qgur[r2][:, :, lc2:lc2+128], [qgu_g], [bu])
                for n in range(2):
                    psg = psA if n == 0 else psB
                    psu = psC if n == 0 else psD
                    for dt in range(ND):
                        k.mm(psg[:, 0:512], bg[:, dt*128:(dt+1)*128],
                             ln3f[:, dt*SEG + n*512: dt*SEG + (n+1)*512], [bg, ln3f], [psg],
                             start=(dt == 0), stop=(dt == ND-1))
                    for dt in range(ND):
                        k.mm(psu[:, 0:512], bu[:, dt*128:(dt+1)*128],
                             ln3f[:, dt*SEG + n*512: dt*SEG + (n+1)*512], [bu, ln3f], [psu],
                             start=(dt == 0), stop=(dt == ND-1))
                    k.act(gs[:, n*512:(n+1)*512], psg[:, 0:512], AF.Silu, [psg], [gs])
                    k.copy(us[:, n*512:(n+1)*512], psu[:, 0:512], [psu], [us])
                k.tt(hw.ap, gs.ap, us.ap, AluOpType.mult, [gs, us], [hw])
                k.dma('qs', hd_d.h[:, hc*SEG:(hc+1)*SEG], hw.ap, [hw], [hd_d])
                k.tt(hsq.ap, hw.ap, hw.ap, AluOpType.mult, [hw], [hsq])
                for n in range(2):
                    k.mm(psQ[0:1, 0:512], o128c.ap, hsq[:, n*512:(n+1)*512], [o128c, hsq], [psQ])
                    k.stt(sqrow[0:1, n*512:(n+1)*512], psQ[0:1, 0:512], 1.0,
                          sqrow[0:1, n*512:(n+1)*512], AluOpType.mult, AluOpType.add,
                          [psQ, sqrow], [sqrow])
            k.act(rstd[0:1, :], sqrow[0:1, :], AF.Ln, [sqrow], [rstd], scale=1.0/(4*D),
                  bias=1.1920929e-7)
            k.act(rstd[0:1, :], rstd[0:1, :], AF.Exp, [rstd], [rstd], scale=-0.5)
            for n in range(2):
                k.mm(psQ[:, 0:512], o1r.ap, rstd[0:1, n*512:(n+1)*512], [o1r, rstd], [psQ])
                k.copy(rstd_b[:, n*512:(n+1)*512], psQ[:, 0:512], [psQ], [rstd_b])
            k.phase_boundary()

        with ExitStack() as ph:
            xt = [k.sbuf(ph, f"d2_xt{i}", [128, D], F32) for i in range(2)]
            stage = k.sbuf(ph, "d2_stage", [128, D], BF16)
            hbf = k.sbuf(ph, "d2_hbf", [128, NHID*512], BF16)
            wp = [k.sbuf(ph, f"d2_wp{i}", [128, NHID*128], BF16) for i in range(2)]
            obf = k.sbuf(ph, "d2_obf", [128, NT*D], BF16)
            x2_t = k.sbuf(ph, "d2_x2t", [128, D], F32)
            psA = k.psum(ph, "d2_psA", [128, 512], F32)
            psB = k.psum(ph, "d2_psB", [128, 512], F32)
            psT = k.psum(ph, "d2_psT", [128, 1024], BF16)

            qpr = qp_g.h.rearrange("(r t p) c -> r p t c", r=NCORE, p=128)
            hr = hd_d.h[:, :].rearrange("p (n t) -> p n t", t=SEG)
            for th in range(2):
                k.dma('qs', hbf.h[:, :].rearrange("p (n t) -> p n t", t=512),
                      hr[:, :, th*512:(th+1)*512], [hd_d], [hbf])
                for dot in range(ND):
                    bp = wp[dot % 2]
                    pcol = dot * 128
                    r, lc = pcol // P_SH, pcol % P_SH
                    k.dma('qg', bp.h[:, :].rearrange("p (t c) -> p t c", c=128),
                          qpr[r][:, :, lc:lc+128], [qp_g], [bp])
                    ps = psA if dot % 2 == 0 else psB
                    for ht in range(NHID):
                        k.mm(ps[:, 0:512], bp[:, ht*128:(ht+1)*128],
                             hbf[:, ht*512:(ht+1)*512], [bp, hbf], [ps],
                             start=(ht == 0), stop=(ht == NHID-1))
                    k.tt(stage[:, 0:512], ps[:, 0:512], rstd_b[:, th*512:(th+1)*512], AluOpType.mult,
                         [ps, rstd_b], [stage])
                    for q4 in range(4):
                        tg = th*4 + q4
                        k.trans(psT[:, 0:128], stage[:, q4*128:(q4+1)*128], ident_s.ap,
                                [stage, ident_s], [psT])
                        k.copy(obf[:, tg*D + dot*128: tg*D + (dot+1)*128], psT[:, 0:128], [psT], [obf])

            x2r = x2_d.h.rearrange("(n p) d -> n p d", p=128)
            for tg in range(NT):
                k.dma('qs', xt[tg % 2].ap, xr1[tg], [x1_d], [xt[tg % 2]])
                emit_ln_tile(k, obf[:, tg*D:(tg+1)*D], obf, x2_t.ap, x2_t, stats)
                k.tt(x2_t.ap, x2_t.ap, xt[tg % 2].ap, AluOpType.add, [x2_t, xt[tg % 2]], [x2_t])
                k.dma('qs', x2r[tg], x2_t.ap, [x2_t], [x2_d])
            k.phase_boundary()

    with ExitStack() as sc1:
        kd = k.sbuf(sc1, "e_kd", [DS, SEG], BF16)
        vd = k.sbuf(sc1, "e_vd", [DS, SEG], BF16)
        qd = k.sbuf(sc1, "e_qd", [DS, SEG], BF16)
        lg = k.sbuf(sc1, "e_lg", [DS, SEG], F32)
        tli = k.sbuf(sc1, "n_tli", [TC, TC], F32)
        sln = k.sbuf(sc1, "n_sln", [TC, TC], BF16)
        sun = k.sbuf(sc1, "n_sun", [TC, TC], BF16)
        su = k.sbuf(sc1, "n_su", [TC, TC], BF16)
        sui = k.sbuf(sc1, "n_sui", [TC, TC], BF16)
        onesc = k.sbuf(sc1, "n_oc", [64, 1], F32)
        onesr = k.sbuf(sc1, "n_or", [1, 64], F32)
        for (d, s_) in ((tli_in, tli), (sln_in, sln), (sun_in, sun), (su_in, su),
                        (sui_in, sui), (oc_in, onesc), (or_in, onesr)):
            k.dma('qs', s_.ap, d.ap, [d], [s_])

        with ExitStack() as ph:
            xt = [k.sbuf(ph, f"e_xt{i}", [128, D], F32) for i in range(2)]
            stage = k.sbuf(ph, "e_stage", [128, D], BF16)
            lnf = k.sbuf(ph, "e_lnf", [128, ND*SEG], BF16)
            wsb = k.sbuf(ph, "e_wsb", [128, ND*4*DS], BF16)
            proj = k.sbuf(ph, "e_proj", [64, 4*SEG], F32)
            sqb = k.sbuf(ph, "e_sqb", [64, SEG], F32)
            nrow = k.sbuf(ph, "e_nrow", [1, SEG], F32)
            rrow = k.sbuf(ph, "e_rrow", [1, SEG], F32)
            wab_s = k.sbuf(ph, "e_wab", [64, 1], F32)
            lam_s = k.sbuf(ph, "e_lam", [64, 4], F32)
            psA = k.psum(ph, "e_psA", [128, 512], F32)
            psB = k.psum(ph, "e_psB", [128, 512], F32)
            psC = k.psum(ph, "e_psC", [128, 512], F32)
            psT = k.psum(ph, "e_psT", [128, 1024], BF16)

            k.dma('qs', wab_s.ap, wab_in.ap, [wab_in], [wab_s])
            k.dma('qs', lam_s[:, 0:1], lam_in.ap, [lam_in], [lam_s])
            xr2 = x2_d.h.rearrange("(n p) d -> n p d", p=128)
            for i in range(NT):
                k.dma('qs', xt[i % 2].ap, xr2[i], [x2_d], [xt[i % 2]])
                emit_ln_tile(k, xt[i % 2].ap, xt[i % 2], stage.ap, stage, stats)
                for dt in range(ND):
                    k.trans(psT[:, 0:128], stage[:, dt*128:(dt+1)*128], ident_s.ap, [stage, ident_s], [psT])
                    k.copy(lnf[:, dt*SEG + i*128: dt*SEG + (i+1)*128], psT[:, 0:128], [psT], [lnf])
            k.dma('qg', wsb.h[:, :].rearrange("p (t c) -> p t c", c=4*DS),
                  wkvqaT_in.h.rearrange("(t p) c -> p t c", p=128), [wkvqaT_in], [wsb])
            for j in range(4):
                for n in range(2):
                    ps = psA if n == 0 else psB
                    for dt in range(ND):
                        k.mm(ps[0:64, 0:512], wsb[:, dt*4*DS + j*DS: dt*4*DS + (j+1)*DS],
                             lnf[:, dt*SEG + n*512: dt*SEG + (n+1)*512], [wsb, lnf], [ps],
                             start=(dt == 0), stop=(dt == ND-1))
                    if j < 3:
                        k.copy(proj[:, j*SEG + n*512: j*SEG + (n+1)*512], ps[0:64, 0:512], [ps], [proj])
                    else:
                        k.act(proj[:, j*SEG + n*512: j*SEG + (n+1)*512], ps[0:64, 0:512], AF.Sigmoid,
                              [ps, wab_s], [proj], bias=wab_s.ap)
            for j, dst in ((0, kd), (2, qd)):
                k.act(sqb.ap, proj[:, j*SEG:(j+1)*SEG], AF.Square, [proj], [sqb])
                for n in range(2):
                    k.mm(psC[0:1, 0:512], onesc.ap, sqb[:, n*512:(n+1)*512], [onesc, sqb], [psC])
                    k.act(nrow[:, n*512:(n+1)*512], psC[0:1, 0:512], AF.Sqrt, [psC], [nrow])
                k.ts(nrow.ap, nrow.ap, 1e-12, None, AluOpType.max, None, [nrow], [nrow])
                k.op('v', lambda e: e.reciprocal(rrow.ap, nrow.ap), reads=[nrow], writes=[rrow])
                for n in range(2):
                    k.mm(psC[0:64, 0:512], onesr.ap, rrow[:, n*512:(n+1)*512], [onesr, rrow], [psC])
                    k.tt(dst[:, n*512:(n+1)*512], proj[:, j*SEG + n*512: j*SEG + (n+1)*512],
                         psC[0:64, 0:512], AluOpType.mult, [proj, psC], [dst])
            k.copy(vd.ap, proj[:, 1*SEG:2*SEG], [proj], [vd])
            k.act(lam_s[:, 1:2], lam_s[:, 0:1], AF.Sigmoid, [lam_s], [lam_s])
            k.act(lam_s[:, 2:3], lam_s[:, 1:2], AF.Ln, [lam_s], [lam_s], bias=1e-8)
            k.ts(lam_s[:, 3:4], lam_s[:, 2:3], 8.0, None, AluOpType.mult, None, [lam_s], [lam_s])
            k.ts(lg.ap, proj[:, 3*SEG:4*SEG], lam_s[:, 3:4], None, AluOpType.mult, None,
                 [proj, lam_s], [lg])
            k.phase_boundary()

        with ExitStack() as sd:
            Hst = k.sbuf(sd, "n_Hst", [64, 64], F32)
            Hbf = k.sbuf(sd, "n_Hbf", [64, 64], BF16)
            lgT = k.sbuf(sd, "n_lgT", [TC, 64], F32)
            ALi = k.sbuf(sd, "n_ALi", [TC, 64], F32)
            A_t = k.sbuf(sd, "n_At", [TC, 64], F32)
            Apv = k.sbuf(sd, "n_Apv", [TC, 64], F32)
            Ainv = k.sbuf(sd, "n_Ainv", [TC, 64], F32)
            a_t = k.sbuf(sd, "n_at", [TC, 64], F32)
            w1 = k.sbuf(sd, "n_w1", [TC, 64], F32)
            tmpf = k.sbuf(sd, "n_tmpf", [TC, 64], F32)
            kc = k.sbuf(sd, "n_kc", [TC, 64], BF16)
            vc = k.sbuf(sd, "n_vc", [TC, 64], BF16)
            qc = k.sbuf(sd, "n_qc", [TC, 64], BF16)
            kt_ = k.sbuf(sd, "n_kt", [TC, 64], BF16)
            qp_ = k.sbuf(sd, "n_qp", [TC, 64], BF16)
            qi_ = k.sbuf(sd, "n_qi", [TC, 64], BF16)
            Dm = k.sbuf(sd, "n_Dm", [TC, 64], BF16)
            DmT = k.sbuf(sd, "n_DmT", [64, TC], BF16)
            ktT = k.sbuf(sd, "n_ktT", [64, TC], BF16)
            qpT_ = k.sbuf(sd, "n_qpT", [64, TC], BF16)
            qiT = k.sbuf(sd, "n_qiT", [64, TC], BF16)
            Npow = [k.sbuf(sd, f"n_N{j}", [TC, TC], BF16) for j in range(5)]
            NTpow = [k.sbuf(sd, f"n_NT{j}", [TC, TC], BF16) for j in range(6)]
            X = k.sbuf(sd, "n_X", [TC, 128], BF16)
            Gt = k.sbuf(sd, "n_Gt", [TC, TC], BF16)
            d2 = k.sbuf(sd, "n_d2", [TC, 64], F32)
            sq2 = k.sbuf(sd, "n_sq2", [TC, 64], F32)
            scol = k.sbuf(sd, "n_scol", [TC, 1], F32)
            Aer = k.sbuf(sd, "n_Aer", [1, 64], F32)
            AeC = k.sbuf(sd, "n_AeC", [64, 1], F32)
            Wn = k.sbuf(sd, "n_Wn", [TC, 64], BF16)
            Mc = k.sbuf(sd, "n_Mc", [TC, TC], BF16)
            McT = k.sbuf(sd, "n_McT", [TC, TC], BF16)
            tmpb = k.sbuf(sd, "n_tmpb", [TC, TC], BF16)
            Mrun = k.sbuf(sd, "n_Mrun", [TC, TC], BF16)
            MrunT = k.sbuf(sd, "n_MrunT", [TC, TC], BF16)
            outf = k.sbuf(sd, "n_outf", [TC, TC], F32)
            ps1 = k.psum(sd, "n_ps1", [128, 512], F32)
            ps2 = k.psum(sd, "n_ps2", [128, 512], F32)
            ps3 = k.psum(sd, "n_ps3", [128, 512], F32)
            psTb = k.psum(sd, "n_psTb", [128, 1024], BF16)
            psF = k.psum(sd, "n_psF", [128, 512], F32)

            def tr(src_ap, src_buf, dst_ap, dst_buf, f32=False, pdim=64):
                if f32:
                    k.trans(psF[0:64, 0:pdim], src_ap, identf[0:pdim, 0:pdim], [src_buf, identf], [psF])
                    k.copy(dst_ap, psF[0:64, 0:pdim], [psF], [dst_buf])
                else:
                    k.trans(psTb[0:64, 0:pdim], src_ap, ident_s[0:pdim, 0:pdim], [src_buf, ident_s], [psTb])
                    k.copy(dst_ap, psTb[0:64, 0:pdim], [psTb], [dst_buf])

            def apply_M(width):
                for j in (5, 4, 3, 2, 1, 0):
                    k.mm(ps3[0:TC, 0:width], ident_s[0:TC, 0:TC], X[:, 0:width], [ident_s, X], [ps3],
                         start=True, stop=False)
                    k.mm(ps3[0:TC, 0:width], NTpow[j].ap, X[:, 0:width], [NTpow[j], X], [ps3],
                         start=False, stop=True)
                    k.copy(X[:, 0:width], ps3[0:TC, 0:width], [ps3], [X])

            def build_powers():
                for lev in range(1, 5):
                    k.mm(ps3[0:TC, 0:TC], NTpow[lev-1].ap, Npow[lev-1].ap, [NTpow[lev-1], Npow[lev-1]], [ps3])
                    k.copy(Npow[lev].ap, ps3[0:TC, 0:TC], [ps3], [Npow[lev]])
                    k.mm(ps2[0:TC, 0:TC], Npow[lev-1].ap, NTpow[lev-1].ap, [Npow[lev-1], NTpow[lev-1]], [ps2])
                    k.copy(NTpow[lev].ap, ps2[0:TC, 0:TC], [ps2], [NTpow[lev]])
                k.mm(ps3[0:TC, 0:TC], Npow[4].ap, NTpow[4].ap, [Npow[4], NTpow[4]], [ps3])
                k.copy(NTpow[5].ap, ps3[0:TC, 0:TC], [ps3], [NTpow[5]])

            def build_LN_masks():
                k.mm(ps1[0:TC, 0:TC], ktT.ap, DmT.ap, [ktT, DmT], [ps1])
                k.tt(Npow[0].ap, ps1[0:TC, 0:TC], sln.ap, AluOpType.mult, [ps1, sln], [Npow[0]])
                k.mm(ps2[0:TC, 0:TC], DmT.ap, ktT.ap, [DmT, ktT], [ps2])
                k.tt(NTpow[0].ap, ps2[0:TC, 0:TC], sun.ap, AluOpType.mult, [ps2, sun], [NTpow[0]])

            def build_Dm(with_s):
                k.tt(Dm.ap, kc.ap, w1.ap, AluOpType.mult, [kc, w1], [Dm])
                if with_s:
                    k.ts(Dm.ap, Dm.ap, scol.ap, None, AluOpType.mult, None, [Dm, scol], [Dm])
                tr(Dm.ap, Dm, DmT.ap, DmT)

            def chunk_prelude(c, pass2):
                cs = slice(c*TC, (c+1)*TC)
                tr(lg[:, cs], lg, lgT.ap, lgT, f32=True)
                k.mm(psF[0:TC, 0:64], tli.ap, lgT.ap, [tli, lgT], [psF])
                k.copy(ALi.ap, psF[0:TC, 0:64], [psF], [ALi])
                k.act(A_t.ap, ALi.ap, AF.Exp, [ALi], [A_t])
                k.tt(tmpf.ap, ALi.ap, lgT.ap, AluOpType.subtract, [ALi, lgT], [tmpf])
                k.act(Apv.ap, tmpf.ap, AF.Exp, [tmpf], [Apv])
                k.act(Ainv.ap, ALi.ap, AF.Exp, [ALi], [Ainv], scale=-1.0)
                k.act(a_t.ap, lgT.ap, AF.Exp, [lgT], [a_t])
                k.mm(psF[0:1, 0:64], onesc.ap, lgT.ap, [onesc, lgT], [psF])
                k.act(Aer.ap, psF[0:1, 0:64], AF.Exp, [psF], [Aer])
                k.trans(psF[0:64, 0:1], Aer.ap, identf[0:1, 0:1], [Aer, identf], [psF])
                k.copy(AeC.ap, psF[0:64, 0:1], [psF], [AeC])
                tr(kd[:, cs], kd, kc.ap, kc)
                tr(vd[:, cs], vd, vc.ap, vc)
                tr(qd[:, cs], qd, qc.ap, qc)
                k.tt(kt_.ap, kc.ap, Apv.ap, AluOpType.mult, [kc, Apv], [kt_])
                k.tt(qp_.ap, qc.ap, Apv.ap, AluOpType.mult, [qc, Apv], [qp_])
                k.tt(qi_.ap, qc.ap, A_t.ap, AluOpType.mult, [qc, A_t], [qi_])
                k.ts(tmpf.ap, a_t.ap, -1.0, 1.0, AluOpType.mult, AluOpType.add, [a_t], [tmpf])
                k.tt(w1.ap, tmpf.ap, Ainv.ap, AluOpType.mult, [tmpf, Ainv], [w1])
                tr(kt_.ap, kt_, ktT.ap, ktT)
                tr(qp_.ap, qp_, qpT_.ap, qpT_)
                if pass2:
                    tr(qi_.ap, qi_, qiT.ap, qiT)

            k.memset(Hst.ap, 0.0, [], [Hst])
            k.memset(Hbf.ap, 0.0, [], [Hbf])
            k.copy(Mrun.ap, ident_s[0:TC, 0:TC], [ident_s], [Mrun])
            k.copy(MrunT.ap, ident_s[0:TC, 0:TC], [ident_s], [MrunT])
            for c in range(NCHK):
                chunk_prelude(c, pass2=False)
                build_Dm(with_s=False)
                build_LN_masks()
                build_powers()
                k.mm(ps1[0:TC, 0:64], ktT.ap, Hbf.ap, [ktT, Hbf], [ps1])
                k.tt(X[:, 0:64], vc.ap, ps1[0:TC, 0:64], AluOpType.subtract, [vc, ps1], [X])
                k.copy(X[:, 64:128], kt_.ap, [kt_], [X])
                apply_M(128)
                k.mm(ps2[0:TC, 0:64], ident_s[0:TC, 0:TC], Hbf.ap, [ident_s, Hbf], [ps2],
                     start=True, stop=False)
                k.mm(ps2[0:TC, 0:64], Dm.ap, X[:, 0:64], [Dm, X], [ps2], start=False, stop=True)
                k.ts(Hst.ap, ps2[0:TC, 0:64], AeC.ap, None, AluOpType.mult, None, [ps2, AeC], [Hst])
                k.copy(Hbf.ap, Hst.ap, [Hst], [Hbf])
                k.ts(Wn.ap, X[:, 64:128], -1.0, None, AluOpType.mult, None, [X], [Wn])
                k.mm(ps1[0:TC, 0:TC], ident_s[0:TC, 0:TC], ident_s[0:TC, 0:TC], [ident_s], [ps1],
                     start=True, stop=False)
                k.mm(ps1[0:TC, 0:TC], Dm.ap, Wn.ap, [Dm, Wn], [ps1], start=False, stop=True)
                k.ts(Mc.ap, ps1[0:TC, 0:TC], AeC.ap, None, AluOpType.mult, None, [ps1, AeC], [Mc])
                k.mm(ps2[0:TC, 0:TC], ident_s[0:TC, 0:TC], ident_s[0:TC, 0:TC], [ident_s], [ps2],
                     start=True, stop=False)
                k.mm(ps2[0:TC, 0:TC], Wn.ap, Dm.ap, [Wn, Dm], [ps2], start=False, stop=True)
                k.copy(tmpb.ap, ps2[0:TC, 0:TC], [ps2], [tmpb])
                k.mm(psF[0:TC, 0:TC], onesr.ap, Aer.ap, [onesr, Aer], [psF])
                k.tt(McT.ap, tmpb.ap, psF[0:TC, 0:TC], AluOpType.mult, [tmpb, psF], [McT])
                k.mm(ps1[0:TC, 0:TC], McT.ap, Mrun.ap, [McT, Mrun], [ps1])
                k.mm(ps2[0:TC, 0:TC], Mrun.ap, McT.ap, [Mrun, McT], [ps2])
                k.copy(Mrun.ap, ps1[0:TC, 0:TC], [ps1], [Mrun])
                k.copy(MrunT.ap, ps2[0:TC, 0:TC], [ps2], [MrunT])
            k.copy(outf.ap, MrunT.ap, [MrunT], [outf])
            k.dma('qs', dn_l.h[0:64, :], outf.ap, [outf], [dn_l])
            k.dma('qs', dn_l.h[64:128, :], Hst.ap, [Hst], [dn_l])
            k.collective(dn_l.h, dn_g.h, [dn_l], dn_g)
            k.phase_boundary()

            with ExitStack() as ph:
                dn_sb = k.sbuf(ph, "g_dnsb", [128, NCORE*64], F32)
                dnsel = k.sbuf(ph, "g_dnsel", [64, 16], F32)
                Mu_f = k.sbuf(ph, "g_Muf", [64, 64], F32)
                Mu_b = k.sbuf(ph, "g_Mub", [64, 64], BF16)
                bu_b = k.sbuf(ph, "g_bub", [64, 64], BF16)
                y_buf = k.sbuf(ph, "g_ybuf", [64, NCHK*64], F32)
                ynf = k.sbuf(ph, "g_ynf", [64, SEG], BF16)
                wos = k.sbuf(ph, "g_wos", [64, D], BF16)
                nwb = k.sbuf(ph, "g_nwb", [64, NCHK*64], F32)
                msq = k.sbuf(ph, "g_msq", [64, NCHK], F32)
                xt = [k.sbuf(ph, f"g_xt{i}", [128, D], F32) for i in range(2)]
                o_t = k.sbuf(ph, "g_ot", [128, D], F32)
                x3_t = k.sbuf(ph, "g_x3t", [128, D], F32)

                k.dma('qs', dnsel.ap, dnsel_in.ap, [dnsel_in], [dnsel])
                k.dma('qs', nwb.ap, nwb_in.ap, [nwb_in], [nwb])
                k.dma('qg', wos.ap, wo1T_in.ap, [wo1T_in], [wos])
                for r in range(NCORE):
                    k.dma('qs', dn_sb[:, r*64:(r+1)*64], dn_g.h[r*128:(r+1)*128, :], [dn_g], [dn_sb])

                k.memset(Hst.ap, 0.0, [], [Hst])
                k.memset(Hbf.ap, 0.0, [], [Hbf])
                for j in range(NCORE - 1):
                    k.ts(Mu_f.ap, dn_sb[0:64, j*64:(j+1)*64], dnsel[:, j:j+1], None,
                         AluOpType.mult, None, [dn_sb, dnsel], [Mu_f])
                    k.stt(Mu_f.ap, identf[0:64, 0:64], dnsel[:, 8+j:9+j], Mu_f.ap,
                          AluOpType.mult, AluOpType.add, [identf, dnsel, Mu_f], [Mu_f])
                    k.copy(Mu_b.ap, Mu_f.ap, [Mu_f], [Mu_b])
                    k.ts(Mu_f.ap, dn_sb[64:128, j*64:(j+1)*64], dnsel[:, j:j+1], None,
                         AluOpType.mult, None, [dn_sb, dnsel], [Mu_f])
                    k.copy(bu_b.ap, Mu_f.ap, [Mu_f], [bu_b])
                    k.mm(ps1[0:TC, 0:64], Mu_b.ap, Hbf.ap, [Mu_b, Hbf], [ps1],
                         start=True, stop=False)
                    k.mm(ps1[0:TC, 0:64], ident_s[0:TC, 0:TC], bu_b.ap, [ident_s, bu_b],
                         [ps1], start=False, stop=True)
                    k.copy(Hst.ap, ps1[0:TC, 0:64], [ps1], [Hst])
                    k.copy(Hbf.ap, Hst.ap, [Hst], [Hbf])

                for c in range(NCHK):
                    cs = slice(c*TC, (c+1)*TC)
                    chunk_prelude(c, pass2=True)
                    build_Dm(with_s=False)
                    build_LN_masks()
                    build_powers()
                    k.mm(ps1[0:TC, 0:64], ktT.ap, Hbf.ap, [ktT, Hbf], [ps1])
                    k.tt(X[:, 0:64], vc.ap, ps1[0:TC, 0:64], AluOpType.subtract, [vc, ps1], [X])
                    apply_M(64)
                    k.mm(ps1[0:TC, 0:TC], DmT.ap, qpT_.ap, [DmT, qpT_], [ps1])
                    k.tt(Gt.ap, ps1[0:TC, 0:TC], su.ap, AluOpType.mult, [ps1, su], [Gt])
                    k.mm(ps2[0:TC, 0:64], qpT_.ap, Hbf.ap, [qpT_, Hbf], [ps2], start=True, stop=False)
                    k.mm(ps2[0:TC, 0:64], Gt.ap, X[:, 0:64], [Gt, X], [ps2], start=False, stop=True)
                    k.tt(d2.ap, vc.ap, ps2[0:TC, 0:64], AluOpType.subtract, [vc, ps2], [d2])
                    k.tt(sq2.ap, d2.ap, d2.ap, AluOpType.mult, [d2], [sq2])
                    k.op('v', lambda e: e.reduce_sum(scol.ap, sq2.ap, AX.X), reads=[sq2], writes=[scol])
                    k.act(scol.ap, scol.ap, AF.Sigmoid, [scol], [scol], scale=float(1.0/(1.0+1e-6)))
                    build_Dm(with_s=True)
                    build_LN_masks()
                    build_powers()
                    k.mm(ps1[0:TC, 0:64], ktT.ap, Hbf.ap, [ktT, Hbf], [ps1])
                    k.tt(X[:, 0:64], vc.ap, ps1[0:TC, 0:64], AluOpType.subtract, [vc, ps1], [X])
                    apply_M(64)
                    k.mm(ps1[0:TC, 0:TC], DmT.ap, qiT.ap, [DmT, qiT], [ps1])
                    k.tt(Gt.ap, ps1[0:TC, 0:TC], sui.ap, AluOpType.mult, [ps1, sui], [Gt])
                    k.mm(ps2[0:TC, 0:64], qiT.ap, Hbf.ap, [qiT, Hbf], [ps2], start=True, stop=False)
                    k.mm(ps2[0:TC, 0:64], Gt.ap, X[:, 0:64], [Gt, X], [ps2], start=False, stop=True)
                    k.copy(y_buf[:, cs], ps2[0:TC, 0:64], [ps2], [y_buf])
                    k.mm(ps2[0:TC, 0:64], ident_s[0:TC, 0:TC], Hbf.ap, [ident_s, Hbf], [ps2],
                         start=True, stop=False)
                    k.mm(ps2[0:TC, 0:64], Dm.ap, X[:, 0:64], [Dm, X], [ps2], start=False, stop=True)
                    k.ts(Hst.ap, ps2[0:TC, 0:64], AeC.ap, None, AluOpType.mult, None, [ps2, AeC], [Hst])
                    k.copy(Hbf.ap, Hst.ap, [Hst], [Hbf])

                for c in range(NCHK):
                    k.tt(sq2.ap, y_buf[:, c*64:(c+1)*64], y_buf[:, c*64:(c+1)*64], AluOpType.mult,
                         [y_buf], [sq2])
                    k.op('v', lambda e, c=c: e.reduce_sum(msq[:, c:c+1], sq2.ap, AX.X),
                         reads=[sq2], writes=[msq])
                k.ts(msq.ap, msq.ap, 1.0/64.0, None, AluOpType.mult, None, [msq], [msq])
                k.act(msq.ap, msq.ap, AF.Ln, [msq], [msq], bias=1e-6)
                k.act(msq.ap, msq.ap, AF.Exp, [msq], [msq], scale=-0.5)
                for c in range(NCHK):
                    k.ts(y_buf[:, c*64:(c+1)*64], y_buf[:, c*64:(c+1)*64], msq[:, c:c+1], None,
                         AluOpType.mult, None, [y_buf, msq], [y_buf])
                k.tt(y_buf.ap, y_buf.ap, nwb.ap, AluOpType.mult, [y_buf, nwb], [y_buf])
                for c in range(NCHK):
                    k.copy(Dm.ap, y_buf[:, c*64:(c+1)*64], [y_buf], [Dm])
                    tr(Dm.ap, Dm, ynf[:, c*64:(c+1)*64], ynf)
                xr2 = x2_d.h.rearrange("(n p) d -> n p d", p=128)
                x3r = x3_d.h.rearrange("(n p) d -> n p d", p=128)
                for tg in range(NT):
                    k.dma('qs', xt[tg % 2].ap, xr2[tg], [x2_d], [xt[tg % 2]])
                    for dc in range(4):
                        ps = ps1 if dc % 2 == 0 else ps2
                        k.mm(ps[:, 0:512], ynf[:, tg*128:(tg+1)*128], wos[:, dc*512:(dc+1)*512],
                             [ynf, wos], [ps])
                        k.copy(o_t[:, dc*512:(dc+1)*512], ps[:, 0:512], [ps], [o_t])
                    emit_ln_tile(k, o_t.ap, o_t, x3_t.ap, x3_t, stats)
                    k.tt(x3_t.ap, x3_t.ap, xt[tg % 2].ap, AluOpType.add, [x3_t, xt[tg % 2]], [x3_t])
                    k.dma('qs', x3r[tg], x3_t.ap, [x3_t], [x3_d])
                k.phase_boundary()

    with ExitStack() as ph:
        xt = [k.sbuf(ph, f"h_xt{i}", [128, D], F32) for i in range(2)]
        stage = k.sbuf(ph, "h_stage", [128, D], BF16)
        ln7f = k.sbuf(ph, "h_ln7f", [128, ND*SEG], BF16)
        y_acc = k.sbuf(ph, "h_yacc", [128, NT*D], F32)
        hbuf = k.sbuf(ph, "h_hbuf", [128, NOC_M*SEG], BF16)
        wb1 = [k.sbuf(ph, f"h_wb1{i}", [128, ND*128], BF16) for i in range(2)]
        wb2 = [k.sbuf(ph, f"h_wb2{i}", [128, ND*128], BF16) for i in range(2)]
        wpb = [k.sbuf(ph, f"h_wpb{i}", [128, NOC_M*512], BF16) for i in range(2)]
        gw_s = k.sbuf(ph, "h_gw", [128, ND*16], BF16)
        router = k.sbuf(ph, "h_router", [128, NT*16], F32)
        comb = k.sbuf(ph, "h_comb", [128, NT*16], F32)
        rst = k.sbuf(ph, "h_rst", [128, 8], F32)
        gs = k.sbuf(ph, "h_gs", [128, SEG], BF16)
        us = k.sbuf(ph, "h_us", [128, SEG], BF16)
        o_t = k.sbuf(ph, "h_ot", [128, D], F32)
        psA = k.psum(ph, "h_psA", [128, 512], F32)
        psB = k.psum(ph, "h_psB", [128, 512], F32)
        psC = k.psum(ph, "h_psC", [128, 512], F32)
        psD = k.psum(ph, "h_psD", [128, 512], F32)
        psT = k.psum(ph, "h_psT", [128, 1024], BF16)

        k.dma('qg', gw_s.h[:, :].rearrange("p (t c) -> p t c", c=16),
              gw_in.h.rearrange("(t p) c -> p t c", p=128), [gw_in], [gw_s])
        xr3 = x3_d.h.rearrange("(n p) d -> n p d", p=128)
        for i in range(NT):
            k.dma('qs', xt[i % 2].ap, xr3[i], [x3_d], [xt[i % 2]])
            emit_ln_tile(k, xt[i % 2].ap, xt[i % 2], stage.ap, stage, stats)
            for dt in range(ND):
                k.trans(psT[:, 0:128], stage[:, dt*128:(dt+1)*128], ident_s.ap, [stage, ident_s], [psT])
                k.copy(ln7f[:, dt*SEG + i*128: dt*SEG + (i+1)*128], psT[:, 0:128], [psT], [ln7f])
        for tg in range(NT):
            ps = psA if tg % 2 == 0 else psB
            for dt in range(ND):
                k.mm(ps[:, 0:16], ln7f[:, dt*SEG + tg*128: dt*SEG + (tg+1)*128],
                     gw_s[:, dt*16:(dt+1)*16], [ln7f, gw_s], [ps],
                     start=(dt == 0), stop=(dt == ND-1))
            k.copy(router[:, tg*16:(tg+1)*16], ps[:, 0:16], [ps], [router])
        for tg in range(NT):
            rsl = router[:, tg*16:tg*16+8]
            csl = comb[:, tg*16:tg*16+8]
            k.act(comb[:, tg*16+8:tg*16+9], router[:, tg*16+8:tg*16+9], AF.Sigmoid, [router], [comb])
            k.op('v', lambda e, rsl=rsl: e.reduce_max(rst[:, 0:1], rsl, AX.X), reads=[router], writes=[rst])
            k.ts(rst[:, 1:2], rst[:, 0:1], -1.0, None, AluOpType.mult, None, [rst], [rst])
            k.act(csl, rsl, AF.Exp, [router, rst], [comb], bias=rst[:, 1:2])
            k.op('v', lambda e, csl=csl: e.reduce_sum(rst[:, 2:3], csl, AX.X), reads=[comb], writes=[rst])
            k.op('v', lambda e: e.reciprocal(rst[:, 3:4], rst[:, 2:3]), reads=[rst], writes=[rst])
            k.ts(csl, csl, rst[:, 3:4], None, AluOpType.mult, None, [comb, rst], [comb])
            k.op('v', lambda e, csl=csl: e.reduce_max(rst[:, 4:5], csl, AX.X), reads=[comb], writes=[rst])
            k.ts(router[:, tg*16:tg*16+8], csl, rst[:, 4:5], None, AluOpType.is_lt, None,
                 [comb, rst], [router])
            k.tt(router[:, tg*16:tg*16+8], router[:, tg*16:tg*16+8], csl, AluOpType.mult,
                 [router, comb], [router])
            k.op('v', lambda e, tg=tg: e.reduce_max(rst[:, 5:6], router[:, tg*16:tg*16+8], AX.X),
                 reads=[router], writes=[rst])
            k.ts(router[:, tg*16+8:tg*16+16], csl, rst[:, 5:6], None, AluOpType.is_ge, None,
                 [comb, rst], [router])
            k.tt(csl, csl, router[:, tg*16+8:tg*16+16], AluOpType.mult, [comb, router], [comb])
        k.memset(y_acc.ap, 0.0, [], [y_acc])
        er1 = efc1_in.h.rearrange("e (t p) c -> e p t c", p=128)
        er2 = efc2_in.h.rearrange("e (t p) c -> e p t c", p=128)
        erp = eprj_in.h.rearrange("e (t p) c -> e p t c", p=128)
        for e in range(NE):
            for oc in range(NOC_M):
                b1, b2 = wb1[oc % 2], wb2[oc % 2]
                k.dma('qg', b1.h[:, :].rearrange("p (t c) -> p t c", c=128),
                      er1[e][:, :, oc*128:(oc+1)*128], [efc1_in], [b1])
                k.dma('qg', b2.h[:, :].rearrange("p (t c) -> p t c", c=128),
                      er2[e][:, :, oc*128:(oc+1)*128], [efc2_in], [b2])
                for n in range(2):
                    psg = psA if n == 0 else psB
                    psu = psC if n == 0 else psD
                    for dt in range(ND):
                        k.mm(psg[:, 0:512], b1[:, dt*128:(dt+1)*128],
                             ln7f[:, dt*SEG + n*512: dt*SEG + (n+1)*512], [b1, ln7f], [psg],
                             start=(dt == 0), stop=(dt == ND-1))
                    for dt in range(ND):
                        k.mm(psu[:, 0:512], b2[:, dt*128:(dt+1)*128],
                             ln7f[:, dt*SEG + n*512: dt*SEG + (n+1)*512], [b2, ln7f], [psu],
                             start=(dt == 0), stop=(dt == ND-1))
                    k.act(gs[:, n*512:(n+1)*512], psg[:, 0:512], AF.Silu, [psg], [gs])
                    k.copy(us[:, n*512:(n+1)*512], psu[:, 0:512], [psu], [us])
                k.tt(hbuf[:, oc*SEG:(oc+1)*SEG], gs.ap, us.ap, AluOpType.mult, [gs, us], [hbuf])
            for dc in range(4):
                bp = wpb[dc % 2]
                k.dma('qg', bp.h[:, :].rearrange("p (t c) -> p t c", c=512),
                      erp[e][:, :, dc*512:(dc+1)*512], [eprj_in], [bp])
                for tg in range(NT):
                    ps = psA if tg % 2 == 0 else psB
                    for ht in range(NOC_M):
                        k.mm(ps[:, 0:512], hbuf[:, ht*SEG + tg*128: ht*SEG + (tg+1)*128],
                             bp[:, ht*512:(ht+1)*512], [hbuf, bp], [ps],
                             start=(ht == 0), stop=(ht == NOC_M-1))
                    k.stt(y_acc[:, tg*D + dc*512: tg*D + (dc+1)*512], ps[:, 0:512],
                          comb[:, tg*16+e:tg*16+e+1], y_acc[:, tg*D + dc*512: tg*D + (dc+1)*512],
                          AluOpType.mult, AluOpType.add, [ps, comb, y_acc], [y_acc])
        sr1 = sfc1_in.h.rearrange("(t p) c -> p t c", p=128)
        sr2 = sfc2_in.h.rearrange("(t p) c -> p t c", p=128)
        srp = sprj_in.h.rearrange("(t p) c -> p t c", p=128)
        for qtr in range(4):
            for ocq in range(NOC_M):
                oc = qtr*NOC_M + ocq
                b1, b2 = wb1[ocq % 2], wb2[ocq % 2]
                k.dma('qg', b1.h[:, :].rearrange("p (t c) -> p t c", c=128),
                      sr1[:, :, oc*128:(oc+1)*128], [sfc1_in], [b1])
                k.dma('qg', b2.h[:, :].rearrange("p (t c) -> p t c", c=128),
                      sr2[:, :, oc*128:(oc+1)*128], [sfc2_in], [b2])
                for n in range(2):
                    psg = psA if n == 0 else psB
                    psu = psC if n == 0 else psD
                    for dt in range(ND):
                        k.mm(psg[:, 0:512], b1[:, dt*128:(dt+1)*128],
                             ln7f[:, dt*SEG + n*512: dt*SEG + (n+1)*512], [b1, ln7f], [psg],
                             start=(dt == 0), stop=(dt == ND-1))
                    for dt in range(ND):
                        k.mm(psu[:, 0:512], b2[:, dt*128:(dt+1)*128],
                             ln7f[:, dt*SEG + n*512: dt*SEG + (n+1)*512], [b2, ln7f], [psu],
                             start=(dt == 0), stop=(dt == ND-1))
                    k.act(gs[:, n*512:(n+1)*512], psg[:, 0:512], AF.Silu, [psg], [gs])
                    k.copy(us[:, n*512:(n+1)*512], psu[:, 0:512], [psu], [us])
                k.tt(hbuf[:, ocq*SEG:(ocq+1)*SEG], gs.ap, us.ap, AluOpType.mult, [gs, us], [hbuf])
            for dc in range(4):
                bp = wpb[dc % 2]
                k.dma('qg', bp.h[:, :].rearrange("p (t c) -> p t c", c=512),
                      srp[:, qtr*NOC_M:(qtr+1)*NOC_M, dc*512:(dc+1)*512], [sprj_in], [bp])
                for tg in range(NT):
                    ps = psA if tg % 2 == 0 else psB
                    for ht in range(NOC_M):
                        k.mm(ps[:, 0:512], hbuf[:, ht*SEG + tg*128: ht*SEG + (tg+1)*128],
                             bp[:, ht*512:(ht+1)*512], [hbuf, bp], [ps],
                             start=(ht == 0), stop=(ht == NOC_M-1))
                    k.stt(y_acc[:, tg*D + dc*512: tg*D + (dc+1)*512], ps[:, 0:512],
                          comb[:, tg*16+8:tg*16+9], y_acc[:, tg*D + dc*512: tg*D + (dc+1)*512],
                          AluOpType.mult, AluOpType.add, [ps, comb, y_acc], [y_acc])
        outr = out_out.h.rearrange("(n p) d -> n p d", p=128)
        for tg in range(NT):
            k.dma('qs', xt[tg % 2].ap, xr3[tg], [x3_d], [xt[tg % 2]])
            emit_ln_tile(k, y_acc[:, tg*D:(tg+1)*D], y_acc, o_t.ap, o_t, stats)
            k.tt(o_t.ap, o_t.ap, xt[tg % 2].ap, AluOpType.add, [o_t, xt[tg % 2]], [o_t])
            k.dma('qs', outr[tg], o_t.ap, [o_t], [out_out])

    if debug:
        dbg1 = k.dram_out("dbg_x1", [SEG, D], F32)
        dbg2 = k.dram_out("dbg_x2", [SEG, D], F32)
        dbg3 = k.dram_out("dbg_x3", [SEG, D], F32)
        k.dma('qs', dbg1.h, x1_d.h, [x1_d], [dbg1])
        k.dma('qs', dbg2.h, x2_d.h, [x2_d], [dbg2])
        k.dma('qs', dbg3.h, x3_d.h, [x3_d], [dbg3])
    nc = k.end()
    return nc, k


_BUILT = {}

def _prep_maps(I, k):
    t = host_tables()
    x = I['L0_attn_x'][0]
    wqkvT = np.ascontiguousarray(np.asarray(I['L0_attn_w_qkv']).T)
    b = np.asarray(I['L0_attn_b_qkv'])
    bqkv_cols = np.ascontiguousarray(b[:2*D].reshape(2*ND, 128).T)
    bqkv_cols = np.concatenate([bqkv_cols, np.zeros((128, ND), np.float32)], axis=1)
    bv_b = np.broadcast_to(b[2*D:][None, :], (128, D)).copy()
    woT = np.ascontiguousarray(np.asarray(I['L0_attn_w_out']).T)
    bo_b = np.broadcast_to(np.asarray(I['L0_attn_b_out'])[None, :], (128, D)).copy()
    wguT = np.ascontiguousarray(np.asarray(I['L0_ffn_w_gate_up']).T)
    wpT = np.ascontiguousarray(np.asarray(I['L0_ffn_w_proj']).T)
    wkvqaT = np.ascontiguousarray(np.concatenate(
        [np.asarray(I['L1_attn_Wk']).T, np.asarray(I['L1_attn_Wv']).T,
         np.asarray(I['L1_attn_Wq']).T, np.asarray(I['L1_attn_Wa_w']).T], axis=1))
    wo1T = np.ascontiguousarray(np.asarray(I['L1_attn_Wo']).T)
    nwb = np.tile(np.asarray(I['L1_attn_norm_w'])[None, :], (64, NCHK)).astype(np.float32)
    gate9 = np.zeros((D, 16), np.float32)
    gate9[:, 0:8] = np.asarray(I['L1_ffn_gate_w']).T
    gate9[:, 8:9] = np.asarray(I['L1_ffn_shared_gate_w']).T
    efc1T = np.ascontiguousarray(np.asarray(I['L1_ffn_expert_fc1']).transpose(0, 2, 1))
    efc2T = np.ascontiguousarray(np.asarray(I['L1_ffn_expert_fc2']).transpose(0, 2, 1))
    eprjT = np.ascontiguousarray(np.asarray(I['L1_ffn_expert_proj']).transpose(0, 2, 1))
    sfc1T = np.ascontiguousarray(np.asarray(I['L1_ffn_shared_fc1']).T)
    sfc2T = np.ascontiguousarray(np.asarray(I['L1_ffn_shared_fc2']).T)
    sprjT = np.ascontiguousarray(np.asarray(I['L1_ffn_shared_proj']).T)
    maps = []
    for c in range(NCORE):
        sw = np.zeros((NCORE, H), np.float32)
        for j in range(c):
            sw[j] = LAM_SEG ** (c - 1 - j)
        dnsel = np.zeros((64, 16), np.float32)
        for j in range(NCORE - 1):
            sel = 1.0 if j < c else 0.0
            dnsel[:, j] = sel
            dnsel[:, 8+j] = 1.0 - sel
        maps.append({
            "consts_pool": k.const_array(),
            "x_seg": np.ascontiguousarray(x[c*SEG:(c+1)*SEG]),
            "wqkvT": wqkvT, "bqkv_cols": bqkv_cols, "bv_b": bv_b,
            "woT": woT, "bo_b": bo_b,
            "sw": np.broadcast_to(sw.reshape(1, -1), (128, NCORE*H)).copy(),
            "wguT_sh": np.ascontiguousarray(wguT[:, c*GU_SH:(c+1)*GU_SH]),
            "wpT_sh": np.ascontiguousarray(wpT[:, c*P_SH:(c+1)*P_SH]),
            "wkvqaT": wkvqaT,
            "wa_b": np.asarray(I['L1_attn_Wa_b']).reshape(64, 1),
            "lam": np.asarray(I['L1_attn_lam']).reshape(64, 1),
            "dnsel": dnsel, "nwb": nwb, "wo1T": wo1T,
            "gate9T": gate9,
            "efc1T": efc1T, "efc2T": efc2T, "eprjT": eprjT,
            "sfc1T": sfc1T, "sfc2T": sfc2T, "sprjT": sprjT,
            "ident_bf": t['ident_bf'], "ident_f32": t['ident_f32'],
            "dq_b": t['dq_b'], "dk_col": t['dk_col'], "dmaskT": t['dmaskT'],
            "tli_f32": t['tli_f32'], "strilneg_bf": t['strilneg_bf'],
            "striuneg_bf": t['striuneg_bf'], "striu_bf": t['striu_bf'],
            "striuincl_bf": t['striuincl_bf'],
            "ones64c_f32": t['ones64c_f32'], "ones64r_f32": t['ones64r_f32'],
            "ones128c_bf": t['ones128c_bf'], "ones1r_f32": t['ones1r_f32'],
            "gmask": t['gmask'], "gmask2f": t['gmask2f'], "gmask2b": t['gmask2b'],
        })
    return maps


def kernel(**inputs):
    if "fused" not in _BUILT:
        _BUILT["fused"] = build_fused()
    nc, kk = _BUILT["fused"]
    maps = _prep_maps(inputs, kk)
    res = run_bass_kernel_spmd(nc, maps, core_ids=list(range(NCORE)))
    out = np.concatenate([np.asarray(res.results[c]["out_seg"]) for c in range(NCORE)], 0)
    return out.reshape(1, N_TOK, D).astype(np.float32)


# version 07
# speedup vs baseline: 1.0466x; 1.0466x over previous
import numpy as np
import math
from contextlib import ExitStack
import ml_dtypes
import concourse.bass as bass
import concourse.mybir as mybir
from concourse.bass_utils import run_bass_kernel_spmd
from concourse.alu_op_type import AluOpType

F32 = mybir.dt.float32
BF16 = mybir.dt.bfloat16
AF = mybir.ActivationFunctionType
AX = mybir.AxisListType
BF = ml_dtypes.bfloat16

QUEUE_ENG = {'qg': 'g', 'qs': 'y', 'qa': 's'}
NSLOT = 6


class Buf:
    __slots__ = ('h', 'name', 'last_w', 'readers')
    def __init__(self, h, name):
        self.h = h
        self.name = name
        self.last_w = None
        self.readers = []
    def __getitem__(self, sl):
        return self.h[sl]
    @property
    def ap(self):
        return self.h[:]


class K:
    def __init__(self, name="prog"):
        self.nc = bass.Bass("TRN2", target_bir_lowering=False, debug=False)
        self.stack = ExitStack()
        self.eng = {'t': self.nc.tensor, 'v': self.nc.vector, 's': self.nc.scalar,
                    'g': self.nc.gpsimd, 'y': self.nc.sync}
        self.cnt = {k: 0 for k in ('t', 'v', 's', 'g', 'y')}
        self.water = {k: {} for k in ('t', 'v', 's', 'g', 'y')}
        self.sems = None
        self.block = None
        self.ops = 0
        self.bufs = []
        self.cc_cnt = 0
        self._ext_in = {}
        self._ext_out = {}

    def dram_in(self, name, shape, dtype=F32):
        ap = self.nc.dram_tensor(name, list(shape), dtype, kind="ExternalInput").ap()
        b = Buf(ap, name)
        self.bufs.append(b)
        self._ext_in[name] = b
        return b

    def dram_out(self, name, shape, dtype=F32):
        ap = self.nc.dram_tensor(name, list(shape), dtype, kind="ExternalOutput").ap()
        b = Buf(ap, name)
        self.bufs.append(b)
        self._ext_out[name] = b
        return b

    def dram_tmp(self, name, shape, dtype=F32):
        ap = self.nc.dram_tensor(name, list(shape), dtype, kind="Internal").ap()
        b = Buf(ap, name)
        self.bufs.append(b)
        return b

    def begin(self, n_phases=12, n_consts=128):
        self._consts_in = self.dram_in("consts_pool", [128, n_consts], F32)
        self.semsets = []
        for p in range(n_phases):
            s = {}
            for kk in ('t', 'v', 's', 'g', 'y'):
                s[kk] = self.stack.enter_context(self.nc.semaphore(f"sm{p}_{kk}"))
            self.semsets.append(s)
        self.dma_sems = {}
        self.dma_uses = {}
        self.dma_issue = {}
        for q in ('qg', 'qs', 'qa'):
            for sl in range(NSLOT):
                self.dma_sems[(q, sl)] = self.stack.enter_context(self.nc.semaphore(f"dm_{q}{sl}"))
                self.dma_uses[(q, sl)] = 0
            self.dma_issue[q] = 0
        self.cc_sem = self.stack.enter_context(self.nc.semaphore("sm_cc"))
        self.phase = 0
        self.sems = dict(self.semsets[0])
        self.sems['cc'] = self.cc_sem
        self.block = self.stack.enter_context(self.nc.Block())
        ids = sorted({s.num for st in self.semsets for s in st.values()}
                     | {s.num for s in self.dma_sems.values()} | {self.cc_sem.num})
        if ids == list(range(ids[0], ids[-1] + 1)):
            self.eng['g'].sem_clear(range(ids[0], ids[-1] + 1))
        else:
            for st in self.semsets:
                for s in st.values():
                    self.eng['g'].sem_clear(s)
            for s in self.dma_sems.values():
                self.eng['g'].sem_clear(s)
            self.eng['g'].sem_clear(self.cc_sem)
        self.nc.all_engine_barrier()
        self._consts_s = self.sbuf(self.stack, "consts_s", [128, n_consts], F32)
        self._const_map = {}
        self._n_consts = n_consts
        self.dma('qs', self._consts_s.ap, self._consts_in.ap, [self._consts_in], [self._consts_s])
        return self

    def constf(self, val):
        val = float(val)
        if val not in self._const_map:
            i = len(self._const_map)
            assert i < self._n_consts, "const pool full"
            self._const_map[val] = i
            self.nc.const_aps.register(F32, val, self._consts_s[:, i:i+1])
        i = self._const_map[val]
        return self._consts_s[:, i:i+1]

    def const_array(self):
        arr = np.zeros((128, self._n_consts), np.float32)
        for v, i in self._const_map.items():
            arr[:, i] = v
        return arr

    def sbuf(self, stack, name, shape, dtype=F32):
        h = stack.enter_context(self.nc.sbuf_tensor(name, list(shape), dtype))
        b = Buf(h, name)
        self.bufs.append(b)
        return b

    def psum(self, stack, name, shape, dtype=F32):
        h = stack.enter_context(self.nc.psum_tensor(name, list(shape), dtype))
        b = Buf(h, name)
        self.bufs.append(b)
        return b

    def end(self):
        self.barrier_all()
        self.stack.close()
        self.nc.finalize()
        return self.nc

    def _wait(self, consumer, dep_key, dep_val):
        if dep_val <= 0:
            return
        w = self.water[consumer]
        if w.get(dep_key, 0) >= dep_val:
            return
        if isinstance(dep_key, tuple):
            self.eng[consumer].wait_ge(self.dma_sems[dep_key], dep_val * 16)
        else:
            self.eng[consumer].wait_ge(self.sems[dep_key], dep_val)
        w[dep_key] = dep_val

    def _deps(self, ekey, reads, writes):
        for b in reads:
            if b.last_w is not None:
                pk, pv = b.last_w
                if pk != ekey or ekey != 't':
                    self._wait(ekey, pk, pv)
        for b in writes:
            if b.last_w is not None:
                pk, pv = b.last_w
                if pk != ekey or ekey != 't':
                    self._wait(ekey, pk, pv)
            for (rk, rv) in b.readers:
                if rk != ekey or ekey != 't':
                    self._wait(ekey, rk, rv)

    def _post(self, key, reads, writes):
        self.cnt[key] += 1
        tag = (key, self.cnt[key])
        for b in reads:
            b.readers.append(tag)
        for b in writes:
            b.last_w = tag
            b.readers = []
        self.ops += 1
        return tag

    def op(self, ekey, fn, reads=(), writes=()):
        self._deps(ekey, list(reads), list(writes))
        inst = fn(self.eng[ekey])
        inst.then_inc(self.sems[ekey], 1)
        return self._post(ekey, list(reads), list(writes))

    def dma(self, qkey, out_ap, in_ap, reads=(), writes=(), **kw):
        ekey = QUEUE_ENG[qkey]
        reads, writes = list(reads), list(writes)
        for b in reads:
            if b.last_w is not None:
                pk, pv = b.last_w
                self._wait(ekey, pk, pv)
        for b in writes:
            if b.last_w is not None:
                pk, pv = b.last_w
                self._wait(ekey, pk, pv)
            for (rk, rv) in b.readers:
                if rk != ekey:
                    self._wait(ekey, rk, rv)
        slot = (qkey, self.dma_issue[qkey] % NSLOT)
        self.dma_issue[qkey] += 1
        self._wait(ekey, slot, self.dma_uses[slot])
        self.dma_uses[slot] += 1
        self.eng[ekey].dma_start(out=out_ap, in_=in_ap, **kw).then_inc(self.dma_sems[slot], 16)
        tag = (slot, self.dma_uses[slot])
        for b in reads:
            b.readers.append(tag)
        for b in writes:
            b.last_w = tag
            b.readers = []
        self.ops += 1
        return tag

    def collective(self, in_ap, out_ap, reads, out_buf):
        ekey = 'g'
        for b in list(reads):
            if b.last_w is not None:
                pk, pv = b.last_w
                self._wait(ekey, pk, pv)
        self.eng['g'].collective_compute(
            "AllGather", AluOpType.bypass,
            replica_groups=[list(range(8))],
            ins=[in_ap.opt()], outs=[out_ap.opt()],
        ).then_inc(self.cc_sem)
        self.cc_cnt += 1
        tag = ('cc', self.cc_cnt)
        for b in list(reads):
            b.readers.append(tag)
        out_buf.last_w = tag
        out_buf.readers = []
        self.ops += 1
        return tag

    def barrier_all(self):
        snap = dict(self.cnt)
        dsnap = dict(self.dma_uses)
        for ekey in ('t', 'v', 's', 'g', 'y'):
            for pk, pv in snap.items():
                if pk != ekey:
                    self._wait(ekey, pk, pv)
            for pk, pv in dsnap.items():
                self._wait(ekey, pk, pv)

    def phase_boundary(self):
        self.barrier_all()
        self.phase += 1
        self.sems = dict(self.semsets[self.phase])
        self.sems['cc'] = self.cc_sem
        self.cnt = {k: 0 for k in ('t', 'v', 's', 'g', 'y')}
        for w in self.water.values():
            keep = {kk: vv for kk, vv in w.items() if kk == 'cc' or isinstance(kk, tuple)}
            w.clear()
            w.update(keep)
        for b in self.bufs:
            if not (b.last_w and (b.last_w[0] == 'cc' or isinstance(b.last_w[0], tuple))):
                b.last_w = None
            b.readers = [r for r in b.readers if r[0] == 'cc' or isinstance(r[0], tuple)]

    def mm(self, out_ap, lT_ap, r_ap, rd, wr, start=True, stop=True):
        return self.op('t', lambda e: e.matmul(out_ap, lT_ap, r_ap, start=start, stop=stop),
                       reads=rd, writes=wr)

    def trans(self, out_ap, in_ap, ident_ap, rd, wr):
        return self.op('t', lambda e: e.transpose(out_ap, in_ap, ident_ap), reads=rd, writes=wr)

    def act(self, out_ap, in_ap, func, rd, wr, **kw):
        rd = list(rd)
        try:
            nparts = out_ap.shape[0]
        except Exception:
            nparts = 128
        for key in ('bias', 'scale'):
            v = kw.get(key)
            if isinstance(v, (int, float)):
                self.constf(v)
                i = self._const_map[float(v)]
                kw[key] = self._consts_s[0:nparts, i:i+1]
                if self._consts_s not in rd:
                    rd.append(self._consts_s)
        return self.op('s', lambda e: e.activation(out_ap, in_ap, func, **kw), reads=rd, writes=wr)

    def tt(self, out_ap, a_ap, b_ap, aop, rd, wr, eng='v'):
        return self.op(eng, lambda e: e.tensor_tensor(out_ap, a_ap, b_ap, aop), reads=rd, writes=wr)

    def ts(self, out_ap, a_ap, s1, s2, op0, op1, rd, wr, eng='v', **kw):
        if s2 is None:
            return self.op(eng, lambda e: e.tensor_scalar(out_ap, a_ap, s1, None, op0, **kw), reads=rd, writes=wr)
        return self.op(eng, lambda e: e.tensor_scalar(out_ap, a_ap, s1, s2, op0, op1, **kw), reads=rd, writes=wr)

    def stt(self, out_ap, in0_ap, scalar, in1_ap, op0, op1, rd, wr, eng='v'):
        return self.op(eng, lambda e: e.scalar_tensor_tensor(out_ap, in0_ap, scalar, in1_ap, op0, op1), reads=rd, writes=wr)

    def copy(self, out_ap, in_ap, rd, wr, eng='v'):
        return self.op(eng, lambda e: e.tensor_copy(out_ap, in_ap), reads=rd, writes=wr)

    def memset(self, ap, val, rd, wr, eng='v'):
        return self.op(eng, lambda e: e.memset(ap, val), reads=rd, writes=wr)


N_TOK, D = 8192, 2048
NCORE = 8
SEG = N_TOK // NCORE
NT = SEG // 128
ND = D // 128
H, HD, CH = 16, 128, 128
NCH = SEG // CH
NHID = 64
TC = 64
NCHK = SEG // TC
DS = 64
NE, FM, FS = 8, 1408, 5632
NOC_M = FM // 128
SERIALIZE_COLL = False
GU_SH = 2 * 4 * D // NCORE
P_SH = D // NCORE


def _slopes(n_heads):
    start = 2.0 ** (-2.0 ** (-(math.log2(n_heads) - 3)))
    return np.array([start ** (i + 1) for i in range(n_heads)], dtype=np.float32)

LAM = np.exp(-_slopes(H)).astype(np.float32)
POS = np.arange(CH, dtype=np.float32)
DECAY_Q = (LAM[:, None] ** (POS + 1.0)).astype(np.float32)
DECAY_K = (LAM[:, None] ** (CH - 1.0 - POS)).astype(np.float32)
_diff = POS[:, None] - POS[None, :]
DMASK = np.where(_diff >= 0, LAM[:, None, None] ** np.maximum(_diff, 0.0), 0.0).astype(np.float32)
LAM_C = (LAM ** CH).astype(np.float32)
LAM_SEG = (LAM ** SEG).astype(np.float32)


def host_tables():
    t = {}
    t['ident_bf'] = np.eye(128, dtype=BF)
    t['ident_f32'] = np.eye(128, dtype=np.float32)
    t['dq_b'] = np.broadcast_to(DECAY_Q[:, None, :], (H, 128, 128)).copy()
    t['dk_col'] = DECAY_K.T.copy()
    t['dmaskT'] = DMASK.transpose(0, 2, 1).copy()
    idx = np.arange(TC)
    t['tli_f32'] = (idx[:, None] <= idx[None, :]).astype(np.float32)
    t['strilneg_bf'] = np.where(idx[:, None] > idx[None, :], -1.0, 0.0).astype(BF)
    t['striuneg_bf'] = np.where(idx[:, None] < idx[None, :], -1.0, 0.0).astype(BF)
    t['striu_bf'] = (idx[:, None] < idx[None, :]).astype(BF)
    t['striuincl_bf'] = (idx[:, None] <= idx[None, :]).astype(BF)
    t['ones64c_f32'] = np.ones((TC, 1), np.float32)
    t['ones64r_f32'] = np.ones((1, TC), np.float32)
    t['ones128c_bf'] = np.ones((128, 1), BF)
    t['ones1r_f32'] = np.ones((1, 128), np.float32)
    gmask = np.zeros((128, 2), np.float32); gmask[0:64, 0] = 1; gmask[64:128, 1] = 1
    t['gmask'] = gmask.astype(BF)
    t['gmask2f'] = gmask.T.copy()
    t['gmask2b'] = gmask.T.astype(BF)
    return t


def emit_ln_tile(k, x_ap, x_buf, out_ap, out_buf, stats, eps=1e-5, width=D):
    nblk = (width + 511) // 512
    ntrip = 2 * nblk
    for i in range(nblk):
        w = min(512, width - i*512)
        k.op('v', lambda e, i=i, w=w: e.bn_stats(stats[:, 6*i:6*i+6], x_ap[:, i*512:i*512+w]),
             reads=[x_buf], writes=[stats])
    sv = stats.h[:, 0:6*nblk].rearrange("p (n t) -> p t n", t=3)
    mview, m2view = sv[:, 1, :], sv[:, 2, :]
    k.op('v', lambda e: e.reduce_sum(stats[:, 24:25], mview, AX.X), reads=[stats], writes=[stats])
    k.op('v', lambda e: e.reduce_sum(stats[:, 25:26], m2view, AX.X), reads=[stats], writes=[stats])
    k.op('v', lambda e: e.tensor_tensor(stats[:, 32:32+ntrip], mview, mview, AluOpType.mult),
         reads=[stats], writes=[stats])
    k.op('v', lambda e: e.reduce_sum(stats[:, 26:27], stats[:, 32:32+ntrip], AX.X),
         reads=[stats], writes=[stats])
    k.ts(stats[:, 27:28], stats[:, 24:25], 1.0/ntrip, None, AluOpType.mult, None, [stats], [stats])
    k.ts(stats[:, 28:29], stats[:, 25:26], 1.0/width, None, AluOpType.mult, None, [stats], [stats])
    k.stt(stats[:, 28:29], stats[:, 26:27], 1.0/ntrip, stats[:, 28:29], AluOpType.mult, AluOpType.add,
          [stats], [stats])
    k.tt(stats[:, 30:31], stats[:, 27:28], stats[:, 27:28], AluOpType.mult, [stats], [stats])
    k.tt(stats[:, 28:29], stats[:, 28:29], stats[:, 30:31], AluOpType.subtract, [stats], [stats])
    k.act(stats[:, 26:27], stats[:, 28:29], AF.Ln, [stats], [stats], bias=float(eps))
    k.act(stats[:, 31:32], stats[:, 26:27], AF.Exp, [stats], [stats], scale=-0.5)
    k.stt(stats[:, 30:31], stats[:, 27:28], -1.0, stats[:, 31:32], AluOpType.mult, AluOpType.mult,
          [stats], [stats])
    k.ts(out_ap, x_ap, stats[:, 31:32], stats[:, 30:31], AluOpType.mult, AluOpType.add,
         [x_buf, stats], [out_buf])


def build_fused(debug=False, debug2=False):
    k = K()
    x_in = k.dram_in("x_seg", [SEG, D], F32)
    wqkvT = k.dram_in("wqkvT", [D, 3*D], F32)
    bqkv_cols = k.dram_in("bqkv_cols", [128, 3*ND], F32)
    bv_b = k.dram_in("bv_b", [128, D], F32)
    woT = k.dram_in("woT", [D, D], F32)
    bo_b = k.dram_in("bo_b", [128, D], F32)
    sw_in = k.dram_in("sw", [128, NCORE*H], F32)
    wgu_in = k.dram_in("wguT_sh", [D, GU_SH], F32)
    wp_in = k.dram_in("wpT_sh", [4*D, P_SH], F32)
    wkvqaT_in = k.dram_in("wkvqaT", [D, 4*DS], F32)
    wab_in = k.dram_in("wa_b", [DS, 1], F32)
    lam_in = k.dram_in("lam", [DS, 1], F32)
    dnsel_in = k.dram_in("dnsel", [64, 16], F32)
    nwb_in = k.dram_in("nwb", [64, NCHK*64], F32)
    wo1T_in = k.dram_in("wo1T", [DS, D], F32)
    gw_in = k.dram_in("gate9T", [D, 16], F32)
    efc1_in = k.dram_in("efc1T", [NE, D, FM], F32)
    efc2_in = k.dram_in("efc2T", [NE, D, FM], F32)
    eprj_in = k.dram_in("eprjT", [NE, FM, D], F32)
    sfc1_in = k.dram_in("sfc1T", [D, FS], F32)
    sfc2_in = k.dram_in("sfc2T", [D, FS], F32)
    sprj_in = k.dram_in("sprjT", [FS, D], F32)
    ident = k.dram_in("ident_bf", [128, 128], BF16)
    identf_in = k.dram_in("ident_f32", [128, 128], F32)
    dq_b = k.dram_in("dq_b", [H, 128, 128], F32)
    dk_col = k.dram_in("dk_col", [128, H], F32)
    dmaskT = k.dram_in("dmaskT", [H, 128, 128], F32)
    tli_in = k.dram_in("tli_f32", [TC, TC], F32)
    sln_in = k.dram_in("strilneg_bf", [TC, TC], BF16)
    sun_in = k.dram_in("striuneg_bf", [TC, TC], BF16)
    su_in = k.dram_in("striu_bf", [TC, TC], BF16)
    sui_in = k.dram_in("striuincl_bf", [TC, TC], BF16)
    oc_in = k.dram_in("ones64c_f32", [64, 1], F32)
    or_in = k.dram_in("ones64r_f32", [1, 64], F32)
    o128c_in = k.dram_in("ones128c_bf", [128, 1], BF16)
    o1r_in = k.dram_in("ones1r_f32", [1, 128], F32)
    gm_in = k.dram_in("gmask", [128, 2], BF16)
    gm2f_in = k.dram_in("gmask2f", [2, 128], F32)
    gm2b_in = k.dram_in("gmask2b", [2, 128], BF16)
    out_out = k.dram_out("out_seg", [SEG, D], F32)
    x1_d = k.dram_tmp("x1_d", [SEG, D], F32)
    x2_d = k.dram_tmp("x2_d", [SEG, D], F32)
    x3_d = k.dram_tmp("x3_d", [SEG, D], F32)
    qgu_sh = k.dram_tmp("qgu_sh", [D, GU_SH], BF16)
    qgu_g = k.dram_tmp("qgu_g", [NCORE*D, GU_SH], BF16)
    qp_sh = k.dram_tmp("qp_sh", [4*D, P_SH], BF16)
    qp_g = k.dram_tmp("qp_g", [NCORE*4*D, P_SH], BF16)
    hd_d = k.dram_tmp("hd_d", [128, NHID*SEG], BF16)
    sseg_l = k.dram_tmp("sseg_l", [H*128, 128], F32)
    sseg_g = k.dram_tmp("sseg_g", [NCORE*H*128, 128], F32)
    dn_l = k.dram_tmp("dn_l", [128, 64], F32)
    dn_g = k.dram_tmp("dn_g", [NCORE*128, 64], F32)

    k.begin(n_phases=12)

    ident_s = k.sbuf(k.stack, "ident_s", [128, 128], BF16)
    identf = k.sbuf(k.stack, "identf", [128, 128], F32)
    stats = k.sbuf(k.stack, "stats", [128, 48], F32)
    k.dma('qs', ident_s.ap, ident.ap, [ident], [ident_s])
    k.dma('qs', identf.ap, identf_in.ap, [identf_in], [identf])

    with ExitStack() as sc0:
        qf = k.sbuf(sc0, "a_qf", [128, ND*SEG], BF16)
        Sst = k.sbuf(sc0, "a_Sst", [128, H*128], F32)
        attn_bf = k.sbuf(sc0, "a_attn", [128, ND*SEG], BF16)

        with ExitStack() as scKV:
            kf = k.sbuf(scKV, "a_kf", [128, ND*SEG], BF16)
            vt = k.sbuf(scKV, "a_vt", [128, NT*D], BF16)

            with ExitStack() as ph:
                xt = [k.sbuf(ph, f"a_xt{i}", [128, D], F32) for i in range(2)]
                stage = k.sbuf(ph, "a_stage", [128, D], BF16)
                ln1f = k.sbuf(ph, "a_ln1f", [128, ND*SEG], BF16)
                wb = [k.sbuf(ph, f"a_wb{i}", [128, ND*128], BF16) for i in range(2)]
                bqk_s = k.sbuf(ph, "a_bqk", [128, 3*ND], F32)
                bvb_s = k.sbuf(ph, "a_bvb", [128, D], BF16)
                psA = k.psum(ph, "a_psA", [128, 512], F32)
                psB = k.psum(ph, "a_psB", [128, 512], F32)
                psT = k.psum(ph, "a_psT", [128, 1024], BF16)

                k.dma('qs', bqk_s.ap, bqkv_cols.ap, [bqkv_cols], [bqk_s])
                k.dma('qg', bvb_s.ap, bv_b.ap, [bv_b], [bvb_s])
                if debug2:
                    stA = k.sbuf(ph, "a_stA", [128, 48*NT], F32)
                xr = x_in.h.rearrange("(n p) d -> n p d", p=128)
                for i in range(NT):
                    k.dma('qs', xt[i % 2].ap, xr[i], [x_in], [xt[i % 2]])
                    if debug2:
                        stv = Buf(stA.h[:, 48*i:48*(i+1)], f"stA{i}")
                        k.bufs.append(stv)
                        emit_ln_tile(k, xt[i % 2].ap, xt[i % 2], stage.ap, stage, stv)
                    else:
                        emit_ln_tile(k, xt[i % 2].ap, xt[i % 2], stage.ap, stage, stats)
                    for dt in range(ND):
                        k.trans(psT[:, 0:128], stage[:, dt*128:(dt+1)*128], ident_s.ap, [stage, ident_s], [psT])
                        k.copy(ln1f[:, dt*SEG + i*128: dt*SEG + (i+1)*128], psT[:, 0:128], [psT], [ln1f])
                wr = wqkvT.h.rearrange("(t p) c -> p t c", p=128)
                for oc in range(2*ND):
                    wbuf = wb[oc % 2]
                    k.dma('qg', wbuf.h[:, 0:ND*128].rearrange("p (t c) -> p t c", c=128),
                          wr[:, :, oc*128:(oc+1)*128], [wqkvT], [wbuf])
                    dst = qf if oc < ND else kf
                    blk = oc % ND
                    for n in range(2):
                        ps = psA if n == 0 else psB
                        for dt in range(ND):
                            k.mm(ps[:, 0:512], wbuf[:, dt*128:(dt+1)*128],
                                 ln1f[:, dt*SEG + n*512: dt*SEG + (n+1)*512],
                                 [wbuf, ln1f], [ps], start=(dt == 0), stop=(dt == ND-1))
                        k.act(dst[:, blk*SEG + n*512: blk*SEG + (n+1)*512], ps[:, 0:512], AF.Identity,
                              [ps, bqk_s], [dst], bias=bqk_s[:, oc:oc+1])
                for vc in range(ND):
                    wbuf = wb[vc % 2]
                    k.dma('qg', wbuf.h[:, 0:ND*128].rearrange("p (t c) -> p t c", c=128),
                          wr[:, :, 2*D + vc*128: 2*D + (vc+1)*128], [wqkvT], [wbuf])
                    for ti in range(NT):
                        ps = psA if ti % 2 == 0 else psB
                        for dt in range(ND):
                            k.mm(ps[:, 0:128], ln1f[:, dt*SEG + ti*128: dt*SEG + (ti+1)*128],
                                 wbuf[:, dt*128:(dt+1)*128],
                                 [wbuf, ln1f], [ps], start=(dt == 0), stop=(dt == ND-1))
                        k.tt(vt[:, ti*D + vc*128: ti*D + (vc+1)*128], ps[:, 0:128],
                             bvb_s[:, vc*128:(vc+1)*128], AluOpType.add, [ps, bvb_s], [vt])
                if debug2:
                    dqa = k.dram_out("dbg_qfA1", [128, ND*SEG], BF16)
                    dlf = k.dram_out("dbg_ln1f", [128, ND*SEG], BF16)
                    dst_ = k.dram_out("dbg_stats", [128, 48*NT], F32)
                    k.dma('qs', dqa.h, qf.ap, [qf], [dqa])
                    k.dma('qs', dlf.h, ln1f.ap, [ln1f], [dlf])
                    k.dma('qs', dst_.h, stA.ap, [stA], [dst_])
                k.phase_boundary()

            with ExitStack() as ph:
                kdt = k.sbuf(ph, "a_kdt", [128, H*NCH*128], BF16)
                Sbf = k.sbuf(ph, "a_Sbf", [128, H*128], BF16)
                dqb_s = k.sbuf(ph, "a_dqb", [128, H*128], BF16)
                dkc_s = k.sbuf(ph, "a_dkc", [128, H], F32)
                dmk_s = k.sbuf(ph, "a_dmk", [128, H*128], BF16)
                qd_c = [k.sbuf(ph, f"a_qdc{i}", [128, 128], BF16) for i in range(2)]
                sc_c = [k.sbuf(ph, f"a_scc{i}", [128, 128], BF16) for i in range(2)]
                psSC = [k.psum(ph, "a_psSC0", [128, 512], F32)] * 2
                psO = [k.psum(ph, "a_psO0", [128, 512], F32)] * 2
                psSU = [k.psum(ph, "a_psSU0", [128, 512], F32)] * 2
                psT2 = [k.psum(ph, f"a_psT2{i}", [128, 1024], BF16) for i in range(2)]

                k.dma('qg', dqb_s.h[:, :].rearrange("p (h c) -> p h c", c=128),
                      dq_b.h.rearrange("h p c -> p h c"), [dq_b], [dqb_s])
                k.dma('qs', dkc_s.ap, dk_col.ap, [dk_col], [dkc_s])
                k.dma('qg', dmk_s.h[:, :].rearrange("p (h c) -> p h c", c=128),
                      dmaskT.h.rearrange("h p c -> p h c"), [dmaskT], [dmk_s])
                for h in range(H):
                    for c in range(NCH):
                        pt = psT2[(h*NCH+c) % 2]
                        k.trans(pt[:, 0:128], kf[:, h*SEG + c*128: h*SEG + (c+1)*128], ident_s.ap,
                                [kf, ident_s], [pt])
                        k.act(kdt[:, (h*NCH+c)*128:(h*NCH+c+1)*128], pt[:, 0:128], AF.Copy,
                              [pt, dkc_s], [kdt], scale=dkc_s[:, h:h+1])
                for c in range(NCH):
                    for h in range(H):
                        par = h % 2
                        k.mm(psSC[par][:, 0:128], kf[:, h*SEG + c*128: h*SEG + (c+1)*128],
                             qf[:, h*SEG + c*128: h*SEG + (c+1)*128], [kf, qf], [psSC[par]])
                        k.tt(sc_c[par].ap, psSC[par][:, 0:128], dmk_s[:, h*128:(h+1)*128], AluOpType.mult,
                             [psSC[par], dmk_s], [sc_c[par]])
                        k.tt(qd_c[par].ap, qf[:, h*SEG + c*128: h*SEG + (c+1)*128], dqb_s[:, h*128:(h+1)*128],
                             AluOpType.mult, [qf, dqb_s], [qd_c[par]])
                        k.mm(psO[par][:, 0:128], vt[:, c*D + h*128: c*D + (h+1)*128], sc_c[par].ap,
                             [vt, sc_c[par]], [psO[par]], start=True, stop=(c == 0))
                        if c > 0:
                            k.mm(psO[par][:, 0:128], Sbf[:, h*128:(h+1)*128], qd_c[par].ap,
                                 [Sbf, qd_c[par]], [psO[par]], start=False, stop=True)
                        k.copy(attn_bf[:, h*SEG + c*128:h*SEG + (c+1)*128], psO[par][:, 0:128],
                               [psO[par]], [attn_bf])
                        k.mm(psSU[par][:, 0:128], kdt[:, (h*NCH+c)*128:(h*NCH+c+1)*128],
                             vt[:, c*D + h*128: c*D + (h+1)*128], [kdt, vt], [psSU[par]])
                        if c == 0:
                            k.copy(Sst[:, h*128:(h+1)*128], psSU[par][:, 0:128], [psSU[par]], [Sst])
                        else:
                            k.stt(Sst[:, h*128:(h+1)*128], Sst[:, h*128:(h+1)*128], float(LAM_C[h]),
                                  psSU[par][:, 0:128], AluOpType.mult, AluOpType.add, [psSU[par], Sst], [Sst])
                        if c < NCH - 1:
                            k.copy(Sbf[:, h*128:(h+1)*128], Sst[:, h*128:(h+1)*128], [Sst], [Sbf])
                k.dma('qs', sseg_l.h.rearrange("(h p) c -> p h c", p=128),
                      Sst.h[:, :].rearrange("p (h c) -> p h c", c=128), [Sst], [sseg_l])
                k.collective(sseg_l.h, sseg_g.h, [sseg_l], sseg_g)
                if debug2:
                    dq = k.dram_out("dbg_qf", [128, ND*SEG], BF16)
                    dk = k.dram_out("dbg_kf", [128, ND*SEG], BF16)
                    dv = k.dram_out("dbg_vt", [128, NT*D], BF16)
                    da = k.dram_out("dbg_attnA", [128, ND*SEG], BF16)
                    dS = k.dram_out("dbg_Sst", [128, H*128], F32)
                    k.dma('qs', dq.h, qf.ap, [qf], [dq])
                    k.dma('qs', dk.h, kf.ap, [kf], [dk])
                    k.dma('qs', dv.h, vt.ap, [vt], [dv])
                    k.dma('qs', da.h, attn_bf.ap, [attn_bf], [da])
                    k.dma('qs', dS.h, Sst.ap, [Sst], [dS])
                k.phase_boundary()

        with ExitStack() as ph:
            sseg_h = [k.sbuf(ph, "b_sseg0", [128, NCORE*128], F32)] * 2
            S_in = k.sbuf(ph, "b_Sin", [128, H*128], F32)
            S_bf = [k.sbuf(ph, f"b_Sbf{i}", [128, 128], BF16) for i in range(2)]
            qd_c = [k.sbuf(ph, f"b_qdc{i}", [128, 128], BF16) for i in range(2)]
            sw_s = k.sbuf(ph, "b_sw", [128, NCORE*H], F32)
            dqb_s = k.sbuf(ph, "b_dqb", [128, H*128], BF16)
            wob = [k.sbuf(ph, f"b_wob{i}", [128, ND*512], BF16) for i in range(4)]
            bob_s = k.sbuf(ph, "b_bob", [128, D], BF16)
            xt = [k.sbuf(ph, f"b_xt{i}", [128, D], F32) for i in range(2)]
            o_t = k.sbuf(ph, "b_ot", [128, D], F32)
            x1_t = k.sbuf(ph, "b_x1t", [128, D], F32)
            psC = [k.psum(ph, f"b_psC{i}", [128, 512], F32) for i in range(2)]
            psO1 = k.psum(ph, "b_psO1", [128, 512], F32)
            psO2 = k.psum(ph, "b_psO2", [128, 512], F32)
            psBQG = k.psum(ph, "b_psQG", [128, 512], F32)
            psBQR = k.psum(ph, "b_psQR", [128, 512], F32)
            psBQS = k.psum(ph, "b_psQS", [128, 512], F32)

            k.dma('qs', sw_s.ap, sw_in.ap, [sw_in], [sw_s])
            k.dma('qg', dqb_s.h[:, :].rearrange("p (h c) -> p h c", c=128),
                  dq_b.h.rearrange("h p c -> p h c"), [dq_b], [dqb_s])
            k.dma('qg', bob_s.ap, bo_b.ap, [bo_b], [bob_s])
            wr = woT.h.rearrange("(t p) c -> p t c", p=128)
            for nc_ in range(4):
                k.dma('qg', wob[nc_].h[:, :].rearrange("p (t c) -> p t c", c=512),
                      wr[:, :, nc_*512:(nc_+1)*512], [woT], [wob[nc_]])

            ssegr = sseg_g.h.rearrange("(r h p) c -> h p r c", r=NCORE, p=128)
            for h in range(H):
                sh = sseg_h[h % 2]
                k.dma('qs', sh.h[:, :].rearrange("p (j c) -> p j c", c=128),
                      ssegr[h], [sseg_g], [sh])
                k.memset(S_in[:, h*128:(h+1)*128], 0.0, [], [S_in])
                for j in range(NCORE - 1):
                    k.stt(S_in[:, h*128:(h+1)*128], sh[:, j*128:(j+1)*128], sw_s[:, j*H+h:j*H+h+1],
                          S_in[:, h*128:(h+1)*128], AluOpType.mult, AluOpType.add,
                          [sh, sw_s, S_in], [S_in])
            for c in range(NCH):
                for h in range(H):
                    par = h % 2
                    k.ts(S_bf[par].ap, S_in[:, h*128:(h+1)*128], float(LAM_C[h] ** c), None,
                         AluOpType.mult, None, [S_in], [S_bf[par]])
                    k.tt(qd_c[par].ap, qf[:, h*SEG + c*128:h*SEG + (c+1)*128], dqb_s[:, h*128:(h+1)*128],
                         AluOpType.mult, [qf, dqb_s], [qd_c[par]])
                    k.mm(psC[par][:, 0:128], S_bf[par].ap, qd_c[par].ap, [S_bf[par], qd_c[par]], [psC[par]])
                    k.tt(attn_bf[:, h*SEG + c*128:h*SEG + (c+1)*128],
                         attn_bf[:, h*SEG + c*128:h*SEG + (c+1)*128],
                         psC[par][:, 0:128], AluOpType.add, [psC[par]], [attn_bf])

            wt = [k.sbuf(ph, f"q_wt{i}", [128, 512], BF16) for i in range(2)]
            sg = k.sbuf(ph, "q_sg", [128, 512], BF16)
            wabs = k.sbuf(ph, "q_wabs", [128, 512], BF16)
            s_bf = k.sbuf(ph, "q_sbf", [2, 512], BF16)
            rs = k.sbuf(ph, "q_rs", [2, 512], F32)
            z = k.sbuf(ph, "q_z", [128, 512], F32)
            m = k.sbuf(ph, "q_m", [128, 512], F32)
            w1 = k.sbuf(ph, "q_w1", [128, 512], BF16)
            wq = [k.sbuf(ph, f"q_wq{i}", [128, 512], BF16) for i in range(2)]
            gm = k.sbuf(ph, "q_gm", [128, 2], BF16)
            gm2f = k.sbuf(ph, "q_gm2f", [2, 128], F32)
            gm2b = k.sbuf(ph, "q_gm2b", [2, 128], BF16)
            k.dma('qs', gm.ap, gm_in.ap, [gm_in], [gm])
            k.dma('qs', gm2f.ap, gm2f_in.ap, [gm2f_in], [gm2f])
            k.dma('qs', gm2b.ap, gm2b_in.ap, [gm2b_in], [gm2b])

            def quant_tile(src_ap, src_buf, dst_ap, dst_buf, F, idx):
                wtb, wqb = wt[idx % 2], wq[idx % 2]
                k.dma('qg', wtb[:, 0:F], src_ap, [src_buf], [wtb])
                k.act(sg[:, 0:F], wtb[:, 0:F], AF.Sign, [wtb], [sg])
                k.act(wabs[:, 0:F], wtb[:, 0:F], AF.Abs, [wtb], [wabs])
                k.mm(psBQG[0:2, 0:F], gm.ap, wabs[:, 0:F], [gm, wabs], [psBQG])
                k.ts(s_bf[:, 0:F], psBQG[0:2, 0:F], 1.0/64.0, 1e-8, AluOpType.mult, AluOpType.max,
                     [psBQG], [s_bf])
                k.op('v', lambda e, F=F: e.reciprocal(rs[:, 0:F], s_bf[:, 0:F]), reads=[s_bf], writes=[rs])
                k.mm(psBQR[:, 0:F], gm2f.ap, rs[:, 0:F], [gm2f, rs], [psBQR])
                k.mm(psBQS[:, 0:F], gm2b.ap, s_bf[:, 0:F], [gm2b, s_bf], [psBQS])
                k.tt(z[:, 0:F], wabs[:, 0:F], psBQR[:, 0:F], AluOpType.mult, [wabs, psBQR], [z])
                k.ts(m[:, 0:F], z[:, 0:F], 0.5, None, AluOpType.is_gt, None, [z], [m])
                k.tt(w1[:, 0:F], m[:, 0:F], psBQS[:, 0:F], AluOpType.mult, [m, psBQS], [w1])
                k.tt(wqb[:, 0:F], w1[:, 0:F], sg[:, 0:F], AluOpType.mult, [w1, sg], [wqb])
                k.dma('qs', dst_ap, wqb[:, 0:F], [wqb], [dst_buf])

            wgur = wgu_in.h.rearrange("(t p) c -> t p c", p=128)
            qgur_o = qgu_sh.h.rearrange("(t p) c -> t p c", p=128)
            idx = 0
            for t in range(D // 128):
                for f in range(GU_SH // 512):
                    quant_tile(wgur[t, :, f*512:(f+1)*512], wgu_in, qgur_o[t, :, f*512:(f+1)*512],
                               qgu_sh, 512, idx); idx += 1
            wpr = wp_in.h.rearrange("(t p) c -> t p c", p=128)
            qpr_o = qp_sh.h.rearrange("(t p) c -> t p c", p=128)
            for t in range(4*D // 128):
                quant_tile(wpr[t, :, 0:P_SH], wp_in, qpr_o[t, :, 0:P_SH], qp_sh, P_SH, idx); idx += 1
            k.collective(qgu_sh.h, qgu_g.h, [qgu_sh], qgu_g)
            k.collective(qp_sh.h, qp_g.h, [qp_sh], qp_g)

            xr = x_in.h.rearrange("(n p) d -> n p d", p=128)
            x1r = x1_d.h.rearrange("(n p) d -> n p d", p=128)
            for tg in range(NT):
                k.dma('qs', xt[tg % 2].ap, xr[tg], [x_in], [xt[tg % 2]])
                for nc_ in range(4):
                    ps = psO1 if nc_ % 2 == 0 else psO2
                    wbuf = wob[nc_]
                    for et in range(ND):
                        k.mm(ps[:, 0:512], attn_bf[:, et*SEG + tg*128: et*SEG + (tg+1)*128],
                             wbuf[:, et*512:(et+1)*512], [attn_bf, wbuf], [ps],
                             start=(et == 0), stop=(et == ND-1))
                    k.tt(o_t[:, nc_*512:(nc_+1)*512], ps[:, 0:512], bob_s[:, nc_*512:(nc_+1)*512],
                         AluOpType.add, [ps, bob_s], [o_t])
                emit_ln_tile(k, o_t.ap, o_t, x1_t.ap, x1_t, stats)
                k.tt(x1_t.ap, x1_t.ap, xt[tg % 2].ap, AluOpType.add, [x1_t, xt[tg % 2]], [x1_t])
                k.dma('qs', x1r[tg], x1_t.ap, [x1_t], [x1_d])
            k.phase_boundary()

    with ExitStack() as dsc:
        o128c = k.sbuf(dsc, "d_o128c", [128, 1], BF16)
        o1r = k.sbuf(dsc, "d_o1r", [1, 128], F32)
        sqrow = k.sbuf(dsc, "d_sqrow", [1, SEG], F32)
        rstd = k.sbuf(dsc, "d_rstd", [1, SEG], F32)
        rstd_b = k.sbuf(dsc, "d_rstdb", [128, SEG], F32)
        k.dma('qs', o128c.ap, o128c_in.ap, [o128c_in], [o128c])
        k.dma('qs', o1r.ap, o1r_in.ap, [o1r_in], [o1r])
        xr1 = x1_d.h.rearrange("(n p) d -> n p d", p=128)

        with ExitStack() as ph:
            xt = [k.sbuf(ph, f"d_xt{i}", [128, D], F32) for i in range(2)]
            stage = k.sbuf(ph, "d_stage", [128, D], BF16)
            ln3f = k.sbuf(ph, "d_ln3f", [128, ND*SEG], BF16)
            hw = k.sbuf(ph, "d_hw", [128, SEG], BF16)
            hsq = k.sbuf(ph, "d_hsq", [128, SEG], BF16)
            wg = [k.sbuf(ph, f"d_wg{i}", [128, ND*128], BF16) for i in range(2)]
            wu = [k.sbuf(ph, f"d_wu{i}", [128, ND*128], BF16) for i in range(2)]
            gs = k.sbuf(ph, "d_gs", [128, SEG], BF16)
            us = k.sbuf(ph, "d_us", [128, SEG], BF16)
            psA = k.psum(ph, "d_psA", [128, 512], F32)
            psB = k.psum(ph, "d_psB", [128, 512], F32)
            psC = k.psum(ph, "d_psC", [128, 512], F32)
            psD = k.psum(ph, "d_psD", [128, 512], F32)
            psT = k.psum(ph, "d_psT", [128, 1024], BF16)
            psQ = k.psum(ph, "d_psQ", [128, 512], F32)

            k.memset(sqrow.ap, 0.0, [], [sqrow])
            for i in range(NT):
                k.dma('qs', xt[i % 2].ap, xr1[i], [x1_d], [xt[i % 2]])
                emit_ln_tile(k, xt[i % 2].ap, xt[i % 2], stage.ap, stage, stats)
                for dt in range(ND):
                    k.trans(psT[:, 0:128], stage[:, dt*128:(dt+1)*128], ident_s.ap, [stage, ident_s], [psT])
                    k.copy(ln3f[:, dt*SEG + i*128: dt*SEG + (i+1)*128], psT[:, 0:128], [psT], [ln3f])

            qgur = qgu_g.h.rearrange("(r t p) c -> r p t c", r=NCORE, p=128)
            NGC = GU_SH
            for hc in range(NHID):
                bg, bu = wg[hc % 2], wu[hc % 2]
                gcol = hc * 128
                r, lc = gcol // NGC, gcol % NGC
                k.dma('qg', bg.h[:, :].rearrange("p (t c) -> p t c", c=128),
                      qgur[r][:, :, lc:lc+128], [qgu_g], [bg])
                ucol = 4*D + hc * 128
                r2, lc2 = ucol // NGC, ucol % NGC
                k.dma('qg', bu.h[:, :].rearrange("p (t c) -> p t c", c=128),
                      qgur[r2][:, :, lc2:lc2+128], [qgu_g], [bu])
                for n in range(2):
                    psg = psA if n == 0 else psB
                    psu = psC if n == 0 else psD
                    for dt in range(ND):
                        k.mm(psg[:, 0:512], bg[:, dt*128:(dt+1)*128],
                             ln3f[:, dt*SEG + n*512: dt*SEG + (n+1)*512], [bg, ln3f], [psg],
                             start=(dt == 0), stop=(dt == ND-1))
                    for dt in range(ND):
                        k.mm(psu[:, 0:512], bu[:, dt*128:(dt+1)*128],
                             ln3f[:, dt*SEG + n*512: dt*SEG + (n+1)*512], [bu, ln3f], [psu],
                             start=(dt == 0), stop=(dt == ND-1))
                    k.act(gs[:, n*512:(n+1)*512], psg[:, 0:512], AF.Silu, [psg], [gs])
                    k.copy(us[:, n*512:(n+1)*512], psu[:, 0:512], [psu], [us])
                k.tt(hw.ap, gs.ap, us.ap, AluOpType.mult, [gs, us], [hw])
                k.dma('qs', hd_d.h[:, hc*SEG:(hc+1)*SEG], hw.ap, [hw], [hd_d])
                k.tt(hsq.ap, hw.ap, hw.ap, AluOpType.mult, [hw], [hsq])
                for n in range(2):
                    k.mm(psQ[0:1, 0:512], o128c.ap, hsq[:, n*512:(n+1)*512], [o128c, hsq], [psQ])
                    k.stt(sqrow[0:1, n*512:(n+1)*512], psQ[0:1, 0:512], 1.0,
                          sqrow[0:1, n*512:(n+1)*512], AluOpType.mult, AluOpType.add,
                          [psQ, sqrow], [sqrow])
            k.act(rstd[0:1, :], sqrow[0:1, :], AF.Ln, [sqrow], [rstd], scale=1.0/(4*D),
                  bias=1.1920929e-7)
            k.act(rstd[0:1, :], rstd[0:1, :], AF.Exp, [rstd], [rstd], scale=-0.5)
            for n in range(2):
                k.mm(psQ[:, 0:512], o1r.ap, rstd[0:1, n*512:(n+1)*512], [o1r, rstd], [psQ])
                k.copy(rstd_b[:, n*512:(n+1)*512], psQ[:, 0:512], [psQ], [rstd_b])
            k.phase_boundary()

        with ExitStack() as ph:
            xt = [k.sbuf(ph, f"d2_xt{i}", [128, D], F32) for i in range(2)]
            stage = k.sbuf(ph, "d2_stage", [128, D], BF16)
            hbf = k.sbuf(ph, "d2_hbf", [128, NHID*512], BF16)
            wp = [k.sbuf(ph, f"d2_wp{i}", [128, NHID*128], BF16) for i in range(2)]
            obf = k.sbuf(ph, "d2_obf", [128, NT*D], BF16)
            x2_t = k.sbuf(ph, "d2_x2t", [128, D], F32)
            psA = k.psum(ph, "d2_psA", [128, 512], F32)
            psB = k.psum(ph, "d2_psB", [128, 512], F32)
            psT = k.psum(ph, "d2_psT", [128, 1024], BF16)

            qpr = qp_g.h.rearrange("(r t p) c -> r p t c", r=NCORE, p=128)
            hr = hd_d.h[:, :].rearrange("p (n t) -> p n t", t=SEG)
            for th in range(2):
                k.dma('qs', hbf.h[:, :].rearrange("p (n t) -> p n t", t=512),
                      hr[:, :, th*512:(th+1)*512], [hd_d], [hbf])
                for dot in range(ND):
                    bp = wp[dot % 2]
                    pcol = dot * 128
                    r, lc = pcol // P_SH, pcol % P_SH
                    k.dma('qg', bp.h[:, :].rearrange("p (t c) -> p t c", c=128),
                          qpr[r][:, :, lc:lc+128], [qp_g], [bp])
                    ps = psA if dot % 2 == 0 else psB
                    for ht in range(NHID):
                        k.mm(ps[:, 0:512], bp[:, ht*128:(ht+1)*128],
                             hbf[:, ht*512:(ht+1)*512], [bp, hbf], [ps],
                             start=(ht == 0), stop=(ht == NHID-1))
                    k.tt(stage[:, 0:512], ps[:, 0:512], rstd_b[:, th*512:(th+1)*512], AluOpType.mult,
                         [ps, rstd_b], [stage])
                    for q4 in range(4):
                        tg = th*4 + q4
                        k.trans(psT[:, 0:128], stage[:, q4*128:(q4+1)*128], ident_s.ap,
                                [stage, ident_s], [psT])
                        k.copy(obf[:, tg*D + dot*128: tg*D + (dot+1)*128], psT[:, 0:128], [psT], [obf])

            x2r = x2_d.h.rearrange("(n p) d -> n p d", p=128)
            for tg in range(NT):
                k.dma('qs', xt[tg % 2].ap, xr1[tg], [x1_d], [xt[tg % 2]])
                emit_ln_tile(k, obf[:, tg*D:(tg+1)*D], obf, x2_t.ap, x2_t, stats)
                k.tt(x2_t.ap, x2_t.ap, xt[tg % 2].ap, AluOpType.add, [x2_t, xt[tg % 2]], [x2_t])
                k.dma('qs', x2r[tg], x2_t.ap, [x2_t], [x2_d])
            k.phase_boundary()

    with ExitStack() as sc1:
        kd = k.sbuf(sc1, "e_kd", [DS, SEG], BF16)
        vd = k.sbuf(sc1, "e_vd", [DS, SEG], BF16)
        qd = k.sbuf(sc1, "e_qd", [DS, SEG], BF16)
        lg = k.sbuf(sc1, "e_lg", [DS, SEG], F32)
        tli = k.sbuf(sc1, "n_tli", [TC, TC], F32)
        sln = k.sbuf(sc1, "n_sln", [TC, TC], BF16)
        sun = k.sbuf(sc1, "n_sun", [TC, TC], BF16)
        su = k.sbuf(sc1, "n_su", [TC, TC], BF16)
        sui = k.sbuf(sc1, "n_sui", [TC, TC], BF16)
        onesc = k.sbuf(sc1, "n_oc", [64, 1], F32)
        onesr = k.sbuf(sc1, "n_or", [1, 64], F32)
        for (d, s_) in ((tli_in, tli), (sln_in, sln), (sun_in, sun), (su_in, su),
                        (sui_in, sui), (oc_in, onesc), (or_in, onesr)):
            k.dma('qs', s_.ap, d.ap, [d], [s_])

        with ExitStack() as ph:
            xt = [k.sbuf(ph, f"e_xt{i}", [128, D], F32) for i in range(2)]
            stage = k.sbuf(ph, "e_stage", [128, D], BF16)
            lnf = k.sbuf(ph, "e_lnf", [128, ND*SEG], BF16)
            wsb = k.sbuf(ph, "e_wsb", [128, ND*4*DS], BF16)
            proj = k.sbuf(ph, "e_proj", [64, 4*SEG], F32)
            sqb = k.sbuf(ph, "e_sqb", [64, SEG], F32)
            nrow = k.sbuf(ph, "e_nrow", [1, SEG], F32)
            rrow = k.sbuf(ph, "e_rrow", [1, SEG], F32)
            wab_s = k.sbuf(ph, "e_wab", [64, 1], F32)
            lam_s = k.sbuf(ph, "e_lam", [64, 4], F32)
            psA = k.psum(ph, "e_psA", [128, 512], F32)
            psB = k.psum(ph, "e_psB", [128, 512], F32)
            psC = k.psum(ph, "e_psC", [128, 512], F32)
            psT = k.psum(ph, "e_psT", [128, 1024], BF16)

            k.dma('qs', wab_s.ap, wab_in.ap, [wab_in], [wab_s])
            k.dma('qs', lam_s[:, 0:1], lam_in.ap, [lam_in], [lam_s])
            xr2 = x2_d.h.rearrange("(n p) d -> n p d", p=128)
            for i in range(NT):
                k.dma('qs', xt[i % 2].ap, xr2[i], [x2_d], [xt[i % 2]])
                emit_ln_tile(k, xt[i % 2].ap, xt[i % 2], stage.ap, stage, stats)
                for dt in range(ND):
                    k.trans(psT[:, 0:128], stage[:, dt*128:(dt+1)*128], ident_s.ap, [stage, ident_s], [psT])
                    k.copy(lnf[:, dt*SEG + i*128: dt*SEG + (i+1)*128], psT[:, 0:128], [psT], [lnf])
            k.dma('qg', wsb.h[:, :].rearrange("p (t c) -> p t c", c=4*DS),
                  wkvqaT_in.h.rearrange("(t p) c -> p t c", p=128), [wkvqaT_in], [wsb])
            for j in range(4):
                for n in range(2):
                    ps = psA if n == 0 else psB
                    for dt in range(ND):
                        k.mm(ps[0:64, 0:512], wsb[:, dt*4*DS + j*DS: dt*4*DS + (j+1)*DS],
                             lnf[:, dt*SEG + n*512: dt*SEG + (n+1)*512], [wsb, lnf], [ps],
                             start=(dt == 0), stop=(dt == ND-1))
                    if j < 3:
                        k.copy(proj[:, j*SEG + n*512: j*SEG + (n+1)*512], ps[0:64, 0:512], [ps], [proj])
                    else:
                        k.act(proj[:, j*SEG + n*512: j*SEG + (n+1)*512], ps[0:64, 0:512], AF.Sigmoid,
                              [ps, wab_s], [proj], bias=wab_s.ap)
            for j, dst in ((0, kd), (2, qd)):
                k.act(sqb.ap, proj[:, j*SEG:(j+1)*SEG], AF.Square, [proj], [sqb])
                for n in range(2):
                    k.mm(psC[0:1, 0:512], onesc.ap, sqb[:, n*512:(n+1)*512], [onesc, sqb], [psC])
                    k.act(nrow[:, n*512:(n+1)*512], psC[0:1, 0:512], AF.Sqrt, [psC], [nrow])
                k.ts(nrow.ap, nrow.ap, 1e-12, None, AluOpType.max, None, [nrow], [nrow])
                k.op('v', lambda e: e.reciprocal(rrow.ap, nrow.ap), reads=[nrow], writes=[rrow])
                for n in range(2):
                    k.mm(psC[0:64, 0:512], onesr.ap, rrow[:, n*512:(n+1)*512], [onesr, rrow], [psC])
                    k.tt(dst[:, n*512:(n+1)*512], proj[:, j*SEG + n*512: j*SEG + (n+1)*512],
                         psC[0:64, 0:512], AluOpType.mult, [proj, psC], [dst])
            k.copy(vd.ap, proj[:, 1*SEG:2*SEG], [proj], [vd])
            k.act(lam_s[:, 1:2], lam_s[:, 0:1], AF.Sigmoid, [lam_s], [lam_s])
            k.act(lam_s[:, 2:3], lam_s[:, 1:2], AF.Ln, [lam_s], [lam_s], bias=1e-8)
            k.ts(lam_s[:, 3:4], lam_s[:, 2:3], 8.0, None, AluOpType.mult, None, [lam_s], [lam_s])
            k.ts(lg.ap, proj[:, 3*SEG:4*SEG], lam_s[:, 3:4], None, AluOpType.mult, None,
                 [proj, lam_s], [lg])
            k.phase_boundary()

        with ExitStack() as sd:
            Hst = k.sbuf(sd, "n_Hst", [64, 64], F32)
            Hbf = k.sbuf(sd, "n_Hbf", [64, 64], BF16)
            lgT = k.sbuf(sd, "n_lgT", [TC, 64], F32)
            ALi = k.sbuf(sd, "n_ALi", [TC, 64], F32)
            A_t = k.sbuf(sd, "n_At", [TC, 64], F32)
            Apv = k.sbuf(sd, "n_Apv", [TC, 64], F32)
            Ainv = k.sbuf(sd, "n_Ainv", [TC, 64], F32)
            a_t = k.sbuf(sd, "n_at", [TC, 64], F32)
            w1 = k.sbuf(sd, "n_w1", [TC, 64], F32)
            tmpf = k.sbuf(sd, "n_tmpf", [TC, 64], F32)
            kc = k.sbuf(sd, "n_kc", [TC, 64], BF16)
            vc = k.sbuf(sd, "n_vc", [TC, 64], BF16)
            qc = k.sbuf(sd, "n_qc", [TC, 64], BF16)
            kt_ = k.sbuf(sd, "n_kt", [TC, 64], BF16)
            qp_ = k.sbuf(sd, "n_qp", [TC, 64], BF16)
            qi_ = k.sbuf(sd, "n_qi", [TC, 64], BF16)
            Dm = k.sbuf(sd, "n_Dm", [TC, 64], BF16)
            DmT = k.sbuf(sd, "n_DmT", [64, TC], BF16)
            ktT = k.sbuf(sd, "n_ktT", [64, TC], BF16)
            qpT_ = k.sbuf(sd, "n_qpT", [64, TC], BF16)
            qiT = k.sbuf(sd, "n_qiT", [64, TC], BF16)
            Npow = [k.sbuf(sd, f"n_N{j}", [TC, TC], BF16) for j in range(5)]
            NTpow = [k.sbuf(sd, f"n_NT{j}", [TC, TC], BF16) for j in range(6)]
            X = k.sbuf(sd, "n_X", [TC, 128], BF16)
            Gt = k.sbuf(sd, "n_Gt", [TC, TC], BF16)
            d2 = k.sbuf(sd, "n_d2", [TC, 64], F32)
            sq2 = k.sbuf(sd, "n_sq2", [TC, 64], F32)
            scol = k.sbuf(sd, "n_scol", [TC, 1], F32)
            Aer = k.sbuf(sd, "n_Aer", [1, 64], F32)
            AeC = k.sbuf(sd, "n_AeC", [64, 1], F32)
            Wn = k.sbuf(sd, "n_Wn", [TC, 64], BF16)
            Mc = k.sbuf(sd, "n_Mc", [TC, TC], BF16)
            McT = k.sbuf(sd, "n_McT", [TC, TC], BF16)
            tmpb = k.sbuf(sd, "n_tmpb", [TC, TC], BF16)
            Mrun = k.sbuf(sd, "n_Mrun", [TC, TC], BF16)
            MrunT = k.sbuf(sd, "n_MrunT", [TC, TC], BF16)
            outf = k.sbuf(sd, "n_outf", [TC, TC], F32)
            ps1 = k.psum(sd, "n_ps1", [128, 512], F32)
            ps2 = k.psum(sd, "n_ps2", [128, 512], F32)
            ps3 = k.psum(sd, "n_ps3", [128, 512], F32)
            psTb = k.psum(sd, "n_psTb", [128, 1024], BF16)
            psF = k.psum(sd, "n_psF", [128, 512], F32)

            def tr(src_ap, src_buf, dst_ap, dst_buf, f32=False, pdim=64):
                if f32:
                    k.trans(psF[0:64, 0:pdim], src_ap, identf[0:pdim, 0:pdim], [src_buf, identf], [psF])
                    k.copy(dst_ap, psF[0:64, 0:pdim], [psF], [dst_buf])
                else:
                    k.trans(psTb[0:64, 0:pdim], src_ap, ident_s[0:pdim, 0:pdim], [src_buf, ident_s], [psTb])
                    k.copy(dst_ap, psTb[0:64, 0:pdim], [psTb], [dst_buf])

            def apply_M(width):
                for j in (5, 4, 3, 2, 1, 0):
                    k.mm(ps3[0:TC, 0:width], ident_s[0:TC, 0:TC], X[:, 0:width], [ident_s, X], [ps3],
                         start=True, stop=False)
                    k.mm(ps3[0:TC, 0:width], NTpow[j].ap, X[:, 0:width], [NTpow[j], X], [ps3],
                         start=False, stop=True)
                    k.copy(X[:, 0:width], ps3[0:TC, 0:width], [ps3], [X])

            def build_powers():
                for lev in range(1, 5):
                    k.mm(ps3[0:TC, 0:TC], NTpow[lev-1].ap, Npow[lev-1].ap, [NTpow[lev-1], Npow[lev-1]], [ps3])
                    k.copy(Npow[lev].ap, ps3[0:TC, 0:TC], [ps3], [Npow[lev]])
                    k.mm(ps2[0:TC, 0:TC], Npow[lev-1].ap, NTpow[lev-1].ap, [Npow[lev-1], NTpow[lev-1]], [ps2])
                    k.copy(NTpow[lev].ap, ps2[0:TC, 0:TC], [ps2], [NTpow[lev]])
                k.mm(ps3[0:TC, 0:TC], Npow[4].ap, NTpow[4].ap, [Npow[4], NTpow[4]], [ps3])
                k.copy(NTpow[5].ap, ps3[0:TC, 0:TC], [ps3], [NTpow[5]])

            def build_LN_masks():
                k.mm(ps1[0:TC, 0:TC], ktT.ap, DmT.ap, [ktT, DmT], [ps1])
                k.tt(Npow[0].ap, ps1[0:TC, 0:TC], sln.ap, AluOpType.mult, [ps1, sln], [Npow[0]])
                k.mm(ps2[0:TC, 0:TC], DmT.ap, ktT.ap, [DmT, ktT], [ps2])
                k.tt(NTpow[0].ap, ps2[0:TC, 0:TC], sun.ap, AluOpType.mult, [ps2, sun], [NTpow[0]])

            def build_Dm(with_s):
                k.tt(Dm.ap, kc.ap, w1.ap, AluOpType.mult, [kc, w1], [Dm])
                if with_s:
                    k.ts(Dm.ap, Dm.ap, scol.ap, None, AluOpType.mult, None, [Dm, scol], [Dm])
                tr(Dm.ap, Dm, DmT.ap, DmT)

            def chunk_prelude(c, pass2):
                cs = slice(c*TC, (c+1)*TC)
                tr(lg[:, cs], lg, lgT.ap, lgT, f32=True)
                k.mm(psF[0:TC, 0:64], tli.ap, lgT.ap, [tli, lgT], [psF])
                k.copy(ALi.ap, psF[0:TC, 0:64], [psF], [ALi])
                k.act(A_t.ap, ALi.ap, AF.Exp, [ALi], [A_t])
                k.tt(tmpf.ap, ALi.ap, lgT.ap, AluOpType.subtract, [ALi, lgT], [tmpf])
                k.act(Apv.ap, tmpf.ap, AF.Exp, [tmpf], [Apv])
                k.act(Ainv.ap, ALi.ap, AF.Exp, [ALi], [Ainv], scale=-1.0)
                k.act(a_t.ap, lgT.ap, AF.Exp, [lgT], [a_t])
                k.mm(psF[0:1, 0:64], onesc.ap, lgT.ap, [onesc, lgT], [psF])
                k.act(Aer.ap, psF[0:1, 0:64], AF.Exp, [psF], [Aer])
                k.trans(psF[0:64, 0:1], Aer.ap, identf[0:1, 0:1], [Aer, identf], [psF])
                k.copy(AeC.ap, psF[0:64, 0:1], [psF], [AeC])
                tr(kd[:, cs], kd, kc.ap, kc)
                tr(vd[:, cs], vd, vc.ap, vc)
                tr(qd[:, cs], qd, qc.ap, qc)
                k.tt(kt_.ap, kc.ap, Apv.ap, AluOpType.mult, [kc, Apv], [kt_])
                k.tt(qp_.ap, qc.ap, Apv.ap, AluOpType.mult, [qc, Apv], [qp_])
                k.tt(qi_.ap, qc.ap, A_t.ap, AluOpType.mult, [qc, A_t], [qi_])
                k.ts(tmpf.ap, a_t.ap, -1.0, 1.0, AluOpType.mult, AluOpType.add, [a_t], [tmpf])
                k.tt(w1.ap, tmpf.ap, Ainv.ap, AluOpType.mult, [tmpf, Ainv], [w1])
                tr(kt_.ap, kt_, ktT.ap, ktT)
                tr(qp_.ap, qp_, qpT_.ap, qpT_)
                if pass2:
                    tr(qi_.ap, qi_, qiT.ap, qiT)

            k.memset(Hst.ap, 0.0, [], [Hst])
            k.memset(Hbf.ap, 0.0, [], [Hbf])
            k.copy(Mrun.ap, ident_s[0:TC, 0:TC], [ident_s], [Mrun])
            k.copy(MrunT.ap, ident_s[0:TC, 0:TC], [ident_s], [MrunT])
            for c in range(NCHK):
                chunk_prelude(c, pass2=False)
                build_Dm(with_s=False)
                build_LN_masks()
                build_powers()
                k.mm(ps1[0:TC, 0:64], ktT.ap, Hbf.ap, [ktT, Hbf], [ps1])
                k.tt(X[:, 0:64], vc.ap, ps1[0:TC, 0:64], AluOpType.subtract, [vc, ps1], [X])
                k.copy(X[:, 64:128], kt_.ap, [kt_], [X])
                apply_M(128)
                k.mm(ps2[0:TC, 0:64], ident_s[0:TC, 0:TC], Hbf.ap, [ident_s, Hbf], [ps2],
                     start=True, stop=False)
                k.mm(ps2[0:TC, 0:64], Dm.ap, X[:, 0:64], [Dm, X], [ps2], start=False, stop=True)
                k.ts(Hst.ap, ps2[0:TC, 0:64], AeC.ap, None, AluOpType.mult, None, [ps2, AeC], [Hst])
                k.copy(Hbf.ap, Hst.ap, [Hst], [Hbf])
                k.ts(Wn.ap, X[:, 64:128], -1.0, None, AluOpType.mult, None, [X], [Wn])
                k.mm(ps1[0:TC, 0:TC], ident_s[0:TC, 0:TC], ident_s[0:TC, 0:TC], [ident_s], [ps1],
                     start=True, stop=False)
                k.mm(ps1[0:TC, 0:TC], Dm.ap, Wn.ap, [Dm, Wn], [ps1], start=False, stop=True)
                k.ts(Mc.ap, ps1[0:TC, 0:TC], AeC.ap, None, AluOpType.mult, None, [ps1, AeC], [Mc])
                k.mm(ps2[0:TC, 0:TC], ident_s[0:TC, 0:TC], ident_s[0:TC, 0:TC], [ident_s], [ps2],
                     start=True, stop=False)
                k.mm(ps2[0:TC, 0:TC], Wn.ap, Dm.ap, [Wn, Dm], [ps2], start=False, stop=True)
                k.copy(tmpb.ap, ps2[0:TC, 0:TC], [ps2], [tmpb])
                k.mm(psF[0:TC, 0:TC], onesr.ap, Aer.ap, [onesr, Aer], [psF])
                k.tt(McT.ap, tmpb.ap, psF[0:TC, 0:TC], AluOpType.mult, [tmpb, psF], [McT])
                k.mm(ps1[0:TC, 0:TC], McT.ap, Mrun.ap, [McT, Mrun], [ps1])
                k.mm(ps2[0:TC, 0:TC], Mrun.ap, McT.ap, [Mrun, McT], [ps2])
                k.copy(Mrun.ap, ps1[0:TC, 0:TC], [ps1], [Mrun])
                k.copy(MrunT.ap, ps2[0:TC, 0:TC], [ps2], [MrunT])
            k.copy(outf.ap, MrunT.ap, [MrunT], [outf])
            k.dma('qs', dn_l.h[0:64, :], outf.ap, [outf], [dn_l])
            k.dma('qs', dn_l.h[64:128, :], Hst.ap, [Hst], [dn_l])
            k.collective(dn_l.h, dn_g.h, [dn_l], dn_g)
            k.phase_boundary()

            with ExitStack() as ph:
                dn_sb = k.sbuf(ph, "g_dnsb", [128, NCORE*64], F32)
                dnsel = k.sbuf(ph, "g_dnsel", [64, 16], F32)
                Mu_f = k.sbuf(ph, "g_Muf", [64, 64], F32)
                Mu_b = k.sbuf(ph, "g_Mub", [64, 64], BF16)
                bu_b = k.sbuf(ph, "g_bub", [64, 64], BF16)
                y_buf = k.sbuf(ph, "g_ybuf", [64, NCHK*64], F32)
                ynf = k.sbuf(ph, "g_ynf", [64, SEG], BF16)
                wos = k.sbuf(ph, "g_wos", [64, D], BF16)
                nwb = k.sbuf(ph, "g_nwb", [64, NCHK*64], F32)
                msq = k.sbuf(ph, "g_msq", [64, NCHK], F32)
                xt = [k.sbuf(ph, f"g_xt{i}", [128, D], F32) for i in range(2)]
                o_t = k.sbuf(ph, "g_ot", [128, D], F32)
                x3_t = k.sbuf(ph, "g_x3t", [128, D], F32)

                k.dma('qs', dnsel.ap, dnsel_in.ap, [dnsel_in], [dnsel])
                k.dma('qs', nwb.ap, nwb_in.ap, [nwb_in], [nwb])
                k.dma('qg', wos.ap, wo1T_in.ap, [wo1T_in], [wos])
                for r in range(NCORE):
                    k.dma('qs', dn_sb[:, r*64:(r+1)*64], dn_g.h[r*128:(r+1)*128, :], [dn_g], [dn_sb])

                k.memset(Hst.ap, 0.0, [], [Hst])
                k.memset(Hbf.ap, 0.0, [], [Hbf])
                for j in range(NCORE - 1):
                    k.ts(Mu_f.ap, dn_sb[0:64, j*64:(j+1)*64], dnsel[:, j:j+1], None,
                         AluOpType.mult, None, [dn_sb, dnsel], [Mu_f])
                    k.stt(Mu_f.ap, identf[0:64, 0:64], dnsel[:, 8+j:9+j], Mu_f.ap,
                          AluOpType.mult, AluOpType.add, [identf, dnsel, Mu_f], [Mu_f])
                    k.copy(Mu_b.ap, Mu_f.ap, [Mu_f], [Mu_b])
                    k.ts(Mu_f.ap, dn_sb[64:128, j*64:(j+1)*64], dnsel[:, j:j+1], None,
                         AluOpType.mult, None, [dn_sb, dnsel], [Mu_f])
                    k.copy(bu_b.ap, Mu_f.ap, [Mu_f], [bu_b])
                    k.mm(ps1[0:TC, 0:64], Mu_b.ap, Hbf.ap, [Mu_b, Hbf], [ps1],
                         start=True, stop=False)
                    k.mm(ps1[0:TC, 0:64], ident_s[0:TC, 0:TC], bu_b.ap, [ident_s, bu_b],
                         [ps1], start=False, stop=True)
                    k.copy(Hst.ap, ps1[0:TC, 0:64], [ps1], [Hst])
                    k.copy(Hbf.ap, Hst.ap, [Hst], [Hbf])

                for c in range(NCHK):
                    cs = slice(c*TC, (c+1)*TC)
                    chunk_prelude(c, pass2=True)
                    build_Dm(with_s=False)
                    build_LN_masks()
                    build_powers()
                    k.mm(ps1[0:TC, 0:64], ktT.ap, Hbf.ap, [ktT, Hbf], [ps1])
                    k.tt(X[:, 0:64], vc.ap, ps1[0:TC, 0:64], AluOpType.subtract, [vc, ps1], [X])
                    apply_M(64)
                    k.mm(ps1[0:TC, 0:TC], DmT.ap, qpT_.ap, [DmT, qpT_], [ps1])
                    k.tt(Gt.ap, ps1[0:TC, 0:TC], su.ap, AluOpType.mult, [ps1, su], [Gt])
                    k.mm(ps2[0:TC, 0:64], qpT_.ap, Hbf.ap, [qpT_, Hbf], [ps2], start=True, stop=False)
                    k.mm(ps2[0:TC, 0:64], Gt.ap, X[:, 0:64], [Gt, X], [ps2], start=False, stop=True)
                    k.tt(d2.ap, vc.ap, ps2[0:TC, 0:64], AluOpType.subtract, [vc, ps2], [d2])
                    k.tt(sq2.ap, d2.ap, d2.ap, AluOpType.mult, [d2], [sq2])
                    k.op('v', lambda e: e.reduce_sum(scol.ap, sq2.ap, AX.X), reads=[sq2], writes=[scol])
                    k.act(scol.ap, scol.ap, AF.Sigmoid, [scol], [scol], scale=float(1.0/(1.0+1e-6)))
                    build_Dm(with_s=True)
                    build_LN_masks()
                    build_powers()
                    k.mm(ps1[0:TC, 0:64], ktT.ap, Hbf.ap, [ktT, Hbf], [ps1])
                    k.tt(X[:, 0:64], vc.ap, ps1[0:TC, 0:64], AluOpType.subtract, [vc, ps1], [X])
                    apply_M(64)
                    k.mm(ps1[0:TC, 0:TC], DmT.ap, qiT.ap, [DmT, qiT], [ps1])
                    k.tt(Gt.ap, ps1[0:TC, 0:TC], sui.ap, AluOpType.mult, [ps1, sui], [Gt])
                    k.mm(ps2[0:TC, 0:64], qiT.ap, Hbf.ap, [qiT, Hbf], [ps2], start=True, stop=False)
                    k.mm(ps2[0:TC, 0:64], Gt.ap, X[:, 0:64], [Gt, X], [ps2], start=False, stop=True)
                    k.copy(y_buf[:, cs], ps2[0:TC, 0:64], [ps2], [y_buf])
                    k.mm(ps2[0:TC, 0:64], ident_s[0:TC, 0:TC], Hbf.ap, [ident_s, Hbf], [ps2],
                         start=True, stop=False)
                    k.mm(ps2[0:TC, 0:64], Dm.ap, X[:, 0:64], [Dm, X], [ps2], start=False, stop=True)
                    k.ts(Hst.ap, ps2[0:TC, 0:64], AeC.ap, None, AluOpType.mult, None, [ps2, AeC], [Hst])
                    k.copy(Hbf.ap, Hst.ap, [Hst], [Hbf])

                for c in range(NCHK):
                    k.tt(sq2.ap, y_buf[:, c*64:(c+1)*64], y_buf[:, c*64:(c+1)*64], AluOpType.mult,
                         [y_buf], [sq2])
                    k.op('v', lambda e, c=c: e.reduce_sum(msq[:, c:c+1], sq2.ap, AX.X),
                         reads=[sq2], writes=[msq])
                k.ts(msq.ap, msq.ap, 1.0/64.0, None, AluOpType.mult, None, [msq], [msq])
                k.act(msq.ap, msq.ap, AF.Ln, [msq], [msq], bias=1e-6)
                k.act(msq.ap, msq.ap, AF.Exp, [msq], [msq], scale=-0.5)
                for c in range(NCHK):
                    k.ts(y_buf[:, c*64:(c+1)*64], y_buf[:, c*64:(c+1)*64], msq[:, c:c+1], None,
                         AluOpType.mult, None, [y_buf, msq], [y_buf])
                k.tt(y_buf.ap, y_buf.ap, nwb.ap, AluOpType.mult, [y_buf, nwb], [y_buf])
                for c in range(NCHK):
                    k.copy(Dm.ap, y_buf[:, c*64:(c+1)*64], [y_buf], [Dm])
                    tr(Dm.ap, Dm, ynf[:, c*64:(c+1)*64], ynf)
                xr2 = x2_d.h.rearrange("(n p) d -> n p d", p=128)
                x3r = x3_d.h.rearrange("(n p) d -> n p d", p=128)
                for tg in range(NT):
                    k.dma('qs', xt[tg % 2].ap, xr2[tg], [x2_d], [xt[tg % 2]])
                    for dc in range(4):
                        ps = ps1 if dc % 2 == 0 else ps2
                        k.mm(ps[:, 0:512], ynf[:, tg*128:(tg+1)*128], wos[:, dc*512:(dc+1)*512],
                             [ynf, wos], [ps])
                        k.copy(o_t[:, dc*512:(dc+1)*512], ps[:, 0:512], [ps], [o_t])
                    emit_ln_tile(k, o_t.ap, o_t, x3_t.ap, x3_t, stats)
                    k.tt(x3_t.ap, x3_t.ap, xt[tg % 2].ap, AluOpType.add, [x3_t, xt[tg % 2]], [x3_t])
                    k.dma('qs', x3r[tg], x3_t.ap, [x3_t], [x3_d])
                k.phase_boundary()

    with ExitStack() as ph:
        xt = [k.sbuf(ph, f"h_xt{i}", [128, D], F32) for i in range(2)]
        stage = k.sbuf(ph, "h_stage", [128, D], BF16)
        ln7f = k.sbuf(ph, "h_ln7f", [128, ND*SEG], BF16)
        y_acc = k.sbuf(ph, "h_yacc", [128, NT*D], F32)
        hbuf = k.sbuf(ph, "h_hbuf", [128, NOC_M*SEG], BF16)
        wb1 = [k.sbuf(ph, f"h_wb1{i}", [128, ND*128], BF16) for i in range(2)]
        wb2 = [k.sbuf(ph, f"h_wb2{i}", [128, ND*128], BF16) for i in range(2)]
        wpb = [k.sbuf(ph, f"h_wpb{i}", [128, NOC_M*512], BF16) for i in range(2)]
        gw_s = k.sbuf(ph, "h_gw", [128, ND*16], BF16)
        router = k.sbuf(ph, "h_router", [128, NT*16], F32)
        comb = k.sbuf(ph, "h_comb", [128, NT*16], F32)
        rst = k.sbuf(ph, "h_rst", [128, 8], F32)
        gs = k.sbuf(ph, "h_gs", [128, SEG], BF16)
        us = k.sbuf(ph, "h_us", [128, SEG], BF16)
        o_t = k.sbuf(ph, "h_ot", [128, D], F32)
        psA = k.psum(ph, "h_psA", [128, 512], F32)
        psB = k.psum(ph, "h_psB", [128, 512], F32)
        psC = k.psum(ph, "h_psC", [128, 512], F32)
        psD = k.psum(ph, "h_psD", [128, 512], F32)
        psT = k.psum(ph, "h_psT", [128, 1024], BF16)

        k.dma('qg', gw_s.h[:, :].rearrange("p (t c) -> p t c", c=16),
              gw_in.h.rearrange("(t p) c -> p t c", p=128), [gw_in], [gw_s])
        xr3 = x3_d.h.rearrange("(n p) d -> n p d", p=128)
        for i in range(NT):
            k.dma('qs', xt[i % 2].ap, xr3[i], [x3_d], [xt[i % 2]])
            emit_ln_tile(k, xt[i % 2].ap, xt[i % 2], stage.ap, stage, stats)
            for dt in range(ND):
                k.trans(psT[:, 0:128], stage[:, dt*128:(dt+1)*128], ident_s.ap, [stage, ident_s], [psT])
                k.copy(ln7f[:, dt*SEG + i*128: dt*SEG + (i+1)*128], psT[:, 0:128], [psT], [ln7f])
        for tg in range(NT):
            ps = psA if tg % 2 == 0 else psB
            for dt in range(ND):
                k.mm(ps[:, 0:16], ln7f[:, dt*SEG + tg*128: dt*SEG + (tg+1)*128],
                     gw_s[:, dt*16:(dt+1)*16], [ln7f, gw_s], [ps],
                     start=(dt == 0), stop=(dt == ND-1))
            k.copy(router[:, tg*16:(tg+1)*16], ps[:, 0:16], [ps], [router])
        for tg in range(NT):
            rsl = router[:, tg*16:tg*16+8]
            csl = comb[:, tg*16:tg*16+8]
            k.act(comb[:, tg*16+8:tg*16+9], router[:, tg*16+8:tg*16+9], AF.Sigmoid, [router], [comb])
            k.op('v', lambda e, rsl=rsl: e.reduce_max(rst[:, 0:1], rsl, AX.X), reads=[router], writes=[rst])
            k.ts(rst[:, 1:2], rst[:, 0:1], -1.0, None, AluOpType.mult, None, [rst], [rst])
            k.act(csl, rsl, AF.Exp, [router, rst], [comb], bias=rst[:, 1:2])
            k.op('v', lambda e, csl=csl: e.reduce_sum(rst[:, 2:3], csl, AX.X), reads=[comb], writes=[rst])
            k.op('v', lambda e: e.reciprocal(rst[:, 3:4], rst[:, 2:3]), reads=[rst], writes=[rst])
            k.ts(csl, csl, rst[:, 3:4], None, AluOpType.mult, None, [comb, rst], [comb])
            k.op('v', lambda e, csl=csl: e.reduce_max(rst[:, 4:5], csl, AX.X), reads=[comb], writes=[rst])
            k.ts(router[:, tg*16:tg*16+8], csl, rst[:, 4:5], None, AluOpType.is_lt, None,
                 [comb, rst], [router])
            k.tt(router[:, tg*16:tg*16+8], router[:, tg*16:tg*16+8], csl, AluOpType.mult,
                 [router, comb], [router])
            k.op('v', lambda e, tg=tg: e.reduce_max(rst[:, 5:6], router[:, tg*16:tg*16+8], AX.X),
                 reads=[router], writes=[rst])
            k.ts(router[:, tg*16+8:tg*16+16], csl, rst[:, 5:6], None, AluOpType.is_ge, None,
                 [comb, rst], [router])
            k.tt(csl, csl, router[:, tg*16+8:tg*16+16], AluOpType.mult, [comb, router], [comb])
        k.memset(y_acc.ap, 0.0, [], [y_acc])
        er1 = efc1_in.h.rearrange("e (t p) c -> e p t c", p=128)
        er2 = efc2_in.h.rearrange("e (t p) c -> e p t c", p=128)
        erp = eprj_in.h.rearrange("e (t p) c -> e p t c", p=128)
        for e in range(NE):
            for oc in range(NOC_M):
                b1, b2 = wb1[oc % 2], wb2[oc % 2]
                k.dma('qg', b1.h[:, :].rearrange("p (t c) -> p t c", c=128),
                      er1[e][:, :, oc*128:(oc+1)*128], [efc1_in], [b1])
                k.dma('qg', b2.h[:, :].rearrange("p (t c) -> p t c", c=128),
                      er2[e][:, :, oc*128:(oc+1)*128], [efc2_in], [b2])
                for n in range(2):
                    psg = psA if n == 0 else psB
                    psu = psC if n == 0 else psD
                    for dt in range(ND):
                        k.mm(psg[:, 0:512], b1[:, dt*128:(dt+1)*128],
                             ln7f[:, dt*SEG + n*512: dt*SEG + (n+1)*512], [b1, ln7f], [psg],
                             start=(dt == 0), stop=(dt == ND-1))
                    for dt in range(ND):
                        k.mm(psu[:, 0:512], b2[:, dt*128:(dt+1)*128],
                             ln7f[:, dt*SEG + n*512: dt*SEG + (n+1)*512], [b2, ln7f], [psu],
                             start=(dt == 0), stop=(dt == ND-1))
                    k.act(gs[:, n*512:(n+1)*512], psg[:, 0:512], AF.Silu, [psg], [gs])
                    k.copy(us[:, n*512:(n+1)*512], psu[:, 0:512], [psu], [us])
                k.tt(hbuf[:, oc*SEG:(oc+1)*SEG], gs.ap, us.ap, AluOpType.mult, [gs, us], [hbuf])
            for dc in range(4):
                bp = wpb[dc % 2]
                k.dma('qg', bp.h[:, :].rearrange("p (t c) -> p t c", c=512),
                      erp[e][:, :, dc*512:(dc+1)*512], [eprj_in], [bp])
                for tg in range(NT):
                    ps = psA if tg % 2 == 0 else psB
                    for ht in range(NOC_M):
                        k.mm(ps[:, 0:512], hbuf[:, ht*SEG + tg*128: ht*SEG + (tg+1)*128],
                             bp[:, ht*512:(ht+1)*512], [hbuf, bp], [ps],
                             start=(ht == 0), stop=(ht == NOC_M-1))
                    k.stt(y_acc[:, tg*D + dc*512: tg*D + (dc+1)*512], ps[:, 0:512],
                          comb[:, tg*16+e:tg*16+e+1], y_acc[:, tg*D + dc*512: tg*D + (dc+1)*512],
                          AluOpType.mult, AluOpType.add, [ps, comb, y_acc], [y_acc])
        sr1 = sfc1_in.h.rearrange("(t p) c -> p t c", p=128)
        sr2 = sfc2_in.h.rearrange("(t p) c -> p t c", p=128)
        srp = sprj_in.h.rearrange("(t p) c -> p t c", p=128)
        for qtr in range(4):
            for ocq in range(NOC_M):
                oc = qtr*NOC_M + ocq
                b1, b2 = wb1[ocq % 2], wb2[ocq % 2]
                k.dma('qg', b1.h[:, :].rearrange("p (t c) -> p t c", c=128),
                      sr1[:, :, oc*128:(oc+1)*128], [sfc1_in], [b1])
                k.dma('qg', b2.h[:, :].rearrange("p (t c) -> p t c", c=128),
                      sr2[:, :, oc*128:(oc+1)*128], [sfc2_in], [b2])
                for n in range(2):
                    psg = psA if n == 0 else psB
                    psu = psC if n == 0 else psD
                    for dt in range(ND):
                        k.mm(psg[:, 0:512], b1[:, dt*128:(dt+1)*128],
                             ln7f[:, dt*SEG + n*512: dt*SEG + (n+1)*512], [b1, ln7f], [psg],
                             start=(dt == 0), stop=(dt == ND-1))
                    for dt in range(ND):
                        k.mm(psu[:, 0:512], b2[:, dt*128:(dt+1)*128],
                             ln7f[:, dt*SEG + n*512: dt*SEG + (n+1)*512], [b2, ln7f], [psu],
                             start=(dt == 0), stop=(dt == ND-1))
                    k.act(gs[:, n*512:(n+1)*512], psg[:, 0:512], AF.Silu, [psg], [gs])
                    k.copy(us[:, n*512:(n+1)*512], psu[:, 0:512], [psu], [us])
                k.tt(hbuf[:, ocq*SEG:(ocq+1)*SEG], gs.ap, us.ap, AluOpType.mult, [gs, us], [hbuf])
            for dc in range(4):
                bp = wpb[dc % 2]
                k.dma('qg', bp.h[:, :].rearrange("p (t c) -> p t c", c=512),
                      srp[:, qtr*NOC_M:(qtr+1)*NOC_M, dc*512:(dc+1)*512], [sprj_in], [bp])
                for tg in range(NT):
                    ps = psA if tg % 2 == 0 else psB
                    for ht in range(NOC_M):
                        k.mm(ps[:, 0:512], hbuf[:, ht*SEG + tg*128: ht*SEG + (tg+1)*128],
                             bp[:, ht*512:(ht+1)*512], [hbuf, bp], [ps],
                             start=(ht == 0), stop=(ht == NOC_M-1))
                    k.stt(y_acc[:, tg*D + dc*512: tg*D + (dc+1)*512], ps[:, 0:512],
                          comb[:, tg*16+8:tg*16+9], y_acc[:, tg*D + dc*512: tg*D + (dc+1)*512],
                          AluOpType.mult, AluOpType.add, [ps, comb, y_acc], [y_acc])
        outr = out_out.h.rearrange("(n p) d -> n p d", p=128)
        for tg in range(NT):
            k.dma('qs', xt[tg % 2].ap, xr3[tg], [x3_d], [xt[tg % 2]])
            emit_ln_tile(k, y_acc[:, tg*D:(tg+1)*D], y_acc, o_t.ap, o_t, stats)
            k.tt(o_t.ap, o_t.ap, xt[tg % 2].ap, AluOpType.add, [o_t, xt[tg % 2]], [o_t])
            k.dma('qs', outr[tg], o_t.ap, [o_t], [out_out])

    if debug:
        dbg1 = k.dram_out("dbg_x1", [SEG, D], F32)
        dbg2 = k.dram_out("dbg_x2", [SEG, D], F32)
        dbg3 = k.dram_out("dbg_x3", [SEG, D], F32)
        k.dma('qs', dbg1.h, x1_d.h, [x1_d], [dbg1])
        k.dma('qs', dbg2.h, x2_d.h, [x2_d], [dbg2])
        k.dma('qs', dbg3.h, x3_d.h, [x3_d], [dbg3])
    nc = k.end()
    return nc, k


_BUILT = {}

def _prep_maps(I, k):
    t = host_tables()
    x = I['L0_attn_x'][0]
    wqkvT = np.ascontiguousarray(np.asarray(I['L0_attn_w_qkv']).T)
    b = np.asarray(I['L0_attn_b_qkv'])
    bqkv_cols = np.ascontiguousarray(b[:2*D].reshape(2*ND, 128).T)
    bqkv_cols = np.concatenate([bqkv_cols, np.zeros((128, ND), np.float32)], axis=1)
    bv_b = np.broadcast_to(b[2*D:][None, :], (128, D)).copy()
    woT = np.ascontiguousarray(np.asarray(I['L0_attn_w_out']).T)
    bo_b = np.broadcast_to(np.asarray(I['L0_attn_b_out'])[None, :], (128, D)).copy()
    wguT = np.ascontiguousarray(np.asarray(I['L0_ffn_w_gate_up']).T)
    wpT = np.ascontiguousarray(np.asarray(I['L0_ffn_w_proj']).T)
    wkvqaT = np.ascontiguousarray(np.concatenate(
        [np.asarray(I['L1_attn_Wk']).T, np.asarray(I['L1_attn_Wv']).T,
         np.asarray(I['L1_attn_Wq']).T, np.asarray(I['L1_attn_Wa_w']).T], axis=1))
    wo1T = np.ascontiguousarray(np.asarray(I['L1_attn_Wo']).T)
    nwb = np.tile(np.asarray(I['L1_attn_norm_w'])[None, :], (64, NCHK)).astype(np.float32)
    gate9 = np.zeros((D, 16), np.float32)
    gate9[:, 0:8] = np.asarray(I['L1_ffn_gate_w']).T
    gate9[:, 8:9] = np.asarray(I['L1_ffn_shared_gate_w']).T
    efc1T = np.ascontiguousarray(np.asarray(I['L1_ffn_expert_fc1']).transpose(0, 2, 1))
    efc2T = np.ascontiguousarray(np.asarray(I['L1_ffn_expert_fc2']).transpose(0, 2, 1))
    eprjT = np.ascontiguousarray(np.asarray(I['L1_ffn_expert_proj']).transpose(0, 2, 1))
    sfc1T = np.ascontiguousarray(np.asarray(I['L1_ffn_shared_fc1']).T)
    sfc2T = np.ascontiguousarray(np.asarray(I['L1_ffn_shared_fc2']).T)
    sprjT = np.ascontiguousarray(np.asarray(I['L1_ffn_shared_proj']).T)
    maps = []
    for c in range(NCORE):
        sw = np.zeros((NCORE, H), np.float32)
        for j in range(c):
            sw[j] = LAM_SEG ** (c - 1 - j)
        dnsel = np.zeros((64, 16), np.float32)
        for j in range(NCORE - 1):
            sel = 1.0 if j < c else 0.0
            dnsel[:, j] = sel
            dnsel[:, 8+j] = 1.0 - sel
        maps.append({
            "consts_pool": k.const_array(),
            "x_seg": np.ascontiguousarray(x[c*SEG:(c+1)*SEG]),
            "wqkvT": wqkvT, "bqkv_cols": bqkv_cols, "bv_b": bv_b,
            "woT": woT, "bo_b": bo_b,
            "sw": np.broadcast_to(sw.reshape(1, -1), (128, NCORE*H)).copy(),
            "wguT_sh": np.ascontiguousarray(wguT[:, c*GU_SH:(c+1)*GU_SH]),
            "wpT_sh": np.ascontiguousarray(wpT[:, c*P_SH:(c+1)*P_SH]),
            "wkvqaT": wkvqaT,
            "wa_b": np.asarray(I['L1_attn_Wa_b']).reshape(64, 1),
            "lam": np.asarray(I['L1_attn_lam']).reshape(64, 1),
            "dnsel": dnsel, "nwb": nwb, "wo1T": wo1T,
            "gate9T": gate9,
            "efc1T": efc1T, "efc2T": efc2T, "eprjT": eprjT,
            "sfc1T": sfc1T, "sfc2T": sfc2T, "sprjT": sprjT,
            "ident_bf": t['ident_bf'], "ident_f32": t['ident_f32'],
            "dq_b": t['dq_b'], "dk_col": t['dk_col'], "dmaskT": t['dmaskT'],
            "tli_f32": t['tli_f32'], "strilneg_bf": t['strilneg_bf'],
            "striuneg_bf": t['striuneg_bf'], "striu_bf": t['striu_bf'],
            "striuincl_bf": t['striuincl_bf'],
            "ones64c_f32": t['ones64c_f32'], "ones64r_f32": t['ones64r_f32'],
            "ones128c_bf": t['ones128c_bf'], "ones1r_f32": t['ones1r_f32'],
            "gmask": t['gmask'], "gmask2f": t['gmask2f'], "gmask2b": t['gmask2b'],
        })
    return maps


def kernel(**inputs):
    if "fused" not in _BUILT:
        _BUILT["fused"] = build_fused()
    nc, kk = _BUILT["fused"]
    maps = _prep_maps(inputs, kk)
    res = run_bass_kernel_spmd(nc, maps, core_ids=list(range(NCORE)))
    out = np.concatenate([np.asarray(res.results[c]["out_seg"]) for c in range(NCORE)], 0)
    return out.reshape(1, N_TOK, D).astype(np.float32)


# version 08
# speedup vs baseline: 1.0605x; 1.0133x over previous
import numpy as np
import math
from contextlib import ExitStack
import ml_dtypes
import concourse.bass as bass
import concourse.mybir as mybir
from concourse.bass_utils import run_bass_kernel_spmd
from concourse.alu_op_type import AluOpType

F32 = mybir.dt.float32
BF16 = mybir.dt.bfloat16
AF = mybir.ActivationFunctionType
AX = mybir.AxisListType
BF = ml_dtypes.bfloat16

QUEUE_ENG = {'qg': 'g', 'qs': 'y', 'qa': 's'}
NSLOT = 6


class Buf:
    __slots__ = ('h', 'name', 'last_w', 'readers')
    def __init__(self, h, name):
        self.h = h
        self.name = name
        self.last_w = None
        self.readers = []
    def __getitem__(self, sl):
        return self.h[sl]
    @property
    def ap(self):
        return self.h[:]


class K:
    def __init__(self, name="prog"):
        self.nc = bass.Bass("TRN2", target_bir_lowering=False, debug=False)
        self.stack = ExitStack()
        self.eng = {'t': self.nc.tensor, 'v': self.nc.vector, 's': self.nc.scalar,
                    'g': self.nc.gpsimd, 'y': self.nc.sync}
        self.cnt = {k: 0 for k in ('t', 'v', 's', 'g', 'y')}
        self.water = {k: {} for k in ('t', 'v', 's', 'g', 'y')}
        self.sems = None
        self.block = None
        self.ops = 0
        self.bufs = []
        self.cc_cnt = 0
        self._ext_in = {}
        self._ext_out = {}

    def dram_in(self, name, shape, dtype=F32):
        ap = self.nc.dram_tensor(name, list(shape), dtype, kind="ExternalInput").ap()
        b = Buf(ap, name)
        self.bufs.append(b)
        self._ext_in[name] = b
        return b

    def dram_out(self, name, shape, dtype=F32):
        ap = self.nc.dram_tensor(name, list(shape), dtype, kind="ExternalOutput").ap()
        b = Buf(ap, name)
        self.bufs.append(b)
        self._ext_out[name] = b
        return b

    def dram_tmp(self, name, shape, dtype=F32):
        ap = self.nc.dram_tensor(name, list(shape), dtype, kind="Internal").ap()
        b = Buf(ap, name)
        self.bufs.append(b)
        return b

    def begin(self, n_phases=12, n_consts=128):
        self._consts_in = self.dram_in("consts_pool", [128, n_consts], F32)
        self.semsets = []
        for p in range(n_phases):
            s = {}
            for kk in ('t', 'v', 's', 'g', 'y'):
                s[kk] = self.stack.enter_context(self.nc.semaphore(f"sm{p}_{kk}"))
            self.semsets.append(s)
        self.dma_sems = {}
        self.dma_uses = {}
        self.dma_issue = {}
        for q in ('qg', 'qs', 'qa'):
            for sl in range(NSLOT):
                self.dma_sems[(q, sl)] = self.stack.enter_context(self.nc.semaphore(f"dm_{q}{sl}"))
                self.dma_uses[(q, sl)] = 0
            self.dma_issue[q] = 0
        self.cc_sem = self.stack.enter_context(self.nc.semaphore("sm_cc"))
        self.phase = 0
        self.sems = dict(self.semsets[0])
        self.sems['cc'] = self.cc_sem
        self.block = self.stack.enter_context(self.nc.Block())
        ids = sorted({s.num for st in self.semsets for s in st.values()}
                     | {s.num for s in self.dma_sems.values()} | {self.cc_sem.num})
        if ids == list(range(ids[0], ids[-1] + 1)):
            self.eng['g'].sem_clear(range(ids[0], ids[-1] + 1))
        else:
            for st in self.semsets:
                for s in st.values():
                    self.eng['g'].sem_clear(s)
            for s in self.dma_sems.values():
                self.eng['g'].sem_clear(s)
            self.eng['g'].sem_clear(self.cc_sem)
        self.nc.all_engine_barrier()
        self._consts_s = self.sbuf(self.stack, "consts_s", [128, n_consts], F32)
        self._const_map = {}
        self._n_consts = n_consts
        self.dma('qs', self._consts_s.ap, self._consts_in.ap, [self._consts_in], [self._consts_s])
        return self

    def constf(self, val):
        val = float(val)
        if val not in self._const_map:
            i = len(self._const_map)
            assert i < self._n_consts, "const pool full"
            self._const_map[val] = i
            self.nc.const_aps.register(F32, val, self._consts_s[:, i:i+1])
        i = self._const_map[val]
        return self._consts_s[:, i:i+1]

    def const_array(self):
        arr = np.zeros((128, self._n_consts), np.float32)
        for v, i in self._const_map.items():
            arr[:, i] = v
        return arr

    def sbuf(self, stack, name, shape, dtype=F32):
        h = stack.enter_context(self.nc.sbuf_tensor(name, list(shape), dtype))
        b = Buf(h, name)
        self.bufs.append(b)
        return b

    def psum(self, stack, name, shape, dtype=F32):
        h = stack.enter_context(self.nc.psum_tensor(name, list(shape), dtype))
        b = Buf(h, name)
        self.bufs.append(b)
        return b

    def end(self):
        self.barrier_all()
        self.stack.close()
        self.nc.finalize()
        return self.nc

    def _wait(self, consumer, dep_key, dep_val):
        if dep_val <= 0:
            return
        w = self.water[consumer]
        if w.get(dep_key, 0) >= dep_val:
            return
        if isinstance(dep_key, tuple):
            self.eng[consumer].wait_ge(self.dma_sems[dep_key], dep_val * 16)
        else:
            self.eng[consumer].wait_ge(self.sems[dep_key], dep_val)
        w[dep_key] = dep_val

    def _deps(self, ekey, reads, writes):
        for b in reads:
            if b.last_w is not None:
                pk, pv = b.last_w
                if pk != ekey or ekey != 't':
                    self._wait(ekey, pk, pv)
        for b in writes:
            if b.last_w is not None:
                pk, pv = b.last_w
                if pk != ekey or ekey != 't':
                    self._wait(ekey, pk, pv)
            for (rk, rv) in b.readers:
                if rk != ekey or ekey != 't':
                    self._wait(ekey, rk, rv)

    def _post(self, key, reads, writes):
        self.cnt[key] += 1
        tag = (key, self.cnt[key])
        for b in reads:
            b.readers.append(tag)
        for b in writes:
            b.last_w = tag
            b.readers = []
        self.ops += 1
        return tag

    def op(self, ekey, fn, reads=(), writes=()):
        self._deps(ekey, list(reads), list(writes))
        inst = fn(self.eng[ekey])
        inst.then_inc(self.sems[ekey], 1)
        return self._post(ekey, list(reads), list(writes))

    def dma(self, qkey, out_ap, in_ap, reads=(), writes=(), **kw):
        ekey = QUEUE_ENG[qkey]
        reads, writes = list(reads), list(writes)
        for b in reads:
            if b.last_w is not None:
                pk, pv = b.last_w
                self._wait(ekey, pk, pv)
        for b in writes:
            if b.last_w is not None:
                pk, pv = b.last_w
                self._wait(ekey, pk, pv)
            for (rk, rv) in b.readers:
                if rk != ekey:
                    self._wait(ekey, rk, rv)
        slot = (qkey, self.dma_issue[qkey] % NSLOT)
        self.dma_issue[qkey] += 1
        self._wait(ekey, slot, self.dma_uses[slot])
        self.dma_uses[slot] += 1
        self.eng[ekey].dma_start(out=out_ap, in_=in_ap, **kw).then_inc(self.dma_sems[slot], 16)
        tag = (slot, self.dma_uses[slot])
        for b in reads:
            b.readers.append(tag)
        for b in writes:
            b.last_w = tag
            b.readers = []
        self.ops += 1
        return tag

    def collective(self, in_ap, out_ap, reads, out_buf):
        ekey = 'g'
        for b in list(reads):
            if b.last_w is not None:
                pk, pv = b.last_w
                self._wait(ekey, pk, pv)
        self.eng['g'].collective_compute(
            "AllGather", AluOpType.bypass,
            replica_groups=[list(range(8))],
            ins=[in_ap.opt()], outs=[out_ap.opt()],
        ).then_inc(self.cc_sem)
        self.cc_cnt += 1
        tag = ('cc', self.cc_cnt)
        for b in list(reads):
            b.readers.append(tag)
        out_buf.last_w = tag
        out_buf.readers = []
        self.ops += 1
        return tag

    def barrier_all(self):
        snap = dict(self.cnt)
        dsnap = dict(self.dma_uses)
        for ekey in ('t', 'v', 's', 'g', 'y'):
            for pk, pv in snap.items():
                if pk != ekey:
                    self._wait(ekey, pk, pv)
            for pk, pv in dsnap.items():
                self._wait(ekey, pk, pv)

    def phase_boundary(self):
        self.barrier_all()
        self.phase += 1
        self.sems = dict(self.semsets[self.phase])
        self.sems['cc'] = self.cc_sem
        self.cnt = {k: 0 for k in ('t', 'v', 's', 'g', 'y')}
        for w in self.water.values():
            keep = {kk: vv for kk, vv in w.items() if kk == 'cc' or isinstance(kk, tuple)}
            w.clear()
            w.update(keep)
        for b in self.bufs:
            if not (b.last_w and (b.last_w[0] == 'cc' or isinstance(b.last_w[0], tuple))):
                b.last_w = None
            b.readers = [r for r in b.readers if r[0] == 'cc' or isinstance(r[0], tuple)]

    def mm(self, out_ap, lT_ap, r_ap, rd, wr, start=True, stop=True):
        return self.op('t', lambda e: e.matmul(out_ap, lT_ap, r_ap, start=start, stop=stop),
                       reads=rd, writes=wr)

    def trans(self, out_ap, in_ap, ident_ap, rd, wr):
        return self.op('t', lambda e: e.transpose(out_ap, in_ap, ident_ap), reads=rd, writes=wr)

    def act(self, out_ap, in_ap, func, rd, wr, **kw):
        rd = list(rd)
        try:
            nparts = out_ap.shape[0]
        except Exception:
            nparts = 128
        for key in ('bias', 'scale'):
            v = kw.get(key)
            if isinstance(v, (int, float)):
                self.constf(v)
                i = self._const_map[float(v)]
                kw[key] = self._consts_s[0:nparts, i:i+1]
                if self._consts_s not in rd:
                    rd.append(self._consts_s)
        return self.op('s', lambda e: e.activation(out_ap, in_ap, func, **kw), reads=rd, writes=wr)

    def tt(self, out_ap, a_ap, b_ap, aop, rd, wr, eng='v'):
        return self.op(eng, lambda e: e.tensor_tensor(out_ap, a_ap, b_ap, aop), reads=rd, writes=wr)

    def ts(self, out_ap, a_ap, s1, s2, op0, op1, rd, wr, eng='v', **kw):
        if s2 is None:
            return self.op(eng, lambda e: e.tensor_scalar(out_ap, a_ap, s1, None, op0, **kw), reads=rd, writes=wr)
        return self.op(eng, lambda e: e.tensor_scalar(out_ap, a_ap, s1, s2, op0, op1, **kw), reads=rd, writes=wr)

    def stt(self, out_ap, in0_ap, scalar, in1_ap, op0, op1, rd, wr, eng='v'):
        return self.op(eng, lambda e: e.scalar_tensor_tensor(out_ap, in0_ap, scalar, in1_ap, op0, op1), reads=rd, writes=wr)

    def copy(self, out_ap, in_ap, rd, wr, eng='v'):
        return self.op(eng, lambda e: e.tensor_copy(out_ap, in_ap), reads=rd, writes=wr)

    def memset(self, ap, val, rd, wr, eng='v'):
        return self.op(eng, lambda e: e.memset(ap, val), reads=rd, writes=wr)


N_TOK, D = 8192, 2048
NCORE = 8
SEG = N_TOK // NCORE
NT = SEG // 128
ND = D // 128
H, HD, CH = 16, 128, 128
NCH = SEG // CH
NHID = 64
TC = 64
NCHK = SEG // TC
DS = 64
NE, FM, FS = 8, 1408, 5632
NOC_M = FM // 128
SERIALIZE_COLL = False
GU_SH = 2 * 4 * D // NCORE
P_SH = D // NCORE


def _slopes(n_heads):
    start = 2.0 ** (-2.0 ** (-(math.log2(n_heads) - 3)))
    return np.array([start ** (i + 1) for i in range(n_heads)], dtype=np.float32)

LAM = np.exp(-_slopes(H)).astype(np.float32)
POS = np.arange(CH, dtype=np.float32)
DECAY_Q = (LAM[:, None] ** (POS + 1.0)).astype(np.float32)
DECAY_K = (LAM[:, None] ** (CH - 1.0 - POS)).astype(np.float32)
_diff = POS[:, None] - POS[None, :]
DMASK = np.where(_diff >= 0, LAM[:, None, None] ** np.maximum(_diff, 0.0), 0.0).astype(np.float32)
LAM_C = (LAM ** CH).astype(np.float32)
LAM_SEG = (LAM ** SEG).astype(np.float32)


def host_tables():
    t = {}
    t['ident_bf'] = np.eye(128, dtype=BF)
    t['ident_f32'] = np.eye(128, dtype=np.float32)
    t['dq_b'] = np.broadcast_to(DECAY_Q[:, None, :], (H, 128, 128)).copy()
    t['dk_col'] = DECAY_K.T.copy()
    t['dmaskT'] = DMASK.transpose(0, 2, 1).copy()
    idx = np.arange(TC)
    t['tli_f32'] = (idx[:, None] <= idx[None, :]).astype(np.float32)
    t['strilneg_bf'] = np.where(idx[:, None] > idx[None, :], -1.0, 0.0).astype(BF)
    t['striuneg_bf'] = np.where(idx[:, None] < idx[None, :], -1.0, 0.0).astype(BF)
    t['striu_bf'] = (idx[:, None] < idx[None, :]).astype(BF)
    t['striuincl_bf'] = (idx[:, None] <= idx[None, :]).astype(BF)
    t['ones64c_f32'] = np.ones((TC, 1), np.float32)
    t['ones64r_f32'] = np.ones((1, TC), np.float32)
    t['ones128c_bf'] = np.ones((128, 1), BF)
    t['ones1r_f32'] = np.ones((1, 128), np.float32)
    gmask = np.zeros((128, 2), np.float32); gmask[0:64, 0] = 1; gmask[64:128, 1] = 1
    t['gmask'] = gmask.astype(BF)
    t['gmask2f'] = gmask.T.copy()
    t['gmask2b'] = gmask.T.astype(BF)
    return t


def emit_ln_tile(k, x_ap, x_buf, out_ap, out_buf, stats, eps=1e-5, width=D):
    nblk = (width + 511) // 512
    ntrip = 2 * nblk
    for i in range(nblk):
        w = min(512, width - i*512)
        k.op('v', lambda e, i=i, w=w: e.bn_stats(stats[:, 6*i:6*i+6], x_ap[:, i*512:i*512+w]),
             reads=[x_buf], writes=[stats])
    sv = stats.h[:, 0:6*nblk].rearrange("p (n t) -> p t n", t=3)
    mview, m2view = sv[:, 1, :], sv[:, 2, :]
    k.op('v', lambda e: e.reduce_sum(stats[:, 24:25], mview, AX.X), reads=[stats], writes=[stats])
    k.op('v', lambda e: e.reduce_sum(stats[:, 25:26], m2view, AX.X), reads=[stats], writes=[stats])
    k.op('v', lambda e: e.tensor_tensor(stats[:, 32:32+ntrip], mview, mview, AluOpType.mult),
         reads=[stats], writes=[stats])
    k.op('v', lambda e: e.reduce_sum(stats[:, 26:27], stats[:, 32:32+ntrip], AX.X),
         reads=[stats], writes=[stats])
    k.ts(stats[:, 27:28], stats[:, 24:25], 1.0/ntrip, None, AluOpType.mult, None, [stats], [stats])
    k.ts(stats[:, 28:29], stats[:, 25:26], 1.0/width, None, AluOpType.mult, None, [stats], [stats])
    k.stt(stats[:, 28:29], stats[:, 26:27], 1.0/ntrip, stats[:, 28:29], AluOpType.mult, AluOpType.add,
          [stats], [stats])
    k.tt(stats[:, 30:31], stats[:, 27:28], stats[:, 27:28], AluOpType.mult, [stats], [stats])
    k.tt(stats[:, 28:29], stats[:, 28:29], stats[:, 30:31], AluOpType.subtract, [stats], [stats])
    k.act(stats[:, 26:27], stats[:, 28:29], AF.Ln, [stats], [stats], bias=float(eps))
    k.act(stats[:, 31:32], stats[:, 26:27], AF.Exp, [stats], [stats], scale=-0.5)
    k.stt(stats[:, 30:31], stats[:, 27:28], -1.0, stats[:, 31:32], AluOpType.mult, AluOpType.mult,
          [stats], [stats])
    k.ts(out_ap, x_ap, stats[:, 31:32], stats[:, 30:31], AluOpType.mult, AluOpType.add,
         [x_buf, stats], [out_buf])


def build_fused(debug=False, debug2=False):
    k = K()
    x_in = k.dram_in("x_seg", [SEG, D], F32)
    wqkvT = k.dram_in("wqkvT", [D, 3*D], F32)
    bqkv_cols = k.dram_in("bqkv_cols", [128, 3*ND], F32)
    bv_b = k.dram_in("bv_b", [128, D], F32)
    woT = k.dram_in("woT", [D, D], F32)
    bo_b = k.dram_in("bo_b", [128, D], F32)
    sw_in = k.dram_in("sw", [128, NCORE*H], F32)
    wgu_in = k.dram_in("wguT_sh", [D, GU_SH], F32)
    wp_in = k.dram_in("wpT_sh", [4*D, P_SH], F32)
    wkvqaT_in = k.dram_in("wkvqaT", [D, 4*DS], F32)
    wab_in = k.dram_in("wa_b", [DS, 1], F32)
    lam_in = k.dram_in("lam", [DS, 1], F32)
    dnsel_in = k.dram_in("dnsel", [64, 16], F32)
    nwb_in = k.dram_in("nwb", [64, NCHK*64], F32)
    wo1T_in = k.dram_in("wo1T", [DS, D], F32)
    gw_in = k.dram_in("gate9T", [D, 16], F32)
    efc1_in = k.dram_in("efc1T", [NE, D, FM], F32)
    efc2_in = k.dram_in("efc2T", [NE, D, FM], F32)
    eprj_in = k.dram_in("eprjT", [NE, FM, D], F32)
    sfc1_in = k.dram_in("sfc1T", [D, FS], F32)
    sfc2_in = k.dram_in("sfc2T", [D, FS], F32)
    sprj_in = k.dram_in("sprjT", [FS, D], F32)
    ident = k.dram_in("ident_bf", [128, 128], BF16)
    identf_in = k.dram_in("ident_f32", [128, 128], F32)
    dq_b = k.dram_in("dq_b", [H, 128, 128], F32)
    dk_col = k.dram_in("dk_col", [128, H], F32)
    dmaskT = k.dram_in("dmaskT", [H, 128, 128], F32)
    tli_in = k.dram_in("tli_f32", [TC, TC], F32)
    sln_in = k.dram_in("strilneg_bf", [TC, TC], BF16)
    sun_in = k.dram_in("striuneg_bf", [TC, TC], BF16)
    su_in = k.dram_in("striu_bf", [TC, TC], BF16)
    sui_in = k.dram_in("striuincl_bf", [TC, TC], BF16)
    oc_in = k.dram_in("ones64c_f32", [64, 1], F32)
    or_in = k.dram_in("ones64r_f32", [1, 64], F32)
    o128c_in = k.dram_in("ones128c_bf", [128, 1], BF16)
    o1r_in = k.dram_in("ones1r_f32", [1, 128], F32)
    gm_in = k.dram_in("gmask", [128, 2], BF16)
    gm2f_in = k.dram_in("gmask2f", [2, 128], F32)
    gm2b_in = k.dram_in("gmask2b", [2, 128], BF16)
    out_out = k.dram_out("out_seg", [SEG, D], F32)
    x1_d = k.dram_tmp("x1_d", [SEG, D], F32)
    x2_d = k.dram_tmp("x2_d", [SEG, D], F32)
    x3_d = k.dram_tmp("x3_d", [SEG, D], F32)
    qgu_sh = k.dram_tmp("qgu_sh", [D, GU_SH], BF16)
    qgu_g = k.dram_tmp("qgu_g", [NCORE*D, GU_SH], BF16)
    qp_sh = k.dram_tmp("qp_sh", [4*D, P_SH], BF16)
    qp_g = k.dram_tmp("qp_g", [NCORE*4*D, P_SH], BF16)
    hd_d = k.dram_tmp("hd_d", [128, NHID*SEG], BF16)
    sseg_l = k.dram_tmp("sseg_l", [H*128, 128], F32)
    sseg_g = k.dram_tmp("sseg_g", [NCORE*H*128, 128], F32)
    dn_l = k.dram_tmp("dn_l", [128, 64], F32)
    dn_g = k.dram_tmp("dn_g", [NCORE*128, 64], F32)

    k.begin(n_phases=12)

    ident_s = k.sbuf(k.stack, "ident_s", [128, 128], BF16)
    identf = k.sbuf(k.stack, "identf", [128, 128], F32)
    stats = k.sbuf(k.stack, "stats", [128, 48], F32)
    k.dma('qs', ident_s.ap, ident.ap, [ident], [ident_s])
    k.dma('qs', identf.ap, identf_in.ap, [identf_in], [identf])

    with ExitStack() as sc0:
        qf = k.sbuf(sc0, "a_qf", [128, ND*SEG], BF16)
        Sst = k.sbuf(sc0, "a_Sst", [128, H*128], F32)
        attn_bf = k.sbuf(sc0, "a_attn", [128, ND*SEG], BF16)

        with ExitStack() as scKV:
            kf = k.sbuf(scKV, "a_kf", [128, ND*SEG], BF16)
            vt = k.sbuf(scKV, "a_vt", [128, NT*D], BF16)

            with ExitStack() as ph:
                xt = [k.sbuf(ph, f"a_xt{i}", [128, D], F32) for i in range(2)]
                stage = k.sbuf(ph, "a_stage", [128, D], BF16)
                ln1f = k.sbuf(ph, "a_ln1f", [128, ND*SEG], BF16)
                wb = [k.sbuf(ph, f"a_wb{i}", [128, ND*128], BF16) for i in range(2)]
                bqk_s = k.sbuf(ph, "a_bqk", [128, 3*ND], F32)
                bvb_s = k.sbuf(ph, "a_bvb", [128, D], BF16)
                psA = k.psum(ph, "a_psA", [128, 512], F32)
                psB = k.psum(ph, "a_psB", [128, 512], F32)
                psT = k.psum(ph, "a_psT", [128, 1024], BF16)

                k.dma('qs', bqk_s.ap, bqkv_cols.ap, [bqkv_cols], [bqk_s])
                k.dma('qg', bvb_s.ap, bv_b.ap, [bv_b], [bvb_s])
                if debug2:
                    stA = k.sbuf(ph, "a_stA", [128, 48*NT], F32)
                xr = x_in.h.rearrange("(n p) d -> n p d", p=128)
                for i in range(NT):
                    k.dma('qs', xt[i % 2].ap, xr[i], [x_in], [xt[i % 2]])
                    if debug2:
                        stv = Buf(stA.h[:, 48*i:48*(i+1)], f"stA{i}")
                        k.bufs.append(stv)
                        emit_ln_tile(k, xt[i % 2].ap, xt[i % 2], stage.ap, stage, stv)
                    else:
                        emit_ln_tile(k, xt[i % 2].ap, xt[i % 2], stage.ap, stage, stats)
                    for dt in range(ND):
                        k.trans(psT[:, 0:128], stage[:, dt*128:(dt+1)*128], ident_s.ap, [stage, ident_s], [psT])
                        k.copy(ln1f[:, dt*SEG + i*128: dt*SEG + (i+1)*128], psT[:, 0:128], [psT], [ln1f])
                wr = wqkvT.h.rearrange("(t p) c -> p t c", p=128)
                for oc in range(2*ND):
                    wbuf = wb[oc % 2]
                    k.dma('qg', wbuf.h[:, 0:ND*128].rearrange("p (t c) -> p t c", c=128),
                          wr[:, :, oc*128:(oc+1)*128], [wqkvT], [wbuf])
                    dst = qf if oc < ND else kf
                    blk = oc % ND
                    for n in range(2):
                        ps = psA if n == 0 else psB
                        for dt in range(ND):
                            k.mm(ps[:, 0:512], wbuf[:, dt*128:(dt+1)*128],
                                 ln1f[:, dt*SEG + n*512: dt*SEG + (n+1)*512],
                                 [wbuf, ln1f], [ps], start=(dt == 0), stop=(dt == ND-1))
                        k.act(dst[:, blk*SEG + n*512: blk*SEG + (n+1)*512], ps[:, 0:512], AF.Identity,
                              [ps, bqk_s], [dst], bias=bqk_s[:, oc:oc+1])
                for vc in range(ND):
                    wbuf = wb[vc % 2]
                    k.dma('qg', wbuf.h[:, 0:ND*128].rearrange("p (t c) -> p t c", c=128),
                          wr[:, :, 2*D + vc*128: 2*D + (vc+1)*128], [wqkvT], [wbuf])
                    for ti in range(NT):
                        ps = psA if ti % 2 == 0 else psB
                        for dt in range(ND):
                            k.mm(ps[:, 0:128], ln1f[:, dt*SEG + ti*128: dt*SEG + (ti+1)*128],
                                 wbuf[:, dt*128:(dt+1)*128],
                                 [wbuf, ln1f], [ps], start=(dt == 0), stop=(dt == ND-1))
                        k.tt(vt[:, ti*D + vc*128: ti*D + (vc+1)*128], ps[:, 0:128],
                             bvb_s[:, vc*128:(vc+1)*128], AluOpType.add, [ps, bvb_s], [vt])
                if debug2:
                    dqa = k.dram_out("dbg_qfA1", [128, ND*SEG], BF16)
                    dlf = k.dram_out("dbg_ln1f", [128, ND*SEG], BF16)
                    dst_ = k.dram_out("dbg_stats", [128, 48*NT], F32)
                    k.dma('qs', dqa.h, qf.ap, [qf], [dqa])
                    k.dma('qs', dlf.h, ln1f.ap, [ln1f], [dlf])
                    k.dma('qs', dst_.h, stA.ap, [stA], [dst_])
                k.phase_boundary()

            with ExitStack() as ph:
                kdt = k.sbuf(ph, "a_kdt", [128, H*NCH*128], BF16)
                Sbf = k.sbuf(ph, "a_Sbf", [128, H*128], BF16)
                dqb_s = k.sbuf(ph, "a_dqb", [128, H*128], BF16)
                dkc_s = k.sbuf(ph, "a_dkc", [128, H], F32)
                dmk_s = k.sbuf(ph, "a_dmk", [128, H*128], BF16)
                qd_c = [k.sbuf(ph, f"a_qdc{i}", [128, 128], BF16) for i in range(2)]
                sc_c = [k.sbuf(ph, f"a_scc{i}", [128, 128], BF16) for i in range(2)]
                psSC = [k.psum(ph, "a_psSC0", [128, 512], F32)] * 2
                psO = [k.psum(ph, "a_psO0", [128, 512], F32)] * 2
                psSU = [k.psum(ph, "a_psSU0", [128, 512], F32)] * 2
                psT2 = [k.psum(ph, f"a_psT2{i}", [128, 1024], BF16) for i in range(2)]

                k.dma('qg', dqb_s.h[:, :].rearrange("p (h c) -> p h c", c=128),
                      dq_b.h.rearrange("h p c -> p h c"), [dq_b], [dqb_s])
                k.dma('qs', dkc_s.ap, dk_col.ap, [dk_col], [dkc_s])
                k.dma('qg', dmk_s.h[:, :].rearrange("p (h c) -> p h c", c=128),
                      dmaskT.h.rearrange("h p c -> p h c"), [dmaskT], [dmk_s])
                for h in range(H):
                    for c in range(NCH):
                        pt = psT2[(h*NCH+c) % 2]
                        k.trans(pt[:, 0:128], kf[:, h*SEG + c*128: h*SEG + (c+1)*128], ident_s.ap,
                                [kf, ident_s], [pt])
                        k.act(kdt[:, (h*NCH+c)*128:(h*NCH+c+1)*128], pt[:, 0:128], AF.Copy,
                              [pt, dkc_s], [kdt], scale=dkc_s[:, h:h+1])
                for c in range(NCH):
                    for h in range(H):
                        par = h % 2
                        k.mm(psSC[par][:, 0:128], kf[:, h*SEG + c*128: h*SEG + (c+1)*128],
                             qf[:, h*SEG + c*128: h*SEG + (c+1)*128], [kf, qf], [psSC[par]])
                        k.tt(sc_c[par].ap, psSC[par][:, 0:128], dmk_s[:, h*128:(h+1)*128], AluOpType.mult,
                             [psSC[par], dmk_s], [sc_c[par]])
                        k.tt(qd_c[par].ap, qf[:, h*SEG + c*128: h*SEG + (c+1)*128], dqb_s[:, h*128:(h+1)*128],
                             AluOpType.mult, [qf, dqb_s], [qd_c[par]])
                        k.mm(psO[par][:, 0:128], vt[:, c*D + h*128: c*D + (h+1)*128], sc_c[par].ap,
                             [vt, sc_c[par]], [psO[par]], start=True, stop=(c == 0))
                        if c > 0:
                            k.mm(psO[par][:, 0:128], Sbf[:, h*128:(h+1)*128], qd_c[par].ap,
                                 [Sbf, qd_c[par]], [psO[par]], start=False, stop=True)
                        k.copy(attn_bf[:, h*SEG + c*128:h*SEG + (c+1)*128], psO[par][:, 0:128],
                               [psO[par]], [attn_bf])
                        k.mm(psSU[par][:, 0:128], kdt[:, (h*NCH+c)*128:(h*NCH+c+1)*128],
                             vt[:, c*D + h*128: c*D + (h+1)*128], [kdt, vt], [psSU[par]])
                        if c == 0:
                            k.copy(Sst[:, h*128:(h+1)*128], psSU[par][:, 0:128], [psSU[par]], [Sst])
                        else:
                            k.stt(Sst[:, h*128:(h+1)*128], Sst[:, h*128:(h+1)*128], float(LAM_C[h]),
                                  psSU[par][:, 0:128], AluOpType.mult, AluOpType.add, [psSU[par], Sst], [Sst])
                        if c < NCH - 1:
                            k.copy(Sbf[:, h*128:(h+1)*128], Sst[:, h*128:(h+1)*128], [Sst], [Sbf])
                k.dma('qs', sseg_l.h.rearrange("(h p) c -> p h c", p=128),
                      Sst.h[:, :].rearrange("p (h c) -> p h c", c=128), [Sst], [sseg_l])
                k.collective(sseg_l.h, sseg_g.h, [sseg_l], sseg_g)
                if debug2:
                    dq = k.dram_out("dbg_qf", [128, ND*SEG], BF16)
                    dk = k.dram_out("dbg_kf", [128, ND*SEG], BF16)
                    dv = k.dram_out("dbg_vt", [128, NT*D], BF16)
                    da = k.dram_out("dbg_attnA", [128, ND*SEG], BF16)
                    dS = k.dram_out("dbg_Sst", [128, H*128], F32)
                    k.dma('qs', dq.h, qf.ap, [qf], [dq])
                    k.dma('qs', dk.h, kf.ap, [kf], [dk])
                    k.dma('qs', dv.h, vt.ap, [vt], [dv])
                    k.dma('qs', da.h, attn_bf.ap, [attn_bf], [da])
                    k.dma('qs', dS.h, Sst.ap, [Sst], [dS])
                k.phase_boundary()

        with ExitStack() as ph:
            sseg_h = [k.sbuf(ph, "b_sseg0", [128, NCORE*128], F32)] * 2
            S_in = k.sbuf(ph, "b_Sin", [128, H*128], F32)
            S_bf = [k.sbuf(ph, f"b_Sbf{i}", [128, 128], BF16) for i in range(2)]
            qd_c = [k.sbuf(ph, f"b_qdc{i}", [128, 128], BF16) for i in range(2)]
            sw_s = k.sbuf(ph, "b_sw", [128, NCORE*H], F32)
            dqb_s = k.sbuf(ph, "b_dqb", [128, H*128], BF16)
            wob = [k.sbuf(ph, f"b_wob{i}", [128, ND*512], BF16) for i in range(4)]
            bob_s = k.sbuf(ph, "b_bob", [128, D], BF16)
            xt = [k.sbuf(ph, f"b_xt{i}", [128, D], F32) for i in range(2)]
            o_t = k.sbuf(ph, "b_ot", [128, D], F32)
            x1_t = k.sbuf(ph, "b_x1t", [128, D], F32)
            psC = [k.psum(ph, f"b_psC{i}", [128, 512], F32) for i in range(2)]
            psO1 = k.psum(ph, "b_psO1", [128, 512], F32)
            psO2 = k.psum(ph, "b_psO2", [128, 512], F32)
            psBQG = k.psum(ph, "b_psQG", [128, 512], F32)
            psBQR = k.psum(ph, "b_psQR", [128, 512], F32)
            psBQS = k.psum(ph, "b_psQS", [128, 512], F32)

            k.dma('qs', sw_s.ap, sw_in.ap, [sw_in], [sw_s])
            k.dma('qg', dqb_s.h[:, :].rearrange("p (h c) -> p h c", c=128),
                  dq_b.h.rearrange("h p c -> p h c"), [dq_b], [dqb_s])
            k.dma('qg', bob_s.ap, bo_b.ap, [bo_b], [bob_s])
            wr = woT.h.rearrange("(t p) c -> p t c", p=128)
            for nc_ in range(4):
                k.dma('qg', wob[nc_].h[:, :].rearrange("p (t c) -> p t c", c=512),
                      wr[:, :, nc_*512:(nc_+1)*512], [woT], [wob[nc_]])

            ssegr = sseg_g.h.rearrange("(r h p) c -> h p r c", r=NCORE, p=128)
            for h in range(H):
                sh = sseg_h[h % 2]
                k.dma('qs', sh.h[:, :].rearrange("p (j c) -> p j c", c=128),
                      ssegr[h], [sseg_g], [sh])
                k.memset(S_in[:, h*128:(h+1)*128], 0.0, [], [S_in])
                for j in range(NCORE - 1):
                    k.stt(S_in[:, h*128:(h+1)*128], sh[:, j*128:(j+1)*128], sw_s[:, j*H+h:j*H+h+1],
                          S_in[:, h*128:(h+1)*128], AluOpType.mult, AluOpType.add,
                          [sh, sw_s, S_in], [S_in])
            for c in range(NCH):
                for h in range(H):
                    par = h % 2
                    k.ts(S_bf[par].ap, S_in[:, h*128:(h+1)*128], float(LAM_C[h] ** c), None,
                         AluOpType.mult, None, [S_in], [S_bf[par]])
                    k.tt(qd_c[par].ap, qf[:, h*SEG + c*128:h*SEG + (c+1)*128], dqb_s[:, h*128:(h+1)*128],
                         AluOpType.mult, [qf, dqb_s], [qd_c[par]])
                    k.mm(psC[par][:, 0:128], S_bf[par].ap, qd_c[par].ap, [S_bf[par], qd_c[par]], [psC[par]])
                    k.tt(attn_bf[:, h*SEG + c*128:h*SEG + (c+1)*128],
                         attn_bf[:, h*SEG + c*128:h*SEG + (c+1)*128],
                         psC[par][:, 0:128], AluOpType.add, [psC[par]], [attn_bf])

            wt = [k.sbuf(ph, f"q_wt{i}", [128, 512], BF16) for i in range(2)]
            sg = k.sbuf(ph, "q_sg", [128, 512], BF16)
            wabs = k.sbuf(ph, "q_wabs", [128, 512], BF16)
            s_bf = k.sbuf(ph, "q_sbf", [2, 512], BF16)
            rs = k.sbuf(ph, "q_rs", [2, 512], F32)
            z = k.sbuf(ph, "q_z", [128, 512], F32)
            m = k.sbuf(ph, "q_m", [128, 512], F32)
            w1 = k.sbuf(ph, "q_w1", [128, 512], BF16)
            wq = [k.sbuf(ph, f"q_wq{i}", [128, 512], BF16) for i in range(2)]
            gm = k.sbuf(ph, "q_gm", [128, 2], BF16)
            gm2f = k.sbuf(ph, "q_gm2f", [2, 128], F32)
            gm2b = k.sbuf(ph, "q_gm2b", [2, 128], BF16)
            k.dma('qs', gm.ap, gm_in.ap, [gm_in], [gm])
            k.dma('qs', gm2f.ap, gm2f_in.ap, [gm2f_in], [gm2f])
            k.dma('qs', gm2b.ap, gm2b_in.ap, [gm2b_in], [gm2b])

            def quant_tile(src_ap, src_buf, dst_ap, dst_buf, F, idx):
                wtb, wqb = wt[idx % 2], wq[idx % 2]
                k.dma('qg', wtb[:, 0:F], src_ap, [src_buf], [wtb])
                k.act(sg[:, 0:F], wtb[:, 0:F], AF.Sign, [wtb], [sg])
                k.act(wabs[:, 0:F], wtb[:, 0:F], AF.Abs, [wtb], [wabs])
                k.mm(psBQG[0:2, 0:F], gm.ap, wabs[:, 0:F], [gm, wabs], [psBQG])
                k.ts(s_bf[:, 0:F], psBQG[0:2, 0:F], 1.0/64.0, 1e-8, AluOpType.mult, AluOpType.max,
                     [psBQG], [s_bf])
                k.op('v', lambda e, F=F: e.reciprocal(rs[:, 0:F], s_bf[:, 0:F]), reads=[s_bf], writes=[rs])
                k.mm(psBQR[:, 0:F], gm2f.ap, rs[:, 0:F], [gm2f, rs], [psBQR])
                k.mm(psBQS[:, 0:F], gm2b.ap, s_bf[:, 0:F], [gm2b, s_bf], [psBQS])
                k.tt(z[:, 0:F], wabs[:, 0:F], psBQR[:, 0:F], AluOpType.mult, [wabs, psBQR], [z])
                k.ts(m[:, 0:F], z[:, 0:F], 0.5, None, AluOpType.is_gt, None, [z], [m])
                k.tt(w1[:, 0:F], m[:, 0:F], psBQS[:, 0:F], AluOpType.mult, [m, psBQS], [w1])
                k.tt(wqb[:, 0:F], w1[:, 0:F], sg[:, 0:F], AluOpType.mult, [w1, sg], [wqb])
                k.dma('qs', dst_ap, wqb[:, 0:F], [wqb], [dst_buf])

            wgur = wgu_in.h.rearrange("(t p) c -> t p c", p=128)
            qgur_o = qgu_sh.h.rearrange("(t p) c -> t p c", p=128)
            wpr = wp_in.h.rearrange("(t p) c -> t p c", p=128)
            qpr_o = qp_sh.h.rearrange("(t p) c -> t p c", p=128)
            qthunks = []
            idx = 0
            for t in range(D // 128):
                for f in range(GU_SH // 512):
                    qthunks.append((wgur[t, :, f*512:(f+1)*512], wgu_in,
                                    qgur_o[t, :, f*512:(f+1)*512], qgu_sh, 512, idx)); idx += 1
            n_gu = idx
            for t in range(4*D // 128):
                qthunks.append((wpr[t, :, 0:P_SH], wp_in, qpr_o[t, :, 0:P_SH], qp_sh, P_SH, idx)); idx += 1

            xr = x_in.h.rearrange("(n p) d -> n p d", p=128)
            x1r = x1_d.h.rearrange("(n p) d -> n p d", p=128)
            NQG = len(qthunks) // NT
            for tg in range(NT):
                k.dma('qs', xt[tg % 2].ap, xr[tg], [x_in], [xt[tg % 2]])
                for nc_ in range(4):
                    ps = psO1 if nc_ % 2 == 0 else psO2
                    wbuf = wob[nc_]
                    for et in range(ND):
                        k.mm(ps[:, 0:512], attn_bf[:, et*SEG + tg*128: et*SEG + (tg+1)*128],
                             wbuf[:, et*512:(et+1)*512], [attn_bf, wbuf], [ps],
                             start=(et == 0), stop=(et == ND-1))
                    k.tt(o_t[:, nc_*512:(nc_+1)*512], ps[:, 0:512], bob_s[:, nc_*512:(nc_+1)*512],
                         AluOpType.add, [ps, bob_s], [o_t])
                emit_ln_tile(k, o_t.ap, o_t, x1_t.ap, x1_t, stats)
                k.tt(x1_t.ap, x1_t.ap, xt[tg % 2].ap, AluOpType.add, [x1_t, xt[tg % 2]], [x1_t])
                k.dma('qs', x1r[tg], x1_t.ap, [x1_t], [x1_d])
                for qi in range(tg*NQG, (tg+1)*NQG):
                    quant_tile(*qthunks[qi])
                    if qi == n_gu - 1:
                        k.collective(qgu_sh.h, qgu_g.h, [qgu_sh], qgu_g)
            k.collective(qp_sh.h, qp_g.h, [qp_sh], qp_g)
            k.phase_boundary()

    with ExitStack() as dsc:
        o128c = k.sbuf(dsc, "d_o128c", [128, 1], BF16)
        o1r = k.sbuf(dsc, "d_o1r", [1, 128], F32)
        sqrow = k.sbuf(dsc, "d_sqrow", [1, SEG], F32)
        rstd = k.sbuf(dsc, "d_rstd", [1, SEG], F32)
        rstd_b = k.sbuf(dsc, "d_rstdb", [128, SEG], F32)
        k.dma('qs', o128c.ap, o128c_in.ap, [o128c_in], [o128c])
        k.dma('qs', o1r.ap, o1r_in.ap, [o1r_in], [o1r])
        xr1 = x1_d.h.rearrange("(n p) d -> n p d", p=128)

        with ExitStack() as ph:
            xt = [k.sbuf(ph, f"d_xt{i}", [128, D], F32) for i in range(2)]
            stage = k.sbuf(ph, "d_stage", [128, D], BF16)
            ln3f = k.sbuf(ph, "d_ln3f", [128, ND*SEG], BF16)
            hw = k.sbuf(ph, "d_hw", [128, SEG], BF16)
            hsq = k.sbuf(ph, "d_hsq", [128, SEG], BF16)
            wg = [k.sbuf(ph, f"d_wg{i}", [128, ND*128], BF16) for i in range(2)]
            wu = [k.sbuf(ph, f"d_wu{i}", [128, ND*128], BF16) for i in range(2)]
            gs = k.sbuf(ph, "d_gs", [128, SEG], BF16)
            us = k.sbuf(ph, "d_us", [128, SEG], BF16)
            psA = k.psum(ph, "d_psA", [128, 512], F32)
            psB = k.psum(ph, "d_psB", [128, 512], F32)
            psC = k.psum(ph, "d_psC", [128, 512], F32)
            psD = k.psum(ph, "d_psD", [128, 512], F32)
            psT = k.psum(ph, "d_psT", [128, 1024], BF16)
            psQ = k.psum(ph, "d_psQ", [128, 512], F32)

            k.memset(sqrow.ap, 0.0, [], [sqrow])
            for i in range(NT):
                k.dma('qs', xt[i % 2].ap, xr1[i], [x1_d], [xt[i % 2]])
                emit_ln_tile(k, xt[i % 2].ap, xt[i % 2], stage.ap, stage, stats)
                for dt in range(ND):
                    k.trans(psT[:, 0:128], stage[:, dt*128:(dt+1)*128], ident_s.ap, [stage, ident_s], [psT])
                    k.copy(ln3f[:, dt*SEG + i*128: dt*SEG + (i+1)*128], psT[:, 0:128], [psT], [ln3f])

            qgur = qgu_g.h.rearrange("(r t p) c -> r p t c", r=NCORE, p=128)
            NGC = GU_SH
            for hc in range(NHID):
                bg, bu = wg[hc % 2], wu[hc % 2]
                gcol = hc * 128
                r, lc = gcol // NGC, gcol % NGC
                k.dma('qg', bg.h[:, :].rearrange("p (t c) -> p t c", c=128),
                      qgur[r][:, :, lc:lc+128], [qgu_g], [bg])
                ucol = 4*D + hc * 128
                r2, lc2 = ucol // NGC, ucol % NGC
                k.dma('qg', bu.h[:, :].rearrange("p (t c) -> p t c", c=128),
                      qgur[r2][:, :, lc2:lc2+128], [qgu_g], [bu])
                for n in range(2):
                    psg = psA if n == 0 else psB
                    psu = psC if n == 0 else psD
                    for dt in range(ND):
                        k.mm(psg[:, 0:512], bg[:, dt*128:(dt+1)*128],
                             ln3f[:, dt*SEG + n*512: dt*SEG + (n+1)*512], [bg, ln3f], [psg],
                             start=(dt == 0), stop=(dt == ND-1))
                    for dt in range(ND):
                        k.mm(psu[:, 0:512], bu[:, dt*128:(dt+1)*128],
                             ln3f[:, dt*SEG + n*512: dt*SEG + (n+1)*512], [bu, ln3f], [psu],
                             start=(dt == 0), stop=(dt == ND-1))
                    k.act(gs[:, n*512:(n+1)*512], psg[:, 0:512], AF.Silu, [psg], [gs])
                    k.copy(us[:, n*512:(n+1)*512], psu[:, 0:512], [psu], [us])
                k.tt(hw.ap, gs.ap, us.ap, AluOpType.mult, [gs, us], [hw])
                k.dma('qs', hd_d.h[:, hc*SEG:(hc+1)*SEG], hw.ap, [hw], [hd_d])
                k.tt(hsq.ap, hw.ap, hw.ap, AluOpType.mult, [hw], [hsq])
                for n in range(2):
                    k.mm(psQ[0:1, 0:512], o128c.ap, hsq[:, n*512:(n+1)*512], [o128c, hsq], [psQ])
                    k.stt(sqrow[0:1, n*512:(n+1)*512], psQ[0:1, 0:512], 1.0,
                          sqrow[0:1, n*512:(n+1)*512], AluOpType.mult, AluOpType.add,
                          [psQ, sqrow], [sqrow])
            k.act(rstd[0:1, :], sqrow[0:1, :], AF.Ln, [sqrow], [rstd], scale=1.0/(4*D),
                  bias=1.1920929e-7)
            k.act(rstd[0:1, :], rstd[0:1, :], AF.Exp, [rstd], [rstd], scale=-0.5)
            for n in range(2):
                k.mm(psQ[:, 0:512], o1r.ap, rstd[0:1, n*512:(n+1)*512], [o1r, rstd], [psQ])
                k.copy(rstd_b[:, n*512:(n+1)*512], psQ[:, 0:512], [psQ], [rstd_b])
            k.phase_boundary()

        with ExitStack() as ph:
            xt = [k.sbuf(ph, f"d2_xt{i}", [128, D], F32) for i in range(2)]
            stage = k.sbuf(ph, "d2_stage", [128, D], BF16)
            hbf = k.sbuf(ph, "d2_hbf", [128, NHID*512], BF16)
            wp = [k.sbuf(ph, f"d2_wp{i}", [128, NHID*128], BF16) for i in range(2)]
            obf = k.sbuf(ph, "d2_obf", [128, NT*D], BF16)
            x2_t = k.sbuf(ph, "d2_x2t", [128, D], F32)
            psA = k.psum(ph, "d2_psA", [128, 512], F32)
            psB = k.psum(ph, "d2_psB", [128, 512], F32)
            psT = k.psum(ph, "d2_psT", [128, 1024], BF16)

            qpr = qp_g.h.rearrange("(r t p) c -> r p t c", r=NCORE, p=128)
            hr = hd_d.h[:, :].rearrange("p (n t) -> p n t", t=SEG)
            for th in range(2):
                k.dma('qs', hbf.h[:, :].rearrange("p (n t) -> p n t", t=512),
                      hr[:, :, th*512:(th+1)*512], [hd_d], [hbf])
                for dot in range(ND):
                    bp = wp[dot % 2]
                    pcol = dot * 128
                    r, lc = pcol // P_SH, pcol % P_SH
                    k.dma('qg', bp.h[:, :].rearrange("p (t c) -> p t c", c=128),
                          qpr[r][:, :, lc:lc+128], [qp_g], [bp])
                    ps = psA if dot % 2 == 0 else psB
                    for ht in range(NHID):
                        k.mm(ps[:, 0:512], bp[:, ht*128:(ht+1)*128],
                             hbf[:, ht*512:(ht+1)*512], [bp, hbf], [ps],
                             start=(ht == 0), stop=(ht == NHID-1))
                    k.tt(stage[:, 0:512], ps[:, 0:512], rstd_b[:, th*512:(th+1)*512], AluOpType.mult,
                         [ps, rstd_b], [stage])
                    for q4 in range(4):
                        tg = th*4 + q4
                        k.trans(psT[:, 0:128], stage[:, q4*128:(q4+1)*128], ident_s.ap,
                                [stage, ident_s], [psT])
                        k.copy(obf[:, tg*D + dot*128: tg*D + (dot+1)*128], psT[:, 0:128], [psT], [obf])

            x2r = x2_d.h.rearrange("(n p) d -> n p d", p=128)
            for tg in range(NT):
                k.dma('qs', xt[tg % 2].ap, xr1[tg], [x1_d], [xt[tg % 2]])
                emit_ln_tile(k, obf[:, tg*D:(tg+1)*D], obf, x2_t.ap, x2_t, stats)
                k.tt(x2_t.ap, x2_t.ap, xt[tg % 2].ap, AluOpType.add, [x2_t, xt[tg % 2]], [x2_t])
                k.dma('qs', x2r[tg], x2_t.ap, [x2_t], [x2_d])
            k.phase_boundary()

    with ExitStack() as sc1:
        kd = k.sbuf(sc1, "e_kd", [DS, SEG], BF16)
        vd = k.sbuf(sc1, "e_vd", [DS, SEG], BF16)
        qd = k.sbuf(sc1, "e_qd", [DS, SEG], BF16)
        lg = k.sbuf(sc1, "e_lg", [DS, SEG], F32)
        tli = k.sbuf(sc1, "n_tli", [TC, TC], F32)
        sln = k.sbuf(sc1, "n_sln", [TC, TC], BF16)
        sun = k.sbuf(sc1, "n_sun", [TC, TC], BF16)
        su = k.sbuf(sc1, "n_su", [TC, TC], BF16)
        sui = k.sbuf(sc1, "n_sui", [TC, TC], BF16)
        onesc = k.sbuf(sc1, "n_oc", [64, 1], F32)
        onesr = k.sbuf(sc1, "n_or", [1, 64], F32)
        for (d, s_) in ((tli_in, tli), (sln_in, sln), (sun_in, sun), (su_in, su),
                        (sui_in, sui), (oc_in, onesc), (or_in, onesr)):
            k.dma('qs', s_.ap, d.ap, [d], [s_])

        with ExitStack() as ph:
            xt = [k.sbuf(ph, f"e_xt{i}", [128, D], F32) for i in range(2)]
            stage = k.sbuf(ph, "e_stage", [128, D], BF16)
            lnf = k.sbuf(ph, "e_lnf", [128, ND*SEG], BF16)
            wsb = k.sbuf(ph, "e_wsb", [128, ND*4*DS], BF16)
            proj = k.sbuf(ph, "e_proj", [64, 4*SEG], F32)
            sqb = k.sbuf(ph, "e_sqb", [64, SEG], F32)
            nrow = k.sbuf(ph, "e_nrow", [1, SEG], F32)
            rrow = k.sbuf(ph, "e_rrow", [1, SEG], F32)
            wab_s = k.sbuf(ph, "e_wab", [64, 1], F32)
            lam_s = k.sbuf(ph, "e_lam", [64, 4], F32)
            psA = k.psum(ph, "e_psA", [128, 512], F32)
            psB = k.psum(ph, "e_psB", [128, 512], F32)
            psC = k.psum(ph, "e_psC", [128, 512], F32)
            psT = k.psum(ph, "e_psT", [128, 1024], BF16)

            k.dma('qs', wab_s.ap, wab_in.ap, [wab_in], [wab_s])
            k.dma('qs', lam_s[:, 0:1], lam_in.ap, [lam_in], [lam_s])
            xr2 = x2_d.h.rearrange("(n p) d -> n p d", p=128)
            for i in range(NT):
                k.dma('qs', xt[i % 2].ap, xr2[i], [x2_d], [xt[i % 2]])
                emit_ln_tile(k, xt[i % 2].ap, xt[i % 2], stage.ap, stage, stats)
                for dt in range(ND):
                    k.trans(psT[:, 0:128], stage[:, dt*128:(dt+1)*128], ident_s.ap, [stage, ident_s], [psT])
                    k.copy(lnf[:, dt*SEG + i*128: dt*SEG + (i+1)*128], psT[:, 0:128], [psT], [lnf])
            k.dma('qg', wsb.h[:, :].rearrange("p (t c) -> p t c", c=4*DS),
                  wkvqaT_in.h.rearrange("(t p) c -> p t c", p=128), [wkvqaT_in], [wsb])
            for j in range(4):
                for n in range(2):
                    ps = psA if n == 0 else psB
                    for dt in range(ND):
                        k.mm(ps[0:64, 0:512], wsb[:, dt*4*DS + j*DS: dt*4*DS + (j+1)*DS],
                             lnf[:, dt*SEG + n*512: dt*SEG + (n+1)*512], [wsb, lnf], [ps],
                             start=(dt == 0), stop=(dt == ND-1))
                    if j < 3:
                        k.copy(proj[:, j*SEG + n*512: j*SEG + (n+1)*512], ps[0:64, 0:512], [ps], [proj])
                    else:
                        k.act(proj[:, j*SEG + n*512: j*SEG + (n+1)*512], ps[0:64, 0:512], AF.Sigmoid,
                              [ps, wab_s], [proj], bias=wab_s.ap)
            for j, dst in ((0, kd), (2, qd)):
                k.act(sqb.ap, proj[:, j*SEG:(j+1)*SEG], AF.Square, [proj], [sqb])
                for n in range(2):
                    k.mm(psC[0:1, 0:512], onesc.ap, sqb[:, n*512:(n+1)*512], [onesc, sqb], [psC])
                    k.act(nrow[:, n*512:(n+1)*512], psC[0:1, 0:512], AF.Sqrt, [psC], [nrow])
                k.ts(nrow.ap, nrow.ap, 1e-12, None, AluOpType.max, None, [nrow], [nrow])
                k.op('v', lambda e: e.reciprocal(rrow.ap, nrow.ap), reads=[nrow], writes=[rrow])
                for n in range(2):
                    k.mm(psC[0:64, 0:512], onesr.ap, rrow[:, n*512:(n+1)*512], [onesr, rrow], [psC])
                    k.tt(dst[:, n*512:(n+1)*512], proj[:, j*SEG + n*512: j*SEG + (n+1)*512],
                         psC[0:64, 0:512], AluOpType.mult, [proj, psC], [dst])
            k.copy(vd.ap, proj[:, 1*SEG:2*SEG], [proj], [vd])
            k.act(lam_s[:, 1:2], lam_s[:, 0:1], AF.Sigmoid, [lam_s], [lam_s])
            k.act(lam_s[:, 2:3], lam_s[:, 1:2], AF.Ln, [lam_s], [lam_s], bias=1e-8)
            k.ts(lam_s[:, 3:4], lam_s[:, 2:3], 8.0, None, AluOpType.mult, None, [lam_s], [lam_s])
            k.ts(lg.ap, proj[:, 3*SEG:4*SEG], lam_s[:, 3:4], None, AluOpType.mult, None,
                 [proj, lam_s], [lg])
            k.phase_boundary()

        with ExitStack() as sd:
            Hst = k.sbuf(sd, "n_Hst", [64, 64], F32)
            Hbf = k.sbuf(sd, "n_Hbf", [64, 64], BF16)
            lgT = k.sbuf(sd, "n_lgT", [TC, 64], F32)
            ALi = k.sbuf(sd, "n_ALi", [TC, 64], F32)
            A_t = k.sbuf(sd, "n_At", [TC, 64], F32)
            Apv = k.sbuf(sd, "n_Apv", [TC, 64], F32)
            Ainv = k.sbuf(sd, "n_Ainv", [TC, 64], F32)
            a_t = k.sbuf(sd, "n_at", [TC, 64], F32)
            w1 = k.sbuf(sd, "n_w1", [TC, 64], F32)
            tmpf = k.sbuf(sd, "n_tmpf", [TC, 64], F32)
            kc = k.sbuf(sd, "n_kc", [TC, 64], BF16)
            vc = k.sbuf(sd, "n_vc", [TC, 64], BF16)
            qc = k.sbuf(sd, "n_qc", [TC, 64], BF16)
            kt_ = k.sbuf(sd, "n_kt", [TC, 64], BF16)
            qp_ = k.sbuf(sd, "n_qp", [TC, 64], BF16)
            qi_ = k.sbuf(sd, "n_qi", [TC, 64], BF16)
            Dm = k.sbuf(sd, "n_Dm", [TC, 64], BF16)
            DmT = k.sbuf(sd, "n_DmT", [64, TC], BF16)
            ktT = k.sbuf(sd, "n_ktT", [64, TC], BF16)
            qpT_ = k.sbuf(sd, "n_qpT", [64, TC], BF16)
            qiT = k.sbuf(sd, "n_qiT", [64, TC], BF16)
            Npow = [k.sbuf(sd, f"n_N{j}", [TC, TC], BF16) for j in range(5)]
            NTpow = [k.sbuf(sd, f"n_NT{j}", [TC, TC], BF16) for j in range(6)]
            X = k.sbuf(sd, "n_X", [TC, 128], BF16)
            Gt = k.sbuf(sd, "n_Gt", [TC, TC], BF16)
            d2 = k.sbuf(sd, "n_d2", [TC, 64], F32)
            sq2 = k.sbuf(sd, "n_sq2", [TC, 64], F32)
            scol = k.sbuf(sd, "n_scol", [TC, 1], F32)
            Aer = k.sbuf(sd, "n_Aer", [1, 64], F32)
            AeC = k.sbuf(sd, "n_AeC", [64, 1], F32)
            Wn = k.sbuf(sd, "n_Wn", [TC, 64], BF16)
            Mc = k.sbuf(sd, "n_Mc", [TC, TC], BF16)
            McT = k.sbuf(sd, "n_McT", [TC, TC], BF16)
            tmpb = k.sbuf(sd, "n_tmpb", [TC, TC], BF16)
            Mrun = k.sbuf(sd, "n_Mrun", [TC, TC], BF16)
            MrunT = k.sbuf(sd, "n_MrunT", [TC, TC], BF16)
            outf = k.sbuf(sd, "n_outf", [TC, TC], F32)
            ps1 = k.psum(sd, "n_ps1", [128, 512], F32)
            ps2 = k.psum(sd, "n_ps2", [128, 512], F32)
            ps3 = k.psum(sd, "n_ps3", [128, 512], F32)
            psTb = k.psum(sd, "n_psTb", [128, 1024], BF16)
            psF = k.psum(sd, "n_psF", [128, 512], F32)

            def tr(src_ap, src_buf, dst_ap, dst_buf, f32=False, pdim=64):
                if f32:
                    k.trans(psF[0:64, 0:pdim], src_ap, identf[0:pdim, 0:pdim], [src_buf, identf], [psF])
                    k.copy(dst_ap, psF[0:64, 0:pdim], [psF], [dst_buf])
                else:
                    k.trans(psTb[0:64, 0:pdim], src_ap, ident_s[0:pdim, 0:pdim], [src_buf, ident_s], [psTb])
                    k.copy(dst_ap, psTb[0:64, 0:pdim], [psTb], [dst_buf])

            def apply_M(width):
                for j in (5, 4, 3, 2, 1, 0):
                    k.mm(ps3[0:TC, 0:width], ident_s[0:TC, 0:TC], X[:, 0:width], [ident_s, X], [ps3],
                         start=True, stop=False)
                    k.mm(ps3[0:TC, 0:width], NTpow[j].ap, X[:, 0:width], [NTpow[j], X], [ps3],
                         start=False, stop=True)
                    k.copy(X[:, 0:width], ps3[0:TC, 0:width], [ps3], [X])

            def build_powers():
                for lev in range(1, 5):
                    k.mm(ps3[0:TC, 0:TC], NTpow[lev-1].ap, Npow[lev-1].ap, [NTpow[lev-1], Npow[lev-1]], [ps3])
                    k.copy(Npow[lev].ap, ps3[0:TC, 0:TC], [ps3], [Npow[lev]])
                    k.mm(ps2[0:TC, 0:TC], Npow[lev-1].ap, NTpow[lev-1].ap, [Npow[lev-1], NTpow[lev-1]], [ps2])
                    k.copy(NTpow[lev].ap, ps2[0:TC, 0:TC], [ps2], [NTpow[lev]])
                k.mm(ps3[0:TC, 0:TC], Npow[4].ap, NTpow[4].ap, [Npow[4], NTpow[4]], [ps3])
                k.copy(NTpow[5].ap, ps3[0:TC, 0:TC], [ps3], [NTpow[5]])

            def build_LN_masks():
                k.mm(ps1[0:TC, 0:TC], ktT.ap, DmT.ap, [ktT, DmT], [ps1])
                k.tt(Npow[0].ap, ps1[0:TC, 0:TC], sln.ap, AluOpType.mult, [ps1, sln], [Npow[0]])
                k.mm(ps2[0:TC, 0:TC], DmT.ap, ktT.ap, [DmT, ktT], [ps2])
                k.tt(NTpow[0].ap, ps2[0:TC, 0:TC], sun.ap, AluOpType.mult, [ps2, sun], [NTpow[0]])

            def build_Dm(with_s):
                k.tt(Dm.ap, kc.ap, w1.ap, AluOpType.mult, [kc, w1], [Dm])
                if with_s:
                    k.ts(Dm.ap, Dm.ap, scol.ap, None, AluOpType.mult, None, [Dm, scol], [Dm])
                tr(Dm.ap, Dm, DmT.ap, DmT)

            def chunk_prelude(c, pass2):
                cs = slice(c*TC, (c+1)*TC)
                tr(lg[:, cs], lg, lgT.ap, lgT, f32=True)
                k.mm(psF[0:TC, 0:64], tli.ap, lgT.ap, [tli, lgT], [psF])
                k.copy(ALi.ap, psF[0:TC, 0:64], [psF], [ALi])
                k.act(A_t.ap, ALi.ap, AF.Exp, [ALi], [A_t])
                k.tt(tmpf.ap, ALi.ap, lgT.ap, AluOpType.subtract, [ALi, lgT], [tmpf])
                k.act(Apv.ap, tmpf.ap, AF.Exp, [tmpf], [Apv])
                k.act(Ainv.ap, ALi.ap, AF.Exp, [ALi], [Ainv], scale=-1.0)
                k.act(a_t.ap, lgT.ap, AF.Exp, [lgT], [a_t])
                k.mm(psF[0:1, 0:64], onesc.ap, lgT.ap, [onesc, lgT], [psF])
                k.act(Aer.ap, psF[0:1, 0:64], AF.Exp, [psF], [Aer])
                k.trans(psF[0:64, 0:1], Aer.ap, identf[0:1, 0:1], [Aer, identf], [psF])
                k.copy(AeC.ap, psF[0:64, 0:1], [psF], [AeC])
                tr(kd[:, cs], kd, kc.ap, kc)
                tr(vd[:, cs], vd, vc.ap, vc)
                tr(qd[:, cs], qd, qc.ap, qc)
                k.tt(kt_.ap, kc.ap, Apv.ap, AluOpType.mult, [kc, Apv], [kt_])
                k.tt(qp_.ap, qc.ap, Apv.ap, AluOpType.mult, [qc, Apv], [qp_])
                k.tt(qi_.ap, qc.ap, A_t.ap, AluOpType.mult, [qc, A_t], [qi_])
                k.ts(tmpf.ap, a_t.ap, -1.0, 1.0, AluOpType.mult, AluOpType.add, [a_t], [tmpf])
                k.tt(w1.ap, tmpf.ap, Ainv.ap, AluOpType.mult, [tmpf, Ainv], [w1])
                tr(kt_.ap, kt_, ktT.ap, ktT)
                tr(qp_.ap, qp_, qpT_.ap, qpT_)
                if pass2:
                    tr(qi_.ap, qi_, qiT.ap, qiT)

            k.memset(Hst.ap, 0.0, [], [Hst])
            k.memset(Hbf.ap, 0.0, [], [Hbf])
            k.copy(Mrun.ap, ident_s[0:TC, 0:TC], [ident_s], [Mrun])
            k.copy(MrunT.ap, ident_s[0:TC, 0:TC], [ident_s], [MrunT])
            for c in range(NCHK):
                chunk_prelude(c, pass2=False)
                build_Dm(with_s=False)
                build_LN_masks()
                build_powers()
                k.mm(ps1[0:TC, 0:64], ktT.ap, Hbf.ap, [ktT, Hbf], [ps1])
                k.tt(X[:, 0:64], vc.ap, ps1[0:TC, 0:64], AluOpType.subtract, [vc, ps1], [X])
                k.copy(X[:, 64:128], kt_.ap, [kt_], [X])
                apply_M(128)
                k.mm(ps2[0:TC, 0:64], ident_s[0:TC, 0:TC], Hbf.ap, [ident_s, Hbf], [ps2],
                     start=True, stop=False)
                k.mm(ps2[0:TC, 0:64], Dm.ap, X[:, 0:64], [Dm, X], [ps2], start=False, stop=True)
                k.ts(Hst.ap, ps2[0:TC, 0:64], AeC.ap, None, AluOpType.mult, None, [ps2, AeC], [Hst])
                k.copy(Hbf.ap, Hst.ap, [Hst], [Hbf])
                k.ts(Wn.ap, X[:, 64:128], -1.0, None, AluOpType.mult, None, [X], [Wn])
                k.mm(ps1[0:TC, 0:TC], ident_s[0:TC, 0:TC], ident_s[0:TC, 0:TC], [ident_s], [ps1],
                     start=True, stop=False)
                k.mm(ps1[0:TC, 0:TC], Dm.ap, Wn.ap, [Dm, Wn], [ps1], start=False, stop=True)
                k.ts(Mc.ap, ps1[0:TC, 0:TC], AeC.ap, None, AluOpType.mult, None, [ps1, AeC], [Mc])
                k.mm(ps2[0:TC, 0:TC], ident_s[0:TC, 0:TC], ident_s[0:TC, 0:TC], [ident_s], [ps2],
                     start=True, stop=False)
                k.mm(ps2[0:TC, 0:TC], Wn.ap, Dm.ap, [Wn, Dm], [ps2], start=False, stop=True)
                k.copy(tmpb.ap, ps2[0:TC, 0:TC], [ps2], [tmpb])
                k.mm(psF[0:TC, 0:TC], onesr.ap, Aer.ap, [onesr, Aer], [psF])
                k.tt(McT.ap, tmpb.ap, psF[0:TC, 0:TC], AluOpType.mult, [tmpb, psF], [McT])
                k.mm(ps1[0:TC, 0:TC], McT.ap, Mrun.ap, [McT, Mrun], [ps1])
                k.mm(ps2[0:TC, 0:TC], Mrun.ap, McT.ap, [Mrun, McT], [ps2])
                k.copy(Mrun.ap, ps1[0:TC, 0:TC], [ps1], [Mrun])
                k.copy(MrunT.ap, ps2[0:TC, 0:TC], [ps2], [MrunT])
            k.copy(outf.ap, MrunT.ap, [MrunT], [outf])
            k.dma('qs', dn_l.h[0:64, :], outf.ap, [outf], [dn_l])
            k.dma('qs', dn_l.h[64:128, :], Hst.ap, [Hst], [dn_l])
            k.collective(dn_l.h, dn_g.h, [dn_l], dn_g)
            k.phase_boundary()

            with ExitStack() as ph:
                dn_sb = k.sbuf(ph, "g_dnsb", [128, NCORE*64], F32)
                dnsel = k.sbuf(ph, "g_dnsel", [64, 16], F32)
                Mu_f = k.sbuf(ph, "g_Muf", [64, 64], F32)
                Mu_b = k.sbuf(ph, "g_Mub", [64, 64], BF16)
                bu_b = k.sbuf(ph, "g_bub", [64, 64], BF16)
                y_buf = k.sbuf(ph, "g_ybuf", [64, NCHK*64], F32)
                ynf = k.sbuf(ph, "g_ynf", [64, SEG], BF16)
                wos = k.sbuf(ph, "g_wos", [64, D], BF16)
                nwb = k.sbuf(ph, "g_nwb", [64, NCHK*64], F32)
                msq = k.sbuf(ph, "g_msq", [64, NCHK], F32)
                xt = [k.sbuf(ph, f"g_xt{i}", [128, D], F32) for i in range(2)]
                o_t = k.sbuf(ph, "g_ot", [128, D], F32)
                x3_t = k.sbuf(ph, "g_x3t", [128, D], F32)

                k.dma('qs', dnsel.ap, dnsel_in.ap, [dnsel_in], [dnsel])
                k.dma('qs', nwb.ap, nwb_in.ap, [nwb_in], [nwb])
                k.dma('qg', wos.ap, wo1T_in.ap, [wo1T_in], [wos])
                for r in range(NCORE):
                    k.dma('qs', dn_sb[:, r*64:(r+1)*64], dn_g.h[r*128:(r+1)*128, :], [dn_g], [dn_sb])

                k.memset(Hst.ap, 0.0, [], [Hst])
                k.memset(Hbf.ap, 0.0, [], [Hbf])
                for j in range(NCORE - 1):
                    k.ts(Mu_f.ap, dn_sb[0:64, j*64:(j+1)*64], dnsel[:, j:j+1], None,
                         AluOpType.mult, None, [dn_sb, dnsel], [Mu_f])
                    k.stt(Mu_f.ap, identf[0:64, 0:64], dnsel[:, 8+j:9+j], Mu_f.ap,
                          AluOpType.mult, AluOpType.add, [identf, dnsel, Mu_f], [Mu_f])
                    k.copy(Mu_b.ap, Mu_f.ap, [Mu_f], [Mu_b])
                    k.ts(Mu_f.ap, dn_sb[64:128, j*64:(j+1)*64], dnsel[:, j:j+1], None,
                         AluOpType.mult, None, [dn_sb, dnsel], [Mu_f])
                    k.copy(bu_b.ap, Mu_f.ap, [Mu_f], [bu_b])
                    k.mm(ps1[0:TC, 0:64], Mu_b.ap, Hbf.ap, [Mu_b, Hbf], [ps1],
                         start=True, stop=False)
                    k.mm(ps1[0:TC, 0:64], ident_s[0:TC, 0:TC], bu_b.ap, [ident_s, bu_b],
                         [ps1], start=False, stop=True)
                    k.copy(Hst.ap, ps1[0:TC, 0:64], [ps1], [Hst])
                    k.copy(Hbf.ap, Hst.ap, [Hst], [Hbf])

                for c in range(NCHK):
                    cs = slice(c*TC, (c+1)*TC)
                    chunk_prelude(c, pass2=True)
                    build_Dm(with_s=False)
                    build_LN_masks()
                    build_powers()
                    k.mm(ps1[0:TC, 0:64], ktT.ap, Hbf.ap, [ktT, Hbf], [ps1])
                    k.tt(X[:, 0:64], vc.ap, ps1[0:TC, 0:64], AluOpType.subtract, [vc, ps1], [X])
                    apply_M(64)
                    k.mm(ps1[0:TC, 0:TC], DmT.ap, qpT_.ap, [DmT, qpT_], [ps1])
                    k.tt(Gt.ap, ps1[0:TC, 0:TC], su.ap, AluOpType.mult, [ps1, su], [Gt])
                    k.mm(ps2[0:TC, 0:64], qpT_.ap, Hbf.ap, [qpT_, Hbf], [ps2], start=True, stop=False)
                    k.mm(ps2[0:TC, 0:64], Gt.ap, X[:, 0:64], [Gt, X], [ps2], start=False, stop=True)
                    k.tt(d2.ap, vc.ap, ps2[0:TC, 0:64], AluOpType.subtract, [vc, ps2], [d2])
                    k.tt(sq2.ap, d2.ap, d2.ap, AluOpType.mult, [d2], [sq2])
                    k.op('v', lambda e: e.reduce_sum(scol.ap, sq2.ap, AX.X), reads=[sq2], writes=[scol])
                    k.act(scol.ap, scol.ap, AF.Sigmoid, [scol], [scol], scale=float(1.0/(1.0+1e-6)))
                    build_Dm(with_s=True)
                    build_LN_masks()
                    build_powers()
                    k.mm(ps1[0:TC, 0:64], ktT.ap, Hbf.ap, [ktT, Hbf], [ps1])
                    k.tt(X[:, 0:64], vc.ap, ps1[0:TC, 0:64], AluOpType.subtract, [vc, ps1], [X])
                    apply_M(64)
                    k.mm(ps1[0:TC, 0:TC], DmT.ap, qiT.ap, [DmT, qiT], [ps1])
                    k.tt(Gt.ap, ps1[0:TC, 0:TC], sui.ap, AluOpType.mult, [ps1, sui], [Gt])
                    k.mm(ps2[0:TC, 0:64], qiT.ap, Hbf.ap, [qiT, Hbf], [ps2], start=True, stop=False)
                    k.mm(ps2[0:TC, 0:64], Gt.ap, X[:, 0:64], [Gt, X], [ps2], start=False, stop=True)
                    k.copy(y_buf[:, cs], ps2[0:TC, 0:64], [ps2], [y_buf])
                    k.mm(ps2[0:TC, 0:64], ident_s[0:TC, 0:TC], Hbf.ap, [ident_s, Hbf], [ps2],
                         start=True, stop=False)
                    k.mm(ps2[0:TC, 0:64], Dm.ap, X[:, 0:64], [Dm, X], [ps2], start=False, stop=True)
                    k.ts(Hst.ap, ps2[0:TC, 0:64], AeC.ap, None, AluOpType.mult, None, [ps2, AeC], [Hst])
                    k.copy(Hbf.ap, Hst.ap, [Hst], [Hbf])

                for c in range(NCHK):
                    k.tt(sq2.ap, y_buf[:, c*64:(c+1)*64], y_buf[:, c*64:(c+1)*64], AluOpType.mult,
                         [y_buf], [sq2])
                    k.op('v', lambda e, c=c: e.reduce_sum(msq[:, c:c+1], sq2.ap, AX.X),
                         reads=[sq2], writes=[msq])
                k.ts(msq.ap, msq.ap, 1.0/64.0, None, AluOpType.mult, None, [msq], [msq])
                k.act(msq.ap, msq.ap, AF.Ln, [msq], [msq], bias=1e-6)
                k.act(msq.ap, msq.ap, AF.Exp, [msq], [msq], scale=-0.5)
                for c in range(NCHK):
                    k.ts(y_buf[:, c*64:(c+1)*64], y_buf[:, c*64:(c+1)*64], msq[:, c:c+1], None,
                         AluOpType.mult, None, [y_buf, msq], [y_buf])
                k.tt(y_buf.ap, y_buf.ap, nwb.ap, AluOpType.mult, [y_buf, nwb], [y_buf])
                for c in range(NCHK):
                    k.copy(Dm.ap, y_buf[:, c*64:(c+1)*64], [y_buf], [Dm])
                    tr(Dm.ap, Dm, ynf[:, c*64:(c+1)*64], ynf)
                xr2 = x2_d.h.rearrange("(n p) d -> n p d", p=128)
                x3r = x3_d.h.rearrange("(n p) d -> n p d", p=128)
                for tg in range(NT):
                    k.dma('qs', xt[tg % 2].ap, xr2[tg], [x2_d], [xt[tg % 2]])
                    for dc in range(4):
                        ps = ps1 if dc % 2 == 0 else ps2
                        k.mm(ps[:, 0:512], ynf[:, tg*128:(tg+1)*128], wos[:, dc*512:(dc+1)*512],
                             [ynf, wos], [ps])
                        k.copy(o_t[:, dc*512:(dc+1)*512], ps[:, 0:512], [ps], [o_t])
                    emit_ln_tile(k, o_t.ap, o_t, x3_t.ap, x3_t, stats)
                    k.tt(x3_t.ap, x3_t.ap, xt[tg % 2].ap, AluOpType.add, [x3_t, xt[tg % 2]], [x3_t])
                    k.dma('qs', x3r[tg], x3_t.ap, [x3_t], [x3_d])
                k.phase_boundary()

    with ExitStack() as ph:
        xt = [k.sbuf(ph, f"h_xt{i}", [128, D], F32) for i in range(2)]
        stage = k.sbuf(ph, "h_stage", [128, D], BF16)
        ln7f = k.sbuf(ph, "h_ln7f", [128, ND*SEG], BF16)
        y_acc = k.sbuf(ph, "h_yacc", [128, NT*D], F32)
        hbuf = k.sbuf(ph, "h_hbuf", [128, NOC_M*SEG], BF16)
        wb1 = [k.sbuf(ph, f"h_wb1{i}", [128, ND*128], BF16) for i in range(2)]
        wb2 = [k.sbuf(ph, f"h_wb2{i}", [128, ND*128], BF16) for i in range(2)]
        wpb = [k.sbuf(ph, f"h_wpb{i}", [128, NOC_M*512], BF16) for i in range(2)]
        gw_s = k.sbuf(ph, "h_gw", [128, ND*16], BF16)
        router = k.sbuf(ph, "h_router", [128, NT*16], F32)
        comb = k.sbuf(ph, "h_comb", [128, NT*16], F32)
        rst = k.sbuf(ph, "h_rst", [128, 8], F32)
        gs = k.sbuf(ph, "h_gs", [128, SEG], BF16)
        us = k.sbuf(ph, "h_us", [128, SEG], BF16)
        o_t = k.sbuf(ph, "h_ot", [128, D], F32)
        psA = k.psum(ph, "h_psA", [128, 512], F32)
        psB = k.psum(ph, "h_psB", [128, 512], F32)
        psC = k.psum(ph, "h_psC", [128, 512], F32)
        psD = k.psum(ph, "h_psD", [128, 512], F32)
        psT = k.psum(ph, "h_psT", [128, 1024], BF16)

        k.dma('qg', gw_s.h[:, :].rearrange("p (t c) -> p t c", c=16),
              gw_in.h.rearrange("(t p) c -> p t c", p=128), [gw_in], [gw_s])
        xr3 = x3_d.h.rearrange("(n p) d -> n p d", p=128)
        for i in range(NT):
            k.dma('qs', xt[i % 2].ap, xr3[i], [x3_d], [xt[i % 2]])
            emit_ln_tile(k, xt[i % 2].ap, xt[i % 2], stage.ap, stage, stats)
            for dt in range(ND):
                k.trans(psT[:, 0:128], stage[:, dt*128:(dt+1)*128], ident_s.ap, [stage, ident_s], [psT])
                k.copy(ln7f[:, dt*SEG + i*128: dt*SEG + (i+1)*128], psT[:, 0:128], [psT], [ln7f])
        for tg in range(NT):
            ps = psA if tg % 2 == 0 else psB
            for dt in range(ND):
                k.mm(ps[:, 0:16], ln7f[:, dt*SEG + tg*128: dt*SEG + (tg+1)*128],
                     gw_s[:, dt*16:(dt+1)*16], [ln7f, gw_s], [ps],
                     start=(dt == 0), stop=(dt == ND-1))
            k.copy(router[:, tg*16:(tg+1)*16], ps[:, 0:16], [ps], [router])
        for tg in range(NT):
            rsl = router[:, tg*16:tg*16+8]
            csl = comb[:, tg*16:tg*16+8]
            k.act(comb[:, tg*16+8:tg*16+9], router[:, tg*16+8:tg*16+9], AF.Sigmoid, [router], [comb])
            k.op('v', lambda e, rsl=rsl: e.reduce_max(rst[:, 0:1], rsl, AX.X), reads=[router], writes=[rst])
            k.ts(rst[:, 1:2], rst[:, 0:1], -1.0, None, AluOpType.mult, None, [rst], [rst])
            k.act(csl, rsl, AF.Exp, [router, rst], [comb], bias=rst[:, 1:2])
            k.op('v', lambda e, csl=csl: e.reduce_sum(rst[:, 2:3], csl, AX.X), reads=[comb], writes=[rst])
            k.op('v', lambda e: e.reciprocal(rst[:, 3:4], rst[:, 2:3]), reads=[rst], writes=[rst])
            k.ts(csl, csl, rst[:, 3:4], None, AluOpType.mult, None, [comb, rst], [comb])
            k.op('v', lambda e, csl=csl: e.reduce_max(rst[:, 4:5], csl, AX.X), reads=[comb], writes=[rst])
            k.ts(router[:, tg*16:tg*16+8], csl, rst[:, 4:5], None, AluOpType.is_lt, None,
                 [comb, rst], [router])
            k.tt(router[:, tg*16:tg*16+8], router[:, tg*16:tg*16+8], csl, AluOpType.mult,
                 [router, comb], [router])
            k.op('v', lambda e, tg=tg: e.reduce_max(rst[:, 5:6], router[:, tg*16:tg*16+8], AX.X),
                 reads=[router], writes=[rst])
            k.ts(router[:, tg*16+8:tg*16+16], csl, rst[:, 5:6], None, AluOpType.is_ge, None,
                 [comb, rst], [router])
            k.tt(csl, csl, router[:, tg*16+8:tg*16+16], AluOpType.mult, [comb, router], [comb])
        k.memset(y_acc.ap, 0.0, [], [y_acc])
        er1 = efc1_in.h.rearrange("e (t p) c -> e p t c", p=128)
        er2 = efc2_in.h.rearrange("e (t p) c -> e p t c", p=128)
        erp = eprj_in.h.rearrange("e (t p) c -> e p t c", p=128)
        for e in range(NE):
            for oc in range(NOC_M):
                b1, b2 = wb1[oc % 2], wb2[oc % 2]
                k.dma('qg', b1.h[:, :].rearrange("p (t c) -> p t c", c=128),
                      er1[e][:, :, oc*128:(oc+1)*128], [efc1_in], [b1])
                k.dma('qg', b2.h[:, :].rearrange("p (t c) -> p t c", c=128),
                      er2[e][:, :, oc*128:(oc+1)*128], [efc2_in], [b2])
                for n in range(2):
                    psg = psA if n == 0 else psB
                    psu = psC if n == 0 else psD
                    for dt in range(ND):
                        k.mm(psg[:, 0:512], b1[:, dt*128:(dt+1)*128],
                             ln7f[:, dt*SEG + n*512: dt*SEG + (n+1)*512], [b1, ln7f], [psg],
                             start=(dt == 0), stop=(dt == ND-1))
                    for dt in range(ND):
                        k.mm(psu[:, 0:512], b2[:, dt*128:(dt+1)*128],
                             ln7f[:, dt*SEG + n*512: dt*SEG + (n+1)*512], [b2, ln7f], [psu],
                             start=(dt == 0), stop=(dt == ND-1))
                    k.act(gs[:, n*512:(n+1)*512], psg[:, 0:512], AF.Silu, [psg], [gs])
                    k.copy(us[:, n*512:(n+1)*512], psu[:, 0:512], [psu], [us])
                k.tt(hbuf[:, oc*SEG:(oc+1)*SEG], gs.ap, us.ap, AluOpType.mult, [gs, us], [hbuf])
            for dc in range(4):
                bp = wpb[dc % 2]
                k.dma('qg', bp.h[:, :].rearrange("p (t c) -> p t c", c=512),
                      erp[e][:, :, dc*512:(dc+1)*512], [eprj_in], [bp])
                for tg in range(NT):
                    ps = psA if tg % 2 == 0 else psB
                    for ht in range(NOC_M):
                        k.mm(ps[:, 0:512], hbuf[:, ht*SEG + tg*128: ht*SEG + (tg+1)*128],
                             bp[:, ht*512:(ht+1)*512], [hbuf, bp], [ps],
                             start=(ht == 0), stop=(ht == NOC_M-1))
                    k.stt(y_acc[:, tg*D + dc*512: tg*D + (dc+1)*512], ps[:, 0:512],
                          comb[:, tg*16+e:tg*16+e+1], y_acc[:, tg*D + dc*512: tg*D + (dc+1)*512],
                          AluOpType.mult, AluOpType.add, [ps, comb, y_acc], [y_acc])
        sr1 = sfc1_in.h.rearrange("(t p) c -> p t c", p=128)
        sr2 = sfc2_in.h.rearrange("(t p) c -> p t c", p=128)
        srp = sprj_in.h.rearrange("(t p) c -> p t c", p=128)
        for qtr in range(4):
            for ocq in range(NOC_M):
                oc = qtr*NOC_M + ocq
                b1, b2 = wb1[ocq % 2], wb2[ocq % 2]
                k.dma('qg', b1.h[:, :].rearrange("p (t c) -> p t c", c=128),
                      sr1[:, :, oc*128:(oc+1)*128], [sfc1_in], [b1])
                k.dma('qg', b2.h[:, :].rearrange("p (t c) -> p t c", c=128),
                      sr2[:, :, oc*128:(oc+1)*128], [sfc2_in], [b2])
                for n in range(2):
                    psg = psA if n == 0 else psB
                    psu = psC if n == 0 else psD
                    for dt in range(ND):
                        k.mm(psg[:, 0:512], b1[:, dt*128:(dt+1)*128],
                             ln7f[:, dt*SEG + n*512: dt*SEG + (n+1)*512], [b1, ln7f], [psg],
                             start=(dt == 0), stop=(dt == ND-1))
                    for dt in range(ND):
                        k.mm(psu[:, 0:512], b2[:, dt*128:(dt+1)*128],
                             ln7f[:, dt*SEG + n*512: dt*SEG + (n+1)*512], [b2, ln7f], [psu],
                             start=(dt == 0), stop=(dt == ND-1))
                    k.act(gs[:, n*512:(n+1)*512], psg[:, 0:512], AF.Silu, [psg], [gs])
                    k.copy(us[:, n*512:(n+1)*512], psu[:, 0:512], [psu], [us])
                k.tt(hbuf[:, ocq*SEG:(ocq+1)*SEG], gs.ap, us.ap, AluOpType.mult, [gs, us], [hbuf])
            for dc in range(4):
                bp = wpb[dc % 2]
                k.dma('qg', bp.h[:, :].rearrange("p (t c) -> p t c", c=512),
                      srp[:, qtr*NOC_M:(qtr+1)*NOC_M, dc*512:(dc+1)*512], [sprj_in], [bp])
                for tg in range(NT):
                    ps = psA if tg % 2 == 0 else psB
                    for ht in range(NOC_M):
                        k.mm(ps[:, 0:512], hbuf[:, ht*SEG + tg*128: ht*SEG + (tg+1)*128],
                             bp[:, ht*512:(ht+1)*512], [hbuf, bp], [ps],
                             start=(ht == 0), stop=(ht == NOC_M-1))
                    k.stt(y_acc[:, tg*D + dc*512: tg*D + (dc+1)*512], ps[:, 0:512],
                          comb[:, tg*16+8:tg*16+9], y_acc[:, tg*D + dc*512: tg*D + (dc+1)*512],
                          AluOpType.mult, AluOpType.add, [ps, comb, y_acc], [y_acc])
        outr = out_out.h.rearrange("(n p) d -> n p d", p=128)
        for tg in range(NT):
            k.dma('qs', xt[tg % 2].ap, xr3[tg], [x3_d], [xt[tg % 2]])
            emit_ln_tile(k, y_acc[:, tg*D:(tg+1)*D], y_acc, o_t.ap, o_t, stats)
            k.tt(o_t.ap, o_t.ap, xt[tg % 2].ap, AluOpType.add, [o_t, xt[tg % 2]], [o_t])
            k.dma('qs', outr[tg], o_t.ap, [o_t], [out_out])

    if debug:
        dbg1 = k.dram_out("dbg_x1", [SEG, D], F32)
        dbg2 = k.dram_out("dbg_x2", [SEG, D], F32)
        dbg3 = k.dram_out("dbg_x3", [SEG, D], F32)
        k.dma('qs', dbg1.h, x1_d.h, [x1_d], [dbg1])
        k.dma('qs', dbg2.h, x2_d.h, [x2_d], [dbg2])
        k.dma('qs', dbg3.h, x3_d.h, [x3_d], [dbg3])
    nc = k.end()
    return nc, k


_BUILT = {}

def _prep_maps(I, k):
    t = host_tables()
    x = I['L0_attn_x'][0]
    wqkvT = np.ascontiguousarray(np.asarray(I['L0_attn_w_qkv']).T)
    b = np.asarray(I['L0_attn_b_qkv'])
    bqkv_cols = np.ascontiguousarray(b[:2*D].reshape(2*ND, 128).T)
    bqkv_cols = np.concatenate([bqkv_cols, np.zeros((128, ND), np.float32)], axis=1)
    bv_b = np.broadcast_to(b[2*D:][None, :], (128, D)).copy()
    woT = np.ascontiguousarray(np.asarray(I['L0_attn_w_out']).T)
    bo_b = np.broadcast_to(np.asarray(I['L0_attn_b_out'])[None, :], (128, D)).copy()
    wguT = np.ascontiguousarray(np.asarray(I['L0_ffn_w_gate_up']).T)
    wpT = np.ascontiguousarray(np.asarray(I['L0_ffn_w_proj']).T)
    wkvqaT = np.ascontiguousarray(np.concatenate(
        [np.asarray(I['L1_attn_Wk']).T, np.asarray(I['L1_attn_Wv']).T,
         np.asarray(I['L1_attn_Wq']).T, np.asarray(I['L1_attn_Wa_w']).T], axis=1))
    wo1T = np.ascontiguousarray(np.asarray(I['L1_attn_Wo']).T)
    nwb = np.tile(np.asarray(I['L1_attn_norm_w'])[None, :], (64, NCHK)).astype(np.float32)
    gate9 = np.zeros((D, 16), np.float32)
    gate9[:, 0:8] = np.asarray(I['L1_ffn_gate_w']).T
    gate9[:, 8:9] = np.asarray(I['L1_ffn_shared_gate_w']).T
    efc1T = np.ascontiguousarray(np.asarray(I['L1_ffn_expert_fc1']).transpose(0, 2, 1))
    efc2T = np.ascontiguousarray(np.asarray(I['L1_ffn_expert_fc2']).transpose(0, 2, 1))
    eprjT = np.ascontiguousarray(np.asarray(I['L1_ffn_expert_proj']).transpose(0, 2, 1))
    sfc1T = np.ascontiguousarray(np.asarray(I['L1_ffn_shared_fc1']).T)
    sfc2T = np.ascontiguousarray(np.asarray(I['L1_ffn_shared_fc2']).T)
    sprjT = np.ascontiguousarray(np.asarray(I['L1_ffn_shared_proj']).T)
    maps = []
    for c in range(NCORE):
        sw = np.zeros((NCORE, H), np.float32)
        for j in range(c):
            sw[j] = LAM_SEG ** (c - 1 - j)
        dnsel = np.zeros((64, 16), np.float32)
        for j in range(NCORE - 1):
            sel = 1.0 if j < c else 0.0
            dnsel[:, j] = sel
            dnsel[:, 8+j] = 1.0 - sel
        maps.append({
            "consts_pool": k.const_array(),
            "x_seg": np.ascontiguousarray(x[c*SEG:(c+1)*SEG]),
            "wqkvT": wqkvT, "bqkv_cols": bqkv_cols, "bv_b": bv_b,
            "woT": woT, "bo_b": bo_b,
            "sw": np.broadcast_to(sw.reshape(1, -1), (128, NCORE*H)).copy(),
            "wguT_sh": np.ascontiguousarray(wguT[:, c*GU_SH:(c+1)*GU_SH]),
            "wpT_sh": np.ascontiguousarray(wpT[:, c*P_SH:(c+1)*P_SH]),
            "wkvqaT": wkvqaT,
            "wa_b": np.asarray(I['L1_attn_Wa_b']).reshape(64, 1),
            "lam": np.asarray(I['L1_attn_lam']).reshape(64, 1),
            "dnsel": dnsel, "nwb": nwb, "wo1T": wo1T,
            "gate9T": gate9,
            "efc1T": efc1T, "efc2T": efc2T, "eprjT": eprjT,
            "sfc1T": sfc1T, "sfc2T": sfc2T, "sprjT": sprjT,
            "ident_bf": t['ident_bf'], "ident_f32": t['ident_f32'],
            "dq_b": t['dq_b'], "dk_col": t['dk_col'], "dmaskT": t['dmaskT'],
            "tli_f32": t['tli_f32'], "strilneg_bf": t['strilneg_bf'],
            "striuneg_bf": t['striuneg_bf'], "striu_bf": t['striu_bf'],
            "striuincl_bf": t['striuincl_bf'],
            "ones64c_f32": t['ones64c_f32'], "ones64r_f32": t['ones64r_f32'],
            "ones128c_bf": t['ones128c_bf'], "ones1r_f32": t['ones1r_f32'],
            "gmask": t['gmask'], "gmask2f": t['gmask2f'], "gmask2b": t['gmask2b'],
        })
    return maps


def kernel(**inputs):
    if "fused" not in _BUILT:
        _BUILT["fused"] = build_fused()
    nc, kk = _BUILT["fused"]
    maps = _prep_maps(inputs, kk)
    res = run_bass_kernel_spmd(nc, maps, core_ids=list(range(NCORE)))
    out = np.concatenate([np.asarray(res.results[c]["out_seg"]) for c in range(NCORE)], 0)
    return out.reshape(1, N_TOK, D).astype(np.float32)


# version 09
# speedup vs baseline: 1.0864x; 1.0243x over previous
import numpy as np
import math
from contextlib import ExitStack
import ml_dtypes
import concourse.bass as bass
import concourse.mybir as mybir
from concourse.bass_utils import run_bass_kernel_spmd
from concourse.alu_op_type import AluOpType

F32 = mybir.dt.float32
BF16 = mybir.dt.bfloat16
AF = mybir.ActivationFunctionType
AX = mybir.AxisListType
BF = ml_dtypes.bfloat16

QUEUE_ENG = {'qg': 'g', 'qs': 'y', 'qa': 's'}
NSLOT = 6


class Buf:
    __slots__ = ('h', 'name', 'last_w', 'readers')
    def __init__(self, h, name):
        self.h = h
        self.name = name
        self.last_w = None
        self.readers = []
    def __getitem__(self, sl):
        return self.h[sl]
    @property
    def ap(self):
        return self.h[:]


class K:
    def __init__(self, name="prog"):
        self.nc = bass.Bass("TRN2", target_bir_lowering=False, debug=False)
        self.stack = ExitStack()
        self.eng = {'t': self.nc.tensor, 'v': self.nc.vector, 's': self.nc.scalar,
                    'g': self.nc.gpsimd, 'y': self.nc.sync}
        self.cnt = {k: 0 for k in ('t', 'v', 's', 'g', 'y')}
        self.water = {k: {} for k in ('t', 'v', 's', 'g', 'y')}
        self.sems = None
        self.block = None
        self.ops = 0
        self.bufs = []
        self.cc_cnt = 0
        self._ext_in = {}
        self._ext_out = {}

    def dram_in(self, name, shape, dtype=F32):
        ap = self.nc.dram_tensor(name, list(shape), dtype, kind="ExternalInput").ap()
        b = Buf(ap, name)
        self.bufs.append(b)
        self._ext_in[name] = b
        return b

    def dram_out(self, name, shape, dtype=F32):
        ap = self.nc.dram_tensor(name, list(shape), dtype, kind="ExternalOutput").ap()
        b = Buf(ap, name)
        self.bufs.append(b)
        self._ext_out[name] = b
        return b

    def dram_tmp(self, name, shape, dtype=F32):
        ap = self.nc.dram_tensor(name, list(shape), dtype, kind="Internal").ap()
        b = Buf(ap, name)
        self.bufs.append(b)
        return b

    def begin(self, n_phases=12, n_consts=128):
        self._consts_in = self.dram_in("consts_pool", [128, n_consts], F32)
        self.semsets = []
        for p in range(n_phases):
            s = {}
            for kk in ('t', 'v', 's', 'g', 'y'):
                s[kk] = self.stack.enter_context(self.nc.semaphore(f"sm{p}_{kk}"))
            self.semsets.append(s)
        self.dma_sems = {}
        self.dma_uses = {}
        self.dma_issue = {}
        for q in ('qg', 'qs', 'qa'):
            for sl in range(NSLOT):
                self.dma_sems[(q, sl)] = self.stack.enter_context(self.nc.semaphore(f"dm_{q}{sl}"))
                self.dma_uses[(q, sl)] = 0
            self.dma_issue[q] = 0
        self.cc_sem = self.stack.enter_context(self.nc.semaphore("sm_cc"))
        self.phase = 0
        self.sems = dict(self.semsets[0])
        self.sems['cc'] = self.cc_sem
        self.block = self.stack.enter_context(self.nc.Block())
        ids = sorted({s.num for st in self.semsets for s in st.values()}
                     | {s.num for s in self.dma_sems.values()} | {self.cc_sem.num})
        if ids == list(range(ids[0], ids[-1] + 1)):
            self.eng['g'].sem_clear(range(ids[0], ids[-1] + 1))
        else:
            for st in self.semsets:
                for s in st.values():
                    self.eng['g'].sem_clear(s)
            for s in self.dma_sems.values():
                self.eng['g'].sem_clear(s)
            self.eng['g'].sem_clear(self.cc_sem)
        self.nc.all_engine_barrier()
        self._consts_s = self.sbuf(self.stack, "consts_s", [128, n_consts], F32)
        self._const_map = {}
        self._n_consts = n_consts
        self.dma('qs', self._consts_s.ap, self._consts_in.ap, [self._consts_in], [self._consts_s])
        return self

    def constf(self, val):
        val = float(val)
        if val not in self._const_map:
            i = len(self._const_map)
            assert i < self._n_consts, "const pool full"
            self._const_map[val] = i
            self.nc.const_aps.register(F32, val, self._consts_s[:, i:i+1])
        i = self._const_map[val]
        return self._consts_s[:, i:i+1]

    def const_array(self):
        arr = np.zeros((128, self._n_consts), np.float32)
        for v, i in self._const_map.items():
            arr[:, i] = v
        return arr

    def sbuf(self, stack, name, shape, dtype=F32):
        h = stack.enter_context(self.nc.sbuf_tensor(name, list(shape), dtype))
        b = Buf(h, name)
        self.bufs.append(b)
        return b

    def psum(self, stack, name, shape, dtype=F32):
        h = stack.enter_context(self.nc.psum_tensor(name, list(shape), dtype))
        b = Buf(h, name)
        self.bufs.append(b)
        return b

    def end(self):
        self.barrier_all()
        self.stack.close()
        self.nc.finalize()
        return self.nc

    def _wait(self, consumer, dep_key, dep_val):
        if dep_val <= 0:
            return
        w = self.water[consumer]
        if w.get(dep_key, 0) >= dep_val:
            return
        if isinstance(dep_key, tuple):
            self.eng[consumer].wait_ge(self.dma_sems[dep_key], dep_val * 16)
        else:
            self.eng[consumer].wait_ge(self.sems[dep_key], dep_val)
        w[dep_key] = dep_val

    def _deps(self, ekey, reads, writes):
        for b in reads:
            if b.last_w is not None:
                pk, pv = b.last_w
                if pk != ekey or ekey != 't':
                    self._wait(ekey, pk, pv)
        for b in writes:
            if b.last_w is not None:
                pk, pv = b.last_w
                if pk != ekey or ekey != 't':
                    self._wait(ekey, pk, pv)
            for (rk, rv) in b.readers:
                if rk != ekey or ekey != 't':
                    self._wait(ekey, rk, rv)

    def _post(self, key, reads, writes):
        self.cnt[key] += 1
        tag = (key, self.cnt[key])
        for b in reads:
            b.readers.append(tag)
        for b in writes:
            b.last_w = tag
            b.readers = []
        self.ops += 1
        return tag

    def op(self, ekey, fn, reads=(), writes=()):
        self._deps(ekey, list(reads), list(writes))
        inst = fn(self.eng[ekey])
        inst.then_inc(self.sems[ekey], 1)
        return self._post(ekey, list(reads), list(writes))

    def dma(self, qkey, out_ap, in_ap, reads=(), writes=(), **kw):
        ekey = QUEUE_ENG[qkey]
        reads, writes = list(reads), list(writes)
        for b in reads:
            if b.last_w is not None:
                pk, pv = b.last_w
                self._wait(ekey, pk, pv)
        for b in writes:
            if b.last_w is not None:
                pk, pv = b.last_w
                self._wait(ekey, pk, pv)
            for (rk, rv) in b.readers:
                if rk != ekey:
                    self._wait(ekey, rk, rv)
        slot = (qkey, self.dma_issue[qkey] % NSLOT)
        self.dma_issue[qkey] += 1
        self._wait(ekey, slot, self.dma_uses[slot])
        self.dma_uses[slot] += 1
        self.eng[ekey].dma_start(out=out_ap, in_=in_ap, **kw).then_inc(self.dma_sems[slot], 16)
        tag = (slot, self.dma_uses[slot])
        for b in reads:
            b.readers.append(tag)
        for b in writes:
            b.last_w = tag
            b.readers = []
        self.ops += 1
        return tag

    def collective(self, in_ap, out_ap, reads, out_buf):
        ekey = 'g'
        for b in list(reads):
            if b.last_w is not None:
                pk, pv = b.last_w
                self._wait(ekey, pk, pv)
        self.eng['g'].collective_compute(
            "AllGather", AluOpType.bypass,
            replica_groups=[list(range(8))],
            ins=[in_ap.opt()], outs=[out_ap.opt()],
        ).then_inc(self.cc_sem)
        self.cc_cnt += 1
        tag = ('cc', self.cc_cnt)
        for b in list(reads):
            b.readers.append(tag)
        out_buf.last_w = tag
        out_buf.readers = []
        self.ops += 1
        return tag

    def barrier_all(self):
        snap = dict(self.cnt)
        dsnap = dict(self.dma_uses)
        for ekey in ('t', 'v', 's', 'g', 'y'):
            for pk, pv in snap.items():
                if pk != ekey:
                    self._wait(ekey, pk, pv)
            for pk, pv in dsnap.items():
                self._wait(ekey, pk, pv)

    def phase_boundary(self):
        self.barrier_all()
        self.phase += 1
        self.sems = dict(self.semsets[self.phase])
        self.sems['cc'] = self.cc_sem
        self.cnt = {k: 0 for k in ('t', 'v', 's', 'g', 'y')}
        for w in self.water.values():
            keep = {kk: vv for kk, vv in w.items() if kk == 'cc' or isinstance(kk, tuple)}
            w.clear()
            w.update(keep)
        for b in self.bufs:
            if not (b.last_w and (b.last_w[0] == 'cc' or isinstance(b.last_w[0], tuple))):
                b.last_w = None
            b.readers = [r for r in b.readers if r[0] == 'cc' or isinstance(r[0], tuple)]

    def mm(self, out_ap, lT_ap, r_ap, rd, wr, start=True, stop=True):
        return self.op('t', lambda e: e.matmul(out_ap, lT_ap, r_ap, start=start, stop=stop),
                       reads=rd, writes=wr)

    def trans(self, out_ap, in_ap, ident_ap, rd, wr):
        return self.op('t', lambda e: e.transpose(out_ap, in_ap, ident_ap), reads=rd, writes=wr)

    def act(self, out_ap, in_ap, func, rd, wr, **kw):
        rd = list(rd)
        try:
            nparts = out_ap.shape[0]
        except Exception:
            nparts = 128
        for key in ('bias', 'scale'):
            v = kw.get(key)
            if isinstance(v, (int, float)):
                self.constf(v)
                i = self._const_map[float(v)]
                kw[key] = self._consts_s[0:nparts, i:i+1]
                if self._consts_s not in rd:
                    rd.append(self._consts_s)
        return self.op('s', lambda e: e.activation(out_ap, in_ap, func, **kw), reads=rd, writes=wr)

    def tt(self, out_ap, a_ap, b_ap, aop, rd, wr, eng='v'):
        return self.op(eng, lambda e: e.tensor_tensor(out_ap, a_ap, b_ap, aop), reads=rd, writes=wr)

    def ts(self, out_ap, a_ap, s1, s2, op0, op1, rd, wr, eng='v', **kw):
        if s2 is None:
            return self.op(eng, lambda e: e.tensor_scalar(out_ap, a_ap, s1, None, op0, **kw), reads=rd, writes=wr)
        return self.op(eng, lambda e: e.tensor_scalar(out_ap, a_ap, s1, s2, op0, op1, **kw), reads=rd, writes=wr)

    def stt(self, out_ap, in0_ap, scalar, in1_ap, op0, op1, rd, wr, eng='v'):
        return self.op(eng, lambda e: e.scalar_tensor_tensor(out_ap, in0_ap, scalar, in1_ap, op0, op1), reads=rd, writes=wr)

    def copy(self, out_ap, in_ap, rd, wr, eng='v'):
        return self.op(eng, lambda e: e.tensor_copy(out_ap, in_ap), reads=rd, writes=wr)

    def memset(self, ap, val, rd, wr, eng='v'):
        return self.op(eng, lambda e: e.memset(ap, val), reads=rd, writes=wr)


N_TOK, D = 8192, 2048
NCORE = 8
SEG = N_TOK // NCORE
NT = SEG // 128
ND = D // 128
H, HD, CH = 16, 128, 128
NCH = SEG // CH
NHID = 64
TC = 64
NCHK = SEG // TC
DS = 64
NE, FM, FS = 8, 1408, 5632
NOC_M = FM // 128
SERIALIZE_COLL = False
GU_SH = 2 * 4 * D // NCORE
P_SH = D // NCORE


def _slopes(n_heads):
    start = 2.0 ** (-2.0 ** (-(math.log2(n_heads) - 3)))
    return np.array([start ** (i + 1) for i in range(n_heads)], dtype=np.float32)

LAM = np.exp(-_slopes(H)).astype(np.float32)
POS = np.arange(CH, dtype=np.float32)
DECAY_Q = (LAM[:, None] ** (POS + 1.0)).astype(np.float32)
DECAY_K = (LAM[:, None] ** (CH - 1.0 - POS)).astype(np.float32)
_diff = POS[:, None] - POS[None, :]
DMASK = np.where(_diff >= 0, LAM[:, None, None] ** np.maximum(_diff, 0.0), 0.0).astype(np.float32)
LAM_C = (LAM ** CH).astype(np.float32)
LAM_SEG = (LAM ** SEG).astype(np.float32)


def host_tables():
    t = {}
    t['ident_bf'] = np.eye(128, dtype=BF)
    t['ident_f32'] = np.eye(128, dtype=np.float32)
    t['dq_b'] = np.broadcast_to(DECAY_Q[:, None, :], (H, 128, 128)).copy()
    t['dk_col'] = DECAY_K.T.copy()
    t['dmaskT'] = DMASK.transpose(0, 2, 1).copy()
    idx = np.arange(TC)
    t['tli_f32'] = (idx[:, None] <= idx[None, :]).astype(np.float32)
    t['strilneg_bf'] = np.where(idx[:, None] > idx[None, :], -1.0, 0.0).astype(BF)
    t['striuneg_bf'] = np.where(idx[:, None] < idx[None, :], -1.0, 0.0).astype(BF)
    t['striu_bf'] = (idx[:, None] < idx[None, :]).astype(BF)
    t['striuincl_bf'] = (idx[:, None] <= idx[None, :]).astype(BF)
    t['ones64c_f32'] = np.ones((TC, 1), np.float32)
    t['ones64r_f32'] = np.ones((1, TC), np.float32)
    t['ones128c_bf'] = np.ones((128, 1), BF)
    t['ones1r_f32'] = np.ones((1, 128), np.float32)
    gmask = np.zeros((128, 2), np.float32); gmask[0:64, 0] = 1; gmask[64:128, 1] = 1
    t['gmask'] = gmask.astype(BF)
    t['gmask2f'] = gmask.T.copy()
    t['gmask2b'] = gmask.T.astype(BF)
    return t


def emit_ln_tile(k, x_ap, x_buf, out_ap, out_buf, stats, eps=1e-5, width=D):
    nblk = (width + 511) // 512
    ntrip = 2 * nblk
    for i in range(nblk):
        w = min(512, width - i*512)
        k.op('v', lambda e, i=i, w=w: e.bn_stats(stats[:, 6*i:6*i+6], x_ap[:, i*512:i*512+w]),
             reads=[x_buf], writes=[stats])
    sv = stats.h[:, 0:6*nblk].rearrange("p (n t) -> p t n", t=3)
    mview, m2view = sv[:, 1, :], sv[:, 2, :]
    k.op('v', lambda e: e.reduce_sum(stats[:, 24:25], mview, AX.X), reads=[stats], writes=[stats])
    k.op('v', lambda e: e.reduce_sum(stats[:, 25:26], m2view, AX.X), reads=[stats], writes=[stats])
    k.op('v', lambda e: e.tensor_tensor(stats[:, 32:32+ntrip], mview, mview, AluOpType.mult),
         reads=[stats], writes=[stats])
    k.op('v', lambda e: e.reduce_sum(stats[:, 26:27], stats[:, 32:32+ntrip], AX.X),
         reads=[stats], writes=[stats])
    k.ts(stats[:, 27:28], stats[:, 24:25], 1.0/ntrip, None, AluOpType.mult, None, [stats], [stats])
    k.ts(stats[:, 28:29], stats[:, 25:26], 1.0/width, None, AluOpType.mult, None, [stats], [stats])
    k.stt(stats[:, 28:29], stats[:, 26:27], 1.0/ntrip, stats[:, 28:29], AluOpType.mult, AluOpType.add,
          [stats], [stats])
    k.tt(stats[:, 30:31], stats[:, 27:28], stats[:, 27:28], AluOpType.mult, [stats], [stats])
    k.tt(stats[:, 28:29], stats[:, 28:29], stats[:, 30:31], AluOpType.subtract, [stats], [stats])
    k.act(stats[:, 26:27], stats[:, 28:29], AF.Ln, [stats], [stats], bias=float(eps))
    k.act(stats[:, 31:32], stats[:, 26:27], AF.Exp, [stats], [stats], scale=-0.5)
    k.stt(stats[:, 30:31], stats[:, 27:28], -1.0, stats[:, 31:32], AluOpType.mult, AluOpType.mult,
          [stats], [stats])
    k.ts(out_ap, x_ap, stats[:, 31:32], stats[:, 30:31], AluOpType.mult, AluOpType.add,
         [x_buf, stats], [out_buf])


def build_fused(debug=False, debug2=False):
    k = K()
    x_in = k.dram_in("x_seg", [SEG, D], F32)
    wqkvT = k.dram_in("wqkvT", [D, 3*D], F32)
    bqkv_cols = k.dram_in("bqkv_cols", [128, 3*ND], F32)
    bv_b = k.dram_in("bv_b", [128, D], F32)
    woT = k.dram_in("woT", [D, D], F32)
    bo_b = k.dram_in("bo_b", [128, D], F32)
    sw_in = k.dram_in("sw", [128, NCORE*H], F32)
    wgu_in = k.dram_in("wguT_sh", [D, GU_SH], F32)
    wp_in = k.dram_in("wpT_sh", [4*D, P_SH], F32)
    wkvqaT_in = k.dram_in("wkvqaT", [D, 4*DS], F32)
    wab_in = k.dram_in("wa_b", [DS, 1], F32)
    lam_in = k.dram_in("lam", [DS, 1], F32)
    dnsel_in = k.dram_in("dnsel", [64, 16], F32)
    nwb_in = k.dram_in("nwb", [64, NCHK*64], F32)
    wo1T_in = k.dram_in("wo1T", [DS, D], F32)
    gw_in = k.dram_in("gate9T", [D, 16], F32)
    efc1_in = k.dram_in("efc1T", [NE, D, FM], F32)
    efc2_in = k.dram_in("efc2T", [NE, D, FM], F32)
    eprj_in = k.dram_in("eprjT", [NE, FM, D], F32)
    sfc1_in = k.dram_in("sfc1T", [D, FS], F32)
    sfc2_in = k.dram_in("sfc2T", [D, FS], F32)
    sprj_in = k.dram_in("sprjT", [FS, D], F32)
    ident = k.dram_in("ident_bf", [128, 128], BF16)
    identf_in = k.dram_in("ident_f32", [128, 128], F32)
    dq_b = k.dram_in("dq_b", [H, 128, 128], F32)
    dk_col = k.dram_in("dk_col", [128, H], F32)
    dmaskT = k.dram_in("dmaskT", [H, 128, 128], F32)
    tli_in = k.dram_in("tli_f32", [TC, TC], F32)
    sln_in = k.dram_in("strilneg_bf", [TC, TC], BF16)
    sun_in = k.dram_in("striuneg_bf", [TC, TC], BF16)
    su_in = k.dram_in("striu_bf", [TC, TC], BF16)
    sui_in = k.dram_in("striuincl_bf", [TC, TC], BF16)
    oc_in = k.dram_in("ones64c_f32", [64, 1], F32)
    or_in = k.dram_in("ones64r_f32", [1, 64], F32)
    o128c_in = k.dram_in("ones128c_bf", [128, 1], BF16)
    o1r_in = k.dram_in("ones1r_f32", [1, 128], F32)
    gm_in = k.dram_in("gmask", [128, 2], BF16)
    gm2f_in = k.dram_in("gmask2f", [2, 128], F32)
    gm2b_in = k.dram_in("gmask2b", [2, 128], BF16)
    out_out = k.dram_out("out_seg", [SEG, D], F32)
    x1_d = k.dram_tmp("x1_d", [SEG, D], F32)
    x2_d = k.dram_tmp("x2_d", [SEG, D], F32)
    x3_d = k.dram_tmp("x3_d", [SEG, D], F32)
    qgu_sh = k.dram_tmp("qgu_sh", [D, GU_SH], BF16)
    qgu_g = k.dram_tmp("qgu_g", [NCORE*D, GU_SH], BF16)
    qp_sh = k.dram_tmp("qp_sh", [4*D, P_SH], BF16)
    qp_g = k.dram_tmp("qp_g", [NCORE*4*D, P_SH], BF16)
    hd_d = k.dram_tmp("hd_d", [128, NHID*SEG], BF16)
    sseg_l = k.dram_tmp("sseg_l", [H*128, 128], F32)
    sseg_g = k.dram_tmp("sseg_g", [NCORE*H*128, 128], F32)
    dn_l = k.dram_tmp("dn_l", [128, 64], F32)
    dn_g = k.dram_tmp("dn_g", [NCORE*128, 64], F32)

    k.begin(n_phases=12)

    ident_s = k.sbuf(k.stack, "ident_s", [128, 128], BF16)
    identf = k.sbuf(k.stack, "identf", [128, 128], F32)
    stats = k.sbuf(k.stack, "stats", [128, 48], F32)
    k.dma('qs', ident_s.ap, ident.ap, [ident], [ident_s])
    k.dma('qs', identf.ap, identf_in.ap, [identf_in], [identf])

    with ExitStack() as sc0:
        qf = k.sbuf(sc0, "a_qf", [128, ND*SEG], BF16)
        Sst = k.sbuf(sc0, "a_Sst", [128, H*128], F32)
        attn_bf = k.sbuf(sc0, "a_attn", [128, ND*SEG], BF16)

        with ExitStack() as scKV:
            kf = k.sbuf(scKV, "a_kf", [128, ND*SEG], BF16)
            vt = k.sbuf(scKV, "a_vt", [128, NT*D], BF16)

            with ExitStack() as ph:
                xt = [k.sbuf(ph, f"a_xt{i}", [128, D], F32) for i in range(2)]
                stage = k.sbuf(ph, "a_stage", [128, D], BF16)
                ln1f = k.sbuf(ph, "a_ln1f", [128, ND*SEG], BF16)
                wb = [k.sbuf(ph, f"a_wb{i}", [128, ND*128], BF16) for i in range(2)]
                bqk_s = k.sbuf(ph, "a_bqk", [128, 3*ND], F32)
                bvb_s = k.sbuf(ph, "a_bvb", [128, D], BF16)
                psA = k.psum(ph, "a_psA", [128, 512], F32)
                psB = k.psum(ph, "a_psB", [128, 512], F32)
                psT = k.psum(ph, "a_psT", [128, 1024], BF16)

                k.dma('qs', bqk_s.ap, bqkv_cols.ap, [bqkv_cols], [bqk_s])
                k.dma('qg', bvb_s.ap, bv_b.ap, [bv_b], [bvb_s])
                if debug2:
                    stA = k.sbuf(ph, "a_stA", [128, 48*NT], F32)
                xr = x_in.h.rearrange("(n p) d -> n p d", p=128)
                for i in range(NT):
                    k.dma('qs', xt[i % 2].ap, xr[i], [x_in], [xt[i % 2]])
                    if debug2:
                        stv = Buf(stA.h[:, 48*i:48*(i+1)], f"stA{i}")
                        k.bufs.append(stv)
                        emit_ln_tile(k, xt[i % 2].ap, xt[i % 2], stage.ap, stage, stv)
                    else:
                        emit_ln_tile(k, xt[i % 2].ap, xt[i % 2], stage.ap, stage, stats)
                    for dt in range(ND):
                        k.trans(psT[:, 0:128], stage[:, dt*128:(dt+1)*128], ident_s.ap, [stage, ident_s], [psT])
                        k.copy(ln1f[:, dt*SEG + i*128: dt*SEG + (i+1)*128], psT[:, 0:128], [psT], [ln1f])
                wr = wqkvT.h.rearrange("(t p) c -> p t c", p=128)
                for oc in range(2*ND):
                    wbuf = wb[oc % 2]
                    k.dma('qg', wbuf.h[:, 0:ND*128].rearrange("p (t c) -> p t c", c=128),
                          wr[:, :, oc*128:(oc+1)*128], [wqkvT], [wbuf])
                    dst = qf if oc < ND else kf
                    blk = oc % ND
                    for n in range(2):
                        ps = psA if n == 0 else psB
                        for dt in range(ND):
                            k.mm(ps[:, 0:512], wbuf[:, dt*128:(dt+1)*128],
                                 ln1f[:, dt*SEG + n*512: dt*SEG + (n+1)*512],
                                 [wbuf, ln1f], [ps], start=(dt == 0), stop=(dt == ND-1))
                        k.act(dst[:, blk*SEG + n*512: blk*SEG + (n+1)*512], ps[:, 0:512], AF.Identity,
                              [ps, bqk_s], [dst], bias=bqk_s[:, oc:oc+1])
                for vc in range(ND):
                    wbuf = wb[vc % 2]
                    k.dma('qg', wbuf.h[:, 0:ND*128].rearrange("p (t c) -> p t c", c=128),
                          wr[:, :, 2*D + vc*128: 2*D + (vc+1)*128], [wqkvT], [wbuf])
                    for ti in range(NT):
                        ps = psA if ti % 2 == 0 else psB
                        for dt in range(ND):
                            k.mm(ps[:, 0:128], ln1f[:, dt*SEG + ti*128: dt*SEG + (ti+1)*128],
                                 wbuf[:, dt*128:(dt+1)*128],
                                 [wbuf, ln1f], [ps], start=(dt == 0), stop=(dt == ND-1))
                        k.tt(vt[:, ti*D + vc*128: ti*D + (vc+1)*128], ps[:, 0:128],
                             bvb_s[:, vc*128:(vc+1)*128], AluOpType.add, [ps, bvb_s], [vt])
                if debug2:
                    dqa = k.dram_out("dbg_qfA1", [128, ND*SEG], BF16)
                    dlf = k.dram_out("dbg_ln1f", [128, ND*SEG], BF16)
                    dst_ = k.dram_out("dbg_stats", [128, 48*NT], F32)
                    k.dma('qs', dqa.h, qf.ap, [qf], [dqa])
                    k.dma('qs', dlf.h, ln1f.ap, [ln1f], [dlf])
                    k.dma('qs', dst_.h, stA.ap, [stA], [dst_])
                k.phase_boundary()

            with ExitStack() as ph:
                kdt = k.sbuf(ph, "a_kdt", [128, H*NCH*128], BF16)
                Sbf = k.sbuf(ph, "a_Sbf", [128, H*128], BF16)
                dqb_s = k.sbuf(ph, "a_dqb", [128, H*128], BF16)
                dkc_s = k.sbuf(ph, "a_dkc", [128, H], F32)
                dmk_s = k.sbuf(ph, "a_dmk", [128, H*128], BF16)
                qd_c = [k.sbuf(ph, f"a_qdc{i}", [128, 128], BF16) for i in range(2)]
                sc_c = [k.sbuf(ph, f"a_scc{i}", [128, 128], BF16) for i in range(2)]
                psSC = [k.psum(ph, "a_psSC0", [128, 512], F32)] * 2
                psO = [k.psum(ph, "a_psO0", [128, 512], F32)] * 2
                psSU = [k.psum(ph, "a_psSU0", [128, 512], F32)] * 2
                psT2 = [k.psum(ph, f"a_psT2{i}", [128, 1024], BF16) for i in range(2)]

                k.dma('qg', dqb_s.h[:, :].rearrange("p (h c) -> p h c", c=128),
                      dq_b.h.rearrange("h p c -> p h c"), [dq_b], [dqb_s])
                k.dma('qs', dkc_s.ap, dk_col.ap, [dk_col], [dkc_s])
                k.dma('qg', dmk_s.h[:, :].rearrange("p (h c) -> p h c", c=128),
                      dmaskT.h.rearrange("h p c -> p h c"), [dmaskT], [dmk_s])
                for h in range(H):
                    for c in range(NCH):
                        pt = psT2[(h*NCH+c) % 2]
                        k.trans(pt[:, 0:128], kf[:, h*SEG + c*128: h*SEG + (c+1)*128], ident_s.ap,
                                [kf, ident_s], [pt])
                        k.act(kdt[:, (h*NCH+c)*128:(h*NCH+c+1)*128], pt[:, 0:128], AF.Copy,
                              [pt, dkc_s], [kdt], scale=dkc_s[:, h:h+1])
                for c in range(NCH):
                    for h in range(H):
                        par = h % 2
                        k.mm(psSC[par][:, 0:128], kf[:, h*SEG + c*128: h*SEG + (c+1)*128],
                             qf[:, h*SEG + c*128: h*SEG + (c+1)*128], [kf, qf], [psSC[par]])
                        k.tt(sc_c[par].ap, psSC[par][:, 0:128], dmk_s[:, h*128:(h+1)*128], AluOpType.mult,
                             [psSC[par], dmk_s], [sc_c[par]])
                        k.tt(qd_c[par].ap, qf[:, h*SEG + c*128: h*SEG + (c+1)*128], dqb_s[:, h*128:(h+1)*128],
                             AluOpType.mult, [qf, dqb_s], [qd_c[par]])
                        k.mm(psO[par][:, 0:128], vt[:, c*D + h*128: c*D + (h+1)*128], sc_c[par].ap,
                             [vt, sc_c[par]], [psO[par]], start=True, stop=(c == 0))
                        if c > 0:
                            k.mm(psO[par][:, 0:128], Sbf[:, h*128:(h+1)*128], qd_c[par].ap,
                                 [Sbf, qd_c[par]], [psO[par]], start=False, stop=True)
                        k.copy(attn_bf[:, h*SEG + c*128:h*SEG + (c+1)*128], psO[par][:, 0:128],
                               [psO[par]], [attn_bf])
                        k.mm(psSU[par][:, 0:128], kdt[:, (h*NCH+c)*128:(h*NCH+c+1)*128],
                             vt[:, c*D + h*128: c*D + (h+1)*128], [kdt, vt], [psSU[par]])
                        if c == 0:
                            k.copy(Sst[:, h*128:(h+1)*128], psSU[par][:, 0:128], [psSU[par]], [Sst])
                        else:
                            k.stt(Sst[:, h*128:(h+1)*128], Sst[:, h*128:(h+1)*128], float(LAM_C[h]),
                                  psSU[par][:, 0:128], AluOpType.mult, AluOpType.add, [psSU[par], Sst], [Sst])
                        if c < NCH - 1:
                            k.copy(Sbf[:, h*128:(h+1)*128], Sst[:, h*128:(h+1)*128], [Sst], [Sbf])
                k.dma('qs', sseg_l.h.rearrange("(h p) c -> p h c", p=128),
                      Sst.h[:, :].rearrange("p (h c) -> p h c", c=128), [Sst], [sseg_l])
                k.collective(sseg_l.h, sseg_g.h, [sseg_l], sseg_g)
                if debug2:
                    dq = k.dram_out("dbg_qf", [128, ND*SEG], BF16)
                    dk = k.dram_out("dbg_kf", [128, ND*SEG], BF16)
                    dv = k.dram_out("dbg_vt", [128, NT*D], BF16)
                    da = k.dram_out("dbg_attnA", [128, ND*SEG], BF16)
                    dS = k.dram_out("dbg_Sst", [128, H*128], F32)
                    k.dma('qs', dq.h, qf.ap, [qf], [dq])
                    k.dma('qs', dk.h, kf.ap, [kf], [dk])
                    k.dma('qs', dv.h, vt.ap, [vt], [dv])
                    k.dma('qs', da.h, attn_bf.ap, [attn_bf], [da])
                    k.dma('qs', dS.h, Sst.ap, [Sst], [dS])
                k.phase_boundary()

        with ExitStack() as ph:
            sseg_h = [k.sbuf(ph, "b_sseg0", [128, NCORE*128], F32)] * 2
            S_in = k.sbuf(ph, "b_Sin", [128, H*128], F32)
            S_bf = [k.sbuf(ph, f"b_Sbf{i}", [128, 128], BF16) for i in range(2)]
            qd_c = [k.sbuf(ph, f"b_qdc{i}", [128, 128], BF16) for i in range(2)]
            sw_s = k.sbuf(ph, "b_sw", [128, NCORE*H], F32)
            dqb_s = k.sbuf(ph, "b_dqb", [128, H*128], BF16)
            wob = [k.sbuf(ph, f"b_wob{i}", [128, ND*512], BF16) for i in range(4)]
            bob_s = k.sbuf(ph, "b_bob", [128, D], BF16)
            xt = [k.sbuf(ph, f"b_xt{i}", [128, D], F32) for i in range(2)]
            o_t = k.sbuf(ph, "b_ot", [128, D], F32)
            x1_t = k.sbuf(ph, "b_x1t", [128, D], F32)
            psC = [k.psum(ph, f"b_psC{i}", [128, 512], F32) for i in range(2)]
            psO1 = k.psum(ph, "b_psO1", [128, 512], F32)
            psO2 = k.psum(ph, "b_psO2", [128, 512], F32)
            psBQG = k.psum(ph, "b_psQG", [128, 512], F32)
            psBQR = k.psum(ph, "b_psQR", [128, 512], F32)
            psBQS = k.psum(ph, "b_psQS", [128, 512], F32)

            k.dma('qs', sw_s.ap, sw_in.ap, [sw_in], [sw_s])
            k.dma('qg', dqb_s.h[:, :].rearrange("p (h c) -> p h c", c=128),
                  dq_b.h.rearrange("h p c -> p h c"), [dq_b], [dqb_s])
            k.dma('qg', bob_s.ap, bo_b.ap, [bo_b], [bob_s])
            wr = woT.h.rearrange("(t p) c -> p t c", p=128)
            for nc_ in range(4):
                k.dma('qg', wob[nc_].h[:, :].rearrange("p (t c) -> p t c", c=512),
                      wr[:, :, nc_*512:(nc_+1)*512], [woT], [wob[nc_]])

            ssegr = sseg_g.h.rearrange("(r h p) c -> h p r c", r=NCORE, p=128)
            for h in range(H):
                sh = sseg_h[h % 2]
                k.dma('qs', sh.h[:, :].rearrange("p (j c) -> p j c", c=128),
                      ssegr[h], [sseg_g], [sh])
                k.memset(S_in[:, h*128:(h+1)*128], 0.0, [], [S_in])
                for j in range(NCORE - 1):
                    k.stt(S_in[:, h*128:(h+1)*128], sh[:, j*128:(j+1)*128], sw_s[:, j*H+h:j*H+h+1],
                          S_in[:, h*128:(h+1)*128], AluOpType.mult, AluOpType.add,
                          [sh, sw_s, S_in], [S_in])
            for c in range(NCH):
                for h in range(H):
                    par = h % 2
                    k.ts(S_bf[par].ap, S_in[:, h*128:(h+1)*128], float(LAM_C[h] ** c), None,
                         AluOpType.mult, None, [S_in], [S_bf[par]])
                    k.tt(qd_c[par].ap, qf[:, h*SEG + c*128:h*SEG + (c+1)*128], dqb_s[:, h*128:(h+1)*128],
                         AluOpType.mult, [qf, dqb_s], [qd_c[par]])
                    k.mm(psC[par][:, 0:128], S_bf[par].ap, qd_c[par].ap, [S_bf[par], qd_c[par]], [psC[par]])
                    k.tt(attn_bf[:, h*SEG + c*128:h*SEG + (c+1)*128],
                         attn_bf[:, h*SEG + c*128:h*SEG + (c+1)*128],
                         psC[par][:, 0:128], AluOpType.add, [psC[par]], [attn_bf])

            wt = [k.sbuf(ph, f"q_wt{i}", [128, 512], BF16) for i in range(2)]
            sg = k.sbuf(ph, "q_sg", [128, 512], BF16)
            wabs = k.sbuf(ph, "q_wabs", [128, 512], BF16)
            s_bf = k.sbuf(ph, "q_sbf", [2, 512], BF16)
            rs = k.sbuf(ph, "q_rs", [2, 512], F32)
            z = k.sbuf(ph, "q_z", [128, 512], F32)
            m = k.sbuf(ph, "q_m", [128, 512], F32)
            w1 = k.sbuf(ph, "q_w1", [128, 512], BF16)
            wq = [k.sbuf(ph, f"q_wq{i}", [128, 512], BF16) for i in range(2)]
            gm = k.sbuf(ph, "q_gm", [128, 2], BF16)
            gm2f = k.sbuf(ph, "q_gm2f", [2, 128], F32)
            gm2b = k.sbuf(ph, "q_gm2b", [2, 128], BF16)
            k.dma('qs', gm.ap, gm_in.ap, [gm_in], [gm])
            k.dma('qs', gm2f.ap, gm2f_in.ap, [gm2f_in], [gm2f])
            k.dma('qs', gm2b.ap, gm2b_in.ap, [gm2b_in], [gm2b])

            def quant_tile(src_ap, src_buf, dst_ap, dst_buf, F, idx):
                wtb, wqb = wt[idx % 2], wq[idx % 2]
                k.dma('qg', wtb[:, 0:F], src_ap, [src_buf], [wtb])
                k.act(sg[:, 0:F], wtb[:, 0:F], AF.Sign, [wtb], [sg])
                k.act(wabs[:, 0:F], wtb[:, 0:F], AF.Abs, [wtb], [wabs])
                k.mm(psBQG[0:2, 0:F], gm.ap, wabs[:, 0:F], [gm, wabs], [psBQG])
                k.ts(s_bf[:, 0:F], psBQG[0:2, 0:F], 1.0/64.0, 1e-8, AluOpType.mult, AluOpType.max,
                     [psBQG], [s_bf])
                k.op('v', lambda e, F=F: e.reciprocal(rs[:, 0:F], s_bf[:, 0:F]), reads=[s_bf], writes=[rs])
                k.mm(psBQR[:, 0:F], gm2f.ap, rs[:, 0:F], [gm2f, rs], [psBQR])
                k.mm(psBQS[:, 0:F], gm2b.ap, s_bf[:, 0:F], [gm2b, s_bf], [psBQS])
                k.tt(z[:, 0:F], wabs[:, 0:F], psBQR[:, 0:F], AluOpType.mult, [wabs, psBQR], [z])
                k.ts(m[:, 0:F], z[:, 0:F], 0.5, None, AluOpType.is_gt, None, [z], [m])
                k.tt(w1[:, 0:F], m[:, 0:F], psBQS[:, 0:F], AluOpType.mult, [m, psBQS], [w1])
                k.tt(wqb[:, 0:F], w1[:, 0:F], sg[:, 0:F], AluOpType.mult, [w1, sg], [wqb])
                k.dma('qs', dst_ap, wqb[:, 0:F], [wqb], [dst_buf])

            wgur = wgu_in.h.rearrange("(t p) c -> t p c", p=128)
            qgur_o = qgu_sh.h.rearrange("(t p) c -> t p c", p=128)
            wpr = wp_in.h.rearrange("(t p) c -> t p c", p=128)
            qpr_o = qp_sh.h.rearrange("(t p) c -> t p c", p=128)
            qthunks = []
            idx = 0
            for t in range(D // 128):
                for f in range(GU_SH // 512):
                    qthunks.append((wgur[t, :, f*512:(f+1)*512], wgu_in,
                                    qgur_o[t, :, f*512:(f+1)*512], qgu_sh, 512, idx)); idx += 1
            n_gu = idx
            for t in range(4*D // 128):
                qthunks.append((wpr[t, :, 0:P_SH], wp_in, qpr_o[t, :, 0:P_SH], qp_sh, P_SH, idx)); idx += 1

            xr = x_in.h.rearrange("(n p) d -> n p d", p=128)
            x1r = x1_d.h.rearrange("(n p) d -> n p d", p=128)
            NQG = len(qthunks) // NT
            for tg in range(NT):
                k.dma('qs', xt[tg % 2].ap, xr[tg], [x_in], [xt[tg % 2]])
                for nc_ in range(4):
                    ps = psO1 if nc_ % 2 == 0 else psO2
                    wbuf = wob[nc_]
                    for et in range(ND):
                        k.mm(ps[:, 0:512], attn_bf[:, et*SEG + tg*128: et*SEG + (tg+1)*128],
                             wbuf[:, et*512:(et+1)*512], [attn_bf, wbuf], [ps],
                             start=(et == 0), stop=(et == ND-1))
                    k.tt(o_t[:, nc_*512:(nc_+1)*512], ps[:, 0:512], bob_s[:, nc_*512:(nc_+1)*512],
                         AluOpType.add, [ps, bob_s], [o_t])
                emit_ln_tile(k, o_t.ap, o_t, x1_t.ap, x1_t, stats)
                k.tt(x1_t.ap, x1_t.ap, xt[tg % 2].ap, AluOpType.add, [x1_t, xt[tg % 2]], [x1_t])
                k.dma('qs', x1r[tg], x1_t.ap, [x1_t], [x1_d])
                for qi in range(tg*NQG, (tg+1)*NQG):
                    quant_tile(*qthunks[qi])
                    if qi == n_gu - 1:
                        k.collective(qgu_sh.h, qgu_g.h, [qgu_sh], qgu_g)
            k.collective(qp_sh.h, qp_g.h, [qp_sh], qp_g)
            k.phase_boundary()

    with ExitStack() as dsc:
        o128c = k.sbuf(dsc, "d_o128c", [128, 1], BF16)
        o1r = k.sbuf(dsc, "d_o1r", [1, 128], F32)
        sqrow = k.sbuf(dsc, "d_sqrow", [1, SEG], F32)
        rstd = k.sbuf(dsc, "d_rstd", [1, SEG], F32)
        rstd_b = k.sbuf(dsc, "d_rstdb", [128, SEG], F32)
        k.dma('qs', o128c.ap, o128c_in.ap, [o128c_in], [o128c])
        k.dma('qs', o1r.ap, o1r_in.ap, [o1r_in], [o1r])
        xr1 = x1_d.h.rearrange("(n p) d -> n p d", p=128)

        with ExitStack() as ph:
            xt = [k.sbuf(ph, f"d_xt{i}", [128, D], F32) for i in range(2)]
            stage = k.sbuf(ph, "d_stage", [128, D], BF16)
            ln3f = k.sbuf(ph, "d_ln3f", [128, ND*SEG], BF16)
            hw = k.sbuf(ph, "d_hw", [128, SEG], BF16)
            hsq = k.sbuf(ph, "d_hsq", [128, SEG], BF16)
            wg = [k.sbuf(ph, f"d_wg{i}", [128, ND*128], BF16) for i in range(2)]
            wu = [k.sbuf(ph, f"d_wu{i}", [128, ND*128], BF16) for i in range(2)]
            gs = k.sbuf(ph, "d_gs", [128, SEG], BF16)
            us = k.sbuf(ph, "d_us", [128, SEG], BF16)
            psA = k.psum(ph, "d_psA", [128, 512], F32)
            psB = k.psum(ph, "d_psB", [128, 512], F32)
            psC = k.psum(ph, "d_psC", [128, 512], F32)
            psD = k.psum(ph, "d_psD", [128, 512], F32)
            psT = k.psum(ph, "d_psT", [128, 1024], BF16)
            psQ = k.psum(ph, "d_psQ", [128, 512], F32)

            k.memset(sqrow.ap, 0.0, [], [sqrow])
            for i in range(NT):
                k.dma('qs', xt[i % 2].ap, xr1[i], [x1_d], [xt[i % 2]])
                emit_ln_tile(k, xt[i % 2].ap, xt[i % 2], stage.ap, stage, stats)
                for dt in range(ND):
                    k.trans(psT[:, 0:128], stage[:, dt*128:(dt+1)*128], ident_s.ap, [stage, ident_s], [psT])
                    k.copy(ln3f[:, dt*SEG + i*128: dt*SEG + (i+1)*128], psT[:, 0:128], [psT], [ln3f])

            qgur = qgu_g.h.rearrange("(r t p) c -> r p t c", r=NCORE, p=128)
            NGC = GU_SH
            for hc in range(NHID):
                bg, bu = wg[hc % 2], wu[hc % 2]
                gcol = hc * 128
                r, lc = gcol // NGC, gcol % NGC
                k.dma('qg', bg.h[:, :].rearrange("p (t c) -> p t c", c=128),
                      qgur[r][:, :, lc:lc+128], [qgu_g], [bg])
                ucol = 4*D + hc * 128
                r2, lc2 = ucol // NGC, ucol % NGC
                k.dma('qg', bu.h[:, :].rearrange("p (t c) -> p t c", c=128),
                      qgur[r2][:, :, lc2:lc2+128], [qgu_g], [bu])
                for n in range(2):
                    psg = psA if n == 0 else psB
                    psu = psC if n == 0 else psD
                    for dt in range(ND):
                        k.mm(psg[:, 0:512], bg[:, dt*128:(dt+1)*128],
                             ln3f[:, dt*SEG + n*512: dt*SEG + (n+1)*512], [bg, ln3f], [psg],
                             start=(dt == 0), stop=(dt == ND-1))
                    for dt in range(ND):
                        k.mm(psu[:, 0:512], bu[:, dt*128:(dt+1)*128],
                             ln3f[:, dt*SEG + n*512: dt*SEG + (n+1)*512], [bu, ln3f], [psu],
                             start=(dt == 0), stop=(dt == ND-1))
                    k.act(gs[:, n*512:(n+1)*512], psg[:, 0:512], AF.Silu, [psg], [gs])
                    k.copy(us[:, n*512:(n+1)*512], psu[:, 0:512], [psu], [us])
                k.tt(hw.ap, gs.ap, us.ap, AluOpType.mult, [gs, us], [hw])
                k.dma('qs', hd_d.h[:, hc*SEG:(hc+1)*SEG], hw.ap, [hw], [hd_d])
                k.tt(hsq.ap, hw.ap, hw.ap, AluOpType.mult, [hw], [hsq])
                for n in range(2):
                    k.mm(psQ[0:1, 0:512], o128c.ap, hsq[:, n*512:(n+1)*512], [o128c, hsq], [psQ])
                    k.stt(sqrow[0:1, n*512:(n+1)*512], psQ[0:1, 0:512], 1.0,
                          sqrow[0:1, n*512:(n+1)*512], AluOpType.mult, AluOpType.add,
                          [psQ, sqrow], [sqrow])
            k.act(rstd[0:1, :], sqrow[0:1, :], AF.Ln, [sqrow], [rstd], scale=1.0/(4*D),
                  bias=1.1920929e-7)
            k.act(rstd[0:1, :], rstd[0:1, :], AF.Exp, [rstd], [rstd], scale=-0.5)
            for n in range(2):
                k.mm(psQ[:, 0:512], o1r.ap, rstd[0:1, n*512:(n+1)*512], [o1r, rstd], [psQ])
                k.copy(rstd_b[:, n*512:(n+1)*512], psQ[:, 0:512], [psQ], [rstd_b])
            k.phase_boundary()

        with ExitStack() as ph:
            xt = [k.sbuf(ph, f"d2_xt{i}", [128, D], F32) for i in range(2)]
            stage = k.sbuf(ph, "d2_stage", [128, D], BF16)
            hbf = k.sbuf(ph, "d2_hbf", [128, NHID*512], BF16)
            wp = [k.sbuf(ph, f"d2_wp{i}", [128, NHID*128], BF16) for i in range(2)]
            obf = k.sbuf(ph, "d2_obf", [128, NT*D], BF16)
            x2_t = k.sbuf(ph, "d2_x2t", [128, D], F32)
            psA = k.psum(ph, "d2_psA", [128, 512], F32)
            psB = k.psum(ph, "d2_psB", [128, 512], F32)
            psT = k.psum(ph, "d2_psT", [128, 1024], BF16)

            qpr = qp_g.h.rearrange("(r t p) c -> r p t c", r=NCORE, p=128)
            hr = hd_d.h[:, :].rearrange("p (n t) -> p n t", t=SEG)
            for th in range(2):
                k.dma('qs', hbf.h[:, :].rearrange("p (n t) -> p n t", t=512),
                      hr[:, :, th*512:(th+1)*512], [hd_d], [hbf])
                for dot in range(ND):
                    bp = wp[dot % 2]
                    pcol = dot * 128
                    r, lc = pcol // P_SH, pcol % P_SH
                    k.dma('qg', bp.h[:, :].rearrange("p (t c) -> p t c", c=128),
                          qpr[r][:, :, lc:lc+128], [qp_g], [bp])
                    ps = psA if dot % 2 == 0 else psB
                    for ht in range(NHID):
                        k.mm(ps[:, 0:512], bp[:, ht*128:(ht+1)*128],
                             hbf[:, ht*512:(ht+1)*512], [bp, hbf], [ps],
                             start=(ht == 0), stop=(ht == NHID-1))
                    k.tt(stage[:, 0:512], ps[:, 0:512], rstd_b[:, th*512:(th+1)*512], AluOpType.mult,
                         [ps, rstd_b], [stage])
                    for q4 in range(4):
                        tg = th*4 + q4
                        k.trans(psT[:, 0:128], stage[:, q4*128:(q4+1)*128], ident_s.ap,
                                [stage, ident_s], [psT])
                        k.copy(obf[:, tg*D + dot*128: tg*D + (dot+1)*128], psT[:, 0:128], [psT], [obf])

            x2r = x2_d.h.rearrange("(n p) d -> n p d", p=128)
            for tg in range(NT):
                k.dma('qs', xt[tg % 2].ap, xr1[tg], [x1_d], [xt[tg % 2]])
                emit_ln_tile(k, obf[:, tg*D:(tg+1)*D], obf, x2_t.ap, x2_t, stats)
                k.tt(x2_t.ap, x2_t.ap, xt[tg % 2].ap, AluOpType.add, [x2_t, xt[tg % 2]], [x2_t])
                k.dma('qs', x2r[tg], x2_t.ap, [x2_t], [x2_d])
            k.phase_boundary()

    with ExitStack() as sc1:
        kd = k.sbuf(sc1, "e_kd", [DS, SEG], BF16)
        vd = k.sbuf(sc1, "e_vd", [DS, SEG], BF16)
        qd = k.sbuf(sc1, "e_qd", [DS, SEG], BF16)
        lg = k.sbuf(sc1, "e_lg", [DS, SEG], F32)
        tli = k.sbuf(sc1, "n_tli", [TC, TC], F32)
        sln = k.sbuf(sc1, "n_sln", [TC, TC], BF16)
        sun = k.sbuf(sc1, "n_sun", [TC, TC], BF16)
        su = k.sbuf(sc1, "n_su", [TC, TC], BF16)
        sui = k.sbuf(sc1, "n_sui", [TC, TC], BF16)
        onesc = k.sbuf(sc1, "n_oc", [64, 1], F32)
        onesr = k.sbuf(sc1, "n_or", [1, 64], F32)
        for (d, s_) in ((tli_in, tli), (sln_in, sln), (sun_in, sun), (su_in, su),
                        (sui_in, sui), (oc_in, onesc), (or_in, onesr)):
            k.dma('qs', s_.ap, d.ap, [d], [s_])

        with ExitStack() as ph:
            xt = [k.sbuf(ph, f"e_xt{i}", [128, D], F32) for i in range(2)]
            stage = k.sbuf(ph, "e_stage", [128, D], BF16)
            lnf = k.sbuf(ph, "e_lnf", [128, ND*SEG], BF16)
            wsb = k.sbuf(ph, "e_wsb", [128, ND*4*DS], BF16)
            proj = k.sbuf(ph, "e_proj", [64, 4*SEG], F32)
            sqb = k.sbuf(ph, "e_sqb", [64, SEG], F32)
            nrow = k.sbuf(ph, "e_nrow", [1, SEG], F32)
            rrow = k.sbuf(ph, "e_rrow", [1, SEG], F32)
            wab_s = k.sbuf(ph, "e_wab", [64, 1], F32)
            lam_s = k.sbuf(ph, "e_lam", [64, 4], F32)
            ps1 = k.psum(ph, "n_ps1", [128, 512], F32)
            ps2 = k.psum(ph, "n_ps2", [128, 512], F32)
            ps3 = k.psum(ph, "n_ps3", [128, 512], F32)
            psF = k.psum(ph, "n_psF", [128, 512], F32)
            psTb = k.psum(ph, "n_psTb", [128, 1024], BF16)

            k.dma('qs', wab_s.ap, wab_in.ap, [wab_in], [wab_s])
            k.dma('qs', lam_s[:, 0:1], lam_in.ap, [lam_in], [lam_s])
            xr2 = x2_d.h.rearrange("(n p) d -> n p d", p=128)
            for i in range(NT):
                k.dma('qs', xt[i % 2].ap, xr2[i], [x2_d], [xt[i % 2]])
                emit_ln_tile(k, xt[i % 2].ap, xt[i % 2], stage.ap, stage, stats)
                for dt in range(ND):
                    k.trans(psTb[:, 0:128], stage[:, dt*128:(dt+1)*128], ident_s.ap, [stage, ident_s], [psTb])
                    k.copy(lnf[:, dt*SEG + i*128: dt*SEG + (i+1)*128], psTb[:, 0:128], [psTb], [lnf])
            k.dma('qg', wsb.h[:, :].rearrange("p (t c) -> p t c", c=4*DS),
                  wkvqaT_in.h.rearrange("(t p) c -> p t c", p=128), [wkvqaT_in], [wsb])
            for j in range(4):
                for n in range(2):
                    ps = ps1 if n == 0 else ps2
                    for dt in range(ND):
                        k.mm(ps[0:64, 0:512], wsb[:, dt*4*DS + j*DS: dt*4*DS + (j+1)*DS],
                             lnf[:, dt*SEG + n*512: dt*SEG + (n+1)*512], [wsb, lnf], [ps],
                             start=(dt == 0), stop=(dt == ND-1))
                    if j < 3:
                        k.copy(proj[:, j*SEG + n*512: j*SEG + (n+1)*512], ps[0:64, 0:512], [ps], [proj])
                    else:
                        k.act(proj[:, j*SEG + n*512: j*SEG + (n+1)*512], ps[0:64, 0:512], AF.Sigmoid,
                              [ps, wab_s], [proj], bias=wab_s.ap)
            for j, dst in ((0, kd), (2, qd)):
                k.act(sqb.ap, proj[:, j*SEG:(j+1)*SEG], AF.Square, [proj], [sqb])
                for n in range(2):
                    k.mm(ps3[0:1, 0:512], onesc.ap, sqb[:, n*512:(n+1)*512], [onesc, sqb], [ps3])
                    k.act(nrow[:, n*512:(n+1)*512], ps3[0:1, 0:512], AF.Sqrt, [ps3], [nrow])
                k.ts(nrow.ap, nrow.ap, 1e-12, None, AluOpType.max, None, [nrow], [nrow])
                k.op('v', lambda e: e.reciprocal(rrow.ap, nrow.ap), reads=[nrow], writes=[rrow])
                for n in range(2):
                    k.mm(ps3[0:64, 0:512], onesr.ap, rrow[:, n*512:(n+1)*512], [onesr, rrow], [ps3])
                    k.tt(dst[:, n*512:(n+1)*512], proj[:, j*SEG + n*512: j*SEG + (n+1)*512],
                         ps3[0:64, 0:512], AluOpType.mult, [proj, ps3], [dst])
            k.copy(vd.ap, proj[:, 1*SEG:2*SEG], [proj], [vd])
            k.act(lam_s[:, 1:2], lam_s[:, 0:1], AF.Sigmoid, [lam_s], [lam_s])
            k.act(lam_s[:, 2:3], lam_s[:, 1:2], AF.Ln, [lam_s], [lam_s], bias=1e-8)
            k.ts(lam_s[:, 3:4], lam_s[:, 2:3], 8.0, None, AluOpType.mult, None, [lam_s], [lam_s])
            k.ts(lg.ap, proj[:, 3*SEG:4*SEG], lam_s[:, 3:4], None, AluOpType.mult, None,
                 [proj, lam_s], [lg])

            vcA = k.sbuf(ph, "n_vcA", [TC, NCHK*64], BF16)
            ktTA = k.sbuf(ph, "n_ktTA", [TC, NCHK*64], BF16)
            qpTA = k.sbuf(ph, "n_qpTA", [TC, NCHK*64], BF16)
            qiTA = k.sbuf(ph, "n_qiTA", [TC, NCHK*64], BF16)
            DmA = k.sbuf(ph, "n_DmA", [TC, NCHK*64], BF16)
            NTA = k.sbuf(ph, "n_NTA", [TC, NCHK*64], BF16)
            TTA = k.sbuf(ph, "n_TTA", [TC, NCHK*64], BF16)
            GtA = k.sbuf(ph, "n_GtA", [TC, NCHK*64], BF16)
            GtiA = k.sbuf(ph, "n_GtiA", [TC, NCHK*64], BF16)
            McTA = k.sbuf(ph, "n_McTA", [TC, NCHK*64], BF16)
            AeCA = k.sbuf(ph, "n_AeCA", [TC, NCHK], F32)
            lgT = k.sbuf(ph, "n_lgT", [TC, 64], F32)
            ALi = k.sbuf(ph, "n_ALi", [TC, 64], F32)
            A_t = k.sbuf(ph, "n_At", [TC, 64], F32)
            Apv = k.sbuf(ph, "n_Apv", [TC, 64], F32)
            Ainv = k.sbuf(ph, "n_Ainv", [TC, 64], F32)
            a_t = k.sbuf(ph, "n_at", [TC, 64], F32)
            w1 = k.sbuf(ph, "n_w1", [TC, 64], F32)
            tmpf = k.sbuf(ph, "n_tmpf", [TC, 64], F32)
            kc = k.sbuf(ph, "n_kc", [TC, 64], BF16)
            qc = k.sbuf(ph, "n_qc", [TC, 64], BF16)
            kt_ = k.sbuf(ph, "n_kt", [TC, 64], BF16)
            qp_ = k.sbuf(ph, "n_qp", [TC, 64], BF16)
            qi_ = k.sbuf(ph, "n_qi", [TC, 64], BF16)
            Dm = k.sbuf(ph, "n_Dm", [TC, 64], BF16)
            DmT = k.sbuf(ph, "n_DmT", [64, TC], BF16)
            Aer = k.sbuf(ph, "n_Aer", [1, 64], F32)
            Wn = k.sbuf(ph, "n_Wn", [TC, 64], BF16)
            tmpb = k.sbuf(ph, "n_tmpb", [TC, TC], BF16)
            Eac = k.sbuf(ph, "n_Eac", [TC, TC], BF16)
            Npow = [k.sbuf(ph, f"n_N{j}", [TC, TC], BF16) for j in range(5)]
            NTpow = [k.sbuf(ph, f"n_NT{j}", [TC, TC], BF16) for j in range(6)]

            def tr(src_ap, src_buf, dst_ap, dst_buf, f32=False, pdim=64):
                if f32:
                    k.trans(psF[0:64, 0:pdim], src_ap, identf[0:pdim, 0:pdim], [src_buf, identf], [psF])
                    k.copy(dst_ap, psF[0:64, 0:pdim], [psF], [dst_buf])
                else:
                    k.trans(psTb[0:64, 0:pdim], src_ap, ident_s[0:pdim, 0:pdim], [src_buf, ident_s], [psTb])
                    k.copy(dst_ap, psTb[0:64, 0:pdim], [psTb], [dst_buf])

            def build_powers():
                for lev in range(1, 5):
                    k.mm(ps3[0:TC, 0:TC], NTpow[lev-1].ap, Npow[lev-1].ap, [NTpow[lev-1], Npow[lev-1]], [ps3])
                    k.copy(Npow[lev].ap, ps3[0:TC, 0:TC], [ps3], [Npow[lev]])
                    k.mm(ps2[0:TC, 0:TC], Npow[lev-1].ap, NTpow[lev-1].ap, [Npow[lev-1], NTpow[lev-1]], [ps2])
                    k.copy(NTpow[lev].ap, ps2[0:TC, 0:TC], [ps2], [NTpow[lev]])
                k.mm(ps3[0:TC, 0:TC], Npow[4].ap, NTpow[4].ap, [Npow[4], NTpow[4]], [ps3])
                k.copy(NTpow[5].ap, ps3[0:TC, 0:TC], [ps3], [NTpow[5]])

            for c in range(NCHK):
                cs = slice(c*TC, (c+1)*TC)
                cs64 = slice(c*64, (c+1)*64)
                tr(lg[:, cs], lg, lgT.ap, lgT, f32=True)
                k.mm(psF[0:TC, 0:64], tli.ap, lgT.ap, [tli, lgT], [psF])
                k.copy(ALi.ap, psF[0:TC, 0:64], [psF], [ALi])
                k.act(A_t.ap, ALi.ap, AF.Exp, [ALi], [A_t])
                k.tt(tmpf.ap, ALi.ap, lgT.ap, AluOpType.subtract, [ALi, lgT], [tmpf])
                k.act(Apv.ap, tmpf.ap, AF.Exp, [tmpf], [Apv])
                k.act(Ainv.ap, ALi.ap, AF.Exp, [ALi], [Ainv], scale=-1.0)
                k.act(a_t.ap, lgT.ap, AF.Exp, [lgT], [a_t])
                k.mm(psF[0:1, 0:64], onesc.ap, lgT.ap, [onesc, lgT], [psF])
                k.act(Aer.ap, psF[0:1, 0:64], AF.Exp, [psF], [Aer])
                k.trans(psF[0:64, 0:1], Aer.ap, identf[0:1, 0:1], [Aer, identf], [psF])
                k.copy(AeCA[:, c:c+1], psF[0:64, 0:1], [psF], [AeCA])
                tr(kd[:, cs], kd, kc.ap, kc)
                tr(vd[:, cs], vd, vcA[:, cs64], vcA)
                tr(qd[:, cs], qd, qc.ap, qc)
                k.tt(kt_.ap, kc.ap, Apv.ap, AluOpType.mult, [kc, Apv], [kt_])
                k.tt(qp_.ap, qc.ap, Apv.ap, AluOpType.mult, [qc, Apv], [qp_])
                k.tt(qi_.ap, qc.ap, A_t.ap, AluOpType.mult, [qc, A_t], [qi_])
                k.ts(tmpf.ap, a_t.ap, -1.0, 1.0, AluOpType.mult, AluOpType.add, [a_t], [tmpf])
                k.tt(w1.ap, tmpf.ap, Ainv.ap, AluOpType.mult, [tmpf, Ainv], [w1])
                k.tt(Dm.ap, kc.ap, w1.ap, AluOpType.mult, [kc, w1], [Dm])
                k.copy(DmA[:, cs64], Dm.ap, [Dm], [DmA])
                tr(Dm.ap, Dm, DmT.ap, DmT)
                tr(kt_.ap, kt_, ktTA[:, cs64], ktTA)
                tr(qp_.ap, qp_, qpTA[:, cs64], qpTA)
                tr(qi_.ap, qi_, qiTA[:, cs64], qiTA)
                k.mm(ps1[0:TC, 0:TC], ktTA[:, cs64], DmT.ap, [ktTA, DmT], [ps1])
                k.tt(Npow[0].ap, ps1[0:TC, 0:TC], sln.ap, AluOpType.mult, [ps1, sln], [Npow[0]])
                k.mm(ps2[0:TC, 0:TC], DmT.ap, ktTA[:, cs64], [DmT, ktTA], [ps2])
                k.tt(NTA[:, cs64], ps2[0:TC, 0:TC], sun.ap, AluOpType.mult, [ps2, sun], [NTA])
                k.copy(NTpow[0].ap, NTA[:, cs64], [NTA], [NTpow[0]])
                build_powers()
                k.copy(Eac.ap, ident_s[0:TC, 0:TC], [ident_s], [Eac])
                for j in range(6):
                    k.mm(ps3[0:TC, 0:TC], ident_s[0:TC, 0:TC], Eac.ap, [ident_s, Eac], [ps3],
                         start=True, stop=False)
                    k.mm(ps3[0:TC, 0:TC], NTpow[j].ap, Eac.ap, [NTpow[j], Eac], [ps3],
                         start=False, stop=True)
                    k.copy(Eac.ap, ps3[0:TC, 0:TC], [ps3], [Eac])
                tr(Eac.ap, Eac, TTA[:, cs64], TTA)
                k.mm(ps1[0:TC, 0:TC], DmT.ap, qpTA[:, cs64], [DmT, qpTA], [ps1])
                k.tt(GtA[:, cs64], ps1[0:TC, 0:TC], su.ap, AluOpType.mult, [ps1, su], [GtA])
                k.mm(ps1[0:TC, 0:TC], DmT.ap, qiTA[:, cs64], [DmT, qiTA], [ps1])
                k.tt(GtiA[:, cs64], ps1[0:TC, 0:TC], sui.ap, AluOpType.mult, [ps1, sui], [GtiA])
                k.mm(ps2[0:TC, 0:64], TTA[:, cs64], kt_.ap, [TTA, kt_], [ps2])
                k.ts(Wn.ap, ps2[0:TC, 0:64], -1.0, None, AluOpType.mult, None, [ps2], [Wn])
                k.mm(ps2[0:TC, 0:TC], ident_s[0:TC, 0:TC], ident_s[0:TC, 0:TC], [ident_s], [ps2],
                     start=True, stop=False)
                k.mm(ps2[0:TC, 0:TC], Wn.ap, Dm.ap, [Wn, Dm], [ps2], start=False, stop=True)
                k.copy(tmpb.ap, ps2[0:TC, 0:TC], [ps2], [tmpb])
                k.mm(psF[0:TC, 0:TC], onesr.ap, Aer.ap, [onesr, Aer], [psF])
                k.tt(McTA[:, cs64], tmpb.ap, psF[0:TC, 0:TC], AluOpType.mult, [tmpb, psF], [McTA])

            Hst = k.sbuf(ph, "n_Hst", [64, 64], F32)
            Hbf = k.sbuf(ph, "n_Hbf", [64, 64], BF16)
            Mrun = k.sbuf(ph, "n_Mrun", [TC, TC], BF16)
            MrunT = k.sbuf(ph, "n_MrunT", [TC, TC], BF16)
            X = k.sbuf(ph, "n_X", [TC, 64], BF16)
            Xb = k.sbuf(ph, "n_Xb", [TC, 64], BF16)
            Z = k.sbuf(ph, "n_Z", [TC, 64], BF16)
            NsT = k.sbuf(ph, "n_NsT", [TC, TC], BF16)
            d2 = k.sbuf(ph, "n_d2", [TC, 64], F32)
            sq2 = k.sbuf(ph, "n_sq2", [TC, 64], F32)
            scol = k.sbuf(ph, "n_scol", [TC, 1], F32)
            outf = k.sbuf(ph, "n_outf", [TC, TC], F32)
            k.memset(Hst.ap, 0.0, [], [Hst])
            k.memset(Hbf.ap, 0.0, [], [Hbf])
            k.copy(Mrun.ap, ident_s[0:TC, 0:TC], [ident_s], [Mrun])
            k.copy(MrunT.ap, ident_s[0:TC, 0:TC], [ident_s], [MrunT])
            for c in range(NCHK):
                cs64 = slice(c*64, (c+1)*64)
                k.mm(ps1[0:TC, 0:64], ktTA[:, cs64], Hbf.ap, [ktTA, Hbf], [ps1])
                k.tt(X.ap, vcA[:, cs64], ps1[0:TC, 0:64], AluOpType.subtract, [vcA, ps1], [X])
                k.mm(ps2[0:TC, 0:64], TTA[:, cs64], X.ap, [TTA, X], [ps2])
                k.copy(X.ap, ps2[0:TC, 0:64], [ps2], [X])
                k.mm(ps2[0:TC, 0:64], ident_s[0:TC, 0:TC], Hbf.ap, [ident_s, Hbf], [ps2],
                     start=True, stop=False)
                k.mm(ps2[0:TC, 0:64], DmA[:, cs64], X.ap, [DmA, X], [ps2], start=False, stop=True)
                k.ts(Hst.ap, ps2[0:TC, 0:64], AeCA[:, c:c+1], None, AluOpType.mult, None,
                     [ps2, AeCA], [Hst])
                k.copy(Hbf.ap, Hst.ap, [Hst], [Hbf])
                k.mm(ps1[0:TC, 0:TC], McTA[:, cs64], Mrun.ap, [McTA, Mrun], [ps1])
                k.mm(ps2[0:TC, 0:TC], Mrun.ap, McTA[:, cs64], [Mrun, McTA], [ps2])
                k.copy(Mrun.ap, ps1[0:TC, 0:TC], [ps1], [Mrun])
                k.copy(MrunT.ap, ps2[0:TC, 0:TC], [ps2], [MrunT])
            k.copy(outf.ap, MrunT.ap, [MrunT], [outf])
            k.dma('qs', dn_l.h[0:64, :], outf.ap, [outf], [dn_l])
            k.dma('qs', dn_l.h[64:128, :], Hst.ap, [Hst], [dn_l])
            k.collective(dn_l.h, dn_g.h, [dn_l], dn_g)

            dn_sb = k.sbuf(ph, "g_dnsb", [128, NCORE*64], F32)
            dnsel = k.sbuf(ph, "g_dnsel", [64, 16], F32)
            Mu_f = k.sbuf(ph, "g_Muf", [64, 64], F32)
            Mu_b = k.sbuf(ph, "g_Mub", [64, 64], BF16)
            bu_b = k.sbuf(ph, "g_bub", [64, 64], BF16)
            y_buf = k.sbuf(ph, "g_ybuf", [64, NCHK*64], F32)
            ynf = k.sbuf(ph, "g_ynf", [64, SEG], BF16)
            wos = k.sbuf(ph, "g_wos", [64, D], BF16)
            nwb = k.sbuf(ph, "g_nwb", [64, NCHK*64], F32)
            msq = k.sbuf(ph, "g_msq", [64, NCHK], F32)
            o_t = k.sbuf(ph, "g_ot", [128, D], F32)
            x3_t = k.sbuf(ph, "g_x3t", [128, D], F32)

            k.dma('qs', dnsel.ap, dnsel_in.ap, [dnsel_in], [dnsel])
            k.dma('qs', nwb.ap, nwb_in.ap, [nwb_in], [nwb])
            k.dma('qg', wos.ap, wo1T_in.ap, [wo1T_in], [wos])
            for r in range(NCORE):
                k.dma('qs', dn_sb[:, r*64:(r+1)*64], dn_g.h[r*128:(r+1)*128, :], [dn_g], [dn_sb])
            k.memset(Hst.ap, 0.0, [], [Hst])
            k.memset(Hbf.ap, 0.0, [], [Hbf])
            for j in range(NCORE - 1):
                k.ts(Mu_f.ap, dn_sb[0:64, j*64:(j+1)*64], dnsel[:, j:j+1], None,
                     AluOpType.mult, None, [dn_sb, dnsel], [Mu_f])
                k.stt(Mu_f.ap, identf[0:64, 0:64], dnsel[:, 8+j:9+j], Mu_f.ap,
                      AluOpType.mult, AluOpType.add, [identf, dnsel, Mu_f], [Mu_f])
                k.copy(Mu_b.ap, Mu_f.ap, [Mu_f], [Mu_b])
                k.ts(Mu_f.ap, dn_sb[64:128, j*64:(j+1)*64], dnsel[:, j:j+1], None,
                     AluOpType.mult, None, [dn_sb, dnsel], [Mu_f])
                k.copy(bu_b.ap, Mu_f.ap, [Mu_f], [bu_b])
                k.mm(ps1[0:TC, 0:64], Mu_b.ap, Hbf.ap, [Mu_b, Hbf], [ps1],
                     start=True, stop=False)
                k.mm(ps1[0:TC, 0:64], ident_s[0:TC, 0:TC], bu_b.ap, [ident_s, bu_b],
                     [ps1], start=False, stop=True)
                k.copy(Hst.ap, ps1[0:TC, 0:64], [ps1], [Hst])
                k.copy(Hbf.ap, Hst.ap, [Hst], [Hbf])

            for c in range(NCHK):
                cs64 = slice(c*64, (c+1)*64)
                k.mm(ps1[0:TC, 0:64], ktTA[:, cs64], Hbf.ap, [ktTA, Hbf], [ps1])
                k.tt(Xb.ap, vcA[:, cs64], ps1[0:TC, 0:64], AluOpType.subtract, [vcA, ps1], [Xb])
                k.mm(ps2[0:TC, 0:64], TTA[:, cs64], Xb.ap, [TTA, Xb], [ps2])
                k.copy(X.ap, ps2[0:TC, 0:64], [ps2], [X])
                k.mm(ps2[0:TC, 0:64], qpTA[:, cs64], Hbf.ap, [qpTA, Hbf], [ps2], start=True, stop=False)
                k.mm(ps2[0:TC, 0:64], GtA[:, cs64], X.ap, [GtA, X], [ps2], start=False, stop=True)
                k.tt(d2.ap, vcA[:, cs64], ps2[0:TC, 0:64], AluOpType.subtract, [vcA, ps2], [d2])
                k.tt(sq2.ap, d2.ap, d2.ap, AluOpType.mult, [d2], [sq2])
                k.op('v', lambda e: e.reduce_sum(scol.ap, sq2.ap, AX.X), reads=[sq2], writes=[scol])
                k.act(scol.ap, scol.ap, AF.Sigmoid, [scol], [scol], scale=float(1.0/(1.0+1e-6)))
                k.ts(NsT.ap, NTA[:, cs64], scol.ap, None, AluOpType.mult, None, [NTA, scol], [NsT])
                k.copy(NTpow[0].ap, NsT.ap, [NsT], [NTpow[0]])
                tr(NsT.ap, NsT, Npow[0].ap, Npow[0])
                build_powers()
                for j in (5, 4, 3, 2, 1, 0):
                    k.mm(ps3[0:TC, 0:64], ident_s[0:TC, 0:TC], Xb.ap, [ident_s, Xb], [ps3],
                         start=True, stop=False)
                    k.mm(ps3[0:TC, 0:64], NTpow[j].ap, Xb.ap, [NTpow[j], Xb], [ps3],
                         start=False, stop=True)
                    k.copy(Xb.ap, ps3[0:TC, 0:64], [ps3], [Xb])
                k.ts(Z.ap, Xb.ap, scol.ap, None, AluOpType.mult, None, [Xb, scol], [Z])
                k.mm(ps2[0:TC, 0:64], qiTA[:, cs64], Hbf.ap, [qiTA, Hbf], [ps2], start=True, stop=False)
                k.mm(ps2[0:TC, 0:64], GtiA[:, cs64], Z.ap, [GtiA, Z], [ps2], start=False, stop=True)
                k.copy(y_buf[:, cs64], ps2[0:TC, 0:64], [ps2], [y_buf])
                k.mm(ps2[0:TC, 0:64], ident_s[0:TC, 0:TC], Hbf.ap, [ident_s, Hbf], [ps2],
                     start=True, stop=False)
                k.mm(ps2[0:TC, 0:64], DmA[:, cs64], Z.ap, [DmA, Z], [ps2], start=False, stop=True)
                k.ts(Hst.ap, ps2[0:TC, 0:64], AeCA[:, c:c+1], None, AluOpType.mult, None,
                     [ps2, AeCA], [Hst])
                k.copy(Hbf.ap, Hst.ap, [Hst], [Hbf])

            for c in range(NCHK):
                k.tt(sq2.ap, y_buf[:, c*64:(c+1)*64], y_buf[:, c*64:(c+1)*64], AluOpType.mult,
                     [y_buf], [sq2])
                k.op('v', lambda e, c=c: e.reduce_sum(msq[:, c:c+1], sq2.ap, AX.X),
                     reads=[sq2], writes=[msq])
            k.ts(msq.ap, msq.ap, 1.0/64.0, None, AluOpType.mult, None, [msq], [msq])
            k.act(msq.ap, msq.ap, AF.Ln, [msq], [msq], bias=1e-6)
            k.act(msq.ap, msq.ap, AF.Exp, [msq], [msq], scale=-0.5)
            for c in range(NCHK):
                k.ts(y_buf[:, c*64:(c+1)*64], y_buf[:, c*64:(c+1)*64], msq[:, c:c+1], None,
                     AluOpType.mult, None, [y_buf, msq], [y_buf])
            k.tt(y_buf.ap, y_buf.ap, nwb.ap, AluOpType.mult, [y_buf, nwb], [y_buf])
            for c in range(NCHK):
                k.copy(Dm.ap, y_buf[:, c*64:(c+1)*64], [y_buf], [Dm])
                tr(Dm.ap, Dm, ynf[:, c*64:(c+1)*64], ynf)
            x3r = x3_d.h.rearrange("(n p) d -> n p d", p=128)
            for tg in range(NT):
                k.dma('qs', xt[tg % 2].ap, xr2[tg], [x2_d], [xt[tg % 2]])
                for dc in range(4):
                    ps = ps1 if dc % 2 == 0 else ps2
                    k.mm(ps[:, 0:512], ynf[:, tg*128:(tg+1)*128], wos[:, dc*512:(dc+1)*512],
                         [ynf, wos], [ps])
                    k.copy(o_t[:, dc*512:(dc+1)*512], ps[:, 0:512], [ps], [o_t])
                emit_ln_tile(k, o_t.ap, o_t, x3_t.ap, x3_t, stats)
                k.tt(x3_t.ap, x3_t.ap, xt[tg % 2].ap, AluOpType.add, [x3_t, xt[tg % 2]], [x3_t])
                k.dma('qs', x3r[tg], x3_t.ap, [x3_t], [x3_d])
            k.phase_boundary()

    with ExitStack() as ph:
        xt = [k.sbuf(ph, f"h_xt{i}", [128, D], F32) for i in range(2)]
        stage = k.sbuf(ph, "h_stage", [128, D], BF16)
        ln7f = k.sbuf(ph, "h_ln7f", [128, ND*SEG], BF16)
        y_acc = k.sbuf(ph, "h_yacc", [128, NT*D], F32)
        hbuf = k.sbuf(ph, "h_hbuf", [128, NOC_M*SEG], BF16)
        wb1 = [k.sbuf(ph, f"h_wb1{i}", [128, ND*128], BF16) for i in range(2)]
        wb2 = [k.sbuf(ph, f"h_wb2{i}", [128, ND*128], BF16) for i in range(2)]
        wpb = [k.sbuf(ph, f"h_wpb{i}", [128, NOC_M*512], BF16) for i in range(2)]
        gw_s = k.sbuf(ph, "h_gw", [128, ND*16], BF16)
        router = k.sbuf(ph, "h_router", [128, NT*16], F32)
        comb = k.sbuf(ph, "h_comb", [128, NT*16], F32)
        rst = k.sbuf(ph, "h_rst", [128, 8], F32)
        gs = k.sbuf(ph, "h_gs", [128, SEG], BF16)
        us = k.sbuf(ph, "h_us", [128, SEG], BF16)
        o_t = k.sbuf(ph, "h_ot", [128, D], F32)
        psA = k.psum(ph, "h_psA", [128, 512], F32)
        psB = k.psum(ph, "h_psB", [128, 512], F32)
        psC = k.psum(ph, "h_psC", [128, 512], F32)
        psD = k.psum(ph, "h_psD", [128, 512], F32)
        psT = k.psum(ph, "h_psT", [128, 1024], BF16)

        k.dma('qg', gw_s.h[:, :].rearrange("p (t c) -> p t c", c=16),
              gw_in.h.rearrange("(t p) c -> p t c", p=128), [gw_in], [gw_s])
        xr3 = x3_d.h.rearrange("(n p) d -> n p d", p=128)
        for i in range(NT):
            k.dma('qs', xt[i % 2].ap, xr3[i], [x3_d], [xt[i % 2]])
            emit_ln_tile(k, xt[i % 2].ap, xt[i % 2], stage.ap, stage, stats)
            for dt in range(ND):
                k.trans(psT[:, 0:128], stage[:, dt*128:(dt+1)*128], ident_s.ap, [stage, ident_s], [psT])
                k.copy(ln7f[:, dt*SEG + i*128: dt*SEG + (i+1)*128], psT[:, 0:128], [psT], [ln7f])
        for tg in range(NT):
            ps = psA if tg % 2 == 0 else psB
            for dt in range(ND):
                k.mm(ps[:, 0:16], ln7f[:, dt*SEG + tg*128: dt*SEG + (tg+1)*128],
                     gw_s[:, dt*16:(dt+1)*16], [ln7f, gw_s], [ps],
                     start=(dt == 0), stop=(dt == ND-1))
            k.copy(router[:, tg*16:(tg+1)*16], ps[:, 0:16], [ps], [router])
        for tg in range(NT):
            rsl = router[:, tg*16:tg*16+8]
            csl = comb[:, tg*16:tg*16+8]
            k.act(comb[:, tg*16+8:tg*16+9], router[:, tg*16+8:tg*16+9], AF.Sigmoid, [router], [comb])
            k.op('v', lambda e, rsl=rsl: e.reduce_max(rst[:, 0:1], rsl, AX.X), reads=[router], writes=[rst])
            k.ts(rst[:, 1:2], rst[:, 0:1], -1.0, None, AluOpType.mult, None, [rst], [rst])
            k.act(csl, rsl, AF.Exp, [router, rst], [comb], bias=rst[:, 1:2])
            k.op('v', lambda e, csl=csl: e.reduce_sum(rst[:, 2:3], csl, AX.X), reads=[comb], writes=[rst])
            k.op('v', lambda e: e.reciprocal(rst[:, 3:4], rst[:, 2:3]), reads=[rst], writes=[rst])
            k.ts(csl, csl, rst[:, 3:4], None, AluOpType.mult, None, [comb, rst], [comb])
            k.op('v', lambda e, csl=csl: e.reduce_max(rst[:, 4:5], csl, AX.X), reads=[comb], writes=[rst])
            k.ts(router[:, tg*16:tg*16+8], csl, rst[:, 4:5], None, AluOpType.is_lt, None,
                 [comb, rst], [router])
            k.tt(router[:, tg*16:tg*16+8], router[:, tg*16:tg*16+8], csl, AluOpType.mult,
                 [router, comb], [router])
            k.op('v', lambda e, tg=tg: e.reduce_max(rst[:, 5:6], router[:, tg*16:tg*16+8], AX.X),
                 reads=[router], writes=[rst])
            k.ts(router[:, tg*16+8:tg*16+16], csl, rst[:, 5:6], None, AluOpType.is_ge, None,
                 [comb, rst], [router])
            k.tt(csl, csl, router[:, tg*16+8:tg*16+16], AluOpType.mult, [comb, router], [comb])
        k.memset(y_acc.ap, 0.0, [], [y_acc])
        er1 = efc1_in.h.rearrange("e (t p) c -> e p t c", p=128)
        er2 = efc2_in.h.rearrange("e (t p) c -> e p t c", p=128)
        erp = eprj_in.h.rearrange("e (t p) c -> e p t c", p=128)
        for e in range(NE):
            for oc in range(NOC_M):
                b1, b2 = wb1[oc % 2], wb2[oc % 2]
                k.dma('qg', b1.h[:, :].rearrange("p (t c) -> p t c", c=128),
                      er1[e][:, :, oc*128:(oc+1)*128], [efc1_in], [b1])
                k.dma('qg', b2.h[:, :].rearrange("p (t c) -> p t c", c=128),
                      er2[e][:, :, oc*128:(oc+1)*128], [efc2_in], [b2])
                for n in range(2):
                    psg = psA if n == 0 else psB
                    psu = psC if n == 0 else psD
                    for dt in range(ND):
                        k.mm(psg[:, 0:512], b1[:, dt*128:(dt+1)*128],
                             ln7f[:, dt*SEG + n*512: dt*SEG + (n+1)*512], [b1, ln7f], [psg],
                             start=(dt == 0), stop=(dt == ND-1))
                    for dt in range(ND):
                        k.mm(psu[:, 0:512], b2[:, dt*128:(dt+1)*128],
                             ln7f[:, dt*SEG + n*512: dt*SEG + (n+1)*512], [b2, ln7f], [psu],
                             start=(dt == 0), stop=(dt == ND-1))
                    k.act(gs[:, n*512:(n+1)*512], psg[:, 0:512], AF.Silu, [psg], [gs])
                    k.copy(us[:, n*512:(n+1)*512], psu[:, 0:512], [psu], [us])
                k.tt(hbuf[:, oc*SEG:(oc+1)*SEG], gs.ap, us.ap, AluOpType.mult, [gs, us], [hbuf])
            for dc in range(4):
                bp = wpb[dc % 2]
                k.dma('qg', bp.h[:, :].rearrange("p (t c) -> p t c", c=512),
                      erp[e][:, :, dc*512:(dc+1)*512], [eprj_in], [bp])
                for tg in range(NT):
                    ps = psA if tg % 2 == 0 else psB
                    for ht in range(NOC_M):
                        k.mm(ps[:, 0:512], hbuf[:, ht*SEG + tg*128: ht*SEG + (tg+1)*128],
                             bp[:, ht*512:(ht+1)*512], [hbuf, bp], [ps],
                             start=(ht == 0), stop=(ht == NOC_M-1))
                    k.stt(y_acc[:, tg*D + dc*512: tg*D + (dc+1)*512], ps[:, 0:512],
                          comb[:, tg*16+e:tg*16+e+1], y_acc[:, tg*D + dc*512: tg*D + (dc+1)*512],
                          AluOpType.mult, AluOpType.add, [ps, comb, y_acc], [y_acc])
        sr1 = sfc1_in.h.rearrange("(t p) c -> p t c", p=128)
        sr2 = sfc2_in.h.rearrange("(t p) c -> p t c", p=128)
        srp = sprj_in.h.rearrange("(t p) c -> p t c", p=128)
        for qtr in range(4):
            for ocq in range(NOC_M):
                oc = qtr*NOC_M + ocq
                b1, b2 = wb1[ocq % 2], wb2[ocq % 2]
                k.dma('qg', b1.h[:, :].rearrange("p (t c) -> p t c", c=128),
                      sr1[:, :, oc*128:(oc+1)*128], [sfc1_in], [b1])
                k.dma('qg', b2.h[:, :].rearrange("p (t c) -> p t c", c=128),
                      sr2[:, :, oc*128:(oc+1)*128], [sfc2_in], [b2])
                for n in range(2):
                    psg = psA if n == 0 else psB
                    psu = psC if n == 0 else psD
                    for dt in range(ND):
                        k.mm(psg[:, 0:512], b1[:, dt*128:(dt+1)*128],
                             ln7f[:, dt*SEG + n*512: dt*SEG + (n+1)*512], [b1, ln7f], [psg],
                             start=(dt == 0), stop=(dt == ND-1))
                    for dt in range(ND):
                        k.mm(psu[:, 0:512], b2[:, dt*128:(dt+1)*128],
                             ln7f[:, dt*SEG + n*512: dt*SEG + (n+1)*512], [b2, ln7f], [psu],
                             start=(dt == 0), stop=(dt == ND-1))
                    k.act(gs[:, n*512:(n+1)*512], psg[:, 0:512], AF.Silu, [psg], [gs])
                    k.copy(us[:, n*512:(n+1)*512], psu[:, 0:512], [psu], [us])
                k.tt(hbuf[:, ocq*SEG:(ocq+1)*SEG], gs.ap, us.ap, AluOpType.mult, [gs, us], [hbuf])
            for dc in range(4):
                bp = wpb[dc % 2]
                k.dma('qg', bp.h[:, :].rearrange("p (t c) -> p t c", c=512),
                      srp[:, qtr*NOC_M:(qtr+1)*NOC_M, dc*512:(dc+1)*512], [sprj_in], [bp])
                for tg in range(NT):
                    ps = psA if tg % 2 == 0 else psB
                    for ht in range(NOC_M):
                        k.mm(ps[:, 0:512], hbuf[:, ht*SEG + tg*128: ht*SEG + (tg+1)*128],
                             bp[:, ht*512:(ht+1)*512], [hbuf, bp], [ps],
                             start=(ht == 0), stop=(ht == NOC_M-1))
                    k.stt(y_acc[:, tg*D + dc*512: tg*D + (dc+1)*512], ps[:, 0:512],
                          comb[:, tg*16+8:tg*16+9], y_acc[:, tg*D + dc*512: tg*D + (dc+1)*512],
                          AluOpType.mult, AluOpType.add, [ps, comb, y_acc], [y_acc])
        outr = out_out.h.rearrange("(n p) d -> n p d", p=128)
        for tg in range(NT):
            k.dma('qs', xt[tg % 2].ap, xr3[tg], [x3_d], [xt[tg % 2]])
            emit_ln_tile(k, y_acc[:, tg*D:(tg+1)*D], y_acc, o_t.ap, o_t, stats)
            k.tt(o_t.ap, o_t.ap, xt[tg % 2].ap, AluOpType.add, [o_t, xt[tg % 2]], [o_t])
            k.dma('qs', outr[tg], o_t.ap, [o_t], [out_out])

    if debug:
        dbg1 = k.dram_out("dbg_x1", [SEG, D], F32)
        dbg2 = k.dram_out("dbg_x2", [SEG, D], F32)
        dbg3 = k.dram_out("dbg_x3", [SEG, D], F32)
        k.dma('qs', dbg1.h, x1_d.h, [x1_d], [dbg1])
        k.dma('qs', dbg2.h, x2_d.h, [x2_d], [dbg2])
        k.dma('qs', dbg3.h, x3_d.h, [x3_d], [dbg3])
    nc = k.end()
    return nc, k


_BUILT = {}

def _prep_maps(I, k):
    t = host_tables()
    x = I['L0_attn_x'][0]
    wqkvT = np.ascontiguousarray(np.asarray(I['L0_attn_w_qkv']).T)
    b = np.asarray(I['L0_attn_b_qkv'])
    bqkv_cols = np.ascontiguousarray(b[:2*D].reshape(2*ND, 128).T)
    bqkv_cols = np.concatenate([bqkv_cols, np.zeros((128, ND), np.float32)], axis=1)
    bv_b = np.broadcast_to(b[2*D:][None, :], (128, D)).copy()
    woT = np.ascontiguousarray(np.asarray(I['L0_attn_w_out']).T)
    bo_b = np.broadcast_to(np.asarray(I['L0_attn_b_out'])[None, :], (128, D)).copy()
    wguT = np.ascontiguousarray(np.asarray(I['L0_ffn_w_gate_up']).T)
    wpT = np.ascontiguousarray(np.asarray(I['L0_ffn_w_proj']).T)
    wkvqaT = np.ascontiguousarray(np.concatenate(
        [np.asarray(I['L1_attn_Wk']).T, np.asarray(I['L1_attn_Wv']).T,
         np.asarray(I['L1_attn_Wq']).T, np.asarray(I['L1_attn_Wa_w']).T], axis=1))
    wo1T = np.ascontiguousarray(np.asarray(I['L1_attn_Wo']).T)
    nwb = np.tile(np.asarray(I['L1_attn_norm_w'])[None, :], (64, NCHK)).astype(np.float32)
    gate9 = np.zeros((D, 16), np.float32)
    gate9[:, 0:8] = np.asarray(I['L1_ffn_gate_w']).T
    gate9[:, 8:9] = np.asarray(I['L1_ffn_shared_gate_w']).T
    efc1T = np.ascontiguousarray(np.asarray(I['L1_ffn_expert_fc1']).transpose(0, 2, 1))
    efc2T = np.ascontiguousarray(np.asarray(I['L1_ffn_expert_fc2']).transpose(0, 2, 1))
    eprjT = np.ascontiguousarray(np.asarray(I['L1_ffn_expert_proj']).transpose(0, 2, 1))
    sfc1T = np.ascontiguousarray(np.asarray(I['L1_ffn_shared_fc1']).T)
    sfc2T = np.ascontiguousarray(np.asarray(I['L1_ffn_shared_fc2']).T)
    sprjT = np.ascontiguousarray(np.asarray(I['L1_ffn_shared_proj']).T)
    maps = []
    for c in range(NCORE):
        sw = np.zeros((NCORE, H), np.float32)
        for j in range(c):
            sw[j] = LAM_SEG ** (c - 1 - j)
        dnsel = np.zeros((64, 16), np.float32)
        for j in range(NCORE - 1):
            sel = 1.0 if j < c else 0.0
            dnsel[:, j] = sel
            dnsel[:, 8+j] = 1.0 - sel
        maps.append({
            "consts_pool": k.const_array(),
            "x_seg": np.ascontiguousarray(x[c*SEG:(c+1)*SEG]),
            "wqkvT": wqkvT, "bqkv_cols": bqkv_cols, "bv_b": bv_b,
            "woT": woT, "bo_b": bo_b,
            "sw": np.broadcast_to(sw.reshape(1, -1), (128, NCORE*H)).copy(),
            "wguT_sh": np.ascontiguousarray(wguT[:, c*GU_SH:(c+1)*GU_SH]),
            "wpT_sh": np.ascontiguousarray(wpT[:, c*P_SH:(c+1)*P_SH]),
            "wkvqaT": wkvqaT,
            "wa_b": np.asarray(I['L1_attn_Wa_b']).reshape(64, 1),
            "lam": np.asarray(I['L1_attn_lam']).reshape(64, 1),
            "dnsel": dnsel, "nwb": nwb, "wo1T": wo1T,
            "gate9T": gate9,
            "efc1T": efc1T, "efc2T": efc2T, "eprjT": eprjT,
            "sfc1T": sfc1T, "sfc2T": sfc2T, "sprjT": sprjT,
            "ident_bf": t['ident_bf'], "ident_f32": t['ident_f32'],
            "dq_b": t['dq_b'], "dk_col": t['dk_col'], "dmaskT": t['dmaskT'],
            "tli_f32": t['tli_f32'], "strilneg_bf": t['strilneg_bf'],
            "striuneg_bf": t['striuneg_bf'], "striu_bf": t['striu_bf'],
            "striuincl_bf": t['striuincl_bf'],
            "ones64c_f32": t['ones64c_f32'], "ones64r_f32": t['ones64r_f32'],
            "ones128c_bf": t['ones128c_bf'], "ones1r_f32": t['ones1r_f32'],
            "gmask": t['gmask'], "gmask2f": t['gmask2f'], "gmask2b": t['gmask2b'],
        })
    return maps


def kernel(**inputs):
    if "fused" not in _BUILT:
        _BUILT["fused"] = build_fused()
    nc, kk = _BUILT["fused"]
    maps = _prep_maps(inputs, kk)
    res = run_bass_kernel_spmd(nc, maps, core_ids=list(range(NCORE)))
    out = np.concatenate([np.asarray(res.results[c]["out_seg"]) for c in range(NCORE)], 0)
    return out.reshape(1, N_TOK, D).astype(np.float32)


# version 10
# speedup vs baseline: 1.1043x; 1.0165x over previous
import numpy as np
import math
from contextlib import ExitStack
import ml_dtypes
import concourse.bass as bass
import concourse.mybir as mybir
from concourse.bass_utils import run_bass_kernel_spmd
from concourse.alu_op_type import AluOpType

F32 = mybir.dt.float32
BF16 = mybir.dt.bfloat16
AF = mybir.ActivationFunctionType
AX = mybir.AxisListType
BF = ml_dtypes.bfloat16

QUEUE_ENG = {'qg': 'g', 'qs': 'y', 'qa': 's'}
NSLOT = 6


class Buf:
    __slots__ = ('h', 'name', 'last_w', 'readers')
    def __init__(self, h, name):
        self.h = h
        self.name = name
        self.last_w = None
        self.readers = []
    def __getitem__(self, sl):
        return self.h[sl]
    @property
    def ap(self):
        return self.h[:]


class K:
    def __init__(self, name="prog"):
        self.nc = bass.Bass("TRN2", target_bir_lowering=False, debug=False)
        self.stack = ExitStack()
        self.eng = {'t': self.nc.tensor, 'v': self.nc.vector, 's': self.nc.scalar,
                    'g': self.nc.gpsimd, 'y': self.nc.sync}
        self.cnt = {k: 0 for k in ('t', 'v', 's', 'g', 'y')}
        self.water = {k: {} for k in ('t', 'v', 's', 'g', 'y')}
        self.sems = None
        self.block = None
        self.ops = 0
        self.bufs = []
        self.cc_cnt = 0
        self._ext_in = {}
        self._ext_out = {}

    def dram_in(self, name, shape, dtype=F32):
        ap = self.nc.dram_tensor(name, list(shape), dtype, kind="ExternalInput").ap()
        b = Buf(ap, name)
        self.bufs.append(b)
        self._ext_in[name] = b
        return b

    def dram_out(self, name, shape, dtype=F32):
        ap = self.nc.dram_tensor(name, list(shape), dtype, kind="ExternalOutput").ap()
        b = Buf(ap, name)
        self.bufs.append(b)
        self._ext_out[name] = b
        return b

    def dram_tmp(self, name, shape, dtype=F32):
        ap = self.nc.dram_tensor(name, list(shape), dtype, kind="Internal").ap()
        b = Buf(ap, name)
        self.bufs.append(b)
        return b

    def begin(self, n_phases=12, n_consts=128):
        self._consts_in = self.dram_in("consts_pool", [128, n_consts], F32)
        self.semsets = []
        for p in range(n_phases):
            s = {}
            for kk in ('t', 'v', 's', 'g', 'y'):
                s[kk] = self.stack.enter_context(self.nc.semaphore(f"sm{p}_{kk}"))
            self.semsets.append(s)
        self.dma_sems = {}
        self.dma_uses = {}
        self.dma_issue = {}
        for q in ('qg', 'qs', 'qa'):
            for sl in range(NSLOT):
                self.dma_sems[(q, sl)] = self.stack.enter_context(self.nc.semaphore(f"dm_{q}{sl}"))
                self.dma_uses[(q, sl)] = 0
            self.dma_issue[q] = 0
        self.cc_sem = self.stack.enter_context(self.nc.semaphore("sm_cc"))
        self.phase = 0
        self.sems = dict(self.semsets[0])
        self.sems['cc'] = self.cc_sem
        self.block = self.stack.enter_context(self.nc.Block())
        ids = sorted({s.num for st in self.semsets for s in st.values()}
                     | {s.num for s in self.dma_sems.values()} | {self.cc_sem.num})
        if ids == list(range(ids[0], ids[-1] + 1)):
            self.eng['g'].sem_clear(range(ids[0], ids[-1] + 1))
        else:
            for st in self.semsets:
                for s in st.values():
                    self.eng['g'].sem_clear(s)
            for s in self.dma_sems.values():
                self.eng['g'].sem_clear(s)
            self.eng['g'].sem_clear(self.cc_sem)
        self.nc.all_engine_barrier()
        self._consts_s = self.sbuf(self.stack, "consts_s", [128, n_consts], F32)
        self._const_map = {}
        self._n_consts = n_consts
        self.dma('qs', self._consts_s.ap, self._consts_in.ap, [self._consts_in], [self._consts_s])
        return self

    def constf(self, val):
        val = float(val)
        if val not in self._const_map:
            i = len(self._const_map)
            assert i < self._n_consts, "const pool full"
            self._const_map[val] = i
            self.nc.const_aps.register(F32, val, self._consts_s[:, i:i+1])
        i = self._const_map[val]
        return self._consts_s[:, i:i+1]

    def const_array(self):
        arr = np.zeros((128, self._n_consts), np.float32)
        for v, i in self._const_map.items():
            arr[:, i] = v
        return arr

    def sbuf(self, stack, name, shape, dtype=F32):
        h = stack.enter_context(self.nc.sbuf_tensor(name, list(shape), dtype))
        b = Buf(h, name)
        self.bufs.append(b)
        return b

    def psum(self, stack, name, shape, dtype=F32):
        h = stack.enter_context(self.nc.psum_tensor(name, list(shape), dtype))
        b = Buf(h, name)
        self.bufs.append(b)
        return b

    def end(self):
        self.barrier_all()
        self.stack.close()
        self.nc.finalize()
        return self.nc

    def _wait(self, consumer, dep_key, dep_val):
        if dep_val <= 0:
            return
        w = self.water[consumer]
        if w.get(dep_key, 0) >= dep_val:
            return
        if isinstance(dep_key, tuple):
            self.eng[consumer].wait_ge(self.dma_sems[dep_key], dep_val * 16)
        else:
            self.eng[consumer].wait_ge(self.sems[dep_key], dep_val)
        w[dep_key] = dep_val

    def _deps(self, ekey, reads, writes):
        for b in reads:
            if b.last_w is not None:
                pk, pv = b.last_w
                if pk != ekey or ekey != 't':
                    self._wait(ekey, pk, pv)
        for b in writes:
            if b.last_w is not None:
                pk, pv = b.last_w
                if pk != ekey or ekey != 't':
                    self._wait(ekey, pk, pv)
            for (rk, rv) in b.readers:
                if rk != ekey or ekey != 't':
                    self._wait(ekey, rk, rv)

    def _post(self, key, reads, writes):
        self.cnt[key] += 1
        tag = (key, self.cnt[key])
        for b in reads:
            b.readers.append(tag)
        for b in writes:
            b.last_w = tag
            b.readers = []
        self.ops += 1
        return tag

    def op(self, ekey, fn, reads=(), writes=()):
        self._deps(ekey, list(reads), list(writes))
        inst = fn(self.eng[ekey])
        inst.then_inc(self.sems[ekey], 1)
        return self._post(ekey, list(reads), list(writes))

    def dma(self, qkey, out_ap, in_ap, reads=(), writes=(), **kw):
        ekey = QUEUE_ENG[qkey]
        reads, writes = list(reads), list(writes)
        for b in reads:
            if b.last_w is not None:
                pk, pv = b.last_w
                self._wait(ekey, pk, pv)
        for b in writes:
            if b.last_w is not None:
                pk, pv = b.last_w
                self._wait(ekey, pk, pv)
            for (rk, rv) in b.readers:
                if rk != ekey:
                    self._wait(ekey, rk, rv)
        slot = (qkey, self.dma_issue[qkey] % NSLOT)
        self.dma_issue[qkey] += 1
        self._wait(ekey, slot, self.dma_uses[slot])
        self.dma_uses[slot] += 1
        self.eng[ekey].dma_start(out=out_ap, in_=in_ap, **kw).then_inc(self.dma_sems[slot], 16)
        tag = (slot, self.dma_uses[slot])
        for b in reads:
            b.readers.append(tag)
        for b in writes:
            b.last_w = tag
            b.readers = []
        self.ops += 1
        return tag

    def collective(self, in_ap, out_ap, reads, out_buf):
        ekey = 'g'
        for b in list(reads):
            if b.last_w is not None:
                pk, pv = b.last_w
                self._wait(ekey, pk, pv)
        self.eng['g'].collective_compute(
            "AllGather", AluOpType.bypass,
            replica_groups=[list(range(8))],
            ins=[in_ap.opt()], outs=[out_ap.opt()],
        ).then_inc(self.cc_sem)
        self.cc_cnt += 1
        tag = ('cc', self.cc_cnt)
        for b in list(reads):
            b.readers.append(tag)
        out_buf.last_w = tag
        out_buf.readers = []
        self.ops += 1
        return tag

    def barrier_all(self):
        snap = dict(self.cnt)
        dsnap = dict(self.dma_uses)
        for ekey in ('t', 'v', 's', 'g', 'y'):
            for pk, pv in snap.items():
                if pk != ekey:
                    self._wait(ekey, pk, pv)
            for pk, pv in dsnap.items():
                self._wait(ekey, pk, pv)

    def phase_boundary(self):
        self.barrier_all()
        self.phase += 1
        self.sems = dict(self.semsets[self.phase])
        self.sems['cc'] = self.cc_sem
        self.cnt = {k: 0 for k in ('t', 'v', 's', 'g', 'y')}
        for w in self.water.values():
            keep = {kk: vv for kk, vv in w.items() if kk == 'cc' or isinstance(kk, tuple)}
            w.clear()
            w.update(keep)
        for b in self.bufs:
            if not (b.last_w and (b.last_w[0] == 'cc' or isinstance(b.last_w[0], tuple))):
                b.last_w = None
            b.readers = [r for r in b.readers if r[0] == 'cc' or isinstance(r[0], tuple)]

    def mm(self, out_ap, lT_ap, r_ap, rd, wr, start=True, stop=True):
        return self.op('t', lambda e: e.matmul(out_ap, lT_ap, r_ap, start=start, stop=stop),
                       reads=rd, writes=wr)

    def trans(self, out_ap, in_ap, ident_ap, rd, wr):
        return self.op('t', lambda e: e.transpose(out_ap, in_ap, ident_ap), reads=rd, writes=wr)

    def act(self, out_ap, in_ap, func, rd, wr, **kw):
        rd = list(rd)
        try:
            nparts = out_ap.shape[0]
        except Exception:
            nparts = 128
        for key in ('bias', 'scale'):
            v = kw.get(key)
            if isinstance(v, (int, float)):
                self.constf(v)
                i = self._const_map[float(v)]
                kw[key] = self._consts_s[0:nparts, i:i+1]
                if self._consts_s not in rd:
                    rd.append(self._consts_s)
        return self.op('s', lambda e: e.activation(out_ap, in_ap, func, **kw), reads=rd, writes=wr)

    def tt(self, out_ap, a_ap, b_ap, aop, rd, wr, eng='v'):
        return self.op(eng, lambda e: e.tensor_tensor(out_ap, a_ap, b_ap, aop), reads=rd, writes=wr)

    def ts(self, out_ap, a_ap, s1, s2, op0, op1, rd, wr, eng='v', **kw):
        if s2 is None:
            return self.op(eng, lambda e: e.tensor_scalar(out_ap, a_ap, s1, None, op0, **kw), reads=rd, writes=wr)
        return self.op(eng, lambda e: e.tensor_scalar(out_ap, a_ap, s1, s2, op0, op1, **kw), reads=rd, writes=wr)

    def stt(self, out_ap, in0_ap, scalar, in1_ap, op0, op1, rd, wr, eng='v'):
        return self.op(eng, lambda e: e.scalar_tensor_tensor(out_ap, in0_ap, scalar, in1_ap, op0, op1), reads=rd, writes=wr)

    def copy(self, out_ap, in_ap, rd, wr, eng='v'):
        return self.op(eng, lambda e: e.tensor_copy(out_ap, in_ap), reads=rd, writes=wr)

    def memset(self, ap, val, rd, wr, eng='v'):
        return self.op(eng, lambda e: e.memset(ap, val), reads=rd, writes=wr)


N_TOK, D = 8192, 2048
NCORE = 8
SEG = N_TOK // NCORE
NT = SEG // 128
ND = D // 128
H, HD, CH = 16, 128, 128
NCH = SEG // CH
NHID = 64
TC = 64
NCHK = SEG // TC
DS = 64
NE, FM, FS = 8, 1408, 5632
NOC_M = FM // 128
SERIALIZE_COLL = False
GU_SH = 2 * 4 * D // NCORE
P_SH = D // NCORE


def _slopes(n_heads):
    start = 2.0 ** (-2.0 ** (-(math.log2(n_heads) - 3)))
    return np.array([start ** (i + 1) for i in range(n_heads)], dtype=np.float32)

LAM = np.exp(-_slopes(H)).astype(np.float32)
POS = np.arange(CH, dtype=np.float32)
DECAY_Q = (LAM[:, None] ** (POS + 1.0)).astype(np.float32)
DECAY_K = (LAM[:, None] ** (CH - 1.0 - POS)).astype(np.float32)
_diff = POS[:, None] - POS[None, :]
DMASK = np.where(_diff >= 0, LAM[:, None, None] ** np.maximum(_diff, 0.0), 0.0).astype(np.float32)
LAM_C = (LAM ** CH).astype(np.float32)
LAM_SEG = (LAM ** SEG).astype(np.float32)


def host_tables():
    t = {}
    t['ident_bf'] = np.eye(128, dtype=BF)
    t['ident_f32'] = np.eye(128, dtype=np.float32)
    t['dq_b'] = np.broadcast_to(DECAY_Q[:, None, :], (H, 128, 128)).copy()
    t['dk_col'] = DECAY_K.T.copy()
    t['dmaskT'] = DMASK.transpose(0, 2, 1).copy()
    idx = np.arange(TC)
    t['tli_f32'] = (idx[:, None] <= idx[None, :]).astype(np.float32)
    t['strilneg_bf'] = np.where(idx[:, None] > idx[None, :], -1.0, 0.0).astype(BF)
    t['striuneg_bf'] = np.where(idx[:, None] < idx[None, :], -1.0, 0.0).astype(BF)
    t['striu_bf'] = (idx[:, None] < idx[None, :]).astype(BF)
    t['striuincl_bf'] = (idx[:, None] <= idx[None, :]).astype(BF)
    t['ones64c_f32'] = np.ones((TC, 1), np.float32)
    t['ones64r_f32'] = np.ones((1, TC), np.float32)
    t['ones128c_bf'] = np.ones((128, 1), BF)
    t['ones1r_f32'] = np.ones((1, 128), np.float32)
    gmask = np.zeros((128, 2), np.float32); gmask[0:64, 0] = 1; gmask[64:128, 1] = 1
    t['gmask'] = gmask.astype(BF)
    t['gmask2f'] = gmask.T.copy()
    t['gmask2b'] = gmask.T.astype(BF)
    return t


def emit_ln_tile(k, x_ap, x_buf, out_ap, out_buf, stats, eps=1e-5, width=D):
    nblk = (width + 511) // 512
    ntrip = 2 * nblk
    for i in range(nblk):
        w = min(512, width - i*512)
        k.op('v', lambda e, i=i, w=w: e.bn_stats(stats[:, 6*i:6*i+6], x_ap[:, i*512:i*512+w]),
             reads=[x_buf], writes=[stats])
    sv = stats.h[:, 0:6*nblk].rearrange("p (n t) -> p t n", t=3)
    mview, m2view = sv[:, 1, :], sv[:, 2, :]
    k.op('v', lambda e: e.reduce_sum(stats[:, 24:25], mview, AX.X), reads=[stats], writes=[stats])
    k.op('v', lambda e: e.reduce_sum(stats[:, 25:26], m2view, AX.X), reads=[stats], writes=[stats])
    k.op('v', lambda e: e.tensor_tensor(stats[:, 32:32+ntrip], mview, mview, AluOpType.mult),
         reads=[stats], writes=[stats])
    k.op('v', lambda e: e.reduce_sum(stats[:, 26:27], stats[:, 32:32+ntrip], AX.X),
         reads=[stats], writes=[stats])
    k.ts(stats[:, 27:28], stats[:, 24:25], 1.0/ntrip, None, AluOpType.mult, None, [stats], [stats])
    k.ts(stats[:, 28:29], stats[:, 25:26], 1.0/width, None, AluOpType.mult, None, [stats], [stats])
    k.stt(stats[:, 28:29], stats[:, 26:27], 1.0/ntrip, stats[:, 28:29], AluOpType.mult, AluOpType.add,
          [stats], [stats])
    k.tt(stats[:, 30:31], stats[:, 27:28], stats[:, 27:28], AluOpType.mult, [stats], [stats])
    k.tt(stats[:, 28:29], stats[:, 28:29], stats[:, 30:31], AluOpType.subtract, [stats], [stats])
    k.act(stats[:, 26:27], stats[:, 28:29], AF.Ln, [stats], [stats], bias=float(eps))
    k.act(stats[:, 31:32], stats[:, 26:27], AF.Exp, [stats], [stats], scale=-0.5)
    k.stt(stats[:, 30:31], stats[:, 27:28], -1.0, stats[:, 31:32], AluOpType.mult, AluOpType.mult,
          [stats], [stats])
    k.ts(out_ap, x_ap, stats[:, 31:32], stats[:, 30:31], AluOpType.mult, AluOpType.add,
         [x_buf, stats], [out_buf])


def build_fused(debug=False, debug2=False):
    k = K()
    x_in = k.dram_in("x_seg", [SEG, D], F32)
    wqkvT = k.dram_in("wqkvT", [D, 3*D], F32)
    bqkv_cols = k.dram_in("bqkv_cols", [128, 3*ND], F32)
    bv_b = k.dram_in("bv_b", [128, D], F32)
    woT = k.dram_in("woT", [D, D], F32)
    bo_b = k.dram_in("bo_b", [128, D], F32)
    sw_in = k.dram_in("sw", [128, NCORE*H], F32)
    wgu_in = k.dram_in("wguT_sh", [D, GU_SH], F32)
    wp_in = k.dram_in("wpT_sh", [4*D, P_SH], F32)
    wkvqaT_in = k.dram_in("wkvqaT", [D, 4*DS], F32)
    wab_in = k.dram_in("wa_b", [DS, 1], F32)
    lam_in = k.dram_in("lam", [DS, 1], F32)
    dnsel_in = k.dram_in("dnsel", [64, 16], F32)
    nwb_in = k.dram_in("nwb", [64, NCHK*64], F32)
    wo1T_in = k.dram_in("wo1T", [DS, D], F32)
    gw_in = k.dram_in("gate9T", [D, 16], F32)
    efc1_in = k.dram_in("efc1T", [NE, D, FM], F32)
    efc2_in = k.dram_in("efc2T", [NE, D, FM], F32)
    eprj_in = k.dram_in("eprjT", [NE, FM, D], F32)
    sfc1_in = k.dram_in("sfc1T", [D, FS], F32)
    sfc2_in = k.dram_in("sfc2T", [D, FS], F32)
    sprj_in = k.dram_in("sprjT", [FS, D], F32)
    ident = k.dram_in("ident_bf", [128, 128], BF16)
    identf_in = k.dram_in("ident_f32", [128, 128], F32)
    dq_b = k.dram_in("dq_b", [H, 128, 128], F32)
    dk_col = k.dram_in("dk_col", [128, H], F32)
    dmaskT = k.dram_in("dmaskT", [H, 128, 128], F32)
    tli_in = k.dram_in("tli_f32", [TC, TC], F32)
    sln_in = k.dram_in("strilneg_bf", [TC, TC], BF16)
    sun_in = k.dram_in("striuneg_bf", [TC, TC], BF16)
    su_in = k.dram_in("striu_bf", [TC, TC], BF16)
    sui_in = k.dram_in("striuincl_bf", [TC, TC], BF16)
    oc_in = k.dram_in("ones64c_f32", [64, 1], F32)
    or_in = k.dram_in("ones64r_f32", [1, 64], F32)
    o128c_in = k.dram_in("ones128c_bf", [128, 1], BF16)
    o1r_in = k.dram_in("ones1r_f32", [1, 128], F32)
    gm_in = k.dram_in("gmask", [128, 2], BF16)
    gm2f_in = k.dram_in("gmask2f", [2, 128], F32)
    gm2b_in = k.dram_in("gmask2b", [2, 128], BF16)
    out_out = k.dram_out("out_seg", [SEG, D], F32)
    x1_d = k.dram_tmp("x1_d", [SEG, D], F32)
    x2_d = k.dram_tmp("x2_d", [SEG, D], F32)
    x3_d = k.dram_tmp("x3_d", [SEG, D], F32)
    qgu_sh = k.dram_tmp("qgu_sh", [D, GU_SH], BF16)
    qgu_g = k.dram_tmp("qgu_g", [NCORE*D, GU_SH], BF16)
    qp_sh = k.dram_tmp("qp_sh", [4*D, P_SH], BF16)
    qp_g = k.dram_tmp("qp_g", [NCORE*4*D, P_SH], BF16)
    hd_d = k.dram_tmp("hd_d", [128, NHID*SEG], BF16)
    sseg_l = k.dram_tmp("sseg_l", [H*128, 128], F32)
    sseg_g = k.dram_tmp("sseg_g", [NCORE*H*128, 128], F32)
    dn_l = k.dram_tmp("dn_l", [128, 64], F32)
    dn_g = k.dram_tmp("dn_g", [NCORE*128, 64], F32)

    k.begin(n_phases=12)

    ident_s = k.sbuf(k.stack, "ident_s", [128, 128], BF16)
    identf = k.sbuf(k.stack, "identf", [128, 128], F32)
    stats = k.sbuf(k.stack, "stats", [128, 48], F32)
    k.dma('qs', ident_s.ap, ident.ap, [ident], [ident_s])
    k.dma('qs', identf.ap, identf_in.ap, [identf_in], [identf])

    with ExitStack() as sc0:
        qf = k.sbuf(sc0, "a_qf", [128, ND*SEG], BF16)
        Sst = k.sbuf(sc0, "a_Sst", [128, H*128], F32)
        attn_bf = k.sbuf(sc0, "a_attn", [128, ND*SEG], BF16)

        with ExitStack() as scKV:
            kf = k.sbuf(scKV, "a_kf", [128, ND*SEG], BF16)
            vt = k.sbuf(scKV, "a_vt", [128, NT*D], BF16)

            with ExitStack() as ph:
                xt = [k.sbuf(ph, f"a_xt{i}", [128, D], F32) for i in range(2)]
                stage = k.sbuf(ph, "a_stage", [128, D], BF16)
                ln1f = k.sbuf(ph, "a_ln1f", [128, ND*SEG], BF16)
                wb = [k.sbuf(ph, f"a_wb{i}", [128, ND*128], BF16) for i in range(2)]
                bqk_s = k.sbuf(ph, "a_bqk", [128, 3*ND], F32)
                bvb_s = k.sbuf(ph, "a_bvb", [128, D], BF16)
                psA = k.psum(ph, "a_psA", [128, 512], F32)
                psB = k.psum(ph, "a_psB", [128, 512], F32)
                psT = k.psum(ph, "a_psT", [128, 1024], BF16)

                k.dma('qs', bqk_s.ap, bqkv_cols.ap, [bqkv_cols], [bqk_s])
                k.dma('qg', bvb_s.ap, bv_b.ap, [bv_b], [bvb_s])
                if debug2:
                    stA = k.sbuf(ph, "a_stA", [128, 48*NT], F32)
                xr = x_in.h.rearrange("(n p) d -> n p d", p=128)
                for i in range(NT):
                    k.dma('qs', xt[i % 2].ap, xr[i], [x_in], [xt[i % 2]])
                    if debug2:
                        stv = Buf(stA.h[:, 48*i:48*(i+1)], f"stA{i}")
                        k.bufs.append(stv)
                        emit_ln_tile(k, xt[i % 2].ap, xt[i % 2], stage.ap, stage, stv)
                    else:
                        emit_ln_tile(k, xt[i % 2].ap, xt[i % 2], stage.ap, stage, stats)
                    for dt in range(ND):
                        k.trans(psT[:, 0:128], stage[:, dt*128:(dt+1)*128], ident_s.ap, [stage, ident_s], [psT])
                        k.copy(ln1f[:, dt*SEG + i*128: dt*SEG + (i+1)*128], psT[:, 0:128], [psT], [ln1f])
                wr = wqkvT.h.rearrange("(t p) c -> p t c", p=128)
                for oc in range(2*ND):
                    wbuf = wb[oc % 2]
                    k.dma('qg', wbuf.h[:, 0:ND*128].rearrange("p (t c) -> p t c", c=128),
                          wr[:, :, oc*128:(oc+1)*128], [wqkvT], [wbuf])
                    dst = qf if oc < ND else kf
                    blk = oc % ND
                    for n in range(2):
                        ps = psA if n == 0 else psB
                        for dt in range(ND):
                            k.mm(ps[:, 0:512], wbuf[:, dt*128:(dt+1)*128],
                                 ln1f[:, dt*SEG + n*512: dt*SEG + (n+1)*512],
                                 [wbuf, ln1f], [ps], start=(dt == 0), stop=(dt == ND-1))
                        k.act(dst[:, blk*SEG + n*512: blk*SEG + (n+1)*512], ps[:, 0:512], AF.Identity,
                              [ps, bqk_s], [dst], bias=bqk_s[:, oc:oc+1])
                for vc in range(ND):
                    wbuf = wb[vc % 2]
                    k.dma('qg', wbuf.h[:, 0:ND*128].rearrange("p (t c) -> p t c", c=128),
                          wr[:, :, 2*D + vc*128: 2*D + (vc+1)*128], [wqkvT], [wbuf])
                    for ti in range(NT):
                        ps = psA if ti % 2 == 0 else psB
                        for dt in range(ND):
                            k.mm(ps[:, 0:128], ln1f[:, dt*SEG + ti*128: dt*SEG + (ti+1)*128],
                                 wbuf[:, dt*128:(dt+1)*128],
                                 [wbuf, ln1f], [ps], start=(dt == 0), stop=(dt == ND-1))
                        k.tt(vt[:, ti*D + vc*128: ti*D + (vc+1)*128], ps[:, 0:128],
                             bvb_s[:, vc*128:(vc+1)*128], AluOpType.add, [ps, bvb_s], [vt])
                if debug2:
                    dqa = k.dram_out("dbg_qfA1", [128, ND*SEG], BF16)
                    dlf = k.dram_out("dbg_ln1f", [128, ND*SEG], BF16)
                    dst_ = k.dram_out("dbg_stats", [128, 48*NT], F32)
                    k.dma('qs', dqa.h, qf.ap, [qf], [dqa])
                    k.dma('qs', dlf.h, ln1f.ap, [ln1f], [dlf])
                    k.dma('qs', dst_.h, stA.ap, [stA], [dst_])
                k.phase_boundary()

            with ExitStack() as ph:
                kdt = k.sbuf(ph, "a_kdt", [128, H*NCH*128], BF16)
                Sbf = k.sbuf(ph, "a_Sbf", [128, H*128], BF16)
                dqb_s = k.sbuf(ph, "a_dqb", [128, H*128], BF16)
                dkc_s = k.sbuf(ph, "a_dkc", [128, H], F32)
                dmk_s = k.sbuf(ph, "a_dmk", [128, H*128], BF16)
                qd_c = [k.sbuf(ph, f"a_qdc{i}", [128, 128], BF16) for i in range(2)]
                sc_c = [k.sbuf(ph, f"a_scc{i}", [128, 128], BF16) for i in range(2)]
                psSC = [k.psum(ph, "a_psSC0", [128, 512], F32)] * 2
                psO = [k.psum(ph, "a_psO0", [128, 512], F32)] * 2
                psSU = [k.psum(ph, "a_psSU0", [128, 512], F32)] * 2
                psT2 = [k.psum(ph, f"a_psT2{i}", [128, 1024], BF16) for i in range(2)]

                k.dma('qg', dqb_s.h[:, :].rearrange("p (h c) -> p h c", c=128),
                      dq_b.h.rearrange("h p c -> p h c"), [dq_b], [dqb_s])
                k.dma('qs', dkc_s.ap, dk_col.ap, [dk_col], [dkc_s])
                k.dma('qg', dmk_s.h[:, :].rearrange("p (h c) -> p h c", c=128),
                      dmaskT.h.rearrange("h p c -> p h c"), [dmaskT], [dmk_s])
                for h in range(H):
                    for c in range(NCH):
                        pt = psT2[(h*NCH+c) % 2]
                        k.trans(pt[:, 0:128], kf[:, h*SEG + c*128: h*SEG + (c+1)*128], ident_s.ap,
                                [kf, ident_s], [pt])
                        k.act(kdt[:, (h*NCH+c)*128:(h*NCH+c+1)*128], pt[:, 0:128], AF.Copy,
                              [pt, dkc_s], [kdt], scale=dkc_s[:, h:h+1])
                for c in range(NCH):
                    for h in range(H):
                        par = h % 2
                        k.mm(psSC[par][:, 0:128], kf[:, h*SEG + c*128: h*SEG + (c+1)*128],
                             qf[:, h*SEG + c*128: h*SEG + (c+1)*128], [kf, qf], [psSC[par]])
                        k.tt(sc_c[par].ap, psSC[par][:, 0:128], dmk_s[:, h*128:(h+1)*128], AluOpType.mult,
                             [psSC[par], dmk_s], [sc_c[par]])
                        k.tt(qd_c[par].ap, qf[:, h*SEG + c*128: h*SEG + (c+1)*128], dqb_s[:, h*128:(h+1)*128],
                             AluOpType.mult, [qf, dqb_s], [qd_c[par]])
                        k.mm(psO[par][:, 0:128], vt[:, c*D + h*128: c*D + (h+1)*128], sc_c[par].ap,
                             [vt, sc_c[par]], [psO[par]], start=True, stop=(c == 0))
                        if c > 0:
                            k.mm(psO[par][:, 0:128], Sbf[:, h*128:(h+1)*128], qd_c[par].ap,
                                 [Sbf, qd_c[par]], [psO[par]], start=False, stop=True)
                        k.copy(attn_bf[:, h*SEG + c*128:h*SEG + (c+1)*128], psO[par][:, 0:128],
                               [psO[par]], [attn_bf])
                        k.mm(psSU[par][:, 0:128], kdt[:, (h*NCH+c)*128:(h*NCH+c+1)*128],
                             vt[:, c*D + h*128: c*D + (h+1)*128], [kdt, vt], [psSU[par]])
                        if c == 0:
                            k.copy(Sst[:, h*128:(h+1)*128], psSU[par][:, 0:128], [psSU[par]], [Sst])
                        else:
                            k.stt(Sst[:, h*128:(h+1)*128], Sst[:, h*128:(h+1)*128], float(LAM_C[h]),
                                  psSU[par][:, 0:128], AluOpType.mult, AluOpType.add, [psSU[par], Sst], [Sst])
                        if c < NCH - 1:
                            k.copy(Sbf[:, h*128:(h+1)*128], Sst[:, h*128:(h+1)*128], [Sst], [Sbf])
                k.dma('qs', sseg_l.h.rearrange("(h p) c -> p h c", p=128),
                      Sst.h[:, :].rearrange("p (h c) -> p h c", c=128), [Sst], [sseg_l])
                k.collective(sseg_l.h, sseg_g.h, [sseg_l], sseg_g)
                if debug2:
                    dq = k.dram_out("dbg_qf", [128, ND*SEG], BF16)
                    dk = k.dram_out("dbg_kf", [128, ND*SEG], BF16)
                    dv = k.dram_out("dbg_vt", [128, NT*D], BF16)
                    da = k.dram_out("dbg_attnA", [128, ND*SEG], BF16)
                    dS = k.dram_out("dbg_Sst", [128, H*128], F32)
                    k.dma('qs', dq.h, qf.ap, [qf], [dq])
                    k.dma('qs', dk.h, kf.ap, [kf], [dk])
                    k.dma('qs', dv.h, vt.ap, [vt], [dv])
                    k.dma('qs', da.h, attn_bf.ap, [attn_bf], [da])
                    k.dma('qs', dS.h, Sst.ap, [Sst], [dS])
                k.phase_boundary()

        with ExitStack() as bo:
            wob = [k.sbuf(bo, f"b_wob{i}", [128, ND*512], BF16) for i in range(4)]
            bob_s = k.sbuf(bo, "b_bob", [128, D], BF16)
            xt = [k.sbuf(bo, f"b_xt{i}", [128, D], F32) for i in range(2)]
            o_t = k.sbuf(bo, "b_ot", [128, D], F32)
            x1_t = k.sbuf(bo, "b_x1t", [128, D], F32)
            k.dma('qg', bob_s.ap, bo_b.ap, [bo_b], [bob_s])
            wr = woT.h.rearrange("(t p) c -> p t c", p=128)
            for nc_ in range(4):
                k.dma('qg', wob[nc_].h[:, :].rearrange("p (t c) -> p t c", c=512),
                      wr[:, :, nc_*512:(nc_+1)*512], [woT], [wob[nc_]])

            with ExitStack() as ph:
                sseg_h = [k.sbuf(ph, "b_sseg0", [128, NCORE*128], F32)] * 2
                S_in = k.sbuf(ph, "b_Sin", [128, H*128], F32)
                S_bf = [k.sbuf(ph, f"b_Sbf{i}", [128, 128], BF16) for i in range(2)]
                qd_c = [k.sbuf(ph, f"b_qdc{i}", [128, 128], BF16) for i in range(2)]
                sw_s = k.sbuf(ph, "b_sw", [128, NCORE*H], F32)
                dqb_s = k.sbuf(ph, "b_dqb", [128, H*128], BF16)
                psC = [k.psum(ph, f"b_psC{i}", [128, 512], F32) for i in range(2)]
                k.dma('qs', sw_s.ap, sw_in.ap, [sw_in], [sw_s])
                k.dma('qg', dqb_s.h[:, :].rearrange("p (h c) -> p h c", c=128),
                      dq_b.h.rearrange("h p c -> p h c"), [dq_b], [dqb_s])
                ssegr = sseg_g.h.rearrange("(r h p) c -> h p r c", r=NCORE, p=128)
                for h in range(H):
                    sh = sseg_h[h % 2]
                    k.dma('qs', sh.h[:, :].rearrange("p (j c) -> p j c", c=128),
                          ssegr[h], [sseg_g], [sh])
                    k.memset(S_in[:, h*128:(h+1)*128], 0.0, [], [S_in])
                    for j in range(NCORE - 1):
                        k.stt(S_in[:, h*128:(h+1)*128], sh[:, j*128:(j+1)*128], sw_s[:, j*H+h:j*H+h+1],
                              S_in[:, h*128:(h+1)*128], AluOpType.mult, AluOpType.add,
                              [sh, sw_s, S_in], [S_in])
                for c in range(NCH):
                    for h in range(H):
                        par = h % 2
                        k.ts(S_bf[par].ap, S_in[:, h*128:(h+1)*128], float(LAM_C[h] ** c), None,
                             AluOpType.mult, None, [S_in], [S_bf[par]])
                        k.tt(qd_c[par].ap, qf[:, h*SEG + c*128:h*SEG + (c+1)*128], dqb_s[:, h*128:(h+1)*128],
                             AluOpType.mult, [qf, dqb_s], [qd_c[par]])
                        k.mm(psC[par][:, 0:128], S_bf[par].ap, qd_c[par].ap, [S_bf[par], qd_c[par]], [psC[par]])
                        k.tt(attn_bf[:, h*SEG + c*128:h*SEG + (c+1)*128],
                             attn_bf[:, h*SEG + c*128:h*SEG + (c+1)*128],
                             psC[par][:, 0:128], AluOpType.add, [psC[par]], [attn_bf])
                k.phase_boundary()

            with ExitStack() as ph:
                psO1 = k.psum(ph, "b_psO1", [128, 512], F32)
                psO2 = k.psum(ph, "b_psO2", [128, 512], F32)
                psBQG = k.psum(ph, "b_psQG", [128, 512], F32)
                psBQR = k.psum(ph, "b_psQR", [128, 512], F32)
                psBQS = k.psum(ph, "b_psQS", [128, 512], F32)
                FQ = 1024
                wt = [k.sbuf(ph, f"q_wt{i}", [128, FQ], BF16) for i in range(2)]
                sg = k.sbuf(ph, "q_sg", [128, FQ], BF16)
                wabs = k.sbuf(ph, "q_wabs", [128, FQ], BF16)
                s_bf = k.sbuf(ph, "q_sbf", [2, FQ], BF16)
                rs = k.sbuf(ph, "q_rs", [2, FQ], F32)
                z = k.sbuf(ph, "q_z", [128, FQ], BF16)
                m = k.sbuf(ph, "q_m", [128, FQ], BF16)
                w1 = k.sbuf(ph, "q_w1", [128, FQ], BF16)
                wq = [k.sbuf(ph, f"q_wq{i}", [128, FQ], BF16) for i in range(2)]
                gm = k.sbuf(ph, "q_gm", [128, 2], BF16)
                gm2f = k.sbuf(ph, "q_gm2f", [2, 128], F32)
                gm2b = k.sbuf(ph, "q_gm2b", [2, 128], BF16)
                k.dma('qs', gm.ap, gm_in.ap, [gm_in], [gm])
                k.dma('qs', gm2f.ap, gm2f_in.ap, [gm2f_in], [gm2f])
                k.dma('qs', gm2b.ap, gm2b_in.ap, [gm2b_in], [gm2b])

                def quant_tile(src_ap, src_buf, dst_ap, dst_buf, F, idx):
                    wtb, wqb = wt[idx % 2], wq[idx % 2]
                    k.dma('qg', wtb[:, 0:F], src_ap, [src_buf], [wtb])
                    k.act(sg[:, 0:F], wtb[:, 0:F], AF.Sign, [wtb], [sg])
                    k.act(wabs[:, 0:F], wtb[:, 0:F], AF.Abs, [wtb], [wabs])
                    for f5 in range(0, F, 512):
                        w5 = min(512, F - f5)
                        k.mm(psBQG[0:2, f5 % 512:f5 % 512 + w5], gm.ap, wabs[:, f5:f5+w5],
                             [gm, wabs], [psBQG]) if False else None
                    for f5 in range(0, F, 512):
                        w5 = min(512, F - f5)
                        k.mm(psBQG[0:2, 0:w5], gm.ap, wabs[:, f5:f5+w5], [gm, wabs], [psBQG])
                        k.ts(s_bf[:, f5:f5+w5], psBQG[0:2, 0:w5], 1.0/64.0, 1e-8,
                             AluOpType.mult, AluOpType.max, [psBQG], [s_bf])
                    k.op('v', lambda e, F=F: e.reciprocal(rs[:, 0:F], s_bf[:, 0:F]), reads=[s_bf], writes=[rs])
                    for f5 in range(0, F, 512):
                        w5 = min(512, F - f5)
                        k.mm(psBQR[:, 0:w5], gm2f.ap, rs[:, f5:f5+w5], [gm2f, rs], [psBQR])
                        k.mm(psBQS[:, 0:w5], gm2b.ap, s_bf[:, f5:f5+w5], [gm2b, s_bf], [psBQS])
                        k.tt(z[:, f5:f5+w5], wabs[:, f5:f5+w5], psBQR[:, 0:w5], AluOpType.mult,
                             [wabs, psBQR], [z])
                        k.ts(m[:, f5:f5+w5], z[:, f5:f5+w5], 0.5, None, AluOpType.is_gt, None, [z], [m])
                        k.tt(w1[:, f5:f5+w5], m[:, f5:f5+w5], psBQS[:, 0:w5], AluOpType.mult,
                             [m, psBQS], [w1])
                    k.tt(wqb[:, 0:F], w1[:, 0:F], sg[:, 0:F], AluOpType.mult, [w1, sg], [wqb])
                    k.dma('qs', dst_ap, wqb[:, 0:F], [wqb], [dst_buf])

                wgur = wgu_in.h.rearrange("(t p) c -> t p c", p=128)
                qgur_o = qgu_sh.h.rearrange("(t p) c -> t p c", p=128)
                wpr = wp_in.h.rearrange("(t p) c -> t p c", p=128)
                qpr_o = qp_sh.h.rearrange("(t p) c -> t p c", p=128)
                gu_thunks = []
                idx = 0
                for t in range(D // 128):
                    for f in range(GU_SH // FQ):
                        gu_thunks.append((wgur[t, :, f*FQ:(f+1)*FQ], wgu_in,
                                          qgur_o[t, :, f*FQ:(f+1)*FQ], qgu_sh, FQ, idx)); idx += 1
                wp_thunks = []
                for t in range(4*D // 128):
                    wp_thunks.append((wpr[t, :, 0:P_SH], wp_in, qpr_o[t, :, 0:P_SH], qp_sh, P_SH, idx)); idx += 1

                xr = x_in.h.rearrange("(n p) d -> n p d", p=128)
                x1r = x1_d.h.rearrange("(n p) d -> n p d", p=128)
                NGU = len(gu_thunks) // 4
                NWP = len(wp_thunks) // 4
                for tg in range(NT):
                    k.dma('qs', xt[tg % 2].ap, xr[tg], [x_in], [xt[tg % 2]])
                    for nc_ in range(4):
                        ps = psO1 if nc_ % 2 == 0 else psO2
                        wbuf = wob[nc_]
                        for et in range(ND):
                            k.mm(ps[:, 0:512], attn_bf[:, et*SEG + tg*128: et*SEG + (tg+1)*128],
                                 wbuf[:, et*512:(et+1)*512], [attn_bf, wbuf], [ps],
                                 start=(et == 0), stop=(et == ND-1))
                        k.tt(o_t[:, nc_*512:(nc_+1)*512], ps[:, 0:512], bob_s[:, nc_*512:(nc_+1)*512],
                             AluOpType.add, [ps, bob_s], [o_t])
                    emit_ln_tile(k, o_t.ap, o_t, x1_t.ap, x1_t, stats)
                    k.tt(x1_t.ap, x1_t.ap, xt[tg % 2].ap, AluOpType.add, [x1_t, xt[tg % 2]], [x1_t])
                    k.dma('qs', x1r[tg], x1_t.ap, [x1_t], [x1_d])
                    if tg < 4:
                        for qi in range(tg*NGU, (tg+1)*NGU):
                            quant_tile(*gu_thunks[qi])
                        if tg == 3:
                            k.collective(qgu_sh.h, qgu_g.h, [qgu_sh], qgu_g)
                    else:
                        for qi in range((tg-4)*NWP, (tg-3)*NWP):
                            quant_tile(*wp_thunks[qi])
                k.collective(qp_sh.h, qp_g.h, [qp_sh], qp_g)
                k.phase_boundary()

    with ExitStack() as dsc:
        o128c = k.sbuf(dsc, "d_o128c", [128, 1], BF16)
        o1r = k.sbuf(dsc, "d_o1r", [1, 128], F32)
        sqrow = k.sbuf(dsc, "d_sqrow", [1, SEG], F32)
        rstd = k.sbuf(dsc, "d_rstd", [1, SEG], F32)
        rstd_b = k.sbuf(dsc, "d_rstdb", [128, SEG], F32)
        k.dma('qs', o128c.ap, o128c_in.ap, [o128c_in], [o128c])
        k.dma('qs', o1r.ap, o1r_in.ap, [o1r_in], [o1r])
        xr1 = x1_d.h.rearrange("(n p) d -> n p d", p=128)

        with ExitStack() as ph:
            xt = [k.sbuf(ph, f"d_xt{i}", [128, D], F32) for i in range(2)]
            stage = k.sbuf(ph, "d_stage", [128, D], BF16)
            ln3f = k.sbuf(ph, "d_ln3f", [128, ND*SEG], BF16)
            hw = k.sbuf(ph, "d_hw", [128, SEG], BF16)
            hsq = k.sbuf(ph, "d_hsq", [128, SEG], BF16)
            wg = [k.sbuf(ph, f"d_wg{i}", [128, ND*128], BF16) for i in range(2)]
            wu = [k.sbuf(ph, f"d_wu{i}", [128, ND*128], BF16) for i in range(2)]
            gs = k.sbuf(ph, "d_gs", [128, SEG], BF16)
            us = k.sbuf(ph, "d_us", [128, SEG], BF16)
            psA = k.psum(ph, "d_psA", [128, 512], F32)
            psB = k.psum(ph, "d_psB", [128, 512], F32)
            psC = k.psum(ph, "d_psC", [128, 512], F32)
            psD = k.psum(ph, "d_psD", [128, 512], F32)
            psT = k.psum(ph, "d_psT", [128, 1024], BF16)
            psQ = k.psum(ph, "d_psQ", [128, 512], F32)

            k.memset(sqrow.ap, 0.0, [], [sqrow])
            for i in range(NT):
                k.dma('qs', xt[i % 2].ap, xr1[i], [x1_d], [xt[i % 2]])
                emit_ln_tile(k, xt[i % 2].ap, xt[i % 2], stage.ap, stage, stats)
                for dt in range(ND):
                    k.trans(psT[:, 0:128], stage[:, dt*128:(dt+1)*128], ident_s.ap, [stage, ident_s], [psT])
                    k.copy(ln3f[:, dt*SEG + i*128: dt*SEG + (i+1)*128], psT[:, 0:128], [psT], [ln3f])

            qgur = qgu_g.h.rearrange("(r t p) c -> r p t c", r=NCORE, p=128)
            NGC = GU_SH
            for hc in range(NHID):
                bg, bu = wg[hc % 2], wu[hc % 2]
                gcol = hc * 128
                r, lc = gcol // NGC, gcol % NGC
                k.dma('qg', bg.h[:, :].rearrange("p (t c) -> p t c", c=128),
                      qgur[r][:, :, lc:lc+128], [qgu_g], [bg])
                ucol = 4*D + hc * 128
                r2, lc2 = ucol // NGC, ucol % NGC
                k.dma('qg', bu.h[:, :].rearrange("p (t c) -> p t c", c=128),
                      qgur[r2][:, :, lc2:lc2+128], [qgu_g], [bu])
                for n in range(2):
                    psg = psA if n == 0 else psB
                    psu = psC if n == 0 else psD
                    for dt in range(ND):
                        k.mm(psg[:, 0:512], bg[:, dt*128:(dt+1)*128],
                             ln3f[:, dt*SEG + n*512: dt*SEG + (n+1)*512], [bg, ln3f], [psg],
                             start=(dt == 0), stop=(dt == ND-1))
                    for dt in range(ND):
                        k.mm(psu[:, 0:512], bu[:, dt*128:(dt+1)*128],
                             ln3f[:, dt*SEG + n*512: dt*SEG + (n+1)*512], [bu, ln3f], [psu],
                             start=(dt == 0), stop=(dt == ND-1))
                    k.act(gs[:, n*512:(n+1)*512], psg[:, 0:512], AF.Silu, [psg], [gs])
                    k.copy(us[:, n*512:(n+1)*512], psu[:, 0:512], [psu], [us])
                k.tt(hw.ap, gs.ap, us.ap, AluOpType.mult, [gs, us], [hw])
                k.dma('qs', hd_d.h[:, hc*SEG:(hc+1)*SEG], hw.ap, [hw], [hd_d])
                k.tt(hsq.ap, hw.ap, hw.ap, AluOpType.mult, [hw], [hsq])
                for n in range(2):
                    k.mm(psQ[0:1, 0:512], o128c.ap, hsq[:, n*512:(n+1)*512], [o128c, hsq], [psQ])
                    k.stt(sqrow[0:1, n*512:(n+1)*512], psQ[0:1, 0:512], 1.0,
                          sqrow[0:1, n*512:(n+1)*512], AluOpType.mult, AluOpType.add,
                          [psQ, sqrow], [sqrow])
            k.act(rstd[0:1, :], sqrow[0:1, :], AF.Ln, [sqrow], [rstd], scale=1.0/(4*D),
                  bias=1.1920929e-7)
            k.act(rstd[0:1, :], rstd[0:1, :], AF.Exp, [rstd], [rstd], scale=-0.5)
            for n in range(2):
                k.mm(psQ[:, 0:512], o1r.ap, rstd[0:1, n*512:(n+1)*512], [o1r, rstd], [psQ])
                k.copy(rstd_b[:, n*512:(n+1)*512], psQ[:, 0:512], [psQ], [rstd_b])
            k.phase_boundary()

        with ExitStack() as ph:
            xt = [k.sbuf(ph, f"d2_xt{i}", [128, D], F32) for i in range(2)]
            stage = k.sbuf(ph, "d2_stage", [128, D], BF16)
            hbf = k.sbuf(ph, "d2_hbf", [128, NHID*512], BF16)
            wp = [k.sbuf(ph, f"d2_wp{i}", [128, NHID*128], BF16) for i in range(2)]
            obf = k.sbuf(ph, "d2_obf", [128, NT*D], BF16)
            x2_t = k.sbuf(ph, "d2_x2t", [128, D], F32)
            psA = k.psum(ph, "d2_psA", [128, 512], F32)
            psB = k.psum(ph, "d2_psB", [128, 512], F32)
            psT = k.psum(ph, "d2_psT", [128, 1024], BF16)

            qpr = qp_g.h.rearrange("(r t p) c -> r p t c", r=NCORE, p=128)
            hr = hd_d.h[:, :].rearrange("p (n t) -> p n t", t=SEG)
            for th in range(2):
                k.dma('qs', hbf.h[:, :].rearrange("p (n t) -> p n t", t=512),
                      hr[:, :, th*512:(th+1)*512], [hd_d], [hbf])
                for dot in range(ND):
                    bp = wp[dot % 2]
                    pcol = dot * 128
                    r, lc = pcol // P_SH, pcol % P_SH
                    k.dma('qg', bp.h[:, :].rearrange("p (t c) -> p t c", c=128),
                          qpr[r][:, :, lc:lc+128], [qp_g], [bp])
                    ps = psA if dot % 2 == 0 else psB
                    for ht in range(NHID):
                        k.mm(ps[:, 0:512], bp[:, ht*128:(ht+1)*128],
                             hbf[:, ht*512:(ht+1)*512], [bp, hbf], [ps],
                             start=(ht == 0), stop=(ht == NHID-1))
                    k.tt(stage[:, 0:512], ps[:, 0:512], rstd_b[:, th*512:(th+1)*512], AluOpType.mult,
                         [ps, rstd_b], [stage])
                    for q4 in range(4):
                        tg = th*4 + q4
                        k.trans(psT[:, 0:128], stage[:, q4*128:(q4+1)*128], ident_s.ap,
                                [stage, ident_s], [psT])
                        k.copy(obf[:, tg*D + dot*128: tg*D + (dot+1)*128], psT[:, 0:128], [psT], [obf])

            x2r = x2_d.h.rearrange("(n p) d -> n p d", p=128)
            for tg in range(NT):
                k.dma('qs', xt[tg % 2].ap, xr1[tg], [x1_d], [xt[tg % 2]])
                emit_ln_tile(k, obf[:, tg*D:(tg+1)*D], obf, x2_t.ap, x2_t, stats)
                k.tt(x2_t.ap, x2_t.ap, xt[tg % 2].ap, AluOpType.add, [x2_t, xt[tg % 2]], [x2_t])
                k.dma('qs', x2r[tg], x2_t.ap, [x2_t], [x2_d])
            k.phase_boundary()

    with ExitStack() as sc1:
        kd = k.sbuf(sc1, "e_kd", [DS, SEG], BF16)
        vd = k.sbuf(sc1, "e_vd", [DS, SEG], BF16)
        qd = k.sbuf(sc1, "e_qd", [DS, SEG], BF16)
        lg = k.sbuf(sc1, "e_lg", [DS, SEG], F32)
        tli = k.sbuf(sc1, "n_tli", [TC, TC], F32)
        sln = k.sbuf(sc1, "n_sln", [TC, TC], BF16)
        sun = k.sbuf(sc1, "n_sun", [TC, TC], BF16)
        su = k.sbuf(sc1, "n_su", [TC, TC], BF16)
        sui = k.sbuf(sc1, "n_sui", [TC, TC], BF16)
        onesc = k.sbuf(sc1, "n_oc", [64, 1], F32)
        onesr = k.sbuf(sc1, "n_or", [1, 64], F32)
        for (d, s_) in ((tli_in, tli), (sln_in, sln), (sun_in, sun), (su_in, su),
                        (sui_in, sui), (oc_in, onesc), (or_in, onesr)):
            k.dma('qs', s_.ap, d.ap, [d], [s_])

        with ExitStack() as ph:
            xt = [k.sbuf(ph, f"e_xt{i}", [128, D], F32) for i in range(2)]
            stage = k.sbuf(ph, "e_stage", [128, D], BF16)
            lnf = k.sbuf(ph, "e_lnf", [128, ND*SEG], BF16)
            wsb = k.sbuf(ph, "e_wsb", [128, ND*4*DS], BF16)
            proj = k.sbuf(ph, "e_proj", [64, 4*SEG], F32)
            sqb = k.sbuf(ph, "e_sqb", [64, SEG], F32)
            nrow = k.sbuf(ph, "e_nrow", [1, SEG], F32)
            rrow = k.sbuf(ph, "e_rrow", [1, SEG], F32)
            wab_s = k.sbuf(ph, "e_wab", [64, 1], F32)
            lam_s = k.sbuf(ph, "e_lam", [64, 4], F32)
            ps1 = k.psum(ph, "n_ps1", [128, 512], F32)
            ps2 = k.psum(ph, "n_ps2", [128, 512], F32)
            ps3 = k.psum(ph, "n_ps3", [128, 512], F32)
            psF = k.psum(ph, "n_psF", [128, 512], F32)
            psTb = k.psum(ph, "n_psTb", [128, 1024], BF16)

            k.dma('qs', wab_s.ap, wab_in.ap, [wab_in], [wab_s])
            k.dma('qs', lam_s[:, 0:1], lam_in.ap, [lam_in], [lam_s])
            xr2 = x2_d.h.rearrange("(n p) d -> n p d", p=128)
            for i in range(NT):
                k.dma('qs', xt[i % 2].ap, xr2[i], [x2_d], [xt[i % 2]])
                emit_ln_tile(k, xt[i % 2].ap, xt[i % 2], stage.ap, stage, stats)
                for dt in range(ND):
                    k.trans(psTb[:, 0:128], stage[:, dt*128:(dt+1)*128], ident_s.ap, [stage, ident_s], [psTb])
                    k.copy(lnf[:, dt*SEG + i*128: dt*SEG + (i+1)*128], psTb[:, 0:128], [psTb], [lnf])
            k.dma('qg', wsb.h[:, :].rearrange("p (t c) -> p t c", c=4*DS),
                  wkvqaT_in.h.rearrange("(t p) c -> p t c", p=128), [wkvqaT_in], [wsb])
            for j in range(4):
                for n in range(2):
                    ps = ps1 if n == 0 else ps2
                    for dt in range(ND):
                        k.mm(ps[0:64, 0:512], wsb[:, dt*4*DS + j*DS: dt*4*DS + (j+1)*DS],
                             lnf[:, dt*SEG + n*512: dt*SEG + (n+1)*512], [wsb, lnf], [ps],
                             start=(dt == 0), stop=(dt == ND-1))
                    if j < 3:
                        k.copy(proj[:, j*SEG + n*512: j*SEG + (n+1)*512], ps[0:64, 0:512], [ps], [proj])
                    else:
                        k.act(proj[:, j*SEG + n*512: j*SEG + (n+1)*512], ps[0:64, 0:512], AF.Sigmoid,
                              [ps, wab_s], [proj], bias=wab_s.ap)
            for j, dst in ((0, kd), (2, qd)):
                k.act(sqb.ap, proj[:, j*SEG:(j+1)*SEG], AF.Square, [proj], [sqb])
                for n in range(2):
                    k.mm(ps3[0:1, 0:512], onesc.ap, sqb[:, n*512:(n+1)*512], [onesc, sqb], [ps3])
                    k.act(nrow[:, n*512:(n+1)*512], ps3[0:1, 0:512], AF.Sqrt, [ps3], [nrow])
                k.ts(nrow.ap, nrow.ap, 1e-12, None, AluOpType.max, None, [nrow], [nrow])
                k.op('v', lambda e: e.reciprocal(rrow.ap, nrow.ap), reads=[nrow], writes=[rrow])
                for n in range(2):
                    k.mm(ps3[0:64, 0:512], onesr.ap, rrow[:, n*512:(n+1)*512], [onesr, rrow], [ps3])
                    k.tt(dst[:, n*512:(n+1)*512], proj[:, j*SEG + n*512: j*SEG + (n+1)*512],
                         ps3[0:64, 0:512], AluOpType.mult, [proj, ps3], [dst])
            k.copy(vd.ap, proj[:, 1*SEG:2*SEG], [proj], [vd])
            k.act(lam_s[:, 1:2], lam_s[:, 0:1], AF.Sigmoid, [lam_s], [lam_s])
            k.act(lam_s[:, 2:3], lam_s[:, 1:2], AF.Ln, [lam_s], [lam_s], bias=1e-8)
            k.ts(lam_s[:, 3:4], lam_s[:, 2:3], 8.0, None, AluOpType.mult, None, [lam_s], [lam_s])
            k.ts(lg.ap, proj[:, 3*SEG:4*SEG], lam_s[:, 3:4], None, AluOpType.mult, None,
                 [proj, lam_s], [lg])

            vcA = k.sbuf(ph, "n_vcA", [TC, NCHK*64], BF16)
            ktTA = k.sbuf(ph, "n_ktTA", [TC, NCHK*64], BF16)
            qpTA = k.sbuf(ph, "n_qpTA", [TC, NCHK*64], BF16)
            qiTA = k.sbuf(ph, "n_qiTA", [TC, NCHK*64], BF16)
            DmA = k.sbuf(ph, "n_DmA", [TC, NCHK*64], BF16)
            NTA = k.sbuf(ph, "n_NTA", [TC, NCHK*64], BF16)
            TTA = k.sbuf(ph, "n_TTA", [TC, NCHK*64], BF16)
            GtA = k.sbuf(ph, "n_GtA", [TC, NCHK*64], BF16)
            GtiA = k.sbuf(ph, "n_GtiA", [TC, NCHK*64], BF16)
            McTA = k.sbuf(ph, "n_McTA", [TC, NCHK*64], BF16)
            AeCA = k.sbuf(ph, "n_AeCA", [TC, NCHK], F32)
            lgT = k.sbuf(ph, "n_lgT", [TC, 64], F32)
            ALi = k.sbuf(ph, "n_ALi", [TC, 64], F32)
            A_t = k.sbuf(ph, "n_At", [TC, 64], F32)
            Apv = k.sbuf(ph, "n_Apv", [TC, 64], F32)
            Ainv = k.sbuf(ph, "n_Ainv", [TC, 64], F32)
            a_t = k.sbuf(ph, "n_at", [TC, 64], F32)
            w1 = k.sbuf(ph, "n_w1", [TC, 64], F32)
            tmpf = k.sbuf(ph, "n_tmpf", [TC, 64], F32)
            kc = k.sbuf(ph, "n_kc", [TC, 64], BF16)
            qc = k.sbuf(ph, "n_qc", [TC, 64], BF16)
            kt_ = k.sbuf(ph, "n_kt", [TC, 64], BF16)
            qp_ = k.sbuf(ph, "n_qp", [TC, 64], BF16)
            qi_ = k.sbuf(ph, "n_qi", [TC, 64], BF16)
            Dm = k.sbuf(ph, "n_Dm", [TC, 64], BF16)
            DmT = k.sbuf(ph, "n_DmT", [64, TC], BF16)
            Aer = k.sbuf(ph, "n_Aer", [1, 64], F32)
            Wn = k.sbuf(ph, "n_Wn", [TC, 64], BF16)
            tmpb = k.sbuf(ph, "n_tmpb", [TC, TC], BF16)
            Eac = k.sbuf(ph, "n_Eac", [TC, TC], BF16)
            Npow = [k.sbuf(ph, f"n_N{j}", [TC, TC], BF16) for j in range(5)]
            NTpow = [k.sbuf(ph, f"n_NT{j}", [TC, TC], BF16) for j in range(6)]

            def tr(src_ap, src_buf, dst_ap, dst_buf, f32=False, pdim=64):
                if f32:
                    k.trans(psF[0:64, 0:pdim], src_ap, identf[0:pdim, 0:pdim], [src_buf, identf], [psF])
                    k.copy(dst_ap, psF[0:64, 0:pdim], [psF], [dst_buf])
                else:
                    k.trans(psTb[0:64, 0:pdim], src_ap, ident_s[0:pdim, 0:pdim], [src_buf, ident_s], [psTb])
                    k.copy(dst_ap, psTb[0:64, 0:pdim], [psTb], [dst_buf])

            def build_powers():
                for lev in range(1, 5):
                    k.mm(ps3[0:TC, 0:TC], NTpow[lev-1].ap, Npow[lev-1].ap, [NTpow[lev-1], Npow[lev-1]], [ps3])
                    k.copy(Npow[lev].ap, ps3[0:TC, 0:TC], [ps3], [Npow[lev]])
                    k.mm(ps2[0:TC, 0:TC], Npow[lev-1].ap, NTpow[lev-1].ap, [Npow[lev-1], NTpow[lev-1]], [ps2])
                    k.copy(NTpow[lev].ap, ps2[0:TC, 0:TC], [ps2], [NTpow[lev]])
                k.mm(ps3[0:TC, 0:TC], Npow[4].ap, NTpow[4].ap, [Npow[4], NTpow[4]], [ps3])
                k.copy(NTpow[5].ap, ps3[0:TC, 0:TC], [ps3], [NTpow[5]])

            for c in range(NCHK):
                cs = slice(c*TC, (c+1)*TC)
                cs64 = slice(c*64, (c+1)*64)
                tr(lg[:, cs], lg, lgT.ap, lgT, f32=True)
                k.mm(psF[0:TC, 0:64], tli.ap, lgT.ap, [tli, lgT], [psF])
                k.copy(ALi.ap, psF[0:TC, 0:64], [psF], [ALi])
                k.act(A_t.ap, ALi.ap, AF.Exp, [ALi], [A_t])
                k.tt(tmpf.ap, ALi.ap, lgT.ap, AluOpType.subtract, [ALi, lgT], [tmpf])
                k.act(Apv.ap, tmpf.ap, AF.Exp, [tmpf], [Apv])
                k.act(Ainv.ap, ALi.ap, AF.Exp, [ALi], [Ainv], scale=-1.0)
                k.act(a_t.ap, lgT.ap, AF.Exp, [lgT], [a_t])
                k.mm(psF[0:1, 0:64], onesc.ap, lgT.ap, [onesc, lgT], [psF])
                k.act(Aer.ap, psF[0:1, 0:64], AF.Exp, [psF], [Aer])
                k.trans(psF[0:64, 0:1], Aer.ap, identf[0:1, 0:1], [Aer, identf], [psF])
                k.copy(AeCA[:, c:c+1], psF[0:64, 0:1], [psF], [AeCA])
                tr(kd[:, cs], kd, kc.ap, kc)
                tr(vd[:, cs], vd, vcA[:, cs64], vcA)
                tr(qd[:, cs], qd, qc.ap, qc)
                k.tt(kt_.ap, kc.ap, Apv.ap, AluOpType.mult, [kc, Apv], [kt_])
                k.tt(qp_.ap, qc.ap, Apv.ap, AluOpType.mult, [qc, Apv], [qp_])
                k.tt(qi_.ap, qc.ap, A_t.ap, AluOpType.mult, [qc, A_t], [qi_])
                k.ts(tmpf.ap, a_t.ap, -1.0, 1.0, AluOpType.mult, AluOpType.add, [a_t], [tmpf])
                k.tt(w1.ap, tmpf.ap, Ainv.ap, AluOpType.mult, [tmpf, Ainv], [w1])
                k.tt(Dm.ap, kc.ap, w1.ap, AluOpType.mult, [kc, w1], [Dm])
                k.copy(DmA[:, cs64], Dm.ap, [Dm], [DmA])
                tr(Dm.ap, Dm, DmT.ap, DmT)
                tr(kt_.ap, kt_, ktTA[:, cs64], ktTA)
                tr(qp_.ap, qp_, qpTA[:, cs64], qpTA)
                tr(qi_.ap, qi_, qiTA[:, cs64], qiTA)
                k.mm(ps1[0:TC, 0:TC], ktTA[:, cs64], DmT.ap, [ktTA, DmT], [ps1])
                k.tt(Npow[0].ap, ps1[0:TC, 0:TC], sln.ap, AluOpType.mult, [ps1, sln], [Npow[0]])
                k.mm(ps2[0:TC, 0:TC], DmT.ap, ktTA[:, cs64], [DmT, ktTA], [ps2])
                k.tt(NTA[:, cs64], ps2[0:TC, 0:TC], sun.ap, AluOpType.mult, [ps2, sun], [NTA])
                k.copy(NTpow[0].ap, NTA[:, cs64], [NTA], [NTpow[0]])
                build_powers()
                k.copy(Eac.ap, ident_s[0:TC, 0:TC], [ident_s], [Eac])
                for j in range(6):
                    k.mm(ps3[0:TC, 0:TC], ident_s[0:TC, 0:TC], Eac.ap, [ident_s, Eac], [ps3],
                         start=True, stop=False)
                    k.mm(ps3[0:TC, 0:TC], NTpow[j].ap, Eac.ap, [NTpow[j], Eac], [ps3],
                         start=False, stop=True)
                    k.copy(Eac.ap, ps3[0:TC, 0:TC], [ps3], [Eac])
                tr(Eac.ap, Eac, TTA[:, cs64], TTA)
                k.mm(ps1[0:TC, 0:TC], DmT.ap, qpTA[:, cs64], [DmT, qpTA], [ps1])
                k.tt(GtA[:, cs64], ps1[0:TC, 0:TC], su.ap, AluOpType.mult, [ps1, su], [GtA])
                k.mm(ps1[0:TC, 0:TC], DmT.ap, qiTA[:, cs64], [DmT, qiTA], [ps1])
                k.tt(GtiA[:, cs64], ps1[0:TC, 0:TC], sui.ap, AluOpType.mult, [ps1, sui], [GtiA])
                k.mm(ps2[0:TC, 0:64], TTA[:, cs64], kt_.ap, [TTA, kt_], [ps2])
                k.ts(Wn.ap, ps2[0:TC, 0:64], -1.0, None, AluOpType.mult, None, [ps2], [Wn])
                k.mm(ps2[0:TC, 0:TC], ident_s[0:TC, 0:TC], ident_s[0:TC, 0:TC], [ident_s], [ps2],
                     start=True, stop=False)
                k.mm(ps2[0:TC, 0:TC], Wn.ap, Dm.ap, [Wn, Dm], [ps2], start=False, stop=True)
                k.copy(tmpb.ap, ps2[0:TC, 0:TC], [ps2], [tmpb])
                k.mm(psF[0:TC, 0:TC], onesr.ap, Aer.ap, [onesr, Aer], [psF])
                k.tt(McTA[:, cs64], tmpb.ap, psF[0:TC, 0:TC], AluOpType.mult, [tmpb, psF], [McTA])

            Hst = k.sbuf(ph, "n_Hst", [64, 64], F32)
            Hbf = k.sbuf(ph, "n_Hbf", [64, 64], BF16)
            Mrun = k.sbuf(ph, "n_Mrun", [TC, TC], BF16)
            MrunT = k.sbuf(ph, "n_MrunT", [TC, TC], BF16)
            X = k.sbuf(ph, "n_X", [TC, 64], BF16)
            Xb = k.sbuf(ph, "n_Xb", [TC, 64], BF16)
            Z = k.sbuf(ph, "n_Z", [TC, 64], BF16)
            NsT = k.sbuf(ph, "n_NsT", [TC, TC], BF16)
            d2 = k.sbuf(ph, "n_d2", [TC, 64], F32)
            sq2 = k.sbuf(ph, "n_sq2", [TC, 64], F32)
            scol = k.sbuf(ph, "n_scol", [TC, 1], F32)
            outf = k.sbuf(ph, "n_outf", [TC, TC], F32)
            k.memset(Hst.ap, 0.0, [], [Hst])
            k.memset(Hbf.ap, 0.0, [], [Hbf])
            k.copy(Mrun.ap, ident_s[0:TC, 0:TC], [ident_s], [Mrun])
            k.copy(MrunT.ap, ident_s[0:TC, 0:TC], [ident_s], [MrunT])
            for c in range(NCHK):
                cs64 = slice(c*64, (c+1)*64)
                k.mm(ps1[0:TC, 0:64], ktTA[:, cs64], Hbf.ap, [ktTA, Hbf], [ps1])
                k.tt(X.ap, vcA[:, cs64], ps1[0:TC, 0:64], AluOpType.subtract, [vcA, ps1], [X])
                k.mm(ps2[0:TC, 0:64], TTA[:, cs64], X.ap, [TTA, X], [ps2])
                k.copy(X.ap, ps2[0:TC, 0:64], [ps2], [X])
                k.mm(ps2[0:TC, 0:64], ident_s[0:TC, 0:TC], Hbf.ap, [ident_s, Hbf], [ps2],
                     start=True, stop=False)
                k.mm(ps2[0:TC, 0:64], DmA[:, cs64], X.ap, [DmA, X], [ps2], start=False, stop=True)
                k.ts(Hst.ap, ps2[0:TC, 0:64], AeCA[:, c:c+1], None, AluOpType.mult, None,
                     [ps2, AeCA], [Hst])
                k.copy(Hbf.ap, Hst.ap, [Hst], [Hbf])
                k.mm(ps1[0:TC, 0:TC], McTA[:, cs64], Mrun.ap, [McTA, Mrun], [ps1])
                k.mm(ps2[0:TC, 0:TC], Mrun.ap, McTA[:, cs64], [Mrun, McTA], [ps2])
                k.copy(Mrun.ap, ps1[0:TC, 0:TC], [ps1], [Mrun])
                k.copy(MrunT.ap, ps2[0:TC, 0:TC], [ps2], [MrunT])
            k.copy(outf.ap, MrunT.ap, [MrunT], [outf])
            k.dma('qs', dn_l.h[0:64, :], outf.ap, [outf], [dn_l])
            k.dma('qs', dn_l.h[64:128, :], Hst.ap, [Hst], [dn_l])
            k.collective(dn_l.h, dn_g.h, [dn_l], dn_g)

            dn_sb = k.sbuf(ph, "g_dnsb", [128, NCORE*64], F32)
            dnsel = k.sbuf(ph, "g_dnsel", [64, 16], F32)
            Mu_f = k.sbuf(ph, "g_Muf", [64, 64], F32)
            Mu_b = k.sbuf(ph, "g_Mub", [64, 64], BF16)
            bu_b = k.sbuf(ph, "g_bub", [64, 64], BF16)
            y_buf = k.sbuf(ph, "g_ybuf", [64, NCHK*64], F32)
            ynf = k.sbuf(ph, "g_ynf", [64, SEG], BF16)
            wos = k.sbuf(ph, "g_wos", [64, D], BF16)
            nwb = k.sbuf(ph, "g_nwb", [64, NCHK*64], F32)
            msq = k.sbuf(ph, "g_msq", [64, NCHK], F32)
            o_t = k.sbuf(ph, "g_ot", [128, D], F32)
            x3_t = k.sbuf(ph, "g_x3t", [128, D], F32)

            k.dma('qs', dnsel.ap, dnsel_in.ap, [dnsel_in], [dnsel])
            k.dma('qs', nwb.ap, nwb_in.ap, [nwb_in], [nwb])
            k.dma('qg', wos.ap, wo1T_in.ap, [wo1T_in], [wos])
            for r in range(NCORE):
                k.dma('qs', dn_sb[:, r*64:(r+1)*64], dn_g.h[r*128:(r+1)*128, :], [dn_g], [dn_sb])
            k.memset(Hst.ap, 0.0, [], [Hst])
            k.memset(Hbf.ap, 0.0, [], [Hbf])
            for j in range(NCORE - 1):
                k.ts(Mu_f.ap, dn_sb[0:64, j*64:(j+1)*64], dnsel[:, j:j+1], None,
                     AluOpType.mult, None, [dn_sb, dnsel], [Mu_f])
                k.stt(Mu_f.ap, identf[0:64, 0:64], dnsel[:, 8+j:9+j], Mu_f.ap,
                      AluOpType.mult, AluOpType.add, [identf, dnsel, Mu_f], [Mu_f])
                k.copy(Mu_b.ap, Mu_f.ap, [Mu_f], [Mu_b])
                k.ts(Mu_f.ap, dn_sb[64:128, j*64:(j+1)*64], dnsel[:, j:j+1], None,
                     AluOpType.mult, None, [dn_sb, dnsel], [Mu_f])
                k.copy(bu_b.ap, Mu_f.ap, [Mu_f], [bu_b])
                k.mm(ps1[0:TC, 0:64], Mu_b.ap, Hbf.ap, [Mu_b, Hbf], [ps1],
                     start=True, stop=False)
                k.mm(ps1[0:TC, 0:64], ident_s[0:TC, 0:TC], bu_b.ap, [ident_s, bu_b],
                     [ps1], start=False, stop=True)
                k.copy(Hst.ap, ps1[0:TC, 0:64], [ps1], [Hst])
                k.copy(Hbf.ap, Hst.ap, [Hst], [Hbf])

            for c in range(NCHK):
                cs64 = slice(c*64, (c+1)*64)
                k.mm(ps1[0:TC, 0:64], ktTA[:, cs64], Hbf.ap, [ktTA, Hbf], [ps1])
                k.tt(Xb.ap, vcA[:, cs64], ps1[0:TC, 0:64], AluOpType.subtract, [vcA, ps1], [Xb])
                k.mm(ps2[0:TC, 0:64], TTA[:, cs64], Xb.ap, [TTA, Xb], [ps2])
                k.copy(X.ap, ps2[0:TC, 0:64], [ps2], [X])
                k.mm(ps2[0:TC, 0:64], qpTA[:, cs64], Hbf.ap, [qpTA, Hbf], [ps2], start=True, stop=False)
                k.mm(ps2[0:TC, 0:64], GtA[:, cs64], X.ap, [GtA, X], [ps2], start=False, stop=True)
                k.tt(d2.ap, vcA[:, cs64], ps2[0:TC, 0:64], AluOpType.subtract, [vcA, ps2], [d2])
                k.tt(sq2.ap, d2.ap, d2.ap, AluOpType.mult, [d2], [sq2])
                k.op('v', lambda e: e.reduce_sum(scol.ap, sq2.ap, AX.X), reads=[sq2], writes=[scol])
                k.act(scol.ap, scol.ap, AF.Sigmoid, [scol], [scol], scale=float(1.0/(1.0+1e-6)))
                k.ts(NsT.ap, NTA[:, cs64], scol.ap, None, AluOpType.mult, None, [NTA, scol], [NsT])
                k.copy(NTpow[0].ap, NsT.ap, [NsT], [NTpow[0]])
                tr(NsT.ap, NsT, Npow[0].ap, Npow[0])
                build_powers()
                for j in (5, 4, 3, 2, 1, 0):
                    k.mm(ps3[0:TC, 0:64], ident_s[0:TC, 0:TC], Xb.ap, [ident_s, Xb], [ps3],
                         start=True, stop=False)
                    k.mm(ps3[0:TC, 0:64], NTpow[j].ap, Xb.ap, [NTpow[j], Xb], [ps3],
                         start=False, stop=True)
                    k.copy(Xb.ap, ps3[0:TC, 0:64], [ps3], [Xb])
                k.ts(Z.ap, Xb.ap, scol.ap, None, AluOpType.mult, None, [Xb, scol], [Z])
                k.mm(ps2[0:TC, 0:64], qiTA[:, cs64], Hbf.ap, [qiTA, Hbf], [ps2], start=True, stop=False)
                k.mm(ps2[0:TC, 0:64], GtiA[:, cs64], Z.ap, [GtiA, Z], [ps2], start=False, stop=True)
                k.copy(y_buf[:, cs64], ps2[0:TC, 0:64], [ps2], [y_buf])
                k.mm(ps2[0:TC, 0:64], ident_s[0:TC, 0:TC], Hbf.ap, [ident_s, Hbf], [ps2],
                     start=True, stop=False)
                k.mm(ps2[0:TC, 0:64], DmA[:, cs64], Z.ap, [DmA, Z], [ps2], start=False, stop=True)
                k.ts(Hst.ap, ps2[0:TC, 0:64], AeCA[:, c:c+1], None, AluOpType.mult, None,
                     [ps2, AeCA], [Hst])
                k.copy(Hbf.ap, Hst.ap, [Hst], [Hbf])

            for c in range(NCHK):
                k.tt(sq2.ap, y_buf[:, c*64:(c+1)*64], y_buf[:, c*64:(c+1)*64], AluOpType.mult,
                     [y_buf], [sq2])
                k.op('v', lambda e, c=c: e.reduce_sum(msq[:, c:c+1], sq2.ap, AX.X),
                     reads=[sq2], writes=[msq])
            k.ts(msq.ap, msq.ap, 1.0/64.0, None, AluOpType.mult, None, [msq], [msq])
            k.act(msq.ap, msq.ap, AF.Ln, [msq], [msq], bias=1e-6)
            k.act(msq.ap, msq.ap, AF.Exp, [msq], [msq], scale=-0.5)
            for c in range(NCHK):
                k.ts(y_buf[:, c*64:(c+1)*64], y_buf[:, c*64:(c+1)*64], msq[:, c:c+1], None,
                     AluOpType.mult, None, [y_buf, msq], [y_buf])
            k.tt(y_buf.ap, y_buf.ap, nwb.ap, AluOpType.mult, [y_buf, nwb], [y_buf])
            for c in range(NCHK):
                k.copy(Dm.ap, y_buf[:, c*64:(c+1)*64], [y_buf], [Dm])
                tr(Dm.ap, Dm, ynf[:, c*64:(c+1)*64], ynf)
            x3r = x3_d.h.rearrange("(n p) d -> n p d", p=128)
            for tg in range(NT):
                k.dma('qs', xt[tg % 2].ap, xr2[tg], [x2_d], [xt[tg % 2]])
                for dc in range(4):
                    ps = ps1 if dc % 2 == 0 else ps2
                    k.mm(ps[:, 0:512], ynf[:, tg*128:(tg+1)*128], wos[:, dc*512:(dc+1)*512],
                         [ynf, wos], [ps])
                    k.copy(o_t[:, dc*512:(dc+1)*512], ps[:, 0:512], [ps], [o_t])
                emit_ln_tile(k, o_t.ap, o_t, x3_t.ap, x3_t, stats)
                k.tt(x3_t.ap, x3_t.ap, xt[tg % 2].ap, AluOpType.add, [x3_t, xt[tg % 2]], [x3_t])
                k.dma('qs', x3r[tg], x3_t.ap, [x3_t], [x3_d])
            k.phase_boundary()

    with ExitStack() as ph:
        xt = [k.sbuf(ph, f"h_xt{i}", [128, D], F32) for i in range(2)]
        stage = k.sbuf(ph, "h_stage", [128, D], BF16)
        ln7f = k.sbuf(ph, "h_ln7f", [128, ND*SEG], BF16)
        y_acc = k.sbuf(ph, "h_yacc", [128, NT*D], F32)
        hbuf = k.sbuf(ph, "h_hbuf", [128, NOC_M*SEG], BF16)
        wb1 = [k.sbuf(ph, f"h_wb1{i}", [128, ND*128], BF16) for i in range(2)]
        wb2 = [k.sbuf(ph, f"h_wb2{i}", [128, ND*128], BF16) for i in range(2)]
        wpb = [k.sbuf(ph, f"h_wpb{i}", [128, NOC_M*512], BF16) for i in range(2)]
        gw_s = k.sbuf(ph, "h_gw", [128, ND*16], BF16)
        router = k.sbuf(ph, "h_router", [128, NT*16], F32)
        comb = k.sbuf(ph, "h_comb", [128, NT*16], F32)
        rst = k.sbuf(ph, "h_rst", [128, 8], F32)
        gs = k.sbuf(ph, "h_gs", [128, SEG], BF16)
        us = k.sbuf(ph, "h_us", [128, SEG], BF16)
        o_t = k.sbuf(ph, "h_ot", [128, D], F32)
        psA = k.psum(ph, "h_psA", [128, 512], F32)
        psB = k.psum(ph, "h_psB", [128, 512], F32)
        psC = k.psum(ph, "h_psC", [128, 512], F32)
        psD = k.psum(ph, "h_psD", [128, 512], F32)
        psT = k.psum(ph, "h_psT", [128, 1024], BF16)

        k.dma('qg', gw_s.h[:, :].rearrange("p (t c) -> p t c", c=16),
              gw_in.h.rearrange("(t p) c -> p t c", p=128), [gw_in], [gw_s])
        xr3 = x3_d.h.rearrange("(n p) d -> n p d", p=128)
        for i in range(NT):
            k.dma('qs', xt[i % 2].ap, xr3[i], [x3_d], [xt[i % 2]])
            emit_ln_tile(k, xt[i % 2].ap, xt[i % 2], stage.ap, stage, stats)
            for dt in range(ND):
                k.trans(psT[:, 0:128], stage[:, dt*128:(dt+1)*128], ident_s.ap, [stage, ident_s], [psT])
                k.copy(ln7f[:, dt*SEG + i*128: dt*SEG + (i+1)*128], psT[:, 0:128], [psT], [ln7f])
        for tg in range(NT):
            ps = psA if tg % 2 == 0 else psB
            for dt in range(ND):
                k.mm(ps[:, 0:16], ln7f[:, dt*SEG + tg*128: dt*SEG + (tg+1)*128],
                     gw_s[:, dt*16:(dt+1)*16], [ln7f, gw_s], [ps],
                     start=(dt == 0), stop=(dt == ND-1))
            k.copy(router[:, tg*16:(tg+1)*16], ps[:, 0:16], [ps], [router])
        for tg in range(NT):
            rsl = router[:, tg*16:tg*16+8]
            csl = comb[:, tg*16:tg*16+8]
            k.act(comb[:, tg*16+8:tg*16+9], router[:, tg*16+8:tg*16+9], AF.Sigmoid, [router], [comb])
            k.op('v', lambda e, rsl=rsl: e.reduce_max(rst[:, 0:1], rsl, AX.X), reads=[router], writes=[rst])
            k.ts(rst[:, 1:2], rst[:, 0:1], -1.0, None, AluOpType.mult, None, [rst], [rst])
            k.act(csl, rsl, AF.Exp, [router, rst], [comb], bias=rst[:, 1:2])
            k.op('v', lambda e, csl=csl: e.reduce_sum(rst[:, 2:3], csl, AX.X), reads=[comb], writes=[rst])
            k.op('v', lambda e: e.reciprocal(rst[:, 3:4], rst[:, 2:3]), reads=[rst], writes=[rst])
            k.ts(csl, csl, rst[:, 3:4], None, AluOpType.mult, None, [comb, rst], [comb])
            k.op('v', lambda e, csl=csl: e.reduce_max(rst[:, 4:5], csl, AX.X), reads=[comb], writes=[rst])
            k.ts(router[:, tg*16:tg*16+8], csl, rst[:, 4:5], None, AluOpType.is_lt, None,
                 [comb, rst], [router])
            k.tt(router[:, tg*16:tg*16+8], router[:, tg*16:tg*16+8], csl, AluOpType.mult,
                 [router, comb], [router])
            k.op('v', lambda e, tg=tg: e.reduce_max(rst[:, 5:6], router[:, tg*16:tg*16+8], AX.X),
                 reads=[router], writes=[rst])
            k.ts(router[:, tg*16+8:tg*16+16], csl, rst[:, 5:6], None, AluOpType.is_ge, None,
                 [comb, rst], [router])
            k.tt(csl, csl, router[:, tg*16+8:tg*16+16], AluOpType.mult, [comb, router], [comb])
        k.memset(y_acc.ap, 0.0, [], [y_acc])
        er1 = efc1_in.h.rearrange("e (t p) c -> e p t c", p=128)
        er2 = efc2_in.h.rearrange("e (t p) c -> e p t c", p=128)
        erp = eprj_in.h.rearrange("e (t p) c -> e p t c", p=128)
        for e in range(NE):
            for oc in range(NOC_M):
                b1, b2 = wb1[oc % 2], wb2[oc % 2]
                k.dma('qg', b1.h[:, :].rearrange("p (t c) -> p t c", c=128),
                      er1[e][:, :, oc*128:(oc+1)*128], [efc1_in], [b1])
                k.dma('qg', b2.h[:, :].rearrange("p (t c) -> p t c", c=128),
                      er2[e][:, :, oc*128:(oc+1)*128], [efc2_in], [b2])
                for n in range(2):
                    psg = psA if n == 0 else psB
                    psu = psC if n == 0 else psD
                    for dt in range(ND):
                        k.mm(psg[:, 0:512], b1[:, dt*128:(dt+1)*128],
                             ln7f[:, dt*SEG + n*512: dt*SEG + (n+1)*512], [b1, ln7f], [psg],
                             start=(dt == 0), stop=(dt == ND-1))
                    for dt in range(ND):
                        k.mm(psu[:, 0:512], b2[:, dt*128:(dt+1)*128],
                             ln7f[:, dt*SEG + n*512: dt*SEG + (n+1)*512], [b2, ln7f], [psu],
                             start=(dt == 0), stop=(dt == ND-1))
                    k.act(gs[:, n*512:(n+1)*512], psg[:, 0:512], AF.Silu, [psg], [gs])
                    k.copy(us[:, n*512:(n+1)*512], psu[:, 0:512], [psu], [us])
                k.tt(hbuf[:, oc*SEG:(oc+1)*SEG], gs.ap, us.ap, AluOpType.mult, [gs, us], [hbuf])
            for dc in range(4):
                bp = wpb[dc % 2]
                k.dma('qg', bp.h[:, :].rearrange("p (t c) -> p t c", c=512),
                      erp[e][:, :, dc*512:(dc+1)*512], [eprj_in], [bp])
                for tg in range(NT):
                    ps = psA if tg % 2 == 0 else psB
                    for ht in range(NOC_M):
                        k.mm(ps[:, 0:512], hbuf[:, ht*SEG + tg*128: ht*SEG + (tg+1)*128],
                             bp[:, ht*512:(ht+1)*512], [hbuf, bp], [ps],
                             start=(ht == 0), stop=(ht == NOC_M-1))
                    k.stt(y_acc[:, tg*D + dc*512: tg*D + (dc+1)*512], ps[:, 0:512],
                          comb[:, tg*16+e:tg*16+e+1], y_acc[:, tg*D + dc*512: tg*D + (dc+1)*512],
                          AluOpType.mult, AluOpType.add, [ps, comb, y_acc], [y_acc])
        sr1 = sfc1_in.h.rearrange("(t p) c -> p t c", p=128)
        sr2 = sfc2_in.h.rearrange("(t p) c -> p t c", p=128)
        srp = sprj_in.h.rearrange("(t p) c -> p t c", p=128)
        for qtr in range(4):
            for ocq in range(NOC_M):
                oc = qtr*NOC_M + ocq
                b1, b2 = wb1[ocq % 2], wb2[ocq % 2]
                k.dma('qg', b1.h[:, :].rearrange("p (t c) -> p t c", c=128),
                      sr1[:, :, oc*128:(oc+1)*128], [sfc1_in], [b1])
                k.dma('qg', b2.h[:, :].rearrange("p (t c) -> p t c", c=128),
                      sr2[:, :, oc*128:(oc+1)*128], [sfc2_in], [b2])
                for n in range(2):
                    psg = psA if n == 0 else psB
                    psu = psC if n == 0 else psD
                    for dt in range(ND):
                        k.mm(psg[:, 0:512], b1[:, dt*128:(dt+1)*128],
                             ln7f[:, dt*SEG + n*512: dt*SEG + (n+1)*512], [b1, ln7f], [psg],
                             start=(dt == 0), stop=(dt == ND-1))
                    for dt in range(ND):
                        k.mm(psu[:, 0:512], b2[:, dt*128:(dt+1)*128],
                             ln7f[:, dt*SEG + n*512: dt*SEG + (n+1)*512], [b2, ln7f], [psu],
                             start=(dt == 0), stop=(dt == ND-1))
                    k.act(gs[:, n*512:(n+1)*512], psg[:, 0:512], AF.Silu, [psg], [gs])
                    k.copy(us[:, n*512:(n+1)*512], psu[:, 0:512], [psu], [us])
                k.tt(hbuf[:, ocq*SEG:(ocq+1)*SEG], gs.ap, us.ap, AluOpType.mult, [gs, us], [hbuf])
            for dc in range(4):
                bp = wpb[dc % 2]
                k.dma('qg', bp.h[:, :].rearrange("p (t c) -> p t c", c=512),
                      srp[:, qtr*NOC_M:(qtr+1)*NOC_M, dc*512:(dc+1)*512], [sprj_in], [bp])
                for tg in range(NT):
                    ps = psA if tg % 2 == 0 else psB
                    for ht in range(NOC_M):
                        k.mm(ps[:, 0:512], hbuf[:, ht*SEG + tg*128: ht*SEG + (tg+1)*128],
                             bp[:, ht*512:(ht+1)*512], [hbuf, bp], [ps],
                             start=(ht == 0), stop=(ht == NOC_M-1))
                    k.stt(y_acc[:, tg*D + dc*512: tg*D + (dc+1)*512], ps[:, 0:512],
                          comb[:, tg*16+8:tg*16+9], y_acc[:, tg*D + dc*512: tg*D + (dc+1)*512],
                          AluOpType.mult, AluOpType.add, [ps, comb, y_acc], [y_acc])
        outr = out_out.h.rearrange("(n p) d -> n p d", p=128)
        for tg in range(NT):
            k.dma('qs', xt[tg % 2].ap, xr3[tg], [x3_d], [xt[tg % 2]])
            emit_ln_tile(k, y_acc[:, tg*D:(tg+1)*D], y_acc, o_t.ap, o_t, stats)
            k.tt(o_t.ap, o_t.ap, xt[tg % 2].ap, AluOpType.add, [o_t, xt[tg % 2]], [o_t])
            k.dma('qs', outr[tg], o_t.ap, [o_t], [out_out])

    if debug:
        dbg1 = k.dram_out("dbg_x1", [SEG, D], F32)
        dbg2 = k.dram_out("dbg_x2", [SEG, D], F32)
        dbg3 = k.dram_out("dbg_x3", [SEG, D], F32)
        k.dma('qs', dbg1.h, x1_d.h, [x1_d], [dbg1])
        k.dma('qs', dbg2.h, x2_d.h, [x2_d], [dbg2])
        k.dma('qs', dbg3.h, x3_d.h, [x3_d], [dbg3])
    nc = k.end()
    return nc, k


_BUILT = {}

def _prep_maps(I, k):
    t = host_tables()
    x = I['L0_attn_x'][0]
    wqkvT = np.ascontiguousarray(np.asarray(I['L0_attn_w_qkv']).T)
    b = np.asarray(I['L0_attn_b_qkv'])
    bqkv_cols = np.ascontiguousarray(b[:2*D].reshape(2*ND, 128).T)
    bqkv_cols = np.concatenate([bqkv_cols, np.zeros((128, ND), np.float32)], axis=1)
    bv_b = np.broadcast_to(b[2*D:][None, :], (128, D)).copy()
    woT = np.ascontiguousarray(np.asarray(I['L0_attn_w_out']).T)
    bo_b = np.broadcast_to(np.asarray(I['L0_attn_b_out'])[None, :], (128, D)).copy()
    wguT = np.ascontiguousarray(np.asarray(I['L0_ffn_w_gate_up']).T)
    wpT = np.ascontiguousarray(np.asarray(I['L0_ffn_w_proj']).T)
    wkvqaT = np.ascontiguousarray(np.concatenate(
        [np.asarray(I['L1_attn_Wk']).T, np.asarray(I['L1_attn_Wv']).T,
         np.asarray(I['L1_attn_Wq']).T, np.asarray(I['L1_attn_Wa_w']).T], axis=1))
    wo1T = np.ascontiguousarray(np.asarray(I['L1_attn_Wo']).T)
    nwb = np.tile(np.asarray(I['L1_attn_norm_w'])[None, :], (64, NCHK)).astype(np.float32)
    gate9 = np.zeros((D, 16), np.float32)
    gate9[:, 0:8] = np.asarray(I['L1_ffn_gate_w']).T
    gate9[:, 8:9] = np.asarray(I['L1_ffn_shared_gate_w']).T
    efc1T = np.ascontiguousarray(np.asarray(I['L1_ffn_expert_fc1']).transpose(0, 2, 1))
    efc2T = np.ascontiguousarray(np.asarray(I['L1_ffn_expert_fc2']).transpose(0, 2, 1))
    eprjT = np.ascontiguousarray(np.asarray(I['L1_ffn_expert_proj']).transpose(0, 2, 1))
    sfc1T = np.ascontiguousarray(np.asarray(I['L1_ffn_shared_fc1']).T)
    sfc2T = np.ascontiguousarray(np.asarray(I['L1_ffn_shared_fc2']).T)
    sprjT = np.ascontiguousarray(np.asarray(I['L1_ffn_shared_proj']).T)
    maps = []
    for c in range(NCORE):
        sw = np.zeros((NCORE, H), np.float32)
        for j in range(c):
            sw[j] = LAM_SEG ** (c - 1 - j)
        dnsel = np.zeros((64, 16), np.float32)
        for j in range(NCORE - 1):
            sel = 1.0 if j < c else 0.0
            dnsel[:, j] = sel
            dnsel[:, 8+j] = 1.0 - sel
        maps.append({
            "consts_pool": k.const_array(),
            "x_seg": np.ascontiguousarray(x[c*SEG:(c+1)*SEG]),
            "wqkvT": wqkvT, "bqkv_cols": bqkv_cols, "bv_b": bv_b,
            "woT": woT, "bo_b": bo_b,
            "sw": np.broadcast_to(sw.reshape(1, -1), (128, NCORE*H)).copy(),
            "wguT_sh": np.ascontiguousarray(wguT[:, c*GU_SH:(c+1)*GU_SH]),
            "wpT_sh": np.ascontiguousarray(wpT[:, c*P_SH:(c+1)*P_SH]),
            "wkvqaT": wkvqaT,
            "wa_b": np.asarray(I['L1_attn_Wa_b']).reshape(64, 1),
            "lam": np.asarray(I['L1_attn_lam']).reshape(64, 1),
            "dnsel": dnsel, "nwb": nwb, "wo1T": wo1T,
            "gate9T": gate9,
            "efc1T": efc1T, "efc2T": efc2T, "eprjT": eprjT,
            "sfc1T": sfc1T, "sfc2T": sfc2T, "sprjT": sprjT,
            "ident_bf": t['ident_bf'], "ident_f32": t['ident_f32'],
            "dq_b": t['dq_b'], "dk_col": t['dk_col'], "dmaskT": t['dmaskT'],
            "tli_f32": t['tli_f32'], "strilneg_bf": t['strilneg_bf'],
            "striuneg_bf": t['striuneg_bf'], "striu_bf": t['striu_bf'],
            "striuincl_bf": t['striuincl_bf'],
            "ones64c_f32": t['ones64c_f32'], "ones64r_f32": t['ones64r_f32'],
            "ones128c_bf": t['ones128c_bf'], "ones1r_f32": t['ones1r_f32'],
            "gmask": t['gmask'], "gmask2f": t['gmask2f'], "gmask2b": t['gmask2b'],
        })
    return maps


def kernel(**inputs):
    if "fused" not in _BUILT:
        _BUILT["fused"] = build_fused()
    nc, kk = _BUILT["fused"]
    maps = _prep_maps(inputs, kk)
    res = run_bass_kernel_spmd(nc, maps, core_ids=list(range(NCORE)))
    out = np.concatenate([np.asarray(res.results[c]["out_seg"]) for c in range(NCORE)], 0)
    return out.reshape(1, N_TOK, D).astype(np.float32)
